# version 04
# baseline (speedup 1.0000x reference)
import numpy as np
import concourse.bass as bass
import concourse.mybir as mybir
from concourse.bass_utils import run_bass_kernel_spmd

F32 = mybir.dt.float32
AF = mybir.ActivationFunctionType
ALU = mybir.AluOpType
AX = mybir.AxisListType

TOK = 2048
NCH = 16
D = 1024
FC = 8
INTER = 4096
NG = 4
CPG = 4
CTOT = 16384.0
EPS_LN = 1e-5
EPS_AT = 1e-6


def build_core():
    nc = bass.Bass("TRN2", target_bir_lowering=False)
    dt = lambda n, s: nc.dram_tensor(n, s, F32, kind="ExternalInput").ap()
    x_d = dt("x", [TOK, D])
    wq_d = dt("wq", [D, D]); wk_d = dt("wk", [D, D]); wv_d = dt("wv", [D, D])
    bq_d = dt("bq", [1, D]); bk_d = dt("bk", [1, D])
    bv_d = dt("bv_row", [1, D]); bv8_d = dt("bv8", [8, 128])
    gam_d = dt("gamma", [1, 1])
    w1_d = dt("w1", [D, INTER]); b1_d = dt("b1_32", [32, 128])
    w2_d = dt("w2", [INTER, D]); b2_d = dt("b2", [1, D])
    id_d = dt("ident", [128, 128])
    y_d = nc.dram_tensor("y", [TOK, D], F32, kind="ExternalOutput").ap()
    ccin_h = nc.dram_tensor("ccin", [1025, 1024], F32)
    ccout_h = nc.dram_tensor("ccout", [1025, 1024], F32)
    ccin, ccout = ccin_h.ap(), ccout_h.ap()
    xnt_d = nc.dram_tensor("xnt", [NCH, 128, D], F32).ap()

    import contextlib
    with contextlib.ExitStack() as stack:
        en = stack.enter_context
        big1 = en(nc.sbuf_tensor("big1", [128, 8192], F32))
        big2 = en(nc.sbuf_tensor("big2", [128, 8192], F32))
        big3 = en(nc.sbuf_tensor("big3", [128, 8192], F32))
        big4 = en(nc.sbuf_tensor("big4", [128, 8192], F32))
        util = en(nc.sbuf_tensor("util", [128, 8192], F32))
        w2st = en(nc.sbuf_tensor("w2st", [128, 2048], F32))
        arena = en(nc.sbuf_tensor("arena", [128, 8192], F32))
        cols = en(nc.sbuf_tensor("cols", [128, 64], F32))
        mcol = en(nc.sbuf_tensor("mcol", [128, 128], F32))
        bcast = en(nc.sbuf_tensor("bcast", [128, 1024], F32))
        ident = en(nc.sbuf_tensor("ident_sb", [128, 128], F32))
        ones = en(nc.sbuf_tensor("ones", [128, 128], F32))
        gsc = en(nc.sbuf_tensor("gsc", [128, 8], F32))
        junk = en(nc.sbuf_tensor("junk", [128, 512], F32))
        ps = [en(nc.psum_tensor(f"ps{i}", [128, 512], F32)) for i in range(8)]
        ps4, ps5, ps6, ps7 = ps[4], ps[5], ps[6], ps[7]
        sLD = en(nc.semaphore("sLD"))
        sSP = en(nc.semaphore("sSP"))
        sMM = en(nc.semaphore("sMM"))
        sVE = en(nc.semaphore("sVE"))
        sAC = en(nc.semaphore("sAC"))
        sCC = en(nc.semaphore("sCC"))
        blk = en(nc.Block())
        stA = lambda b: util[:, (0 + b) * 1024:(1 + b) * 1024]
        stB = lambda b: util[:, (2 + b) * 1024:(3 + b) * 1024]
        stC = lambda b: util[:, (4 + b) * 1024:(5 + b) * 1024]
        stD = lambda b: util[:, (6 + b) * 1024:(7 + b) * 1024]
        x1nT = util[:, 0:4096]
        tmp56 = util[:, 0:2048]
        ar = lambda i: arena[0:1, i * 1024:(i + 1) * 1024]
        arh = lambda i, h: arena[0:1, i * 1024 + h * 512: i * 1024 + (h + 1) * 512]
        arsl = lambda i, m: arena[0:1, i * 1024 + m * 128: i * 1024 + (m + 1) * 128]
        cs_col = cols[:, 0:8]; kinv_col = cols[:, 8:16]
        bv_col = cols[:, 16:24]; bvk_col = cols[:, 56:64]
        ones_col = ones[:, 0:1]; ones_row = ones[0:1, 0:128]

        E = {k: [] for k in ("sync", "tensor", "vector", "scalar", "gpsimd")}
        cnt = {"LD": [sLD, 16, 0], "SP": [sSP, 16, 0], "MM": [sMM, 1, 0],
               "VE": [sVE, 1, 0], "AC": [sAC, 1, 0], "CC": [sCC, 1, 0]}
        waits_seen = []

        def emit(eng, f, waits=(), inc_name=None):
            waits = list(waits)
            auto = {"vector": "VE", "scalar": "AC"}.get(eng)
            if auto is not None:
                if inc_name is None:
                    inc_name = auto
                waits.append((auto, cnt[auto][2]))
            if inc_name in ("LD", "SP"):
                waits.append((inc_name, cnt[inc_name][2]))
            wl = [(cnt[w][0], v, w) for (w, v) in waits if v]
            for (_, v, w) in wl:
                waits_seen.append((w, v))
            step = 0
            sem = None
            if inc_name:
                c = cnt[inc_name]
                c[2] += c[1]
                step = c[1]
                sem = c[0]
            tgt = cnt[inc_name][2] if inc_name else None

            def g(eng_obj, f=f, wl=wl, sem=sem, step=step):
                for (s, v, _) in wl:
                    eng_obj.wait_ge(s, v)
                ins = f(eng_obj)
                if sem is not None:
                    ins.then_inc(sem, step)
            E[eng].append(g)
            return tgt

        emit("vector", lambda e: e.memset(ones[:, :], 1.0))
        ve_init = emit("vector", lambda e: e.memset(gsc[:, 2:3], EPS_LN), inc_name="VE")
        eps_col = gsc[:, 2:3]
        ld_ident = emit("sync", lambda e: e.dma_start(out=ident[:, :], in_=id_d), inc_name="LD")
        ld_bq = emit("sync", lambda e: e.dma_start(out=ar(0), in_=bq_d), inc_name="LD")
        ld_bk = emit("sync", lambda e: e.dma_start(out=ar(1), in_=bk_d), inc_name="LD")
        ld_gam = emit("sync", lambda e: e.dma_start(out=gsc[0:1, 0:1], in_=gam_d), inc_name="LD")
        ld_bv8 = emit("sync", lambda e: e.dma_start(out=big2[0:8, 0:128], in_=bv8_d), inc_name="LD")
        ld_b132 = emit("sync", lambda e: e.dma_start(out=big2[0:32, 128:256], in_=b1_d), inc_name="LD")
        t_bv = emit("tensor", lambda e: e.transpose(ps6[:, 0:8], big2[0:8, 0:128], ident[0:8, 0:8]),
                    waits=[("LD", ld_bv8), ("LD", ld_ident)], inc_name="MM")
        t_b1 = emit("tensor", lambda e: e.transpose(ps7[:, 0:32], big2[0:32, 128:256], ident[0:32, 0:32]),
                    waits=[("LD", ld_b132)], inc_name="MM")
        v_bv = emit("vector", lambda e: e.tensor_copy(bv_col, ps6[:, 0:8]), waits=[("MM", t_bv)],
                    inc_name="VE")
        v_b1 = emit("vector", lambda e: e.tensor_copy(cols[:, 24:56], ps7[:, 0:32]),
                    waits=[("MM", t_b1)], inc_name="VE")

        x_ld = [0] * NCH
        xn_ac = [0] * NCH
        sp_done = [0] * NCH
        ev_done = [0] * NCH
        cs_mm_c = [0] * NCH
        tr_ev = {0: v_bv, 1: v_b1}
        for c in range(NCH):
            w = [("AC", xn_ac[c - 2])] if c >= 2 else []
            x_ld[c] = emit("sync", lambda e, c=c: e.dma_start(
                out=stC(c % 2), in_=x_d[c * 128:(c + 1) * 128, :]), waits=w, inc_name="LD")
            b = (c % 2) * 16
            sum_c = mcol[:, b + 0:b + 1]; sq0 = mcol[:, b + 1:b + 2]; sq1 = mcol[:, b + 2:b + 3]
            mean_c = mcol[:, b + 3:b + 4]; msq = mcol[:, b + 4:b + 5]; var_c = mcol[:, b + 5:b + 6]
            std_c = mcol[:, b + 6:b + 7]
            rstd_c = mcol[:, 32 + c:33 + c]
            negmr_c = mcol[:, 48 + c:49 + c]
            emit("vector", lambda e, s=sum_c, c=c: e.reduce_sum(s, stC(c % 2), axis=AX.X),
                 waits=[("LD", x_ld[c])])
            emit("scalar", lambda e, s=sq0, c=c: e.activation(
                junk[:, :], stC(c % 2)[:, 0:512], AF.Square, accum_out=s),
                waits=[("LD", x_ld[c])], inc_name="AC")
            ac2 = emit("scalar", lambda e, s=sq1, c=c: e.activation(
                junk[:, :], stC(c % 2)[:, 512:1024], AF.Square, accum_out=s), inc_name="AC")
            emit("vector", lambda e, m=mean_c, s=sum_c: e.tensor_scalar(
                m, s, 1.0 / D, None, ALU.mult, ALU.bypass), waits=[("AC", ac2)])
            emit("vector", lambda e, q=sq0, q1=sq1, o=msq: e.tensor_tensor(o, q, q1, ALU.add))
            emit("vector", lambda e, m=mean_c, o=var_c: e.scalar_tensor_tensor(
                o, m, -1.0, m, ALU.mult, ALU.mult))
            v_var = emit("vector", lambda e, q=msq, o=var_c: e.scalar_tensor_tensor(
                o, q, 1.0 / D, o, ALU.mult, ALU.add))
            ac3 = emit("scalar", lambda e, v=var_c, o=std_c: e.activation(
                o, v, AF.Sqrt, bias=eps_col),
                waits=([("VE", ve_init), ("VE", v_var)] if c == 0 else [("VE", v_var)]),
                inc_name="AC")
            emit("vector", lambda e, o=rstd_c, i=std_c: e.reciprocal(o, i), waits=[("AC", ac3)])
            st_ve = emit("vector", lambda e, o=negmr_c, m=mean_c, r=rstd_c: e.scalar_tensor_tensor(
                o, m, -1.0, r, ALU.mult, ALU.mult), inc_name="VE")
            w = [("VE", st_ve)]
            if c >= 2:
                w.append(("MM", cs_mm_c[c - 2]))
            xn_ac[c] = emit("scalar", lambda e, c=c, r=rstd_c, n=negmr_c: e.activation(
                stA(c % 2), stC(c % 2), AF.Identity, bias=n, scale=r), waits=w, inc_name="AC")
            for r in range(FC):
                pt = ps[6 + (r % 2)]
                w = [("AC", xn_ac[c]), ("VE", tr_ev[r % 2])]
                m = emit("tensor", lambda e, p=pt, c=c, r=r: e.transpose(
                    p[:, 0:128], stA(c % 2)[:, r * 128:(r + 1) * 128], ident[:, :]),
                    waits=w, inc_name="MM")
                wv = [("MM", m)]
                if r == 0 and c >= 2:
                    wv.append(("SP", sp_done[c - 2]))
                ev = emit("vector", lambda e, p=pt, c=c, r=r: e.tensor_copy(
                    stB(c % 2)[:, r * 128:(r + 1) * 128], p[:, 0:128]), waits=wv, inc_name="VE")
                tr_ev[r % 2] = ev
            ev_done[c] = ev
            emit("tensor", lambda e, c=c: e.matmul(
                ps4[0:1, :], ones_col, stA(c % 2)[:, 0:512],
                start=(c == 0), stop=(c == NCH - 1)), inc_name="MM")
            cs_mm_c[c] = emit("tensor", lambda e, c=c: e.matmul(
                ps5[0:1, :], ones_col, stA(c % 2)[:, 512:1024],
                start=(c == 0), stop=(c == NCH - 1)), inc_name="MM")
            sp_done[c] = emit("gpsimd", lambda e, c=c: e.dma_start(
                out=xnt_d[c], in_=stB(c % 2)), waits=[("VE", ev_done[c])], inc_name="SP")

        emit("vector", lambda e: e.tensor_copy(arh(6, 0), ps4[0:1, :]),
             waits=[("MM", cs_mm_c[NCH - 1])])
        cs_ve = emit("vector", lambda e: e.tensor_copy(arh(6, 1), ps5[0:1, :]), inc_name="VE")

        s_evacs = []
        last_apply = {0: xn_ac[NCH - 2], 1: xn_ac[NCH - 1]}
        last_s_mm = {0: 0, 1: 0}
        stA_free = {0: cs_mm_c[NCH - 2], 1: cs_mm_c[NCH - 1]}
        for p_i in range(2):
            for c in range(NCH):
                lx = emit("sync", lambda e, c=c: e.dma_start(
                    out=stC(c % 2), in_=x_d[c * 128:(c + 1) * 128, :]),
                    waits=[("AC", last_apply[c % 2])], inc_name="LD")
                aply = emit("scalar", lambda e, c=c: e.activation(
                    stA(c % 2), stC(c % 2), AF.Identity,
                    bias=mcol[:, 48 + c:49 + c], scale=mcol[:, 32 + c:33 + c]),
                    waits=[("LD", lx), ("MM", stA_free[c % 2])], inc_name="AC")
                last_apply[c % 2] = aply
                wmm = [("AC", aply)]
                if c == 0 and p_i == 0:
                    wmm += [("VE", cs_ve), ("VE", ev_done[NCH - 1])]
                if c == 0 and p_i == 1:
                    wmm.append(("VE", s_evacs[-1]))
                for m in range(4):
                    for h in range(2):
                        mm = emit("tensor", lambda e, m=m, h=h, c=c, p_i=p_i: e.matmul(
                            ps[2 * m + h][:, :],
                            stA(c % 2)[:, (m + 4 * p_i) * 128:(m + 4 * p_i + 1) * 128],
                            stA(c % 2)[:, h * 512:(h + 1) * 512],
                            start=(c == 0), stop=(c == NCH - 1)),
                            waits=(wmm if (m == 0 and h == 0) else []), inc_name="MM")
                last_s_mm[c % 2] = mm
                stA_free[c % 2] = mm
            for m in range(4):
                for h in range(2):
                    mg = m + 4 * p_i
                    sev = emit("vector", lambda e, m=m, h=h, mg=mg: e.tensor_copy(
                        big1[:, mg * 1024 + h * 512: mg * 1024 + (h + 1) * 512],
                        ps[2 * m + h][:, :]),
                        waits=([("MM", mm)] if (m == 0 and h == 0) else []), inc_name="VE")
                    s_evacs.append(sev)

        spx = 0
        for r in range(FC):
            spx = emit("gpsimd", lambda e, r=r: e.dma_start(
                out=ccin[r * 128:(r + 1) * 128, :], in_=big1[:, r * 1024:(r + 1) * 1024]),
                waits=([("VE", s_evacs[-1]), ("VE", cs_ve)] if r == 0 else []), inc_name="SP")
        spx = emit("gpsimd", lambda e: e.dma_start(out=ccin[1024:1025, :], in_=ar(6)),
                   inc_name="SP")
        cc_t = emit("gpsimd", lambda e: e.collective_compute(
            "AllReduce", ALU.add, replica_groups=[list(range(8))],
            ins=[ccin_h.ap().opt()], outs=[ccout_h.ap().opt()]),
            waits=[("SP", spx)], inc_name="CC")
        for r in range(FC):
            emit("sync", lambda e, r=r: e.dma_start(
                out=big1[:, r * 1024:(r + 1) * 1024], in_=ccout[r * 128:(r + 1) * 128, :]),
                waits=([("CC", cc_t)] if r == 0 else []), inc_name="LD")
        gl_ld = emit("sync", lambda e: e.dma_start(out=ar(6), in_=ccout[1024:1025, :]),
                     inc_name="LD")
        wq_ld = 0
        for r in range(FC):
            wq_ld = emit("sync", lambda e, r=r: e.dma_start(
                out=big3[:, r * 1024:(r + 1) * 1024], in_=wq_d[r * 128:(r + 1) * 128, :]),
                inc_name="LD")

        def transpose_1024(src, dst, wait_first):
            evs = []
            for rr in range(FC):
                for cc2 in range(FC):
                    i = rr * FC + cc2
                    pt = ps[i % 4]
                    w = list(wait_first) if i == 0 else []
                    if i >= 4:
                        w.append(("VE", evs[i - 4]))
                    m = emit("tensor", lambda e, p=pt, rr=rr, cc2=cc2, src=src: e.transpose(
                        p[:, 0:128], src[:, rr * 1024 + cc2 * 128: rr * 1024 + (cc2 + 1) * 128],
                        ident[:, :]), waits=w, inc_name="MM")
                    ev = emit("vector", lambda e, p=pt, rr=rr, cc2=cc2, dst=dst: e.tensor_copy(
                        dst[:, cc2 * 1024 + rr * 128: cc2 * 1024 + (rr + 1) * 128], p[:, 0:128]),
                        waits=[("MM", m)], inc_name="VE")
                    evs.append(ev)
            return m, ev

        def mm1024(lhsT_fn, rhs_fn, evac_fn, wait_first, extra_rank1=()):
            evs = {}
            lastmm = 0
            for m in range(FC):
                for h in range(2):
                    i = m * 2 + h
                    pt = ps[i % 4]
                    w = list(wait_first) if i == 0 else []
                    if i >= 4:
                        w.append(("VE", evs[i - 4]))
                    nr = len(extra_rank1)
                    for k in range(FC):
                        lastmm = emit("tensor", lambda e, p=pt, k=k, m=m, h=h: e.matmul(
                            p[:, :], lhsT_fn(k, m), rhs_fn(k, h),
                            start=(k == 0), stop=(k == FC - 1 and nr == 0)),
                            waits=(w if k == 0 else []), inc_name="MM")
                    for j, (lf, rf) in enumerate(extra_rank1):
                        lastmm = emit("tensor", lambda e, p=pt, lf=lf, rf=rf, m=m, h=h,
                                      st=(j == nr - 1): e.matmul(
                            p[:, :], lf(m), rf(h), start=False, stop=st), inc_name="MM")
                    evs[i] = evac_fn(m, h, pt, lastmm)
            return lastmm, evs[15]

        def ev_to(dst):
            def f(m, h, pt, lastmm):
                return emit("vector", lambda e, m=m, h=h, p=pt: e.tensor_copy(
                    dst[:, m * 1024 + h * 512: m * 1024 + (h + 1) * 512], p[:, :]),
                    waits=[("MM", lastmm)], inc_name="VE")
            return f

        def matvec_row(rhs_src, dst_slot, wait_first, lhs_fn):
            last_ev = 0
            for h in range(2):
                for k in range(FC):
                    m = emit("tensor", lambda e, k=k, h=h: e.matmul(
                        ps[4 + h][0:1, :], lhs_fn(k),
                        rhs_src[:, k * 1024 + h * 512: k * 1024 + (h + 1) * 512],
                        start=(k == 0), stop=(k == FC - 1)),
                        waits=(list(wait_first) if (h == 0 and k == 0) else []), inc_name="MM")
                last_ev = emit("vector", lambda e, h=h, m=m, d=dst_slot: e.tensor_copy(
                    arh(d, h), ps[4 + h][0:1, :]), waits=[("MM", m)], inc_name="VE")
            return last_ev

        def row_transpose8(src_row, dst_col8, wait_first):
            last = 0
            for k in range(FC):
                w = list(wait_first) if k == 0 else [("VE", last)]
                m = emit("tensor", lambda e, k=k, s=src_row: e.transpose(
                    ps[4][:, 0:1], s[0:1, k * 128:(k + 1) * 128], ident[0:1, 0:1]),
                    waits=w, inc_name="MM")
                last = emit("vector", lambda e, k=k, m=m, d=dst_col8: e.tensor_copy(
                    d[:, k:k + 1], ps[4][:, 0:1]), waits=[("MM", m)], inc_name="VE")
            return last

        csc_ev = row_transpose8(ar(6), cs_col, [("LD", gl_ld), ("VE", s_evacs[-1])])
        _, wqt_ev = transpose_1024(big3, big2, [("LD", wq_ld), ("VE", s_evacs[-1])])
        swqt_mm, swqt_ev = mm1024(
            lambda k, m: big1[:, k * 1024 + m * 128: k * 1024 + (m + 1) * 128],
            lambda k, h: big2[:, k * 1024 + h * 512: k * 1024 + (h + 1) * 512],
            ev_to(big4), [("VE", wqt_ev)])
        wqcs_ev = matvec_row(big2, 4, [("VE", csc_ev), ("VE", swqt_ev)], lambda k: cs_col[:, k:k + 1])
        pq_ev = 0
        for r in range(FC):
            pq_ev = emit("vector", lambda e, r=r: e.tensor_tensor(
                big4[:, r * 1024:(r + 1) * 1024], big4[:, r * 1024:(r + 1) * 1024],
                big2[:, r * 1024:(r + 1) * 1024], ALU.mult),
                waits=([("VE", swqt_ev)] if r == 0 else []), inc_name="VE")
        qsq_ev = matvec_row(big4, 2, [("VE", pq_ev), ("VE", wqcs_ev)], lambda k: ones_col)
        wk_ld = 0
        for r in range(FC):
            wk_ld = emit("sync", lambda e, r=r: e.dma_start(
                out=big3[:, r * 1024:(r + 1) * 1024], in_=wk_d[r * 128:(r + 1) * 128, :]),
                waits=([("MM", swqt_mm)] if r == 0 else []), inc_name="LD")
        _, wkt_ev = transpose_1024(big3, big4, [("LD", wk_ld), ("VE", qsq_ev)])
        swkt_mm, swkt_ev = mm1024(
            lambda k, m: big1[:, k * 1024 + m * 128: k * 1024 + (m + 1) * 128],
            lambda k, h: big4[:, k * 1024 + h * 512: k * 1024 + (h + 1) * 512],
            ev_to(big3), [("VE", wkt_ev)])
        wkcs_ev = matvec_row(big4, 5, [("VE", swkt_ev)], lambda k: cs_col[:, k:k + 1])
        pk_ev = 0
        for r in range(FC):
            pk_ev = emit("vector", lambda e, r=r: e.tensor_tensor(
                big4[:, r * 1024:(r + 1) * 1024], big3[:, r * 1024:(r + 1) * 1024],
                big4[:, r * 1024:(r + 1) * 1024], ALU.mult),
                waits=([("VE", wkcs_ev)] if r == 0 else []), inc_name="VE")
        ksq_ev = matvec_row(big4, 3, [("VE", pk_ev)], lambda k: ones_col)

        tmprow = bcast[0:1, :]
        qcs_ve = emit("vector", lambda e: e.scalar_tensor_tensor(
            ar(4), ar(0), CTOT, ar(4), ALU.mult, ALU.add),
            waits=[("VE", ksq_ev), ("LD", ld_bq), ("LD", ld_bk), ("LD", ld_gam)], inc_name="VE")
        emit("vector", lambda e: e.scalar_tensor_tensor(tmprow, ar(0), -CTOT, ar(4), ALU.mult, ALU.add))
        emit("vector", lambda e: e.tensor_tensor(tmprow, tmprow, ar(4), ALU.add))
        emit("vector", lambda e: e.tensor_tensor(tmprow, tmprow, ar(0), ALU.mult))
        emit("vector", lambda e: e.tensor_tensor(ar(2), ar(2), tmprow, ALU.add))
        emit("vector", lambda e: e.scalar_tensor_tensor(tmprow, ar(1), CTOT, ar(5), ALU.mult, ALU.add))
        emit("vector", lambda e: e.tensor_tensor(tmprow, tmprow, ar(5), ALU.add))
        emit("vector", lambda e: e.tensor_tensor(tmprow, tmprow, ar(1), ALU.mult))
        kfull_ve = emit("vector", lambda e: e.tensor_tensor(ar(3), ar(3), tmprow, ALU.add),
                        inc_name="VE")
        acq = emit("scalar", lambda e: e.activation(ar(2), ar(2), AF.Sqrt),
                   waits=[("VE", kfull_ve)], inc_name="AC")
        emit("vector", lambda e: e.tensor_copy(junk[0:1, 0:512], ar(2)[0:1, 0:512]),
             waits=[("AC", acq)])
        qsq_cp = emit("vector", lambda e: e.tensor_copy(w2st[0:1, 0:512], ar(2)[0:1, 512:1024]),
                      inc_name="VE")
        qinv_ve = emit("vector", lambda e: e.reciprocal(ar(2), ar(2)), inc_name="VE")
        ack = emit("scalar", lambda e: e.activation(ar(3), ar(3), AF.Sqrt),
                   waits=[("VE", kfull_ve)], inc_name="AC")
        kinv_ve = emit("vector", lambda e: e.reciprocal(ar(3), ar(3)), waits=[("AC", ack)],
                       inc_name="VE")
        kc_ev = row_transpose8(ar(3), kinv_col, [("VE", kinv_ve), ("VE", qsq_ev)])
        bvk_ve = emit("vector", lambda e: e.tensor_tensor(bvk_col, bv_col, kinv_col, ALU.mult),
                      waits=[("VE", kc_ev), ("VE", v_bv)], inc_name="VE")

        def ev_kg(m, h, pt, lastmm):
            return emit("vector", lambda e, m=m, h=h, p=pt: e.tensor_scalar(
                big4[:, m * 1024 + h * 512: m * 1024 + (h + 1) * 512], p[:, :],
                kinv_col[:, m:m + 1], None, ALU.mult, ALU.bypass),
                waits=[("MM", lastmm)], inc_name="VE")
        gt_mm, kg_ev = mm1024(
            lambda k, m: big3[:, k * 1024 + m * 128: k * 1024 + (m + 1) * 128],
            lambda k, h: big2[:, k * 1024 + h * 512: k * 1024 + (h + 1) * 512],
            ev_kg, [("VE", bvk_ve), ("VE", qcs_ve)],
            extra_rank1=[
                (lambda m: arsl(5, m), lambda h: arh(0, h)),
                (lambda m: arsl(1, m), lambda h: arh(4, h)),
            ])

        kivg_ev = matvec_row(big4, 7, [("VE", kg_ev)], lambda k: ones_col)
        emit("vector", lambda e: e.scalar_tensor_tensor(ar(7), ar(4), EPS_AT, ar(7), ALU.mult, ALU.add),
             waits=[("VE", kivg_ev), ("VE", qinv_ve)])
        emit("vector", lambda e: e.tensor_tensor(ar(7), ar(7), ar(2), ALU.mult))
        emit("vector", lambda e: e.tensor_scalar(ar(7), ar(7), float(D), None, ALU.add, ALU.bypass))
        tail_ve = emit("vector", lambda e: e.reciprocal(ar(5), ar(7)), inc_name="VE")
        emit("vector", lambda e: e.tensor_tensor(ar(6), ar(2), ar(5), ALU.mult),
             waits=[("VE", tail_ve)])
        csc_ve2 = emit("vector", lambda e: e.tensor_scalar(
            ar(6), ar(6), gsc[0:1, 0:1], None, ALU.mult, ALU.bypass), inc_name="VE")
        emit("vector", lambda e: e.tensor_copy(arh(1, 0), junk[0:1, 0:512]),
             waits=[("VE", qsq_cp), ("MM", gt_mm)])
        qsqr_ve = emit("vector", lambda e: e.tensor_copy(arh(1, 1), w2st[0:1, 0:512]),
                       inc_name="VE")
        bv_ld = emit("sync", lambda e: e.dma_start(out=ar(0), in_=bv_d), waits=[("MM", gt_mm)],
                     inc_name="LD")
        bvs_ve = emit("vector", lambda e: e.reduce_sum(gsc[0:1, 1:2], ar(0), axis=AX.X),
                      waits=[("LD", bv_ld)], inc_name="VE")
        bvkg_ev = matvec_row(big4, 4, [("VE", kivg_ev), ("VE", tail_ve), ("VE", v_bv)], lambda k: bv_col[:, k:k + 1])
        emit("vector", lambda e: e.tensor_tensor(ar(4), ar(4), ar(6), ALU.mult),
             waits=[("VE", bvkg_ev), ("VE", csc_ve2), ("VE", bvs_ve)])
        emit("vector", lambda e: e.tensor_scalar(
            ar(0), ar(5), gsc[0:1, 1:2], gsc[0:1, 0:1], ALU.mult, ALU.mult))
        gcon_ve = emit("vector", lambda e: e.tensor_tensor(ar(0), ar(0), ar(4), ALU.add),
                       inc_name="VE")
        bc_mm = emit("tensor", lambda e: e.matmul(ps[4][:, :], ones_row, arh(6, 0),
                                                  start=True, stop=True),
                     waits=[("VE", gcon_ve), ("VE", qsqr_ve)], inc_name="MM")
        bc_mm = emit("tensor", lambda e: e.matmul(ps[5][:, :], ones_row, arh(6, 1),
                                                  start=True, stop=True), inc_name="MM")
        emit("vector", lambda e: e.tensor_copy(bcast[:, 0:512], ps[4][:, :]),
             waits=[("MM", bc_mm)])
        bc_ev = emit("vector", lambda e: e.tensor_copy(bcast[:, 512:1024], ps[5][:, :]),
                     inc_name="VE")
        wv_ld = 0
        for r in range(FC):
            wv_ld = emit("sync", lambda e, r=r: e.dma_start(
                out=big3[:, r * 1024:(r + 1) * 1024], in_=wv_d[r * 128:(r + 1) * 128, :]),
                waits=([("MM", gt_mm)] if r == 0 else []), inc_name="LD")
        wvs_ev = matvec_row(big3, 5, [("LD", wv_ld), ("VE", gcon_ve), ("VE", bc_ev)], lambda k: ones_col)
        def ev_d2(m, h, pt, lastmm):
            return emit("vector", lambda e, m=m, h=h, p=pt: e.tensor_tensor(
                big1[:, m * 1024 + h * 512: m * 1024 + (h + 1) * 512], p[:, :],
                bcast[:, h * 512:(h + 1) * 512], ALU.mult),
                waits=[("MM", lastmm)], inc_name="VE")
        d2_mm, d2_ev = mm1024(
            lambda k, m: big3[:, k * 1024 + m * 128: k * 1024 + (m + 1) * 128],
            lambda k, h: big4[:, k * 1024 + h * 512: k * 1024 + (h + 1) * 512],
            ev_d2, [("VE", wvs_ev)],
            extra_rank1=[(lambda m: arsl(5, m), lambda h: arh(1, h))])
        dg_ve = 0
        for r in range(FC):
            dg_ve = emit("vector", lambda e, r=r: e.tensor_tensor(
                big1[:, r * 1024 + r * 128: r * 1024 + (r + 1) * 128],
                big1[:, r * 1024 + r * 128: r * 1024 + (r + 1) * 128],
                ident[:, :], ALU.add),
                waits=([("VE", d2_ev)] if r == 0 else []), inc_name="VE")
        b2_ld = emit("sync", lambda e: e.dma_start(out=ar(7), in_=b2_d),
                     waits=[("VE", tail_ve)], inc_name="LD")

        def x1sl(c, h=None):
            t = big2 if c < 8 else big3
            cc = c % 8
            if h is None:
                return t[:, cc * 1024:(cc + 1) * 1024]
            return t[:, cc * 1024 + h * 512: cc * 1024 + (h + 1) * 512]

        att_ve = [0] * NCH
        att_ev_ph = {}
        last_att_mm = {}
        mm_h = {}
        for c in range(NCH):
            w = []
            if c == 0:
                w = [("MM", d2_mm), ("SP", sp_done[NCH - 1])]
            if c >= 2:
                w.append(("MM", last_att_mm[c - 2]))
            xnt_ld = emit("sync", lambda e, c=c: e.dma_start(out=stB(c % 2), in_=xnt_d[c]),
                          waits=w, inc_name="LD")
            w2_ = []
            if c < 2:
                w2_.append(("AC", last_apply[c % 2]))
            else:
                w2_.append(("VE", att_ve[c - 2]))
            xr_ld = emit("sync", lambda e, c=c: e.dma_start(
                out=stC(c % 2), in_=x_d[c * 128:(c + 1) * 128, :]), waits=w2_, inc_name="LD")
            for h in range(2):
                pt = ps[(2 * c + h) % 4]
                w = [("LD", xnt_ld)]
                if c == 0 and h == 0:
                    w += [("VE", dg_ve)]
                if 2 * c + h >= 4:
                    w.append(("VE", att_ev_ph[(2 * c + h) - 4]))
                for r in range(FC):
                    mm = emit("tensor", lambda e, p=pt, c=c, r=r, h=h: e.matmul(
                        p[:, :], stB(c % 2)[:, r * 128:(r + 1) * 128],
                        big1[:, r * 1024 + h * 512: r * 1024 + (h + 1) * 512],
                        start=(r == 0), stop=False),
                        waits=(w if r == 0 else []), inc_name="MM")
                mm = emit("tensor", lambda e, p=pt, h=h: e.matmul(
                    p[:, :], ones_row, arh(0, h), start=False, stop=True), inc_name="MM")
                mm_h[h] = mm
            last_att_mm[c] = mm
            b = (c % 2) * 16
            sum0 = mcol[:, b + 0:b + 1]; sum1 = mcol[:, b + 1:b + 2]
            sq0 = mcol[:, b + 2:b + 3]; sq1 = mcol[:, b + 3:b + 4]
            mean_c = mcol[:, b + 4:b + 5]; var_c = mcol[:, b + 5:b + 6]
            std_c = mcol[:, b + 6:b + 7]; rstd_c = mcol[:, b + 7:b + 8]
            negmr_c = mcol[:, b + 8:b + 9]; msc = mcol[:, b + 9:b + 10]
            p0_, p1_ = ps[(2 * c) % 4], ps[(2 * c + 1) % 4]
            t5a = tmp56[:, (c % 2) * 1024:(c % 2) * 1024 + 512]
            t5b = tmp56[:, (c % 2) * 1024 + 512:(c % 2) * 1024 + 1024]
            att_ev_ph[2 * c] = emit("vector", lambda e, t=t5a, p=p0_: e.tensor_copy(t, p[:, :]),
                                    waits=[("MM", mm_h[0])], inc_name="VE")
            ev1 = emit("vector", lambda e, t=t5b, p=p1_: e.tensor_copy(t, p[:, :]),
                       waits=[("MM", mm_h[1])], inc_name="VE")
            att_ev_ph[2 * c + 1] = ev1
            emit("vector", lambda e, s=sum0, t=t5a: e.reduce_sum(s, t, axis=AX.X))
            emit("vector", lambda e, s=sum1, t=t5b: e.reduce_sum(s, t, axis=AX.X))
            emit("scalar", lambda e, s=sq0, t=t5a: e.activation(
                junk[:, :], t, AF.Square, accum_out=s), waits=[("VE", ev1)], inc_name="AC")
            a2 = emit("scalar", lambda e, s=sq1, t=t5b: e.activation(
                junk[:, :], t, AF.Square, accum_out=s), inc_name="AC")
            emit("vector", lambda e, a=sum0, b_=sum1, o=mean_c: e.tensor_tensor(o, a, b_, ALU.add),
                 waits=[("AC", a2)])
            emit("vector", lambda e, o=mean_c: e.tensor_scalar(o, o, 1.0 / D, None, ALU.mult,
                                                               ALU.bypass))
            emit("vector", lambda e, a=sq0, b_=sq1, o=var_c: e.tensor_tensor(o, a, b_, ALU.add))
            emit("vector", lambda e, o=var_c: e.tensor_scalar(o, o, 1.0 / D, None, ALU.mult,
                                                              ALU.bypass))
            emit("vector", lambda e, m=mean_c, o=msc: e.scalar_tensor_tensor(
                o, m, -1.0, m, ALU.mult, ALU.mult))
            v_var = emit("vector", lambda e, o=var_c, s=msc: e.tensor_tensor(o, o, s, ALU.add))
            a3 = emit("scalar", lambda e, v=var_c, o=std_c: e.activation(
                o, v, AF.Sqrt, bias=eps_col), waits=[("VE", v_var)], inc_name="AC")
            emit("vector", lambda e, o=rstd_c, i=std_c: e.reciprocal(o, i), waits=[("AC", a3)])
            nm_ve = emit("vector", lambda e, o=negmr_c, m=mean_c, r=rstd_c: e.scalar_tensor_tensor(
                o, m, -1.0, r, ALU.mult, ALU.mult), inc_name="VE")
            for h in range(2):
                th = tmp56[:, (c % 2) * 1024 + h * 512:(c % 2) * 1024 + (h + 1) * 512]
                emit("vector", lambda e, c=c, h=h, n=negmr_c: e.tensor_scalar(
                    x1sl(c, h), stC(c % 2)[:, h * 512:(h + 1) * 512], n, None,
                    ALU.add, ALU.bypass), waits=[("LD", xr_ld)])
                att_ve[c] = emit("vector", lambda e, c=c, h=h, t=th, r=rstd_c: e.scalar_tensor_tensor(
                    x1sl(c, h), t, r, x1sl(c, h), ALU.mult, ALU.add), inc_name="VE")

        def h1sl(mi, sl=None):
            t = big4 if mi < 16 else big1
            m = mi % 16
            if sl is None:
                return t[:, m * 512:(m + 1) * 512]
            return t[:, m * 512 + sl * 128: m * 512 + (sl + 1) * 128]

        st_done = [0] * NCH
        trv_prev = {0: 0, 1: 0}
        w1_mm_prev = {0: 0, 1: 0}
        w2_mm_prev = {0: 0, 1: 0}
        last_tr3 = {}
        last_psum_free = 0
        out_ve_prev = 0
        for g in range(NG):
            for cc in range(CPG):
                c = g * CPG + cc
                b = 64 + (cc % 2) * 16
                sum_c = mcol[:, b + 0:b + 1]; sq0 = mcol[:, b + 1:b + 2]; sq1 = mcol[:, b + 2:b + 3]
                mean_c = mcol[:, b + 3:b + 4]; var_c = mcol[:, b + 5:b + 6]
                std_c = mcol[:, b + 6:b + 7]; rstd_c = mcol[:, b + 7:b + 8]
                negmr_c = mcol[:, b + 8:b + 9]; msc = mcol[:, b + 9:b + 10]
                w = [("VE", att_ve[c])]
                emit("vector", lambda e, s=sum_c, c=c: e.reduce_sum(s, x1sl(c), axis=AX.X), waits=w)
                emit("scalar", lambda e, s=sq0, c=c: e.activation(
                    junk[:, :], x1sl(c, 0), AF.Square, accum_out=s), waits=w, inc_name="AC")
                a2 = emit("scalar", lambda e, s=sq1, c=c: e.activation(
                    junk[:, :], x1sl(c, 1), AF.Square, accum_out=s), inc_name="AC")
                emit("vector", lambda e, m=mean_c, s=sum_c: e.tensor_scalar(
                    m, s, 1.0 / D, None, ALU.mult, ALU.bypass), waits=[("AC", a2)])
                emit("vector", lambda e, q0=sq0, q1=sq1, o=var_c: e.tensor_tensor(o, q0, q1, ALU.add))
                emit("vector", lambda e, o=var_c: e.tensor_scalar(o, o, 1.0 / D, None, ALU.mult,
                                                                  ALU.bypass))
                emit("vector", lambda e, m=mean_c, o=msc: e.scalar_tensor_tensor(
                    o, m, -1.0, m, ALU.mult, ALU.mult))
                v_var = emit("vector", lambda e, o=var_c, s=msc: e.tensor_tensor(o, o, s, ALU.add))
                a3 = emit("scalar", lambda e, v=var_c, o=std_c: e.activation(
                    o, v, AF.Sqrt, bias=eps_col), waits=[("VE", v_var)], inc_name="AC")
                emit("vector", lambda e, o=rstd_c, i=std_c: e.reciprocal(o, i), waits=[("AC", a3)])
                nm_ve = emit("vector", lambda e, o=negmr_c, m=mean_c, r=rstd_c: e.scalar_tensor_tensor(
                    o, m, -1.0, r, ALU.mult, ALU.mult), inc_name="VE")
                wa = [("VE", nm_ve)]
                if g == 0 and cc < 2:
                    wa.append(("VE", att_ve[NCH - 2 + cc]))
                if (g, cc) in last_tr3 or (g > 0 or cc >= 2):
                    key = (g, cc - 2) if cc >= 2 else (g - 1, cc + 2)
                    wa.append(("MM", last_tr3.get(key, 0)))
                x1n_ac = emit("scalar", lambda e, c=c, cc=cc, r=rstd_c, n=negmr_c: e.activation(
                    stC(cc % 2), x1sl(c), AF.Identity, bias=n, scale=r), waits=wa, inc_name="AC")
                for r in range(FC):
                    pt = ps[6 + (r % 2)]
                    w = [("AC", x1n_ac), ("VE", trv_prev[r % 2])]
                    if g == 0 and cc == 0 and r < 2:
                        w.append(("VE", att_ve[NCH - 1]))
                    m = emit("tensor", lambda e, p=pt, cc=cc, r=r: e.transpose(
                        p[:, 0:128], stC(cc % 2)[:, r * 128:(r + 1) * 128], ident[:, :]),
                        waits=w, inc_name="MM")
                    tv = emit("vector", lambda e, p=pt, cc=cc, r=r: e.tensor_copy(
                        x1nT[:, r * 512 + cc * 128: r * 512 + (cc + 1) * 128], p[:, 0:128]),
                        waits=[("MM", m)], inc_name="VE")
                    trv_prev[r % 2] = tv
                last_tr3[(g, cc)] = m
                last_tr_ve = tv
            relu_ac = {}
            for ig in range(4):
                for ka in range(FC):
                    w = [("MM", w1_mm_prev[ka % 2])]
                    ld = emit("sync", lambda e, ig=ig, ka=ka: e.dma_start(
                        out=stD(ka % 2),
                        in_=w1_d[ka * 128:(ka + 1) * 128, ig * 1024:(ig + 1) * 1024]),
                        waits=w, inc_name="LD")
                    wmm = [("LD", ld)]
                    if ka == 0:
                        wmm.append(("VE", last_tr_ve))
                        wmm.append(("VE", last_psum_free))
                    for mi_l in range(8):
                        mm = emit("tensor", lambda e, p=ps[mi_l], ka=ka, mi_l=mi_l: e.matmul(
                            p[:, :], stD(ka % 2)[:, mi_l * 128:(mi_l + 1) * 128],
                            x1nT[:, ka * 512:(ka + 1) * 512],
                            start=(ka == 0), stop=(ka == FC - 1)),
                            waits=(wmm if mi_l == 0 else []), inc_name="MM")
                    w1_mm_prev[ka % 2] = mm
                for mi_l in range(8):
                    mi = ig * 8 + mi_l
                    ra = emit("vector", lambda e, p=ps[mi_l], mi=mi: e.tensor_scalar(
                        h1sl(mi), p[:, :], cols[:, 24 + mi:25 + mi], 0.0, ALU.add, ALU.max),
                        waits=([("MM", mm)] if mi_l == 0 else []), inc_name="VE")
                    relu_ac[mi] = ra
                last_psum_free = ra
            for mi in range(32):
                w = [("MM", w2_mm_prev[mi % 2])]
                ld = emit("sync", lambda e, mi=mi: e.dma_start(
                    out=w2st[:, (mi % 2) * 1024:(mi % 2 + 1) * 1024],
                    in_=w2_d[mi * 128:(mi + 1) * 128, :]), waits=w, inc_name="LD")
                wmm = [("LD", ld), ("VE", relu_ac[mi])]
                if mi == 0:
                    wmm.append(("VE", last_psum_free))
                    wmm.append(("VE", out_ve_prev))
                for cc in range(CPG):
                    for h in range(2):
                        mm = emit("tensor", lambda e, p=ps[cc * 2 + h], mi=mi, cc=cc, h=h: e.matmul(
                            p[:, :], h1sl(mi, cc),
                            w2st[:, (mi % 2) * 1024 + h * 512:(mi % 2) * 1024 + (h + 1) * 512],
                            start=(mi == 0), stop=False),
                            waits=(wmm if (cc == 0 and h == 0) else []), inc_name="MM")
                w2_mm_prev[mi % 2] = mm
            for cc in range(CPG):
                for h in range(2):
                    mm = emit("tensor", lambda e, p=ps[cc * 2 + h], h=h: e.matmul(
                        p[:, :], ones_row, arh(7, h), start=False, stop=True),
                        waits=([("LD", b2_ld)] if (g == 0 and cc == 0 and h == 0) else []),
                        inc_name="MM")
            ffn2_done = mm
            for cc in range(CPG):
                c = g * CPG + cc
                b = (cc % 2) * 16
                sum0 = mcol[:, b + 0:b + 1]; sum1 = mcol[:, b + 1:b + 2]
                sq0 = mcol[:, b + 2:b + 3]; sq1 = mcol[:, b + 3:b + 4]
                mean_c = mcol[:, b + 4:b + 5]; var_c = mcol[:, b + 5:b + 6]
                std_c = mcol[:, b + 6:b + 7]; rstd_c = mcol[:, b + 7:b + 8]
                negmr_c = mcol[:, b + 8:b + 9]; msc = mcol[:, b + 9:b + 10]
                p0_, p1_ = ps[cc * 2], ps[cc * 2 + 1]
                t6a = tmp56[:, (cc % 2) * 1024:(cc % 2) * 1024 + 512]
                t6b = tmp56[:, (cc % 2) * 1024 + 512:(cc % 2) * 1024 + 1024]
                wst = [("MM", ffn2_done)]
                emit("vector", lambda e, t=t6a, p=p0_: e.tensor_copy(t, p[:, :]), waits=wst)
                ev1 = emit("vector", lambda e, t=t6b, p=p1_: e.tensor_copy(t, p[:, :]),
                           inc_name="VE")
                emit("vector", lambda e, s=sum0, t=t6a: e.reduce_sum(s, t, axis=AX.X))
                emit("vector", lambda e, s=sum1, t=t6b: e.reduce_sum(s, t, axis=AX.X))
                emit("scalar", lambda e, s=sq0, t=t6a: e.activation(
                    junk[:, :], t, AF.Square, accum_out=s), waits=[("VE", ev1)], inc_name="AC")
                a2 = emit("scalar", lambda e, s=sq1, t=t6b: e.activation(
                    junk[:, :], t, AF.Square, accum_out=s), inc_name="AC")
                emit("vector", lambda e, a=sum0, b_=sum1, o=mean_c: e.tensor_tensor(
                    o, a, b_, ALU.add), waits=[("AC", a2)])
                emit("vector", lambda e, o=mean_c: e.tensor_scalar(o, o, 1.0 / D, None, ALU.mult,
                                                                   ALU.bypass))
                emit("vector", lambda e, a=sq0, b_=sq1, o=var_c: e.tensor_tensor(o, a, b_, ALU.add))
                emit("vector", lambda e, o=var_c: e.tensor_scalar(o, o, 1.0 / D, None, ALU.mult,
                                                                  ALU.bypass))
                emit("vector", lambda e, m=mean_c, o=msc: e.scalar_tensor_tensor(
                    o, m, -1.0, m, ALU.mult, ALU.mult))
                v_var = emit("vector", lambda e, o=var_c, s=msc: e.tensor_tensor(o, o, s, ALU.add))
                a3 = emit("scalar", lambda e, v=var_c, o=std_c: e.activation(
                    o, v, AF.Sqrt, bias=eps_col), waits=[("VE", v_var)], inc_name="AC")
                emit("vector", lambda e, o=rstd_c, i=std_c: e.reciprocal(o, i), waits=[("AC", a3)])
                nm_ve = emit("vector", lambda e, o=negmr_c, m=mean_c, r=rstd_c: e.scalar_tensor_tensor(
                    o, m, -1.0, r, ALU.mult, ALU.mult), inc_name="VE")
                for h in range(2):
                    t6h = tmp56[:, (cc % 2) * 1024 + h * 512:(cc % 2) * 1024 + (h + 1) * 512]
                    emit("vector", lambda e, c=c, h=h, n=negmr_c: e.tensor_scalar(
                        x1sl(c, h), x1sl(c, h), n, None, ALU.add, ALU.bypass),
                        waits=[("VE", nm_ve)])
                    ov = emit("vector", lambda e, c=c, h=h, t=t6h, r=rstd_c: e.scalar_tensor_tensor(
                        x1sl(c, h), t, r, x1sl(c, h), ALU.mult, ALU.add), inc_name="VE")
                out_ve_prev = ov
                last_psum_free_cand = ev1
                st_done[c] = emit("gpsimd", lambda e, c=c: e.dma_start(
                    out=y_d[c * 128:(c + 1) * 128, :], in_=x1sl(c)),
                    waits=[("VE", ov)], inc_name="SP")
            last_psum_free = last_psum_free_cand

        emit("gpsimd", lambda e: e.nop(), waits=[("SP", st_done[NCH - 1])])

        totals = {k: v[2] for k, v in cnt.items()}
        for (w, v) in waits_seen:
            assert v <= totals[w], f"wait {w} {v} > total {totals[w]}"

        @blk.sync
        def _(eng):
            for f in E["sync"]:
                f(eng)

        @blk.tensor
        def _(eng):
            for f in E["tensor"]:
                f(eng)

        @blk.vector
        def _(eng):
            for f in E["vector"]:
                f(eng)

        @blk.scalar
        def _(eng):
            for f in E["scalar"]:
                f(eng)

        @blk.gpsimd
        def _(eng):
            for f in E["gpsimd"]:
                f(eng)

    return nc


_NC_CACHE = None


def kernel(L0_attn_x, L0_attn_Wq, L0_attn_bq, L0_attn_Wk, L0_attn_bk,
           L0_attn_Wv, L0_attn_bv, L0_attn_gamma,
           L0_ffn_W1, L0_ffn_b1, L0_ffn_W2, L0_ffn_b2):
    global _NC_CACHE
    x = np.ascontiguousarray(np.asarray(L0_attn_x, dtype=np.float32).reshape(16384, 1024))
    com = dict(
        wq=np.ascontiguousarray(np.asarray(L0_attn_Wq, dtype=np.float32)),
        wk=np.ascontiguousarray(np.asarray(L0_attn_Wk, dtype=np.float32)),
        wv=np.ascontiguousarray(np.asarray(L0_attn_Wv, dtype=np.float32)),
        bq=np.asarray(L0_attn_bq, dtype=np.float32).reshape(1, 1024),
        bk=np.asarray(L0_attn_bk, dtype=np.float32).reshape(1, 1024),
        bv_row=np.asarray(L0_attn_bv, dtype=np.float32).reshape(1, 1024),
        bv8=np.ascontiguousarray(np.asarray(L0_attn_bv, dtype=np.float32).reshape(8, 128)),
        gamma=np.asarray(L0_attn_gamma, dtype=np.float32).reshape(1, 1),
        w1=np.ascontiguousarray(np.asarray(L0_ffn_W1, dtype=np.float32)),
        b1_32=np.ascontiguousarray(np.asarray(L0_ffn_b1, dtype=np.float32).reshape(32, 128)),
        w2=np.ascontiguousarray(np.asarray(L0_ffn_W2, dtype=np.float32)),
        b2=np.asarray(L0_ffn_b2, dtype=np.float32).reshape(1, 1024),
        ident=np.eye(128, dtype=np.float32),
    )
    if _NC_CACHE is None:
        _NC_CACHE = build_core()
    nc = _NC_CACHE
    in_maps = []
    for i in range(8):
        m = dict(com)
        m["x"] = np.ascontiguousarray(x[i * TOK:(i + 1) * TOK, :])
        in_maps.append(m)
    res = run_bass_kernel_spmd(nc, in_maps, core_ids=list(range(8)))
    out = np.concatenate([res.results[i]["y"] for i in range(8)], axis=0)
    return out.reshape(1, 16384, 1024).astype(np.float32)


# version 05
# speedup vs baseline: 1.0752x; 1.0752x over previous
import numpy as np
import concourse.bass as bass
import concourse.mybir as mybir
from concourse.bass_utils import run_bass_kernel_spmd

F32 = mybir.dt.float32
BF16 = mybir.dt.bfloat16
AF = mybir.ActivationFunctionType
ALU = mybir.AluOpType
AX = mybir.AxisListType

TOK = 2048
NCH = 16
D = 1024
FC = 8
INTER = 4096
NG = 4
CPG = 4
CTOT = 16384.0
EPS_LN = 1e-5
EPS_AT = 1e-6


def build_core():
    nc = bass.Bass("TRN2", target_bir_lowering=False)
    dt = lambda n, s: nc.dram_tensor(n, s, F32, kind="ExternalInput").ap()
    x_d = dt("x", [TOK, D])
    wq_d = dt("wq", [D, D]); wk_d = dt("wk", [D, D]); wv_d = dt("wv", [D, D])
    bq_d = dt("bq", [1, D]); bk_d = dt("bk", [1, D])
    bv_d = dt("bv_row", [1, D]); bv8_d = dt("bv8", [8, 128])
    gam_d = dt("gamma", [1, 1])
    w1_d = dt("w1", [D, INTER]); b1_d = dt("b1_32", [32, 128])
    w2_d = dt("w2", [INTER, D]); b2_d = dt("b2", [1, D])
    id_d = dt("ident", [128, 128])
    y_d = nc.dram_tensor("y", [TOK, D], F32, kind="ExternalOutput").ap()
    ccin_h = nc.dram_tensor("ccin", [1025, 1024], F32)
    ccout_h = nc.dram_tensor("ccout", [1025, 1024], F32)
    ccin, ccout = ccin_h.ap(), ccout_h.ap()
    xnt_d = nc.dram_tensor("xnt", [NCH, 128, D], F32).ap()
    w1s_d = nc.dram_tensor("w1s", [D, 2 * INTER], BF16).ap()
    w2s_d = nc.dram_tensor("w2s", [INTER, 2 * D], BF16).ap()

    import contextlib
    with contextlib.ExitStack() as stack:
        en = stack.enter_context
        big1 = en(nc.sbuf_tensor("big1", [128, 8192], F32))
        big2 = en(nc.sbuf_tensor("big2", [128, 8192], F32))
        big3 = en(nc.sbuf_tensor("big3", [128, 8192], F32))
        big4 = en(nc.sbuf_tensor("big4", [128, 8192], F32))
        util = en(nc.sbuf_tensor("util", [128, 8192], F32))
        w2st = en(nc.sbuf_tensor("w2st", [128, 2048], F32))
        arena = en(nc.sbuf_tensor("arena", [128, 8192], F32))
        cols = en(nc.sbuf_tensor("cols", [128, 64], F32))
        mcol = en(nc.sbuf_tensor("mcol", [128, 128], F32))
        bcast = en(nc.sbuf_tensor("bcast", [128, 1024], F32))
        ident = en(nc.sbuf_tensor("ident_sb", [128, 128], F32))
        ones = en(nc.sbuf_tensor("ones", [128, 128], F32))
        gsc = en(nc.sbuf_tensor("gsc", [128, 8], F32))
        junk = en(nc.sbuf_tensor("junk", [128, 512], F32))
        pa = [en(nc.psum_tensor(f"pa{i}", [128, 1024], F32)) for i in range(4)]
        ps = [pa[i // 2][:, (i % 2) * 512:(i % 2 + 1) * 512] for i in range(8)]
        ps4, ps5, ps6, ps7 = ps[4], ps[5], ps[6], ps[7]
        sLD = en(nc.semaphore("sLD"))
        sSP = en(nc.semaphore("sSP"))
        sMM = en(nc.semaphore("sMM"))
        sVE = en(nc.semaphore("sVE"))
        sAC = en(nc.semaphore("sAC"))
        sCC = en(nc.semaphore("sCC"))
        blk = en(nc.Block())
        stA = lambda b: util[:, (0 + b) * 1024:(1 + b) * 1024]
        stB = lambda b: util[:, (2 + b) * 1024:(3 + b) * 1024]
        stC = lambda b: util[:, (4 + b) * 1024:(5 + b) * 1024]
        stD = lambda b: util[:, (6 + b) * 1024:(7 + b) * 1024]
        x1nT = util[:, 0:4096]
        tmp56 = util[:, 0:2048]
        big1b = big1.bitcast(BF16)
        big2b = big2.bitcast(BF16)
        big4b = big4.bitcast(BF16)
        utilb = util.bitcast(BF16)
        w2stb = w2st.bitcast(BF16)
        x1nTb = utilb[:, 0:8192]
        stBb = lambda b: utilb[:, (2 + b) * 2048:(3 + b) * 2048]
        stDb = lambda b: utilb[:, (6 + b) * 2048:(7 + b) * 2048]
        ar = lambda i: arena[0:1, i * 1024:(i + 1) * 1024]
        arh = lambda i, h: arena[0:1, i * 1024 + h * 512: i * 1024 + (h + 1) * 512]
        arsl = lambda i, m: arena[0:1, i * 1024 + m * 128: i * 1024 + (m + 1) * 128]
        cs_col = cols[:, 0:8]; kinv_col = cols[:, 8:16]
        bv_col = cols[:, 16:24]; bvk_col = cols[:, 56:64]
        ones_col = ones[:, 0:1]; ones_row = ones[0:1, 0:128]

        E = {k: [] for k in ("sync", "tensor", "vector", "scalar", "gpsimd")}
        cnt = {"LD": [sLD, 16, 0], "SP": [sSP, 16, 0], "MM": [sMM, 1, 0],
               "VE": [sVE, 1, 0], "AC": [sAC, 1, 0], "CC": [sCC, 1, 0]}
        waits_seen = []

        def emit(eng, f, waits=(), inc_name=None):
            waits = list(waits)
            auto = {"vector": "VE", "scalar": "AC"}.get(eng)
            if auto is not None:
                if inc_name is None:
                    inc_name = auto
                waits.append((auto, cnt[auto][2]))
            if inc_name in ("LD", "SP"):
                waits.append((inc_name, cnt[inc_name][2]))
            wl = [(cnt[w][0], v, w) for (w, v) in waits if v]
            for (_, v, w) in wl:
                waits_seen.append((w, v))
            step = 0
            sem = None
            if inc_name:
                c = cnt[inc_name]
                c[2] += c[1]
                step = c[1]
                sem = c[0]
            tgt = cnt[inc_name][2] if inc_name else None

            def g(eng_obj, f=f, wl=wl, sem=sem, step=step):
                for (s, v, _) in wl:
                    eng_obj.wait_ge(s, v)
                ins = f(eng_obj)
                if sem is not None:
                    ins.then_inc(sem, step)
            E[eng].append(g)
            return tgt

        emit("vector", lambda e: e.memset(ones[:, :], 1.0))
        ve_init = emit("vector", lambda e: e.memset(gsc[:, 2:3], EPS_LN), inc_name="VE")
        eps_col = gsc[:, 2:3]
        ld_ident = emit("sync", lambda e: e.dma_start(out=ident[:, :], in_=id_d), inc_name="LD")
        ld_bq = emit("sync", lambda e: e.dma_start(out=ar(0), in_=bq_d), inc_name="LD")
        ld_bk = emit("sync", lambda e: e.dma_start(out=ar(1), in_=bk_d), inc_name="LD")
        ld_gam = emit("sync", lambda e: e.dma_start(out=gsc[0:1, 0:1], in_=gam_d), inc_name="LD")
        ld_bv8 = emit("sync", lambda e: e.dma_start(out=big2[0:8, 0:128], in_=bv8_d), inc_name="LD")
        ld_b132 = emit("sync", lambda e: e.dma_start(out=big2[0:32, 128:256], in_=b1_d), inc_name="LD")
        t_bv = emit("tensor", lambda e: e.transpose(ps6[:, 0:8], big2[0:8, 0:128], ident[0:8, 0:8]),
                    waits=[("LD", ld_bv8), ("LD", ld_ident)], inc_name="MM")
        t_b1 = emit("tensor", lambda e: e.transpose(ps7[:, 0:32], big2[0:32, 128:256], ident[0:32, 0:32]),
                    waits=[("LD", ld_b132)], inc_name="MM")
        v_bv = emit("vector", lambda e: e.tensor_copy(bv_col, ps6[:, 0:8]), waits=[("MM", t_bv)],
                    inc_name="VE")
        v_b1 = emit("vector", lambda e: e.tensor_copy(cols[:, 24:56], ps7[:, 0:32]),
                    waits=[("MM", t_b1)], inc_name="VE")

        x_ld = [0] * NCH
        xn_ac = [0] * NCH
        sp_done = [0] * NCH
        ev_done = [0] * NCH
        cs_mm_c = [0] * NCH
        tr_ev = {0: v_bv, 1: v_b1}
        w2_in = {0: 0, 1: 0}; w2_lsub = {0: 0, 1: 0}; w2_out = {0: 0, 1: 0}
        for c in range(NCH):
            w = [("AC", xn_ac[c - 2])] if c >= 2 else []
            x_ld[c] = emit("sync", lambda e, c=c: e.dma_start(
                out=stC(c % 2), in_=x_d[c * 128:(c + 1) * 128, :]), waits=w, inc_name="LD")
            b = (c % 2) * 16
            sum_c = mcol[:, b + 0:b + 1]; sq0 = mcol[:, b + 1:b + 2]; sq1 = mcol[:, b + 2:b + 3]
            mean_c = mcol[:, b + 3:b + 4]; msq = mcol[:, b + 4:b + 5]; var_c = mcol[:, b + 5:b + 6]
            std_c = mcol[:, b + 6:b + 7]
            rstd_c = mcol[:, 32 + c:33 + c]
            negmr_c = mcol[:, 48 + c:49 + c]
            emit("vector", lambda e, s=sum_c, c=c: e.reduce_sum(s, stC(c % 2), axis=AX.X),
                 waits=[("LD", x_ld[c])])
            emit("scalar", lambda e, s=sq0, c=c: e.activation(
                junk[:, :], stC(c % 2)[:, 0:512], AF.Square, accum_out=s),
                waits=[("LD", x_ld[c])], inc_name="AC")
            ac2 = emit("scalar", lambda e, s=sq1, c=c: e.activation(
                junk[:, :], stC(c % 2)[:, 512:1024], AF.Square, accum_out=s), inc_name="AC")
            emit("vector", lambda e, m=mean_c, s=sum_c: e.tensor_scalar(
                m, s, 1.0 / D, None, ALU.mult, ALU.bypass), waits=[("AC", ac2)])
            emit("vector", lambda e, q=sq0, q1=sq1, o=msq: e.tensor_tensor(o, q, q1, ALU.add))
            emit("vector", lambda e, m=mean_c, o=var_c: e.scalar_tensor_tensor(
                o, m, -1.0, m, ALU.mult, ALU.mult))
            v_var = emit("vector", lambda e, q=msq, o=var_c: e.scalar_tensor_tensor(
                o, q, 1.0 / D, o, ALU.mult, ALU.add))
            ac3 = emit("scalar", lambda e, v=var_c, o=std_c: e.activation(
                o, v, AF.Sqrt, bias=eps_col),
                waits=([("VE", ve_init), ("VE", v_var)] if c == 0 else [("VE", v_var)]),
                inc_name="AC")
            emit("vector", lambda e, o=rstd_c, i=std_c: e.reciprocal(o, i), waits=[("AC", ac3)])
            st_ve = emit("vector", lambda e, o=negmr_c, m=mean_c, r=rstd_c: e.scalar_tensor_tensor(
                o, m, -1.0, r, ALU.mult, ALU.mult), inc_name="VE")
            w = [("VE", st_ve)]
            if c >= 2:
                w.append(("MM", cs_mm_c[c - 2]))
            xn_ac[c] = emit("scalar", lambda e, c=c, r=rstd_c, n=negmr_c: e.activation(
                stA(c % 2), stC(c % 2), AF.Identity, bias=n, scale=r), waits=w, inc_name="AC")
            for r in range(FC):
                pt = ps[6 + (r % 2)]
                w = [("AC", xn_ac[c]), ("VE", tr_ev[r % 2])]
                m = emit("tensor", lambda e, p=pt, c=c, r=r: e.transpose(
                    p[:, 0:128], stA(c % 2)[:, r * 128:(r + 1) * 128], ident[:, :]),
                    waits=w, inc_name="MM")
                wv = [("MM", m)]
                if r == 0 and c >= 2:
                    wv.append(("SP", sp_done[c - 2]))
                ev = emit("vector", lambda e, p=pt, c=c, r=r: e.tensor_copy(
                    stB(c % 2)[:, r * 128:(r + 1) * 128], p[:, 0:128]), waits=wv, inc_name="VE")
                tr_ev[r % 2] = ev
            ev_done[c] = ev
            emit("tensor", lambda e, c=c: e.matmul(
                ps4[0:1, :], ones_col, stA(c % 2)[:, 0:512],
                start=(c == 0), stop=(c == NCH - 1)), inc_name="MM")
            cs_mm_c[c] = emit("tensor", lambda e, c=c: e.matmul(
                ps5[0:1, :], ones_col, stA(c % 2)[:, 512:1024],
                start=(c == 0), stop=(c == NCH - 1)), inc_name="MM")
            sp_done[c] = emit("gpsimd", lambda e, c=c: e.dma_start(
                out=xnt_d[c], in_=stB(c % 2)), waits=[("VE", ev_done[c])], inc_name="SP")
            for j in (2 * c, 2 * c + 1):
                buf = j % 2
                srcw = big2[:, buf * 1024:(buf + 1) * 1024]
                dstw = big2b[:, 8192 + buf * 2048:8192 + (buf + 1) * 2048]
                wlw = [("VE", w2_lsub[buf])] if w2_lsub[buf] else [("MM", t_b1)]
                w2_in[buf] = emit("sync", lambda e, j=j, s=srcw: e.dma_start(
                    out=s, in_=w2_d[j * 128:(j + 1) * 128, :]), waits=wlw, inc_name="LD")
                wlh = [("LD", w2_in[buf])]
                if w2_out[buf]:
                    wlh.append(("SP", w2_out[buf]))
                hcp = emit("scalar", lambda e, s=srcw, d=dstw: e.activation(
                    d[:, 0:1024], s, AF.Identity), waits=wlh, inc_name="AC")
                w2_lsub[buf] = emit("vector", lambda e, s=srcw, d=dstw: e.tensor_tensor(
                    d[:, 1024:2048], s, d[:, 0:1024], ALU.subtract), waits=[("AC", hcp)],
                    inc_name="VE")
                w2_out[buf] = emit("gpsimd", lambda e, j=j, d=dstw: e.dma_start(
                    out=w2s_d[j * 128:(j + 1) * 128, :], in_=d),
                    waits=[("VE", w2_lsub[buf])], inc_name="SP")

        w2s_done = w2_out[1]
        emit("vector", lambda e: e.tensor_copy(arh(6, 0), ps4[0:1, :]),
             waits=[("MM", cs_mm_c[NCH - 1])])
        cs_ve = emit("vector", lambda e: e.tensor_copy(arh(6, 1), ps5[0:1, :]), inc_name="VE")

        s_evacs = []
        last_apply = {0: xn_ac[NCH - 2], 1: xn_ac[NCH - 1]}
        last_s_mm = {0: 0, 1: 0}
        split_free = {0: cs_mm_c[NCH - 2], 1: cs_mm_c[NCH - 1]}
        split_kind = {0: "MM", 1: "MM"}
        stB_free = {0: sp_done[NCH - 2], 1: sp_done[NCH - 1]}
        w1_in = 0; w1_lsub = 0; w1_out = 0
        for p_i in range(2):
            for c in range(NCH):
                lx = emit("sync", lambda e, c=c: e.dma_start(
                    out=stC(c % 2), in_=x_d[c * 128:(c + 1) * 128, :]),
                    waits=[("AC", last_apply[c % 2])], inc_name="LD")
                aply = emit("scalar", lambda e, c=c: e.activation(
                    stA(c % 2), stC(c % 2), AF.Identity,
                    bias=mcol[:, 48 + c:49 + c], scale=mcol[:, 32 + c:33 + c]),
                    waits=[("LD", lx), (split_kind[c % 2], split_free[c % 2])], inc_name="AC")
                last_apply[c % 2] = aply
                wsp = [("AC", aply), ("SP", stB_free[c % 2])]
                if last_s_mm[c % 2]:
                    wsp.append(("MM", last_s_mm[c % 2]))
                hcp = emit("scalar", lambda e, c=c: e.activation(
                    stBb(c % 2)[:, 0:1024], stA(c % 2), AF.Identity), waits=wsp, inc_name="AC")
                lsub = emit("vector", lambda e, c=c: e.tensor_tensor(
                    stBb(c % 2)[:, 1024:2048], stA(c % 2), stBb(c % 2)[:, 0:1024],
                    ALU.subtract), waits=[("AC", hcp)], inc_name="VE")
                split_free[c % 2] = lsub
                split_kind[c % 2] = "VE"
                stB_free[c % 2] = 0
                wmm = [("VE", lsub)]
                if c == 0 and p_i == 0:
                    wmm += [("VE", cs_ve), ("VE", ev_done[NCH - 1])]
                if c == 0 and p_i == 1:
                    wmm.append(("VE", s_evacs[-1]))
                for m in range(4):
                    mg = m + 4 * p_i
                    for pp in range(3):
                        lt = (stBb(c % 2)[:, mg * 128:(mg + 1) * 128] if pp < 2
                              else stBb(c % 2)[:, 1024 + mg * 128:1024 + (mg + 1) * 128])
                        rbase = 0 if pp != 1 else 1024
                        for hh in range(2):
                            rt = stBb(c % 2)[:, rbase + hh * 512:rbase + (hh + 1) * 512]
                            mm = emit("tensor", lambda e, m=m, lt=lt, rt=rt, hh=hh, c=c, pp=pp: e.matmul(
                                pa[m][:, hh * 512:(hh + 1) * 512], lt, rt,
                                start=(c == 0 and pp == 0), stop=(c == NCH - 1 and pp == 2)),
                                waits=(wmm if (m == 0 and pp == 0 and hh == 0) else []),
                                inc_name="MM")
                last_s_mm[c % 2] = mm
                if c % 4 == 0:
                    k = p_i * 4 + c // 4
                    wlw = []
                    if k == 0:
                        wlw = [("VE", w2_lsub[1]), ("SP", w2_out[1]), ("SP", w2_out[0])]
                    elif w1_lsub:
                        wlw = [("VE", w1_lsub), ("SP", w1_out)]
                    w1_in = emit("sync", lambda e, k=k: e.dma_start(
                        out=big2[:, 0:4096], in_=w1_d[k * 128:(k + 1) * 128, :]),
                        waits=wlw, inc_name="LD")
                    for q in range(4):
                        wlh = [("LD", w1_in)] if q == 0 else []
                        if q == 0 and w1_out:
                            wlh.append(("SP", w1_out))
                        hq = emit("scalar", lambda e, q=q: e.activation(
                            big2b[:, 8192 + q * 1024:8192 + (q + 1) * 1024],
                            big2[:, q * 1024:(q + 1) * 1024], AF.Identity),
                            waits=wlh, inc_name="AC")
                        w1_lsub = emit("vector", lambda e, q=q: e.tensor_tensor(
                            big2b[:, 12288 + q * 1024:12288 + (q + 1) * 1024],
                            big2[:, q * 1024:(q + 1) * 1024],
                            big2b[:, 8192 + q * 1024:8192 + (q + 1) * 1024],
                            ALU.subtract), waits=[("AC", hq)], inc_name="VE")
                    emit("gpsimd", lambda e, k=k: e.dma_start(
                        out=w1s_d[k * 128:(k + 1) * 128, 0:4096], in_=big2b[:, 8192:12288]),
                        waits=[("VE", w1_lsub)], inc_name="SP")
                    w1_out = emit("gpsimd", lambda e, k=k: e.dma_start(
                        out=w1s_d[k * 128:(k + 1) * 128, 4096:8192], in_=big2b[:, 12288:16384]),
                        inc_name="SP")
            for m in range(4):
                mg = m + 4 * p_i
                sev = emit("vector", lambda e, m=m, mg=mg: e.tensor_copy(
                    big1[:, mg * 1024:(mg + 1) * 1024], pa[m][:, :]),
                    waits=([("MM", mm)] if m == 0 else []), inc_name="VE")
                s_evacs.append(sev)
        w1s_done = w1_out

        spx = 0
        for r in range(FC):
            spx = emit("gpsimd", lambda e, r=r: e.dma_start(
                out=ccin[r * 128:(r + 1) * 128, :], in_=big1[:, r * 1024:(r + 1) * 1024]),
                waits=([("VE", s_evacs[-1]), ("VE", cs_ve)] if r == 0 else []), inc_name="SP")
        spx = emit("gpsimd", lambda e: e.dma_start(out=ccin[1024:1025, :], in_=ar(6)),
                   inc_name="SP")
        cc_t = emit("gpsimd", lambda e: e.collective_compute(
            "AllReduce", ALU.add, replica_groups=[list(range(8))],
            ins=[ccin_h.ap().opt()], outs=[ccout_h.ap().opt()]),
            waits=[("SP", spx)], inc_name="CC")
        for r in range(FC):
            emit("sync", lambda e, r=r: e.dma_start(
                out=big1[:, r * 1024:(r + 1) * 1024], in_=ccout[r * 128:(r + 1) * 128, :]),
                waits=([("CC", cc_t)] if r == 0 else []), inc_name="LD")
        gl_ld = emit("sync", lambda e: e.dma_start(out=ar(6), in_=ccout[1024:1025, :]),
                     inc_name="LD")
        wq_ld = 0
        for r in range(FC):
            wq_ld = emit("sync", lambda e, r=r: e.dma_start(
                out=big3[:, r * 1024:(r + 1) * 1024], in_=wq_d[r * 128:(r + 1) * 128, :]),
                inc_name="LD")

        def transpose_1024(src, dst, wait_first):
            evs = []
            for rr in range(FC):
                for cc2 in range(FC):
                    i = rr * FC + cc2
                    pt = ps[i % 4]
                    w = list(wait_first) if i == 0 else []
                    if i >= 4:
                        w.append(("VE", evs[i - 4]))
                    m = emit("tensor", lambda e, p=pt, rr=rr, cc2=cc2, src=src: e.transpose(
                        p[:, 0:128], src[:, rr * 1024 + cc2 * 128: rr * 1024 + (cc2 + 1) * 128],
                        ident[:, :]), waits=w, inc_name="MM")
                    ev = emit("vector", lambda e, p=pt, rr=rr, cc2=cc2, dst=dst: e.tensor_copy(
                        dst[:, cc2 * 1024 + rr * 128: cc2 * 1024 + (rr + 1) * 128], p[:, 0:128]),
                        waits=[("MM", m)], inc_name="VE")
                    evs.append(ev)
            return m, ev

        def mm1024(lhsT_fn, rhs_fn, evac_fn, wait_first, extra_rank1=()):
            evs = {}
            lastmm = 0
            for m in range(FC):
                for h in range(2):
                    i = m * 2 + h
                    pt = ps[i % 4]
                    w = list(wait_first) if i == 0 else []
                    if i >= 4:
                        w.append(("VE", evs[i - 4]))
                    nr = len(extra_rank1)
                    for k in range(FC):
                        lastmm = emit("tensor", lambda e, p=pt, k=k, m=m, h=h: e.matmul(
                            p[:, :], lhsT_fn(k, m), rhs_fn(k, h),
                            start=(k == 0), stop=(k == FC - 1 and nr == 0)),
                            waits=(w if k == 0 else []), inc_name="MM")
                    for j, (lf, rf) in enumerate(extra_rank1):
                        lastmm = emit("tensor", lambda e, p=pt, lf=lf, rf=rf, m=m, h=h,
                                      st=(j == nr - 1): e.matmul(
                            p[:, :], lf(m), rf(h), start=False, stop=st), inc_name="MM")
                    evs[i] = evac_fn(m, h, pt, lastmm)
            return lastmm, evs[15]

        def ev_to(dst):
            def f(m, h, pt, lastmm):
                return emit("vector", lambda e, m=m, h=h, p=pt: e.tensor_copy(
                    dst[:, m * 1024 + h * 512: m * 1024 + (h + 1) * 512], p[:, :]),
                    waits=[("MM", lastmm)], inc_name="VE")
            return f

        def matvec_row(rhs_src, dst_slot, wait_first, lhs_fn):
            last_ev = 0
            for h in range(2):
                for k in range(FC):
                    m = emit("tensor", lambda e, k=k, h=h: e.matmul(
                        ps[4 + h][0:1, :], lhs_fn(k),
                        rhs_src[:, k * 1024 + h * 512: k * 1024 + (h + 1) * 512],
                        start=(k == 0), stop=(k == FC - 1)),
                        waits=(list(wait_first) if (h == 0 and k == 0) else []), inc_name="MM")
                last_ev = emit("vector", lambda e, h=h, m=m, d=dst_slot: e.tensor_copy(
                    arh(d, h), ps[4 + h][0:1, :]), waits=[("MM", m)], inc_name="VE")
            return last_ev

        def row_transpose8(src_row, dst_col8, wait_first):
            last = 0
            for k in range(FC):
                w = list(wait_first) if k == 0 else [("VE", last)]
                m = emit("tensor", lambda e, k=k, s=src_row: e.transpose(
                    ps[4][:, 0:1], s[0:1, k * 128:(k + 1) * 128], ident[0:1, 0:1]),
                    waits=w, inc_name="MM")
                last = emit("vector", lambda e, k=k, m=m, d=dst_col8: e.tensor_copy(
                    d[:, k:k + 1], ps[4][:, 0:1]), waits=[("MM", m)], inc_name="VE")
            return last

        csc_ev = row_transpose8(ar(6), cs_col, [("LD", gl_ld), ("VE", s_evacs[-1])])
        _, wqt_ev = transpose_1024(big3, big2, [("LD", wq_ld), ("VE", s_evacs[-1]), ("SP", w1s_done)])
        swqt_mm, swqt_ev = mm1024(
            lambda k, m: big1[:, k * 1024 + m * 128: k * 1024 + (m + 1) * 128],
            lambda k, h: big2[:, k * 1024 + h * 512: k * 1024 + (h + 1) * 512],
            ev_to(big4), [("VE", wqt_ev)])
        wqcs_ev = matvec_row(big2, 4, [("VE", csc_ev), ("VE", swqt_ev)], lambda k: cs_col[:, k:k + 1])
        pq_ev = 0
        for r in range(FC):
            pq_ev = emit("vector", lambda e, r=r: e.tensor_tensor(
                big4[:, r * 1024:(r + 1) * 1024], big4[:, r * 1024:(r + 1) * 1024],
                big2[:, r * 1024:(r + 1) * 1024], ALU.mult),
                waits=([("VE", swqt_ev)] if r == 0 else []), inc_name="VE")
        qsq_ev = matvec_row(big4, 2, [("VE", pq_ev), ("VE", wqcs_ev)], lambda k: ones_col)
        wk_ld = 0
        for r in range(FC):
            wk_ld = emit("sync", lambda e, r=r: e.dma_start(
                out=big3[:, r * 1024:(r + 1) * 1024], in_=wk_d[r * 128:(r + 1) * 128, :]),
                waits=([("MM", swqt_mm)] if r == 0 else []), inc_name="LD")
        _, wkt_ev = transpose_1024(big3, big4, [("LD", wk_ld), ("VE", qsq_ev)])
        swkt_mm, swkt_ev = mm1024(
            lambda k, m: big1[:, k * 1024 + m * 128: k * 1024 + (m + 1) * 128],
            lambda k, h: big4[:, k * 1024 + h * 512: k * 1024 + (h + 1) * 512],
            ev_to(big3), [("VE", wkt_ev)])
        wkcs_ev = matvec_row(big4, 5, [("VE", swkt_ev)], lambda k: cs_col[:, k:k + 1])
        pk_ev = 0
        for r in range(FC):
            pk_ev = emit("vector", lambda e, r=r: e.tensor_tensor(
                big4[:, r * 1024:(r + 1) * 1024], big3[:, r * 1024:(r + 1) * 1024],
                big4[:, r * 1024:(r + 1) * 1024], ALU.mult),
                waits=([("VE", wkcs_ev)] if r == 0 else []), inc_name="VE")
        ksq_ev = matvec_row(big4, 3, [("VE", pk_ev)], lambda k: ones_col)

        tmprow = bcast[0:1, :]
        qcs_ve = emit("vector", lambda e: e.scalar_tensor_tensor(
            ar(4), ar(0), CTOT, ar(4), ALU.mult, ALU.add),
            waits=[("VE", ksq_ev), ("LD", ld_bq), ("LD", ld_bk), ("LD", ld_gam)], inc_name="VE")
        emit("vector", lambda e: e.scalar_tensor_tensor(tmprow, ar(0), -CTOT, ar(4), ALU.mult, ALU.add))
        emit("vector", lambda e: e.tensor_tensor(tmprow, tmprow, ar(4), ALU.add))
        emit("vector", lambda e: e.tensor_tensor(tmprow, tmprow, ar(0), ALU.mult))
        emit("vector", lambda e: e.tensor_tensor(ar(2), ar(2), tmprow, ALU.add))
        emit("vector", lambda e: e.scalar_tensor_tensor(tmprow, ar(1), CTOT, ar(5), ALU.mult, ALU.add))
        emit("vector", lambda e: e.tensor_tensor(tmprow, tmprow, ar(5), ALU.add))
        emit("vector", lambda e: e.tensor_tensor(tmprow, tmprow, ar(1), ALU.mult))
        kfull_ve = emit("vector", lambda e: e.tensor_tensor(ar(3), ar(3), tmprow, ALU.add),
                        inc_name="VE")
        acq = emit("scalar", lambda e: e.activation(ar(2), ar(2), AF.Sqrt),
                   waits=[("VE", kfull_ve)], inc_name="AC")
        emit("vector", lambda e: e.tensor_copy(junk[0:1, 0:512], ar(2)[0:1, 0:512]),
             waits=[("AC", acq)])
        qsq_cp = emit("vector", lambda e: e.tensor_copy(w2st[0:1, 0:512], ar(2)[0:1, 512:1024]),
                      inc_name="VE")
        qinv_ve = emit("vector", lambda e: e.reciprocal(ar(2), ar(2)), inc_name="VE")
        ack = emit("scalar", lambda e: e.activation(ar(3), ar(3), AF.Sqrt),
                   waits=[("VE", kfull_ve)], inc_name="AC")
        kinv_ve = emit("vector", lambda e: e.reciprocal(ar(3), ar(3)), waits=[("AC", ack)],
                       inc_name="VE")
        kc_ev = row_transpose8(ar(3), kinv_col, [("VE", kinv_ve), ("VE", qsq_ev)])
        bvk_ve = emit("vector", lambda e: e.tensor_tensor(bvk_col, bv_col, kinv_col, ALU.mult),
                      waits=[("VE", kc_ev), ("VE", v_bv)], inc_name="VE")

        def ev_kg(m, h, pt, lastmm):
            return emit("vector", lambda e, m=m, h=h, p=pt: e.tensor_scalar(
                big4[:, m * 1024 + h * 512: m * 1024 + (h + 1) * 512], p[:, :],
                kinv_col[:, m:m + 1], None, ALU.mult, ALU.bypass),
                waits=[("MM", lastmm)], inc_name="VE")
        gt_mm, kg_ev = mm1024(
            lambda k, m: big3[:, k * 1024 + m * 128: k * 1024 + (m + 1) * 128],
            lambda k, h: big2[:, k * 1024 + h * 512: k * 1024 + (h + 1) * 512],
            ev_kg, [("VE", bvk_ve), ("VE", qcs_ve)],
            extra_rank1=[
                (lambda m: arsl(5, m), lambda h: arh(0, h)),
                (lambda m: arsl(1, m), lambda h: arh(4, h)),
            ])

        kivg_ev = matvec_row(big4, 7, [("VE", kg_ev)], lambda k: ones_col)
        emit("vector", lambda e: e.scalar_tensor_tensor(ar(7), ar(4), EPS_AT, ar(7), ALU.mult, ALU.add),
             waits=[("VE", kivg_ev), ("VE", qinv_ve)])
        emit("vector", lambda e: e.tensor_tensor(ar(7), ar(7), ar(2), ALU.mult))
        emit("vector", lambda e: e.tensor_scalar(ar(7), ar(7), float(D), None, ALU.add, ALU.bypass))
        tail_ve = emit("vector", lambda e: e.reciprocal(ar(5), ar(7)), inc_name="VE")
        emit("vector", lambda e: e.tensor_tensor(ar(6), ar(2), ar(5), ALU.mult),
             waits=[("VE", tail_ve)])
        csc_ve2 = emit("vector", lambda e: e.tensor_scalar(
            ar(6), ar(6), gsc[0:1, 0:1], None, ALU.mult, ALU.bypass), inc_name="VE")
        emit("vector", lambda e: e.tensor_copy(arh(1, 0), junk[0:1, 0:512]),
             waits=[("VE", qsq_cp), ("MM", gt_mm)])
        qsqr_ve = emit("vector", lambda e: e.tensor_copy(arh(1, 1), w2st[0:1, 0:512]),
                       inc_name="VE")
        bv_ld = emit("sync", lambda e: e.dma_start(out=ar(0), in_=bv_d), waits=[("MM", gt_mm)],
                     inc_name="LD")
        bvs_ve = emit("vector", lambda e: e.reduce_sum(gsc[0:1, 1:2], ar(0), axis=AX.X),
                      waits=[("LD", bv_ld)], inc_name="VE")
        bvkg_ev = matvec_row(big4, 4, [("VE", kivg_ev), ("VE", tail_ve), ("VE", v_bv)], lambda k: bv_col[:, k:k + 1])
        emit("vector", lambda e: e.tensor_tensor(ar(4), ar(4), ar(6), ALU.mult),
             waits=[("VE", bvkg_ev), ("VE", csc_ve2), ("VE", bvs_ve)])
        emit("vector", lambda e: e.tensor_scalar(
            ar(0), ar(5), gsc[0:1, 1:2], gsc[0:1, 0:1], ALU.mult, ALU.mult))
        gcon_ve = emit("vector", lambda e: e.tensor_tensor(ar(0), ar(0), ar(4), ALU.add),
                       inc_name="VE")
        bc_mm = emit("tensor", lambda e: e.matmul(ps[4][:, :], ones_row, arh(6, 0),
                                                  start=True, stop=True),
                     waits=[("VE", gcon_ve), ("VE", qsqr_ve)], inc_name="MM")
        bc_mm = emit("tensor", lambda e: e.matmul(ps[5][:, :], ones_row, arh(6, 1),
                                                  start=True, stop=True), inc_name="MM")
        emit("vector", lambda e: e.tensor_copy(bcast[:, 0:512], ps[4][:, :]),
             waits=[("MM", bc_mm)])
        bc_ev = emit("vector", lambda e: e.tensor_copy(bcast[:, 512:1024], ps[5][:, :]),
                     inc_name="VE")
        wv_ld = 0
        for r in range(FC):
            wv_ld = emit("sync", lambda e, r=r: e.dma_start(
                out=big3[:, r * 1024:(r + 1) * 1024], in_=wv_d[r * 128:(r + 1) * 128, :]),
                waits=([("MM", gt_mm)] if r == 0 else []), inc_name="LD")
        wvs_ev = matvec_row(big3, 5, [("LD", wv_ld), ("VE", gcon_ve), ("VE", bc_ev)], lambda k: ones_col)
        def ev_d2(m, h, pt, lastmm):
            return emit("vector", lambda e, m=m, h=h, p=pt: e.tensor_tensor(
                big1[:, m * 1024 + h * 512: m * 1024 + (h + 1) * 512], p[:, :],
                bcast[:, h * 512:(h + 1) * 512], ALU.mult),
                waits=[("MM", lastmm)], inc_name="VE")
        d2_mm, d2_ev = mm1024(
            lambda k, m: big3[:, k * 1024 + m * 128: k * 1024 + (m + 1) * 128],
            lambda k, h: big4[:, k * 1024 + h * 512: k * 1024 + (h + 1) * 512],
            ev_d2, [("VE", wvs_ev)],
            extra_rank1=[(lambda m: arsl(5, m), lambda h: arh(1, h))])
        dg_ve = 0
        for r in range(FC):
            dg_ve = emit("vector", lambda e, r=r: e.tensor_tensor(
                big1[:, r * 1024 + r * 128: r * 1024 + (r + 1) * 128],
                big1[:, r * 1024 + r * 128: r * 1024 + (r + 1) * 128],
                ident[:, :], ALU.add),
                waits=([("VE", d2_ev)] if r == 0 else []), inc_name="VE")
        b2_ld = emit("sync", lambda e: e.dma_start(out=ar(7), in_=b2_d),
                     waits=[("VE", tail_ve)], inc_name="LD")

        def x1sl(c, h=None):
            t = big2 if c < 8 else big3
            cc = c % 8
            if h is None:
                return t[:, cc * 1024:(cc + 1) * 1024]
            return t[:, cc * 1024 + h * 512: cc * 1024 + (h + 1) * 512]

        att_ve = [0] * NCH
        att_ev_ph = {}
        last_att_mm = {}
        mm_h = {}
        for c in range(NCH):
            w = []
            if c == 0:
                w = [("MM", d2_mm), ("SP", sp_done[NCH - 1])]
            if c >= 2:
                w.append(("MM", last_att_mm[c - 2]))
            xnt_ld = emit("sync", lambda e, c=c: e.dma_start(out=stB(c % 2), in_=xnt_d[c]),
                          waits=w, inc_name="LD")
            w2_ = []
            if c < 2:
                w2_.append(("AC", last_apply[c % 2]))
            else:
                w2_.append(("VE", att_ve[c - 2]))
            xr_ld = emit("sync", lambda e, c=c: e.dma_start(
                out=stC(c % 2), in_=x_d[c * 128:(c + 1) * 128, :]), waits=w2_, inc_name="LD")
            for h in range(2):
                pt = ps[(2 * c + h) % 4]
                w = [("LD", xnt_ld)]
                if c == 0 and h == 0:
                    w += [("VE", dg_ve)]
                if 2 * c + h >= 4:
                    w.append(("VE", att_ev_ph[(2 * c + h) - 4]))
                for r in range(FC):
                    mm = emit("tensor", lambda e, p=pt, c=c, r=r, h=h: e.matmul(
                        p[:, :], stB(c % 2)[:, r * 128:(r + 1) * 128],
                        big1[:, r * 1024 + h * 512: r * 1024 + (h + 1) * 512],
                        start=(r == 0), stop=False),
                        waits=(w if r == 0 else []), inc_name="MM")
                mm = emit("tensor", lambda e, p=pt, h=h: e.matmul(
                    p[:, :], ones_row, arh(0, h), start=False, stop=True), inc_name="MM")
                mm_h[h] = mm
            last_att_mm[c] = mm
            b = (c % 2) * 16
            sum0 = mcol[:, b + 0:b + 1]; sum1 = mcol[:, b + 1:b + 2]
            sq0 = mcol[:, b + 2:b + 3]; sq1 = mcol[:, b + 3:b + 4]
            mean_c = mcol[:, b + 4:b + 5]; var_c = mcol[:, b + 5:b + 6]
            std_c = mcol[:, b + 6:b + 7]; rstd_c = mcol[:, b + 7:b + 8]
            negmr_c = mcol[:, b + 8:b + 9]; msc = mcol[:, b + 9:b + 10]
            p0_, p1_ = ps[(2 * c) % 4], ps[(2 * c + 1) % 4]
            t5a = tmp56[:, (c % 2) * 1024:(c % 2) * 1024 + 512]
            t5b = tmp56[:, (c % 2) * 1024 + 512:(c % 2) * 1024 + 1024]
            att_ev_ph[2 * c] = emit("vector", lambda e, t=t5a, p=p0_: e.tensor_copy(t, p[:, :]),
                                    waits=[("MM", mm_h[0])], inc_name="VE")
            ev1 = emit("vector", lambda e, t=t5b, p=p1_: e.tensor_copy(t, p[:, :]),
                       waits=[("MM", mm_h[1])], inc_name="VE")
            att_ev_ph[2 * c + 1] = ev1
            emit("vector", lambda e, s=sum0, t=t5a: e.reduce_sum(s, t, axis=AX.X))
            emit("vector", lambda e, s=sum1, t=t5b: e.reduce_sum(s, t, axis=AX.X))
            emit("scalar", lambda e, s=sq0, t=t5a: e.activation(
                junk[:, :], t, AF.Square, accum_out=s), waits=[("VE", ev1)], inc_name="AC")
            a2 = emit("scalar", lambda e, s=sq1, t=t5b: e.activation(
                junk[:, :], t, AF.Square, accum_out=s), inc_name="AC")
            emit("vector", lambda e, a=sum0, b_=sum1, o=mean_c: e.tensor_tensor(o, a, b_, ALU.add),
                 waits=[("AC", a2)])
            emit("vector", lambda e, o=mean_c: e.tensor_scalar(o, o, 1.0 / D, None, ALU.mult,
                                                               ALU.bypass))
            emit("vector", lambda e, a=sq0, b_=sq1, o=var_c: e.tensor_tensor(o, a, b_, ALU.add))
            emit("vector", lambda e, o=var_c: e.tensor_scalar(o, o, 1.0 / D, None, ALU.mult,
                                                              ALU.bypass))
            emit("vector", lambda e, m=mean_c, o=msc: e.scalar_tensor_tensor(
                o, m, -1.0, m, ALU.mult, ALU.mult))
            v_var = emit("vector", lambda e, o=var_c, s=msc: e.tensor_tensor(o, o, s, ALU.add))
            a3 = emit("scalar", lambda e, v=var_c, o=std_c: e.activation(
                o, v, AF.Sqrt, bias=eps_col), waits=[("VE", v_var)], inc_name="AC")
            emit("vector", lambda e, o=rstd_c, i=std_c: e.reciprocal(o, i), waits=[("AC", a3)])
            nm_ve = emit("vector", lambda e, o=negmr_c, m=mean_c, r=rstd_c: e.scalar_tensor_tensor(
                o, m, -1.0, r, ALU.mult, ALU.mult), inc_name="VE")
            for h in range(2):
                th = tmp56[:, (c % 2) * 1024 + h * 512:(c % 2) * 1024 + (h + 1) * 512]
                emit("vector", lambda e, c=c, h=h, n=negmr_c: e.tensor_scalar(
                    x1sl(c, h), stC(c % 2)[:, h * 512:(h + 1) * 512], n, None,
                    ALU.add, ALU.bypass), waits=[("LD", xr_ld)])
                att_ve[c] = emit("vector", lambda e, c=c, h=h, t=th, r=rstd_c: e.scalar_tensor_tensor(
                    x1sl(c, h), t, r, x1sl(c, h), ALU.mult, ALU.add), inc_name="VE")

        def h1sl(mi, sl=None):
            t = big4 if mi < 16 else big1
            m = mi % 16
            if sl is None:
                return t[:, m * 512:(m + 1) * 512]
            return t[:, m * 512 + sl * 128: m * 512 + (sl + 1) * 128]

        st_done = [0] * NCH
        trv_prev = {0: 0, 1: 0}
        w1_mm_prev = {0: 0, 1: 0}
        w2_mm_prev = {0: 0, 1: 0}
        last_tr3 = {}
        last_psum_free = att_ev_ph[2 * NCH - 1]
        out_ve_prev = 0
        for g in range(NG):
            for cc in range(CPG):
                c = g * CPG + cc
                b = 64 + (cc % 2) * 16
                sum_c = mcol[:, b + 0:b + 1]; sq0 = mcol[:, b + 1:b + 2]; sq1 = mcol[:, b + 2:b + 3]
                mean_c = mcol[:, b + 3:b + 4]; var_c = mcol[:, b + 5:b + 6]
                std_c = mcol[:, b + 6:b + 7]; rstd_c = mcol[:, b + 7:b + 8]
                negmr_c = mcol[:, b + 8:b + 9]; msc = mcol[:, b + 9:b + 10]
                w = [("VE", att_ve[c])]
                emit("vector", lambda e, s=sum_c, c=c: e.reduce_sum(s, x1sl(c), axis=AX.X), waits=w)
                emit("scalar", lambda e, s=sq0, c=c: e.activation(
                    junk[:, :], x1sl(c, 0), AF.Square, accum_out=s), waits=w, inc_name="AC")
                a2 = emit("scalar", lambda e, s=sq1, c=c: e.activation(
                    junk[:, :], x1sl(c, 1), AF.Square, accum_out=s), inc_name="AC")
                emit("vector", lambda e, m=mean_c, s=sum_c: e.tensor_scalar(
                    m, s, 1.0 / D, None, ALU.mult, ALU.bypass), waits=[("AC", a2)])
                emit("vector", lambda e, q0=sq0, q1=sq1, o=var_c: e.tensor_tensor(o, q0, q1, ALU.add))
                emit("vector", lambda e, o=var_c: e.tensor_scalar(o, o, 1.0 / D, None, ALU.mult,
                                                                  ALU.bypass))
                emit("vector", lambda e, m=mean_c, o=msc: e.scalar_tensor_tensor(
                    o, m, -1.0, m, ALU.mult, ALU.mult))
                v_var = emit("vector", lambda e, o=var_c, s=msc: e.tensor_tensor(o, o, s, ALU.add))
                a3 = emit("scalar", lambda e, v=var_c, o=std_c: e.activation(
                    o, v, AF.Sqrt, bias=eps_col), waits=[("VE", v_var)], inc_name="AC")
                emit("vector", lambda e, o=rstd_c, i=std_c: e.reciprocal(o, i), waits=[("AC", a3)])
                nm_ve = emit("vector", lambda e, o=negmr_c, m=mean_c, r=rstd_c: e.scalar_tensor_tensor(
                    o, m, -1.0, r, ALU.mult, ALU.mult), inc_name="VE")
                wa = [("VE", nm_ve)]
                if g == 0 and cc < 2:
                    wa.append(("VE", att_ve[NCH - 2 + cc]))
                if (g, cc) in last_tr3 or (g > 0 or cc >= 2):
                    key = (g, cc - 2) if cc >= 2 else (g - 1, cc + 2)
                    wa.append(("MM", last_tr3.get(key, 0)))
                x1n_ac = emit("scalar", lambda e, c=c, cc=cc, r=rstd_c, n=negmr_c: e.activation(
                    stC(cc % 2), x1sl(c), AF.Identity, bias=n, scale=r), waits=wa, inc_name="AC")
                for r in range(FC):
                    pt = ps[6 + (r % 2)]
                    w = [("AC", x1n_ac), ("VE", trv_prev[r % 2])]
                    if g == 0 and cc == 0 and r < 2:
                        w.append(("VE", att_ve[NCH - 1]))
                    m = emit("tensor", lambda e, p=pt, cc=cc, r=r: e.transpose(
                        p[:, 0:128], stC(cc % 2)[:, r * 128:(r + 1) * 128], ident[:, :]),
                        waits=w, inc_name="MM")
                    emit("vector", lambda e, p=pt, cc=cc, r=r: e.tensor_copy(
                        x1nTb[:, r * 1024 + cc * 128: r * 1024 + (cc + 1) * 128], p[:, 0:128]),
                        waits=[("MM", m)], inc_name="VE")
                    tv = emit("vector", lambda e, p=pt, cc=cc, r=r: e.tensor_tensor(
                        x1nTb[:, r * 1024 + 512 + cc * 128: r * 1024 + 512 + (cc + 1) * 128],
                        p[:, 0:128],
                        x1nTb[:, r * 1024 + cc * 128: r * 1024 + (cc + 1) * 128],
                        ALU.subtract), inc_name="VE")
                    trv_prev[r % 2] = tv
                last_tr3[(g, cc)] = m
                last_tr_ve = tv
            relu_ac = {}
            bank_free = {i: last_psum_free for i in range(8)}
            for ig in range(4):
                for ka in range(FC):
                    w = [("MM", w1_mm_prev[ka % 2])]
                    if g == 0 and ig == 0 and ka == 0:
                        w.append(("SP", w1s_done))
                    ld = emit("sync", lambda e, ig=ig, ka=ka: e.dma_start(
                        out=stDb(ka % 2)[:, 0:1024],
                        in_=w1s_d[ka * 128:(ka + 1) * 128, ig * 1024:(ig + 1) * 1024]),
                        waits=w, inc_name="LD")
                    ld = emit("sync", lambda e, ig=ig, ka=ka: e.dma_start(
                        out=stDb(ka % 2)[:, 1024:2048],
                        in_=w1s_d[ka * 128:(ka + 1) * 128, 4096 + ig * 1024:4096 + (ig + 1) * 1024]),
                        inc_name="LD")
                    wmm0 = [("LD", ld)]
                    if ka == 0:
                        wmm0.append(("VE", last_tr_ve))
                    for mi_l in range(8):
                        wmm = list(wmm0) if mi_l == 0 else []
                        if ka == 0 and bank_free[mi_l]:
                            wmm.append(("VE", bank_free[mi_l]))
                        for pp in range(3):
                            lt = (stDb(ka % 2)[:, mi_l * 128:(mi_l + 1) * 128] if pp < 2
                                  else stDb(ka % 2)[:, 1024 + mi_l * 128:1024 + (mi_l + 1) * 128])
                            rt = (x1nTb[:, ka * 1024:ka * 1024 + 512] if pp != 1
                                  else x1nTb[:, ka * 1024 + 512:ka * 1024 + 1024])
                            mm = emit("tensor", lambda e, p=ps[mi_l], lt=lt, rt=rt, ka=ka, pp=pp: e.matmul(
                                p[:, :], lt, rt,
                                start=(ka == 0 and pp == 0), stop=(ka == FC - 1 and pp == 2)),
                                waits=(wmm if pp == 0 else []), inc_name="MM")
                    w1_mm_prev[ka % 2] = mm
                for mi_l in range(8):
                    mi = ig * 8 + mi_l
                    hb = (big4b if mi < 16 else big1b)
                    hoff = (mi % 16) * 1024
                    rv = emit("vector", lambda e, p=ps[mi_l], mi=mi: e.tensor_scalar(
                        junk[:, 0:512], p[:, :], cols[:, 24 + mi:25 + mi], 0.0, ALU.add, ALU.max),
                        waits=([("MM", mm)] if mi_l == 0 else []), inc_name="VE")
                    hc = emit("scalar", lambda e, hb=hb, hoff=hoff: e.activation(
                        hb[:, hoff:hoff + 512], junk[:, 0:512], AF.Identity),
                        waits=[("VE", rv)], inc_name="AC")
                    ra = emit("vector", lambda e, hb=hb, hoff=hoff: e.tensor_tensor(
                        hb[:, hoff + 512:hoff + 1024], junk[:, 0:512], hb[:, hoff:hoff + 512],
                        ALU.subtract), waits=[("AC", hc)], inc_name="VE")
                    relu_ac[mi] = ra
                    bank_free[mi_l] = rv
                last_psum_free = ra
            for mi in range(32):
                w = [("MM", w2_mm_prev[mi % 2])]
                if g == 0 and mi == 0:
                    w.append(("SP", w2s_done))
                ld = emit("sync", lambda e, mi=mi: e.dma_start(
                    out=w2stb[:, (mi % 2) * 2048:(mi % 2 + 1) * 2048],
                    in_=w2s_d[mi * 128:(mi + 1) * 128, :]), waits=w, inc_name="LD")
                wmm0 = [("LD", ld), ("VE", relu_ac[mi])]
                if mi == 0:
                    wmm0.append(("VE", last_psum_free))
                    wmm0.append(("VE", out_ve_prev))
                hb = (big4b if mi < 16 else big1b)
                hoff = (mi % 16) * 1024
                for cc in range(CPG):
                    for pp in range(3):
                        lt_off = hoff + cc * 128 if pp < 2 else hoff + 512 + cc * 128
                        rbase = (mi % 2) * 2048 + (0 if pp != 1 else 1024)
                        for hh in range(2):
                            rt = w2stb[:, rbase + hh * 512:rbase + (hh + 1) * 512]
                            mm = emit("tensor", lambda e, p=pa[cc], hb=hb, lo=lt_off, rt=rt, hh=hh,
                                      mi=mi, pp=pp: e.matmul(
                                p[:, hh * 512:(hh + 1) * 512], hb[:, lo:lo + 128], rt,
                                start=(mi == 0 and pp == 0), stop=False),
                                waits=(wmm0 if (cc == 0 and pp == 0 and hh == 0) else []),
                                inc_name="MM")
                w2_mm_prev[mi % 2] = mm
            for cc in range(CPG):
                for h in range(2):
                    mm = emit("tensor", lambda e, p=ps[cc * 2 + h], h=h: e.matmul(
                        p[:, :], ones_row, arh(7, h), start=False, stop=True),
                        waits=([("LD", b2_ld)] if (g == 0 and cc == 0 and h == 0) else []),
                        inc_name="MM")
            ffn2_done = mm
            for cc in range(CPG):
                c = g * CPG + cc
                b = (cc % 2) * 16
                sum0 = mcol[:, b + 0:b + 1]; sum1 = mcol[:, b + 1:b + 2]
                sq0 = mcol[:, b + 2:b + 3]; sq1 = mcol[:, b + 3:b + 4]
                mean_c = mcol[:, b + 4:b + 5]; var_c = mcol[:, b + 5:b + 6]
                std_c = mcol[:, b + 6:b + 7]; rstd_c = mcol[:, b + 7:b + 8]
                negmr_c = mcol[:, b + 8:b + 9]; msc = mcol[:, b + 9:b + 10]
                p0_, p1_ = ps[cc * 2], ps[cc * 2 + 1]
                t6a = tmp56[:, (cc % 2) * 1024:(cc % 2) * 1024 + 512]
                t6b = tmp56[:, (cc % 2) * 1024 + 512:(cc % 2) * 1024 + 1024]
                wst = [("MM", ffn2_done)]
                emit("vector", lambda e, t=t6a, p=p0_: e.tensor_copy(t, p[:, :]), waits=wst)
                ev1 = emit("vector", lambda e, t=t6b, p=p1_: e.tensor_copy(t, p[:, :]),
                           inc_name="VE")
                emit("vector", lambda e, s=sum0, t=t6a: e.reduce_sum(s, t, axis=AX.X))
                emit("vector", lambda e, s=sum1, t=t6b: e.reduce_sum(s, t, axis=AX.X))
                emit("scalar", lambda e, s=sq0, t=t6a: e.activation(
                    junk[:, :], t, AF.Square, accum_out=s), waits=[("VE", ev1)], inc_name="AC")
                a2 = emit("scalar", lambda e, s=sq1, t=t6b: e.activation(
                    junk[:, :], t, AF.Square, accum_out=s), inc_name="AC")
                emit("vector", lambda e, a=sum0, b_=sum1, o=mean_c: e.tensor_tensor(
                    o, a, b_, ALU.add), waits=[("AC", a2)])
                emit("vector", lambda e, o=mean_c: e.tensor_scalar(o, o, 1.0 / D, None, ALU.mult,
                                                                   ALU.bypass))
                emit("vector", lambda e, a=sq0, b_=sq1, o=var_c: e.tensor_tensor(o, a, b_, ALU.add))
                emit("vector", lambda e, o=var_c: e.tensor_scalar(o, o, 1.0 / D, None, ALU.mult,
                                                                  ALU.bypass))
                emit("vector", lambda e, m=mean_c, o=msc: e.scalar_tensor_tensor(
                    o, m, -1.0, m, ALU.mult, ALU.mult))
                v_var = emit("vector", lambda e, o=var_c, s=msc: e.tensor_tensor(o, o, s, ALU.add))
                a3 = emit("scalar", lambda e, v=var_c, o=std_c: e.activation(
                    o, v, AF.Sqrt, bias=eps_col), waits=[("VE", v_var)], inc_name="AC")
                emit("vector", lambda e, o=rstd_c, i=std_c: e.reciprocal(o, i), waits=[("AC", a3)])
                nm_ve = emit("vector", lambda e, o=negmr_c, m=mean_c, r=rstd_c: e.scalar_tensor_tensor(
                    o, m, -1.0, r, ALU.mult, ALU.mult), inc_name="VE")
                for h in range(2):
                    t6h = tmp56[:, (cc % 2) * 1024 + h * 512:(cc % 2) * 1024 + (h + 1) * 512]
                    emit("vector", lambda e, c=c, h=h, n=negmr_c: e.tensor_scalar(
                        x1sl(c, h), x1sl(c, h), n, None, ALU.add, ALU.bypass),
                        waits=[("VE", nm_ve)])
                    ov = emit("vector", lambda e, c=c, h=h, t=t6h, r=rstd_c: e.scalar_tensor_tensor(
                        x1sl(c, h), t, r, x1sl(c, h), ALU.mult, ALU.add), inc_name="VE")
                out_ve_prev = ov
                last_psum_free_cand = ev1
                st_done[c] = emit("gpsimd", lambda e, c=c: e.dma_start(
                    out=y_d[c * 128:(c + 1) * 128, :], in_=x1sl(c)),
                    waits=[("VE", ov)], inc_name="SP")
            last_psum_free = last_psum_free_cand

        emit("gpsimd", lambda e: e.nop(), waits=[("SP", st_done[NCH - 1])])

        totals = {k: v[2] for k, v in cnt.items()}
        for (w, v) in waits_seen:
            assert v <= totals[w], f"wait {w} {v} > total {totals[w]}"

        @blk.sync
        def _(eng):
            for f in E["sync"]:
                f(eng)

        @blk.tensor
        def _(eng):
            for f in E["tensor"]:
                f(eng)

        @blk.vector
        def _(eng):
            for f in E["vector"]:
                f(eng)

        @blk.scalar
        def _(eng):
            for f in E["scalar"]:
                f(eng)

        @blk.gpsimd
        def _(eng):
            for f in E["gpsimd"]:
                f(eng)

    return nc


_NC_CACHE = None


def kernel(L0_attn_x, L0_attn_Wq, L0_attn_bq, L0_attn_Wk, L0_attn_bk,
           L0_attn_Wv, L0_attn_bv, L0_attn_gamma,
           L0_ffn_W1, L0_ffn_b1, L0_ffn_W2, L0_ffn_b2):
    global _NC_CACHE
    x = np.ascontiguousarray(np.asarray(L0_attn_x, dtype=np.float32).reshape(16384, 1024))
    com = dict(
        wq=np.ascontiguousarray(np.asarray(L0_attn_Wq, dtype=np.float32)),
        wk=np.ascontiguousarray(np.asarray(L0_attn_Wk, dtype=np.float32)),
        wv=np.ascontiguousarray(np.asarray(L0_attn_Wv, dtype=np.float32)),
        bq=np.asarray(L0_attn_bq, dtype=np.float32).reshape(1, 1024),
        bk=np.asarray(L0_attn_bk, dtype=np.float32).reshape(1, 1024),
        bv_row=np.asarray(L0_attn_bv, dtype=np.float32).reshape(1, 1024),
        bv8=np.ascontiguousarray(np.asarray(L0_attn_bv, dtype=np.float32).reshape(8, 128)),
        gamma=np.asarray(L0_attn_gamma, dtype=np.float32).reshape(1, 1),
        w1=np.ascontiguousarray(np.asarray(L0_ffn_W1, dtype=np.float32)),
        b1_32=np.ascontiguousarray(np.asarray(L0_ffn_b1, dtype=np.float32).reshape(32, 128)),
        w2=np.ascontiguousarray(np.asarray(L0_ffn_W2, dtype=np.float32)),
        b2=np.asarray(L0_ffn_b2, dtype=np.float32).reshape(1, 1024),
        ident=np.eye(128, dtype=np.float32),
    )
    if _NC_CACHE is None:
        _NC_CACHE = build_core()
    nc = _NC_CACHE
    in_maps = []
    for i in range(8):
        m = dict(com)
        m["x"] = np.ascontiguousarray(x[i * TOK:(i + 1) * TOK, :])
        in_maps.append(m)
    res = run_bass_kernel_spmd(nc, in_maps, core_ids=list(range(8)))
    out = np.concatenate([res.results[i]["y"] for i in range(8)], axis=0)
    return out.reshape(1, 16384, 1024).astype(np.float32)


# version 07
# speedup vs baseline: 1.0927x; 1.0163x over previous
import numpy as np
import concourse.bass as bass
import concourse.mybir as mybir
from concourse.bass_utils import run_bass_kernel_spmd

F32 = mybir.dt.float32
BF16 = mybir.dt.bfloat16
AF = mybir.ActivationFunctionType
ALU = mybir.AluOpType
AX = mybir.AxisListType

TOK = 2048
NCH = 16
D = 1024
FC = 8
INTER = 4096
NG = 4
CPG = 4
CTOT = 16384.0
EPS_LN = 1e-5
EPS_AT = 1e-6


def build_core():
    nc = bass.Bass("TRN2", target_bir_lowering=False)
    dt = lambda n, s: nc.dram_tensor(n, s, F32, kind="ExternalInput").ap()
    x_d = dt("x", [TOK, D])
    wq_d = dt("wq", [D, D]); wk_d = dt("wk", [D, D]); wv_d = dt("wv", [D, D])
    bq_d = dt("bq", [1, D]); bk_d = dt("bk", [1, D])
    bv_d = dt("bv_row", [1, D]); bv8_d = dt("bv8", [8, 128])
    gam_d = dt("gamma", [1, 1])
    w1_d = dt("w1", [D, INTER]); b1_d = dt("b1_32", [32, 128])
    w2_d = dt("w2", [INTER, D]); b2_d = dt("b2", [1, D])
    id_d = dt("ident", [128, 128])
    y_d = nc.dram_tensor("y", [TOK, D], F32, kind="ExternalOutput").ap()
    ccin_h = nc.dram_tensor("ccin", [1025, 1024], F32)
    ccout_h = nc.dram_tensor("ccout", [1025, 1024], F32)
    ccin, ccout = ccin_h.ap(), ccout_h.ap()
    xnt_d = nc.dram_tensor("xnt", [NCH, 128, D], F32).ap()
    w1s_d = nc.dram_tensor("w1s", [D, 2 * INTER], BF16).ap()
    w2s_d = nc.dram_tensor("w2s", [INTER, 2 * D], BF16).ap()

    import contextlib
    with contextlib.ExitStack() as stack:
        en = stack.enter_context
        big1 = en(nc.sbuf_tensor("big1", [128, 8192], F32))
        big2 = en(nc.sbuf_tensor("big2", [128, 8192], F32))
        big3 = en(nc.sbuf_tensor("big3", [128, 8192], F32))
        big4 = en(nc.sbuf_tensor("big4", [128, 8192], F32))
        util = en(nc.sbuf_tensor("util", [128, 8192], F32))
        w2st = en(nc.sbuf_tensor("w2st", [128, 2048], F32))
        arena = en(nc.sbuf_tensor("arena", [128, 8192], F32))
        cols = en(nc.sbuf_tensor("cols", [128, 64], F32))
        mcol = en(nc.sbuf_tensor("mcol", [128, 128], F32))
        bcast = en(nc.sbuf_tensor("bcast", [128, 1024], F32))
        ident = en(nc.sbuf_tensor("ident_sb", [128, 128], F32))
        ones = en(nc.sbuf_tensor("ones", [128, 128], F32))
        gsc = en(nc.sbuf_tensor("gsc", [128, 8], F32))
        junk = en(nc.sbuf_tensor("junk", [128, 512], F32))
        pa = [en(nc.psum_tensor(f"pa{i}", [128, 1024], F32)) for i in range(4)]
        ps = [pa[i // 2][:, (i % 2) * 512:(i % 2 + 1) * 512] for i in range(8)]
        ps4, ps5, ps6, ps7 = ps[4], ps[5], ps[6], ps[7]
        sLD = en(nc.semaphore("sLD"))
        sSP = en(nc.semaphore("sSP"))
        sMM = en(nc.semaphore("sMM"))
        sVE = en(nc.semaphore("sVE"))
        sAC = en(nc.semaphore("sAC"))
        sCC = en(nc.semaphore("sCC"))
        sL2 = en(nc.semaphore("sL2"))
        blk = en(nc.Block())
        stA = lambda b: util[:, (0 + b) * 1024:(1 + b) * 1024]
        stB = lambda b: util[:, (2 + b) * 1024:(3 + b) * 1024]
        stC = lambda b: util[:, (4 + b) * 1024:(5 + b) * 1024]
        stD = lambda b: util[:, (6 + b) * 1024:(7 + b) * 1024]
        x1nT = util[:, 0:4096]
        tmp56 = util[:, 0:2048]
        big1b = big1.bitcast(BF16)
        big2b = big2.bitcast(BF16)
        big4b = big4.bitcast(BF16)
        utilb = util.bitcast(BF16)
        w2stb = w2st.bitcast(BF16)
        x1nTb = utilb[:, 0:8192]
        stBb = lambda b: utilb[:, (2 + b) * 2048:(3 + b) * 2048]
        stDb = lambda b: utilb[:, (6 + b) * 2048:(7 + b) * 2048]
        ar = lambda i: arena[0:1, i * 1024:(i + 1) * 1024]
        arh = lambda i, h: arena[0:1, i * 1024 + h * 512: i * 1024 + (h + 1) * 512]
        arsl = lambda i, m: arena[0:1, i * 1024 + m * 128: i * 1024 + (m + 1) * 128]
        cs_col = cols[:, 0:8]; kinv_col = cols[:, 8:16]
        bv_col = cols[:, 16:24]; bvk_col = cols[:, 56:64]
        ones_col = ones[:, 0:1]; ones_row = ones[0:1, 0:128]

        E = {k: [] for k in ("sync", "tensor", "vector", "scalar", "gpsimd")}
        cnt = {"LD": [sLD, 16, 0], "SP": [sSP, 16, 0], "MM": [sMM, 1, 0],
               "VE": [sVE, 1, 0], "AC": [sAC, 1, 0], "CC": [sCC, 1, 0],
               "L2": [sL2, 16, 0]}
        waits_seen = []

        def emit(eng, f, waits=(), inc_name=None):
            waits = list(waits)
            auto = {"vector": "VE", "scalar": "AC"}.get(eng)
            if auto is not None:
                if inc_name is None:
                    inc_name = auto
                waits.append((auto, cnt[auto][2]))
            if inc_name in ("LD", "SP", "L2"):
                waits.append((inc_name, cnt[inc_name][2]))
            wl = [(cnt[w][0], v, w) for (w, v) in waits if v]
            for (_, v, w) in wl:
                waits_seen.append((w, v))
            step = 0
            sem = None
            if inc_name:
                c = cnt[inc_name]
                c[2] += c[1]
                step = c[1]
                sem = c[0]
            tgt = cnt[inc_name][2] if inc_name else None

            def g(eng_obj, f=f, wl=wl, sem=sem, step=step):
                for (s, v, _) in wl:
                    eng_obj.wait_ge(s, v)
                ins = f(eng_obj)
                if sem is not None:
                    ins.then_inc(sem, step)
            E[eng].append(g)
            return tgt

        emit("vector", lambda e: e.memset(ones[:, :], 1.0))
        ve_init = emit("vector", lambda e: e.memset(gsc[:, 2:3], EPS_LN), inc_name="VE")
        eps_col = gsc[:, 2:3]
        ld_ident = emit("sync", lambda e: e.dma_start(out=ident[:, :], in_=id_d), inc_name="LD")
        ld_bq = emit("sync", lambda e: e.dma_start(out=ar(0), in_=bq_d), inc_name="LD")
        ld_bk = emit("sync", lambda e: e.dma_start(out=ar(1), in_=bk_d), inc_name="LD")
        ld_gam = emit("sync", lambda e: e.dma_start(out=gsc[0:1, 0:1], in_=gam_d), inc_name="LD")
        ld_bv8 = emit("sync", lambda e: e.dma_start(out=big2[0:8, 0:128], in_=bv8_d), inc_name="LD")
        ld_b132 = emit("sync", lambda e: e.dma_start(out=big2[0:32, 128:256], in_=b1_d), inc_name="LD")
        t_bv = emit("tensor", lambda e: e.transpose(ps6[:, 0:8], big2[0:8, 0:128], ident[0:8, 0:8]),
                    waits=[("LD", ld_bv8), ("LD", ld_ident)], inc_name="MM")
        t_b1 = emit("tensor", lambda e: e.transpose(ps7[:, 0:32], big2[0:32, 128:256], ident[0:32, 0:32]),
                    waits=[("LD", ld_b132)], inc_name="MM")
        v_bv = emit("vector", lambda e: e.tensor_copy(bv_col, ps6[:, 0:8]), waits=[("MM", t_bv)],
                    inc_name="VE")
        v_b1 = emit("vector", lambda e: e.tensor_copy(cols[:, 24:56], ps7[:, 0:32]),
                    waits=[("MM", t_b1)], inc_name="VE")

        x_ld = [0] * NCH
        xn_ac = [0] * NCH
        sp_done = [0] * NCH
        ev_done = [0] * NCH
        cs_mm_c = [0] * NCH
        tr_ev = {0: v_bv, 1: v_b1}
        w2_in = {0: 0, 1: 0}; w2_lsub = {0: 0, 1: 0}; w2_out = {0: 0, 1: 0}
        for c in range(NCH):
            w = [("AC", xn_ac[c - 2])] if c >= 2 else []
            x_ld[c] = emit("sync", lambda e, c=c: e.dma_start(
                out=stC(c % 2), in_=x_d[c * 128:(c + 1) * 128, :]), waits=w, inc_name="LD")
            b = (c % 2) * 16
            sum_c = mcol[:, b + 0:b + 1]; sq0 = mcol[:, b + 1:b + 2]; sq1 = mcol[:, b + 2:b + 3]
            mean_c = mcol[:, b + 3:b + 4]; msq = mcol[:, b + 4:b + 5]; var_c = mcol[:, b + 5:b + 6]
            std_c = mcol[:, b + 6:b + 7]
            rstd_c = mcol[:, 32 + c:33 + c]
            negmr_c = mcol[:, 48 + c:49 + c]
            emit("vector", lambda e, s=sum_c, c=c: e.reduce_sum(s, stC(c % 2), axis=AX.X),
                 waits=[("LD", x_ld[c])])
            emit("scalar", lambda e, s=sq0, c=c: e.activation(
                junk[:, :], stC(c % 2)[:, 0:512], AF.Square, accum_out=s),
                waits=[("LD", x_ld[c])], inc_name="AC")
            ac2 = emit("scalar", lambda e, s=sq1, c=c: e.activation(
                junk[:, :], stC(c % 2)[:, 512:1024], AF.Square, accum_out=s), inc_name="AC")
            emit("vector", lambda e, m=mean_c, s=sum_c: e.tensor_scalar(
                m, s, 1.0 / D, None, ALU.mult, ALU.bypass), waits=[("AC", ac2)])
            emit("vector", lambda e, q=sq0, q1=sq1, o=msq: e.tensor_tensor(o, q, q1, ALU.add))
            emit("vector", lambda e, m=mean_c, o=var_c: e.scalar_tensor_tensor(
                o, m, -1.0, m, ALU.mult, ALU.mult))
            v_var = emit("vector", lambda e, q=msq, o=var_c: e.scalar_tensor_tensor(
                o, q, 1.0 / D, o, ALU.mult, ALU.add))
            ac3 = emit("scalar", lambda e, v=var_c, o=std_c: e.activation(
                o, v, AF.Sqrt, bias=eps_col),
                waits=([("VE", ve_init), ("VE", v_var)] if c == 0 else [("VE", v_var)]),
                inc_name="AC")
            emit("vector", lambda e, o=rstd_c, i=std_c: e.reciprocal(o, i), waits=[("AC", ac3)])
            st_ve = emit("vector", lambda e, o=negmr_c, m=mean_c, r=rstd_c: e.scalar_tensor_tensor(
                o, m, -1.0, r, ALU.mult, ALU.mult), inc_name="VE")
            w = [("VE", st_ve)]
            if c >= 2:
                w.append(("MM", cs_mm_c[c - 2]))
            xn_ac[c] = emit("scalar", lambda e, c=c, r=rstd_c, n=negmr_c: e.activation(
                stA(c % 2), stC(c % 2), AF.Identity, bias=n, scale=r), waits=w, inc_name="AC")
            for r in range(FC):
                pt = ps[6 + (r % 2)]
                w = [("AC", xn_ac[c]), ("VE", tr_ev[r % 2])]
                m = emit("tensor", lambda e, p=pt, c=c, r=r: e.transpose(
                    p[:, 0:128], stA(c % 2)[:, r * 128:(r + 1) * 128], ident[:, :]),
                    waits=w, inc_name="MM")
                wv = [("MM", m)]
                if r == 0 and c >= 2:
                    wv.append(("SP", sp_done[c - 2]))
                emit("vector", lambda e, p=pt, c=c, r=r: e.tensor_copy(
                    stBb(c % 2)[:, r * 256:r * 256 + 128], p[:, 0:128]), waits=wv, inc_name="VE")
                ev = emit("vector", lambda e, p=pt, c=c, r=r: e.tensor_tensor(
                    stBb(c % 2)[:, r * 256 + 128:r * 256 + 256], p[:, 0:128],
                    stBb(c % 2)[:, r * 256:r * 256 + 128], ALU.subtract), inc_name="VE")
                tr_ev[r % 2] = ev
            ev_done[c] = ev
            emit("tensor", lambda e, c=c: e.matmul(
                ps4[0:1, :], ones_col, stA(c % 2)[:, 0:512],
                start=(c == 0), stop=(c == NCH - 1)), inc_name="MM")
            cs_mm_c[c] = emit("tensor", lambda e, c=c: e.matmul(
                ps5[0:1, :], ones_col, stA(c % 2)[:, 512:1024],
                start=(c == 0), stop=(c == NCH - 1)), inc_name="MM")
            sp_done[c] = emit("gpsimd", lambda e, c=c: e.dma_start(
                out=xnt_d[c], in_=stB(c % 2)), waits=[("VE", ev_done[c])], inc_name="SP")
            for j in (2 * c, 2 * c + 1):
                buf = j % 2
                srcw = big2[:, buf * 1024:(buf + 1) * 1024]
                dstw = big2b[:, 8192 + buf * 2048:8192 + (buf + 1) * 2048]
                wlw = [("VE", w2_lsub[buf])] if w2_lsub[buf] else [("MM", t_b1)]
                w2_in[buf] = emit("sync", lambda e, j=j, s=srcw: e.dma_start(
                    out=s, in_=w2_d[j * 128:(j + 1) * 128, :]), waits=wlw, inc_name="LD")
                wlh = [("LD", w2_in[buf])]
                if w2_out[buf]:
                    wlh.append(("SP", w2_out[buf]))
                hcp = emit("scalar", lambda e, s=srcw, d=dstw: e.activation(
                    d[:, 0:1024], s, AF.Identity), waits=wlh, inc_name="AC")
                w2_lsub[buf] = emit("vector", lambda e, s=srcw, d=dstw: e.tensor_tensor(
                    d[:, 1024:2048], s, d[:, 0:1024], ALU.subtract), waits=[("AC", hcp)],
                    inc_name="VE")
                w2_out[buf] = emit("gpsimd", lambda e, j=j, d=dstw: e.dma_start(
                    out=w2s_d[j * 128:(j + 1) * 128, :], in_=d),
                    waits=[("VE", w2_lsub[buf])], inc_name="SP")

        w2s_done = w2_out[1]
        emit("vector", lambda e: e.tensor_copy(arh(6, 0), ps4[0:1, :]),
             waits=[("MM", cs_mm_c[NCH - 1])])
        cs_ve = emit("vector", lambda e: e.tensor_copy(arh(6, 1), ps5[0:1, :]), inc_name="VE")

        s_evacs = []
        last_apply = {0: xn_ac[NCH - 2], 1: xn_ac[NCH - 1]}
        last_s_mm = {0: 0, 1: 0}
        split_free = {0: cs_mm_c[NCH - 2], 1: cs_mm_c[NCH - 1]}
        split_kind = {0: "MM", 1: "MM"}
        stB_free = {0: sp_done[NCH - 2], 1: sp_done[NCH - 1]}
        w1_in = 0; w1_lsub = 0; w1_out = 0
        for p_i in range(2):
            for c in range(NCH):
                lx = emit("sync", lambda e, c=c: e.dma_start(
                    out=stC(c % 2), in_=x_d[c * 128:(c + 1) * 128, :]),
                    waits=[("AC", last_apply[c % 2])], inc_name="LD")
                aply = emit("scalar", lambda e, c=c: e.activation(
                    stA(c % 2), stC(c % 2), AF.Identity,
                    bias=mcol[:, 48 + c:49 + c], scale=mcol[:, 32 + c:33 + c]),
                    waits=[("LD", lx), (split_kind[c % 2], split_free[c % 2])], inc_name="AC")
                last_apply[c % 2] = aply
                wsp = [("AC", aply), ("SP", stB_free[c % 2])]
                if last_s_mm[c % 2]:
                    wsp.append(("MM", last_s_mm[c % 2]))
                hcp = emit("scalar", lambda e, c=c: e.activation(
                    stBb(c % 2)[:, 0:1024], stA(c % 2), AF.Identity), waits=wsp, inc_name="AC")
                lsub = emit("vector", lambda e, c=c: e.tensor_tensor(
                    stBb(c % 2)[:, 1024:2048], stA(c % 2), stBb(c % 2)[:, 0:1024],
                    ALU.subtract), waits=[("AC", hcp)], inc_name="VE")
                split_free[c % 2] = lsub
                split_kind[c % 2] = "VE"
                stB_free[c % 2] = 0
                wmm = [("VE", lsub)]
                if c == 0 and p_i == 0:
                    wmm += [("VE", cs_ve), ("VE", ev_done[NCH - 1])]
                if c == 0 and p_i == 1:
                    wmm.append(("VE", s_evacs[-1]))
                for m in range(4):
                    mg = m + 4 * p_i
                    for pp in range(3):
                        lt = (stBb(c % 2)[:, mg * 128:(mg + 1) * 128] if pp < 2
                              else stBb(c % 2)[:, 1024 + mg * 128:1024 + (mg + 1) * 128])
                        rbase = 0 if pp != 1 else 1024
                        for hh in range(2):
                            rt = stBb(c % 2)[:, rbase + hh * 512:rbase + (hh + 1) * 512]
                            mm = emit("tensor", lambda e, m=m, lt=lt, rt=rt, hh=hh, c=c, pp=pp: e.matmul(
                                pa[m][:, hh * 512:(hh + 1) * 512], lt, rt,
                                start=(c == 0 and pp == 0), stop=(c == NCH - 1 and pp == 2)),
                                waits=(wmm if (m == 0 and pp == 0 and hh == 0) else []),
                                inc_name="MM")
                last_s_mm[c % 2] = mm
                if c % 4 == 0:
                    k = p_i * 4 + c // 4
                    wlw = []
                    if k == 0:
                        wlw = [("VE", w2_lsub[1]), ("SP", w2_out[1]), ("SP", w2_out[0])]
                    elif w1_lsub:
                        wlw = [("VE", w1_lsub), ("SP", w1_out)]
                    w1_in = emit("sync", lambda e, k=k: e.dma_start(
                        out=big2[:, 0:4096], in_=w1_d[k * 128:(k + 1) * 128, :]),
                        waits=wlw, inc_name="LD")
                    for q in range(4):
                        wlh = [("LD", w1_in)] if q == 0 else []
                        if q == 0 and w1_out:
                            wlh.append(("SP", w1_out))
                        hq = emit("scalar", lambda e, q=q: e.activation(
                            big2b[:, 8192 + q * 2048:8192 + q * 2048 + 1024],
                            big2[:, q * 1024:(q + 1) * 1024], AF.Identity),
                            waits=wlh, inc_name="AC")
                        w1_lsub = emit("vector", lambda e, q=q: e.tensor_tensor(
                            big2b[:, 8192 + q * 2048 + 1024:8192 + (q + 1) * 2048],
                            big2[:, q * 1024:(q + 1) * 1024],
                            big2b[:, 8192 + q * 2048:8192 + q * 2048 + 1024],
                            ALU.subtract), waits=[("AC", hq)], inc_name="VE")
                    w1_out = emit("gpsimd", lambda e, k=k: e.dma_start(
                        out=w1s_d[k * 128:(k + 1) * 128, :], in_=big2b[:, 8192:16384]),
                        waits=[("VE", w1_lsub)], inc_name="SP")
            for m in range(4):
                mg = m + 4 * p_i
                sev = emit("vector", lambda e, m=m, mg=mg: e.tensor_copy(
                    big1[:, mg * 1024:(mg + 1) * 1024], pa[m][:, :]),
                    waits=([("MM", mm)] if m == 0 else []), inc_name="VE")
                s_evacs.append(sev)
        w1s_done = w1_out

        spx = 0
        for r in range(FC):
            spx = emit("gpsimd", lambda e, r=r: e.dma_start(
                out=ccin[r * 128:(r + 1) * 128, :], in_=big1[:, r * 1024:(r + 1) * 1024]),
                waits=([("VE", s_evacs[-1]), ("VE", cs_ve)] if r == 0 else []), inc_name="SP")
        spx = emit("gpsimd", lambda e: e.dma_start(out=ccin[1024:1025, :], in_=ar(6)),
                   inc_name="SP")
        cc_t = emit("gpsimd", lambda e: e.collective_compute(
            "AllReduce", ALU.add, replica_groups=[list(range(8))],
            ins=[ccin_h.ap().opt()], outs=[ccout_h.ap().opt()]),
            waits=[("SP", spx)], inc_name="CC")

        def transpose_1024(src, dst, wait_first):
            evs = []
            for rr in range(FC):
                for cc2 in range(FC):
                    i = rr * FC + cc2
                    pt = ps[i % 4]
                    w = list(wait_first) if i == 0 else []
                    if i >= 4:
                        w.append(("VE", evs[i - 4]))
                    m = emit("tensor", lambda e, p=pt, rr=rr, cc2=cc2, src=src: e.transpose(
                        p[:, 0:128], src[:, rr * 1024 + cc2 * 128: rr * 1024 + (cc2 + 1) * 128],
                        ident[:, :]), waits=w, inc_name="MM")
                    ev = emit("vector", lambda e, p=pt, rr=rr, cc2=cc2, dst=dst: e.tensor_copy(
                        dst[:, cc2 * 1024 + rr * 128: cc2 * 1024 + (rr + 1) * 128], p[:, 0:128]),
                        waits=[("MM", m)], inc_name="VE")
                    evs.append(ev)
            return m, ev

        def mm1024(lhsT_fn, rhs_fn, evac_fn, wait_first, extra_rank1=()):
            evs = {}
            lastmm = 0
            for m in range(FC):
                for h in range(2):
                    i = m * 2 + h
                    pt = ps[i % 4]
                    w = list(wait_first) if i == 0 else []
                    if i >= 4:
                        w.append(("VE", evs[i - 4]))
                    nr = len(extra_rank1)
                    for k in range(FC):
                        lastmm = emit("tensor", lambda e, p=pt, k=k, m=m, h=h: e.matmul(
                            p[:, :], lhsT_fn(k, m), rhs_fn(k, h),
                            start=(k == 0), stop=(k == FC - 1 and nr == 0)),
                            waits=(w if k == 0 else []), inc_name="MM")
                    for j, (lf, rf) in enumerate(extra_rank1):
                        lastmm = emit("tensor", lambda e, p=pt, lf=lf, rf=rf, m=m, h=h,
                                      st=(j == nr - 1): e.matmul(
                            p[:, :], lf(m), rf(h), start=False, stop=st), inc_name="MM")
                    evs[i] = evac_fn(m, h, pt, lastmm)
            return lastmm, evs[15]

        def ev_to(dst):
            def f(m, h, pt, lastmm):
                return emit("vector", lambda e, m=m, h=h, p=pt: e.tensor_copy(
                    dst[:, m * 1024 + h * 512: m * 1024 + (h + 1) * 512], p[:, :]),
                    waits=[("MM", lastmm)], inc_name="VE")
            return f

        def matvec_row(rhs_src, dst_slot, wait_first, lhs_fn):
            last_ev = 0
            for h in range(2):
                for k in range(FC):
                    m = emit("tensor", lambda e, k=k, h=h: e.matmul(
                        ps[4 + h][0:1, :], lhs_fn(k),
                        rhs_src[:, k * 1024 + h * 512: k * 1024 + (h + 1) * 512],
                        start=(k == 0), stop=(k == FC - 1)),
                        waits=(list(wait_first) if (h == 0 and k == 0) else []), inc_name="MM")
                last_ev = emit("vector", lambda e, h=h, m=m, d=dst_slot: e.tensor_copy(
                    arh(d, h), ps[4 + h][0:1, :]), waits=[("MM", m)], inc_name="VE")
            return last_ev

        def row_transpose8(src_row, dst_col8, wait_first):
            last = 0
            for k in range(FC):
                w = list(wait_first) if k == 0 else [("VE", last)]
                m = emit("tensor", lambda e, k=k, s=src_row: e.transpose(
                    ps[4][:, 0:1], s[0:1, k * 128:(k + 1) * 128], ident[0:1, 0:1]),
                    waits=w, inc_name="MM")
                last = emit("vector", lambda e, k=k, m=m, d=dst_col8: e.tensor_copy(
                    d[:, k:k + 1], ps[4][:, 0:1]), waits=[("MM", m)], inc_name="VE")
            return last


        wq_ld = 0
        for r in range(FC):
            wq_ld = emit("sync", lambda e, r=r: e.dma_start(
                out=big3[:, r * 1024:(r + 1) * 1024], in_=wq_d[r * 128:(r + 1) * 128, :]),
                inc_name="LD")
        wqt_mm, wqt_ev = transpose_1024(big3, big2, [
            ("LD", wq_ld), ("VE", s_evacs[-1]), ("SP", w1s_done), ("VE", w1_lsub)])
        wk_ld = 0
        for r in range(FC):
            wk_ld = emit("sync", lambda e, r=r: e.dma_start(
                out=big3[:, r * 1024:(r + 1) * 1024], in_=wk_d[r * 128:(r + 1) * 128, :]),
                waits=([("MM", wqt_mm)] if r == 0 else []), inc_name="LD")
        _, wkt_ev = transpose_1024(big3, big4, [("LD", wk_ld), ("VE", wqt_ev)])
        for r in range(FC):
            emit("sync", lambda e, r=r: e.dma_start(
                out=big1[:, r * 1024:(r + 1) * 1024], in_=ccout[r * 128:(r + 1) * 128, :]),
                waits=([("CC", cc_t)] if r == 0 else []), inc_name="LD")
        gl_ld = emit("sync", lambda e: e.dma_start(out=ar(6), in_=ccout[1024:1025, :]),
                     inc_name="LD")
        csc_ev = row_transpose8(ar(6), cs_col, [("LD", gl_ld), ("VE", wkt_ev)])
        swqt_mm, swqt_ev = mm1024(
            lambda k, m: big1[:, k * 1024 + m * 128: k * 1024 + (m + 1) * 128],
            lambda k, h: big2[:, k * 1024 + h * 512: k * 1024 + (h + 1) * 512],
            ev_to(big3), [("LD", gl_ld), ("VE", wkt_ev)])
        wqcs_ev = matvec_row(big2, 4, [("VE", csc_ev), ("VE", swqt_ev)], lambda k: cs_col[:, k:k + 1])
        pq_ev = 0
        for r in range(FC):
            pq_ev = emit("vector", lambda e, r=r: e.tensor_tensor(
                big3[:, r * 1024:(r + 1) * 1024], big3[:, r * 1024:(r + 1) * 1024],
                big2[:, r * 1024:(r + 1) * 1024], ALU.mult),
                waits=([("VE", swqt_ev)] if r == 0 else []), inc_name="VE")
        qsq_ev = matvec_row(big3, 2, [("VE", pq_ev), ("VE", wqcs_ev)], lambda k: ones_col)
        swkt_mm, swkt_ev = mm1024(
            lambda k, m: big1[:, k * 1024 + m * 128: k * 1024 + (m + 1) * 128],
            lambda k, h: big4[:, k * 1024 + h * 512: k * 1024 + (h + 1) * 512],
            ev_to(big3), [("VE", qsq_ev)])
        wkcs_ev = matvec_row(big4, 5, [("VE", swkt_ev)], lambda k: cs_col[:, k:k + 1])
        pk_ev = 0
        for r in range(FC):
            pk_ev = emit("vector", lambda e, r=r: e.tensor_tensor(
                big4[:, r * 1024:(r + 1) * 1024], big3[:, r * 1024:(r + 1) * 1024],
                big4[:, r * 1024:(r + 1) * 1024], ALU.mult),
                waits=([("VE", wkcs_ev)] if r == 0 else []), inc_name="VE")
        ksq_ev = matvec_row(big4, 3, [("VE", pk_ev)], lambda k: ones_col)

        tmprow = bcast[0:1, :]
        qcs_ve = emit("vector", lambda e: e.scalar_tensor_tensor(
            ar(4), ar(0), CTOT, ar(4), ALU.mult, ALU.add),
            waits=[("VE", ksq_ev), ("LD", ld_bq), ("LD", ld_bk), ("LD", ld_gam)], inc_name="VE")
        emit("vector", lambda e: e.scalar_tensor_tensor(tmprow, ar(0), -CTOT, ar(4), ALU.mult, ALU.add))
        emit("vector", lambda e: e.tensor_tensor(tmprow, tmprow, ar(4), ALU.add))
        emit("vector", lambda e: e.tensor_tensor(tmprow, tmprow, ar(0), ALU.mult))
        emit("vector", lambda e: e.tensor_tensor(ar(2), ar(2), tmprow, ALU.add))
        emit("vector", lambda e: e.scalar_tensor_tensor(tmprow, ar(1), CTOT, ar(5), ALU.mult, ALU.add))
        emit("vector", lambda e: e.tensor_tensor(tmprow, tmprow, ar(5), ALU.add))
        emit("vector", lambda e: e.tensor_tensor(tmprow, tmprow, ar(1), ALU.mult))
        kfull_ve = emit("vector", lambda e: e.tensor_tensor(ar(3), ar(3), tmprow, ALU.add),
                        inc_name="VE")
        acq = emit("scalar", lambda e: e.activation(ar(2), ar(2), AF.Sqrt),
                   waits=[("VE", kfull_ve)], inc_name="AC")
        emit("vector", lambda e: e.tensor_copy(junk[0:1, 0:512], ar(2)[0:1, 0:512]),
             waits=[("AC", acq)])
        qsq_cp = emit("vector", lambda e: e.tensor_copy(w2st[0:1, 0:512], ar(2)[0:1, 512:1024]),
                      inc_name="VE")
        qinv_ve = emit("vector", lambda e: e.reciprocal(ar(2), ar(2)), inc_name="VE")
        ack = emit("scalar", lambda e: e.activation(ar(3), ar(3), AF.Sqrt),
                   waits=[("VE", kfull_ve)], inc_name="AC")
        kinv_ve = emit("vector", lambda e: e.reciprocal(ar(3), ar(3)), waits=[("AC", ack)],
                       inc_name="VE")
        kc_ev = row_transpose8(ar(3), kinv_col, [("VE", kinv_ve), ("VE", qsq_ev)])
        bvk_ve = emit("vector", lambda e: e.tensor_tensor(bvk_col, bv_col, kinv_col, ALU.mult),
                      waits=[("VE", kc_ev), ("VE", v_bv)], inc_name="VE")

        def ev_kg(m, h, pt, lastmm):
            return emit("vector", lambda e, m=m, h=h, p=pt: e.tensor_scalar(
                big4[:, m * 1024 + h * 512: m * 1024 + (h + 1) * 512], p[:, :],
                kinv_col[:, m:m + 1], None, ALU.mult, ALU.bypass),
                waits=[("MM", lastmm)], inc_name="VE")
        gt_mm, kg_ev = mm1024(
            lambda k, m: big3[:, k * 1024 + m * 128: k * 1024 + (m + 1) * 128],
            lambda k, h: big2[:, k * 1024 + h * 512: k * 1024 + (h + 1) * 512],
            ev_kg, [("VE", bvk_ve), ("VE", qcs_ve)],
            extra_rank1=[
                (lambda m: arsl(5, m), lambda h: arh(0, h)),
                (lambda m: arsl(1, m), lambda h: arh(4, h)),
            ])

        kivg_ev = matvec_row(big4, 7, [("VE", kg_ev)], lambda k: ones_col)
        emit("vector", lambda e: e.scalar_tensor_tensor(ar(7), ar(4), EPS_AT, ar(7), ALU.mult, ALU.add),
             waits=[("VE", kivg_ev), ("VE", qinv_ve)])
        emit("vector", lambda e: e.tensor_tensor(ar(7), ar(7), ar(2), ALU.mult))
        emit("vector", lambda e: e.tensor_scalar(ar(7), ar(7), float(D), None, ALU.add, ALU.bypass))
        tail_ve = emit("vector", lambda e: e.reciprocal(ar(5), ar(7)), inc_name="VE")
        emit("vector", lambda e: e.tensor_tensor(ar(6), ar(2), ar(5), ALU.mult),
             waits=[("VE", tail_ve)])
        csc_ve2 = emit("vector", lambda e: e.tensor_scalar(
            ar(6), ar(6), gsc[0:1, 0:1], None, ALU.mult, ALU.bypass), inc_name="VE")
        emit("vector", lambda e: e.tensor_copy(arh(1, 0), junk[0:1, 0:512]),
             waits=[("VE", qsq_cp), ("MM", gt_mm)])
        qsqr_ve = emit("vector", lambda e: e.tensor_copy(arh(1, 1), w2st[0:1, 0:512]),
                       inc_name="VE")
        bv_ld = emit("sync", lambda e: e.dma_start(out=ar(0), in_=bv_d), waits=[("MM", gt_mm)],
                     inc_name="LD")
        bvs_ve = emit("vector", lambda e: e.reduce_sum(gsc[0:1, 1:2], ar(0), axis=AX.X),
                      waits=[("LD", bv_ld)], inc_name="VE")
        bvkg_ev = matvec_row(big4, 4, [("VE", kivg_ev), ("VE", tail_ve), ("VE", v_bv)], lambda k: bv_col[:, k:k + 1])
        emit("vector", lambda e: e.tensor_tensor(ar(4), ar(4), ar(6), ALU.mult),
             waits=[("VE", bvkg_ev), ("VE", csc_ve2), ("VE", bvs_ve)])
        emit("vector", lambda e: e.tensor_scalar(
            ar(0), ar(5), gsc[0:1, 1:2], gsc[0:1, 0:1], ALU.mult, ALU.mult))
        gcon_ve = emit("vector", lambda e: e.tensor_tensor(ar(0), ar(0), ar(4), ALU.add),
                       inc_name="VE")
        bc_mm = emit("tensor", lambda e: e.matmul(ps[4][:, :], ones_row, arh(6, 0),
                                                  start=True, stop=True),
                     waits=[("VE", gcon_ve), ("VE", qsqr_ve)], inc_name="MM")
        bc_mm = emit("tensor", lambda e: e.matmul(ps[5][:, :], ones_row, arh(6, 1),
                                                  start=True, stop=True), inc_name="MM")
        emit("vector", lambda e: e.tensor_copy(bcast[:, 0:512], ps[4][:, :]),
             waits=[("MM", bc_mm)])
        bc_ev = emit("vector", lambda e: e.tensor_copy(bcast[:, 512:1024], ps[5][:, :]),
                     inc_name="VE")
        wv_ld = 0
        for r in range(FC):
            wv_ld = emit("sync", lambda e, r=r: e.dma_start(
                out=big3[:, r * 1024:(r + 1) * 1024], in_=wv_d[r * 128:(r + 1) * 128, :]),
                waits=([("MM", gt_mm)] if r == 0 else []), inc_name="LD")
        wvs_ev = matvec_row(big3, 5, [("LD", wv_ld), ("VE", gcon_ve), ("VE", bc_ev)], lambda k: ones_col)
        def ev_d2(m, h, pt, lastmm):
            return emit("vector", lambda e, m=m, h=h, p=pt: e.tensor_tensor(
                big1[:, m * 1024 + h * 512: m * 1024 + (h + 1) * 512], p[:, :],
                bcast[:, h * 512:(h + 1) * 512], ALU.mult),
                waits=[("MM", lastmm)], inc_name="VE")
        d2_mm, d2_ev = mm1024(
            lambda k, m: big3[:, k * 1024 + m * 128: k * 1024 + (m + 1) * 128],
            lambda k, h: big4[:, k * 1024 + h * 512: k * 1024 + (h + 1) * 512],
            ev_d2, [("VE", wvs_ev)],
            extra_rank1=[(lambda m: arsl(5, m), lambda h: arh(1, h))])
        dg_ve = 0
        for r in range(FC):
            dg_ve = emit("vector", lambda e, r=r: e.tensor_tensor(
                big1[:, r * 1024 + r * 128: r * 1024 + (r + 1) * 128],
                big1[:, r * 1024 + r * 128: r * 1024 + (r + 1) * 128],
                ident[:, :], ALU.add),
                waits=([("VE", d2_ev)] if r == 0 else []), inc_name="VE")
        dsp = 0
        for r in range(FC):
            wls = [("VE", dg_ve), ("MM", d2_mm)] if r == 0 else []
            hq = emit("scalar", lambda e, r=r: e.activation(
                big4b[:, r * 2048:r * 2048 + 1024], big1[:, r * 1024:(r + 1) * 1024],
                AF.Identity), waits=wls, inc_name="AC")
            dsp = emit("vector", lambda e, r=r: e.tensor_tensor(
                big4b[:, r * 2048 + 1024:(r + 1) * 2048], big1[:, r * 1024:(r + 1) * 1024],
                big4b[:, r * 2048:r * 2048 + 1024], ALU.subtract),
                waits=[("AC", hq)], inc_name="VE")
        d2sp_ve = dsp
        b2_ld = emit("sync", lambda e: e.dma_start(out=ar(7), in_=b2_d),
                     waits=[("VE", tail_ve)], inc_name="LD")

        def x1sl(c, h=None):
            t = big2 if c < 8 else big3
            cc = c % 8
            if h is None:
                return t[:, cc * 1024:(cc + 1) * 1024]
            return t[:, cc * 1024 + h * 512: cc * 1024 + (h + 1) * 512]

        att_ve = [0] * NCH
        att_ev_ph = {}
        last_att_mm = {}
        mm_h = {}
        for c in range(NCH):
            w = []
            if c == 0:
                w = [("MM", d2_mm), ("SP", sp_done[NCH - 1])]
            if c >= 2:
                w.append(("MM", last_att_mm[c - 2]))
            xnt_ld = emit("sync", lambda e, c=c: e.dma_start(out=stB(c % 2), in_=xnt_d[c]),
                          waits=w, inc_name="LD")
            w2_ = []
            if c < 2:
                w2_.append(("AC", last_apply[c % 2]))
            else:
                w2_.append(("VE", att_ve[c - 2]))
            xr_ld = emit("sync", lambda e, c=c: e.dma_start(
                out=stC(c % 2), in_=x_d[c * 128:(c + 1) * 128, :]), waits=w2_, inc_name="LD")
            for h in range(2):
                pt = ps[(2 * c + h) % 4]
                w = [("LD", xnt_ld)]
                if c == 0 and h == 0:
                    w += [("VE", d2sp_ve)]
                if 2 * c + h >= 4:
                    w.append(("VE", att_ev_ph[(2 * c + h) - 4]))
                first = True
                for r in range(FC):
                    for pp in range(3):
                        lt = (stBb(c % 2)[:, r * 256:r * 256 + 128] if pp < 2
                              else stBb(c % 2)[:, r * 256 + 128:r * 256 + 256])
                        rt = (big4b[:, r * 2048 + h * 512: r * 2048 + (h + 1) * 512] if pp != 1
                              else big4b[:, r * 2048 + 1024 + h * 512: r * 2048 + 1024 + (h + 1) * 512])
                        mm = emit("tensor", lambda e, p=pt, lt=lt, rt=rt, st=first: e.matmul(
                            p[:, :], lt, rt, start=st, stop=False),
                            waits=(w if first else []), inc_name="MM")
                        first = False
                mm = emit("tensor", lambda e, p=pt, h=h: e.matmul(
                    p[:, :], ones_row, arh(0, h), start=False, stop=True), inc_name="MM")
                mm_h[h] = mm
            last_att_mm[c] = mm
            b = (c % 2) * 16
            sum0 = mcol[:, b + 0:b + 1]; sum1 = mcol[:, b + 1:b + 2]
            sq0 = mcol[:, b + 2:b + 3]; sq1 = mcol[:, b + 3:b + 4]
            mean_c = mcol[:, b + 4:b + 5]; var_c = mcol[:, b + 5:b + 6]
            std_c = mcol[:, b + 6:b + 7]; rstd_c = mcol[:, b + 7:b + 8]
            negmr_c = mcol[:, b + 8:b + 9]; msc = mcol[:, b + 9:b + 10]
            p0_, p1_ = ps[(2 * c) % 4], ps[(2 * c + 1) % 4]
            t5a = tmp56[:, (c % 2) * 1024:(c % 2) * 1024 + 512]
            t5b = tmp56[:, (c % 2) * 1024 + 512:(c % 2) * 1024 + 1024]
            att_ev_ph[2 * c] = emit("vector", lambda e, t=t5a, p=p0_: e.tensor_copy(t, p[:, :]),
                                    waits=[("MM", mm_h[0])], inc_name="VE")
            ev1 = emit("vector", lambda e, t=t5b, p=p1_: e.tensor_copy(t, p[:, :]),
                       waits=[("MM", mm_h[1])], inc_name="VE")
            att_ev_ph[2 * c + 1] = ev1
            emit("vector", lambda e, s=sum0, t=t5a: e.reduce_sum(s, t, axis=AX.X))
            emit("vector", lambda e, s=sum1, t=t5b: e.reduce_sum(s, t, axis=AX.X))
            emit("scalar", lambda e, s=sq0, t=t5a: e.activation(
                junk[:, :], t, AF.Square, accum_out=s), waits=[("VE", ev1)], inc_name="AC")
            a2 = emit("scalar", lambda e, s=sq1, t=t5b: e.activation(
                junk[:, :], t, AF.Square, accum_out=s), inc_name="AC")
            emit("vector", lambda e, a=sum0, b_=sum1, o=mean_c: e.tensor_tensor(o, a, b_, ALU.add),
                 waits=[("AC", a2)])
            emit("vector", lambda e, o=mean_c: e.tensor_scalar(o, o, 1.0 / D, None, ALU.mult,
                                                               ALU.bypass))
            emit("vector", lambda e, a=sq0, b_=sq1, o=var_c: e.tensor_tensor(o, a, b_, ALU.add))
            emit("vector", lambda e, o=var_c: e.tensor_scalar(o, o, 1.0 / D, None, ALU.mult,
                                                              ALU.bypass))
            emit("vector", lambda e, m=mean_c, o=msc: e.scalar_tensor_tensor(
                o, m, -1.0, m, ALU.mult, ALU.mult))
            v_var = emit("vector", lambda e, o=var_c, s=msc: e.tensor_tensor(o, o, s, ALU.add))
            a3 = emit("scalar", lambda e, v=var_c, o=std_c: e.activation(
                o, v, AF.Sqrt, bias=eps_col), waits=[("VE", v_var)], inc_name="AC")
            emit("vector", lambda e, o=rstd_c, i=std_c: e.reciprocal(o, i), waits=[("AC", a3)])
            nm_ve = emit("vector", lambda e, o=negmr_c, m=mean_c, r=rstd_c: e.scalar_tensor_tensor(
                o, m, -1.0, r, ALU.mult, ALU.mult), inc_name="VE")
            for h in range(2):
                th = tmp56[:, (c % 2) * 1024 + h * 512:(c % 2) * 1024 + (h + 1) * 512]
                emit("vector", lambda e, c=c, h=h, n=negmr_c: e.tensor_scalar(
                    x1sl(c, h), stC(c % 2)[:, h * 512:(h + 1) * 512], n, None,
                    ALU.add, ALU.bypass), waits=[("LD", xr_ld)])
                att_ve[c] = emit("vector", lambda e, c=c, h=h, t=th, r=rstd_c: e.scalar_tensor_tensor(
                    x1sl(c, h), t, r, x1sl(c, h), ALU.mult, ALU.add), inc_name="VE")

        def h1sl(mi, sl=None):
            t = big4 if mi < 16 else big1
            m = mi % 16
            if sl is None:
                return t[:, m * 512:(m + 1) * 512]
            return t[:, m * 512 + sl * 128: m * 512 + (sl + 1) * 128]

        st_done = [0] * NCH
        trv_prev = {0: 0, 1: 0}
        w1_mm_prev = {0: 0, 1: 0}
        w2_mm_prev = {0: 0, 1: 0}
        last_tr3 = {}
        st6 = {"last_tr_ve": 0}
        out_ve_prev = 0

        def do_ln3(g):
            for cc in range(CPG):
                c = g * CPG + cc
                b = 64 + (cc % 2) * 16
                sum_c = mcol[:, b + 0:b + 1]; sq0 = mcol[:, b + 1:b + 2]; sq1 = mcol[:, b + 2:b + 3]
                mean_c = mcol[:, b + 3:b + 4]; var_c = mcol[:, b + 5:b + 6]
                std_c = mcol[:, b + 6:b + 7]; rstd_c = mcol[:, b + 7:b + 8]
                negmr_c = mcol[:, b + 8:b + 9]; msc = mcol[:, b + 9:b + 10]
                w = [("VE", att_ve[c])]
                emit("vector", lambda e, s=sum_c, c=c: e.reduce_sum(s, x1sl(c), axis=AX.X), waits=w)
                emit("scalar", lambda e, s=sq0, c=c: e.activation(
                    junk[:, :], x1sl(c, 0), AF.Square, accum_out=s), waits=w, inc_name="AC")
                a2 = emit("scalar", lambda e, s=sq1, c=c: e.activation(
                    junk[:, :], x1sl(c, 1), AF.Square, accum_out=s), inc_name="AC")
                emit("vector", lambda e, m=mean_c, s=sum_c: e.tensor_scalar(
                    m, s, 1.0 / D, None, ALU.mult, ALU.bypass), waits=[("AC", a2)])
                emit("vector", lambda e, q0=sq0, q1=sq1, o=var_c: e.tensor_tensor(o, q0, q1, ALU.add))
                emit("vector", lambda e, o=var_c: e.tensor_scalar(o, o, 1.0 / D, None, ALU.mult,
                                                                  ALU.bypass))
                emit("vector", lambda e, m=mean_c, o=msc: e.scalar_tensor_tensor(
                    o, m, -1.0, m, ALU.mult, ALU.mult))
                v_var = emit("vector", lambda e, o=var_c, s=msc: e.tensor_tensor(o, o, s, ALU.add))
                a3 = emit("scalar", lambda e, v=var_c, o=std_c: e.activation(
                    o, v, AF.Sqrt, bias=eps_col), waits=[("VE", v_var)], inc_name="AC")
                emit("vector", lambda e, o=rstd_c, i=std_c: e.reciprocal(o, i), waits=[("AC", a3)])
                nm_ve = emit("vector", lambda e, o=negmr_c, m=mean_c, r=rstd_c: e.scalar_tensor_tensor(
                    o, m, -1.0, r, ALU.mult, ALU.mult), inc_name="VE")
                wa = [("VE", nm_ve)]
                if g == 0 and cc < 2:
                    wa.append(("VE", att_ve[NCH - 2 + cc]))
                key = (g, cc - 2) if cc >= 2 else (g - 1, cc + 2)
                if key in last_tr3:
                    wa.append(("MM", last_tr3[key]))
                x1n_ac = emit("scalar", lambda e, c=c, cc=cc, r=rstd_c, n=negmr_c: e.activation(
                    stC(cc % 2), x1sl(c), AF.Identity, bias=n, scale=r), waits=wa, inc_name="AC")
                for r in range(FC):
                    pt = ps[6 + (r % 2)]
                    w = [("AC", x1n_ac), ("VE", trv_prev[r % 2])]
                    if g == 0 and cc == 0 and r < 2:
                        w.append(("VE", att_ve[NCH - 1]))
                    m = emit("tensor", lambda e, p=pt, cc=cc, r=r: e.transpose(
                        p[:, 0:128], stC(cc % 2)[:, r * 128:(r + 1) * 128], ident[:, :]),
                        waits=w, inc_name="MM")
                    emit("vector", lambda e, p=pt, cc=cc, r=r: e.tensor_copy(
                        x1nTb[:, r * 1024 + cc * 128: r * 1024 + (cc + 1) * 128], p[:, 0:128]),
                        waits=[("MM", m)], inc_name="VE")
                    tv = emit("vector", lambda e, p=pt, cc=cc, r=r: e.tensor_tensor(
                        x1nTb[:, r * 1024 + 512 + cc * 128: r * 1024 + 512 + (cc + 1) * 128],
                        p[:, 0:128],
                        x1nTb[:, r * 1024 + cc * 128: r * 1024 + (cc + 1) * 128],
                        ALU.subtract), inc_name="VE")
                    trv_prev[r % 2] = tv
                last_tr3[(g, cc)] = m
                st6["last_tr_ve"] = tv

        def do_ln4(g, cc, ffn2_done):
            c = g * CPG + cc
            b = (cc % 2) * 16
            sum0 = mcol[:, b + 0:b + 1]; sum1 = mcol[:, b + 1:b + 2]
            sq0 = mcol[:, b + 2:b + 3]; sq1 = mcol[:, b + 3:b + 4]
            mean_c = mcol[:, b + 4:b + 5]; var_c = mcol[:, b + 5:b + 6]
            std_c = mcol[:, b + 6:b + 7]; rstd_c = mcol[:, b + 7:b + 8]
            negmr_c = mcol[:, b + 8:b + 9]; msc = mcol[:, b + 9:b + 10]
            p0_, p1_ = ps[cc * 2], ps[cc * 2 + 1]
            t6a = bcast[:, 0:512]
            t6b = bcast[:, 512:1024]
            emit("vector", lambda e, t=t6a, p=p0_: e.tensor_copy(t, p[:, :]),
                 waits=[("MM", ffn2_done), ("AC", st6.get("ln4_a2", 0))])
            ev1 = emit("vector", lambda e, t=t6b, p=p1_: e.tensor_copy(t, p[:, :]),
                       inc_name="VE")
            emit("vector", lambda e, s=sum0, t=t6a: e.reduce_sum(s, t, axis=AX.X))
            emit("vector", lambda e, s=sum1, t=t6b: e.reduce_sum(s, t, axis=AX.X))
            emit("scalar", lambda e, s=sq0, t=t6a: e.activation(
                junk[:, :], t, AF.Square, accum_out=s), waits=[("VE", ev1)], inc_name="AC")
            a2 = emit("scalar", lambda e, s=sq1, t=t6b: e.activation(
                junk[:, :], t, AF.Square, accum_out=s), inc_name="AC")
            st6["ln4_a2"] = a2
            emit("vector", lambda e, a=sum0, b_=sum1, o=mean_c: e.tensor_tensor(
                o, a, b_, ALU.add), waits=[("AC", a2)])
            emit("vector", lambda e, o=mean_c: e.tensor_scalar(o, o, 1.0 / D, None, ALU.mult,
                                                               ALU.bypass))
            emit("vector", lambda e, a=sq0, b_=sq1, o=var_c: e.tensor_tensor(o, a, b_, ALU.add))
            emit("vector", lambda e, o=var_c: e.tensor_scalar(o, o, 1.0 / D, None, ALU.mult,
                                                              ALU.bypass))
            emit("vector", lambda e, m=mean_c, o=msc: e.scalar_tensor_tensor(
                o, m, -1.0, m, ALU.mult, ALU.mult))
            v_var = emit("vector", lambda e, o=var_c, s=msc: e.tensor_tensor(o, o, s, ALU.add))
            a3 = emit("scalar", lambda e, v=var_c, o=std_c: e.activation(
                o, v, AF.Sqrt, bias=eps_col), waits=[("VE", v_var)], inc_name="AC")
            emit("vector", lambda e, o=rstd_c, i=std_c: e.reciprocal(o, i), waits=[("AC", a3)])
            nm_ve = emit("vector", lambda e, o=negmr_c, m=mean_c, r=rstd_c: e.scalar_tensor_tensor(
                o, m, -1.0, r, ALU.mult, ALU.mult), inc_name="VE")
            for h in range(2):
                t6h = bcast[:, h * 512:(h + 1) * 512]
                emit("vector", lambda e, c=c, h=h, n=negmr_c: e.tensor_scalar(
                    x1sl(c, h), x1sl(c, h), n, None, ALU.add, ALU.bypass),
                    waits=[("VE", nm_ve)])
                ov = emit("vector", lambda e, c=c, h=h, t=t6h, r=rstd_c: e.scalar_tensor_tensor(
                    x1sl(c, h), t, r, x1sl(c, h), ALU.mult, ALU.add), inc_name="VE")
            st_done[c] = emit("gpsimd", lambda e, c=c: e.dma_start(
                out=y_d[c * 128:(c + 1) * 128, :], in_=x1sl(c)),
                waits=[("VE", ov)], inc_name="SP")
            return ev1, ov

        do_ln3(0)
        bank_free_next = {i: att_ev_ph[2 * NCH - 1] for i in range(8)}
        for g in range(NG):
            relu_ac = {}
            bank_free = dict(bank_free_next)
            last_tr_ve = st6["last_tr_ve"]
            for ig in range(4):
                for ka in range(FC):
                    w = [("MM", w1_mm_prev[ka % 2])]
                    if g == 0 and ig == 0 and ka == 0:
                        w.append(("SP", w1s_done))
                    qeng, qsem = ("sync", "LD") if ka % 2 == 0 else ("scalar", "L2")
                    ld = emit(qeng, lambda e, ig=ig, ka=ka: e.dma_start(
                        out=stDb(ka % 2)[:, 0:2048],
                        in_=w1s_d[ka * 128:(ka + 1) * 128, ig * 2048:(ig + 1) * 2048]),
                        waits=w, inc_name=qsem)
                    wmm0 = [(qsem, ld)]
                    if ka == 0:
                        wmm0.append(("VE", last_tr_ve))
                    for mi_l in range(8):
                        wmm = list(wmm0) if mi_l == 0 else []
                        if ka == 0 and bank_free[mi_l]:
                            wmm.append(("VE", bank_free[mi_l]))
                        for pp in range(3):
                            lt = (stDb(ka % 2)[:, mi_l * 128:(mi_l + 1) * 128] if pp < 2
                                  else stDb(ka % 2)[:, 1024 + mi_l * 128:1024 + (mi_l + 1) * 128])
                            rt = (x1nTb[:, ka * 1024:ka * 1024 + 512] if pp != 1
                                  else x1nTb[:, ka * 1024 + 512:ka * 1024 + 1024])
                            mm = emit("tensor", lambda e, p=ps[mi_l], lt=lt, rt=rt, ka=ka, pp=pp: e.matmul(
                                p[:, :], lt, rt,
                                start=(ka == 0 and pp == 0), stop=(ka == FC - 1 and pp == 2)),
                                waits=(wmm if pp == 0 else []), inc_name="MM")
                    w1_mm_prev[ka % 2] = mm
                for mi_l in range(8):
                    mi = ig * 8 + mi_l
                    hb = (big4b if mi < 16 else big1b)
                    hoff = (mi % 16) * 1024
                    rv = emit("vector", lambda e, p=ps[mi_l], mi=mi: e.tensor_scalar(
                        junk[:, 0:512], p[:, :], cols[:, 24 + mi:25 + mi], 0.0, ALU.add, ALU.max),
                        waits=([("MM", mm)] if mi_l == 0 else []), inc_name="VE")
                    hc = emit("scalar", lambda e, hb=hb, hoff=hoff: e.activation(
                        hb[:, hoff:hoff + 512], junk[:, 0:512], AF.Identity),
                        waits=[("VE", rv)], inc_name="AC")
                    ra = emit("vector", lambda e, hb=hb, hoff=hoff: e.tensor_tensor(
                        hb[:, hoff + 512:hoff + 1024], junk[:, 0:512], hb[:, hoff:hoff + 512],
                        ALU.subtract), waits=[("AC", hc)], inc_name="VE")
                    relu_ac[mi] = ra
                    bank_free[mi_l] = rv
                last_psum_free = ra
            for mi in range(32):
                w = [("MM", w2_mm_prev[mi % 2])]
                if g == 0 and mi < 2:
                    w.append(("VE", qsqr_ve))
                qeng, qsem = ("gpsimd", "SP") if mi % 2 == 0 else ("sync", "LD")
                ld = emit(qeng, lambda e, mi=mi: e.dma_start(
                    out=w2stb[:, (mi % 2) * 2048:(mi % 2 + 1) * 2048],
                    in_=w2s_d[mi * 128:(mi + 1) * 128, :]), waits=w, inc_name=qsem)
                wmm0 = [(qsem, ld), ("VE", relu_ac[mi])]
                if mi == 0:
                    wmm0.append(("VE", last_psum_free))
                    wmm0.append(("VE", out_ve_prev))
                hb = (big4b if mi < 16 else big1b)
                hoff = (mi % 16) * 1024
                for cc in range(CPG):
                    for pp in range(3):
                        lt_off = hoff + cc * 128 if pp < 2 else hoff + 512 + cc * 128
                        rbase = (mi % 2) * 2048 + (0 if pp != 1 else 1024)
                        for hh in range(2):
                            rt = w2stb[:, rbase + hh * 512:rbase + (hh + 1) * 512]
                            mm = emit("tensor", lambda e, p=pa[cc], hb=hb, lo=lt_off, rt=rt, hh=hh,
                                      mi=mi, pp=pp: e.matmul(
                                p[:, hh * 512:(hh + 1) * 512], hb[:, lo:lo + 128], rt,
                                start=(mi == 0 and pp == 0), stop=False),
                                waits=(wmm0 if (cc == 0 and pp == 0 and hh == 0) else []),
                                inc_name="MM")
                w2_mm_prev[mi % 2] = mm
            for cc in range(CPG):
                for h in range(2):
                    mm = emit("tensor", lambda e, p=ps[cc * 2 + h], h=h: e.matmul(
                        p[:, :], ones_row, arh(7, h), start=False, stop=True),
                        waits=([("LD", b2_ld)] if (g == 0 and cc == 0 and h == 0) else []),
                        inc_name="MM")
            ffn2_done = mm
            ev3, ov = do_ln4(g, 3, ffn2_done)
            evs = {3: ev3}
            if g + 1 < NG:
                do_ln3(g + 1)
            for cc in (0, 1, 2):
                evs[cc], ov = do_ln4(g, cc, ffn2_done)
            out_ve_prev = ov
            bank_free_next = {}
            for cc in range(4):
                bank_free_next[2 * cc] = evs[cc]
                bank_free_next[2 * cc + 1] = evs[cc]

        emit("gpsimd", lambda e: e.nop(), waits=[("SP", st_done[NCH - 1])])

        totals = {k: v[2] for k, v in cnt.items()}
        for (w, v) in waits_seen:
            assert v <= totals[w], f"wait {w} {v} > total {totals[w]}"

        @blk.sync
        def _(eng):
            for f in E["sync"]:
                f(eng)

        @blk.tensor
        def _(eng):
            for f in E["tensor"]:
                f(eng)

        @blk.vector
        def _(eng):
            for f in E["vector"]:
                f(eng)

        @blk.scalar
        def _(eng):
            for f in E["scalar"]:
                f(eng)

        @blk.gpsimd
        def _(eng):
            for f in E["gpsimd"]:
                f(eng)

    return nc


_NC_CACHE = None


def kernel(L0_attn_x, L0_attn_Wq, L0_attn_bq, L0_attn_Wk, L0_attn_bk,
           L0_attn_Wv, L0_attn_bv, L0_attn_gamma,
           L0_ffn_W1, L0_ffn_b1, L0_ffn_W2, L0_ffn_b2):
    global _NC_CACHE
    x = np.ascontiguousarray(np.asarray(L0_attn_x, dtype=np.float32).reshape(16384, 1024))
    com = dict(
        wq=np.ascontiguousarray(np.asarray(L0_attn_Wq, dtype=np.float32)),
        wk=np.ascontiguousarray(np.asarray(L0_attn_Wk, dtype=np.float32)),
        wv=np.ascontiguousarray(np.asarray(L0_attn_Wv, dtype=np.float32)),
        bq=np.asarray(L0_attn_bq, dtype=np.float32).reshape(1, 1024),
        bk=np.asarray(L0_attn_bk, dtype=np.float32).reshape(1, 1024),
        bv_row=np.asarray(L0_attn_bv, dtype=np.float32).reshape(1, 1024),
        bv8=np.ascontiguousarray(np.asarray(L0_attn_bv, dtype=np.float32).reshape(8, 128)),
        gamma=np.asarray(L0_attn_gamma, dtype=np.float32).reshape(1, 1),
        w1=np.ascontiguousarray(np.asarray(L0_ffn_W1, dtype=np.float32)),
        b1_32=np.ascontiguousarray(np.asarray(L0_ffn_b1, dtype=np.float32).reshape(32, 128)),
        w2=np.ascontiguousarray(np.asarray(L0_ffn_W2, dtype=np.float32)),
        b2=np.asarray(L0_ffn_b2, dtype=np.float32).reshape(1, 1024),
        ident=np.eye(128, dtype=np.float32),
    )
    if _NC_CACHE is None:
        _NC_CACHE = build_core()
    nc = _NC_CACHE
    in_maps = []
    for i in range(8):
        m = dict(com)
        m["x"] = np.ascontiguousarray(x[i * TOK:(i + 1) * TOK, :])
        in_maps.append(m)
    res = run_bass_kernel_spmd(nc, in_maps, core_ids=list(range(8)))
    out = np.concatenate([res.results[i]["y"] for i in range(8)], axis=0)
    return out.reshape(1, 16384, 1024).astype(np.float32)


# version 08
# speedup vs baseline: 1.0982x; 1.0051x over previous
import numpy as np
import concourse.bass as bass
import concourse.mybir as mybir
from concourse.bass_utils import run_bass_kernel_spmd

F32 = mybir.dt.float32
BF16 = mybir.dt.bfloat16
AF = mybir.ActivationFunctionType
ALU = mybir.AluOpType
AX = mybir.AxisListType

TOK = 2048
NCH = 16
D = 1024
FC = 8
INTER = 4096
NG = 4
CPG = 4
CTOT = 16384.0
EPS_LN = 1e-5
EPS_AT = 1e-6


def build_core():
    nc = bass.Bass("TRN2", target_bir_lowering=False)
    dt = lambda n, s: nc.dram_tensor(n, s, F32, kind="ExternalInput").ap()
    x_d = dt("x", [TOK, D])
    wq_d = dt("wq", [D, D]); wk_d = dt("wk", [D, D]); wv_d = dt("wv", [D, D])
    bq_d = dt("bq", [1, D]); bk_d = dt("bk", [1, D])
    bv_d = dt("bv_row", [1, D]); bv8_d = dt("bv8", [8, 128])
    gam_d = dt("gamma", [1, 1])
    w1_d = dt("w1", [D, INTER]); b1_d = dt("b1_32", [32, 128])
    w2_d = dt("w2", [INTER, D]); b2_d = dt("b2", [1, D])
    id_d = dt("ident", [128, 128])
    y_d = nc.dram_tensor("y", [TOK, D], F32, kind="ExternalOutput").ap()
    ccin_h = nc.dram_tensor("ccin", [1025, 1024], F32)
    ccout_h = nc.dram_tensor("ccout", [1025, 1024], F32)
    ccin, ccout = ccin_h.ap(), ccout_h.ap()
    xnt_d = nc.dram_tensor("xnt", [NCH, 128, D], F32).ap()
    w1s_d = nc.dram_tensor("w1s", [D, 2 * INTER], BF16).ap()
    w2s_d = nc.dram_tensor("w2s", [INTER, 2 * D], BF16).ap()

    import contextlib
    with contextlib.ExitStack() as stack:
        en = stack.enter_context
        big1 = en(nc.sbuf_tensor("big1", [128, 8192], F32))
        big2 = en(nc.sbuf_tensor("big2", [128, 8192], F32))
        big3 = en(nc.sbuf_tensor("big3", [128, 8192], F32))
        big4 = en(nc.sbuf_tensor("big4", [128, 8192], F32))
        util = en(nc.sbuf_tensor("util", [128, 8192], F32))
        w2st = en(nc.sbuf_tensor("w2st", [128, 2048], F32))
        arena = en(nc.sbuf_tensor("arena", [128, 8192], F32))
        cols = en(nc.sbuf_tensor("cols", [128, 64], F32))
        mcol = en(nc.sbuf_tensor("mcol", [128, 128], F32))
        bcast = en(nc.sbuf_tensor("bcast", [128, 1024], F32))
        ident = en(nc.sbuf_tensor("ident_sb", [128, 128], F32))
        ones = en(nc.sbuf_tensor("ones", [128, 128], F32))
        gsc = en(nc.sbuf_tensor("gsc", [128, 8], F32))
        junk = en(nc.sbuf_tensor("junk", [128, 512], F32))
        pa = [en(nc.psum_tensor(f"pa{i}", [128, 1024], F32)) for i in range(4)]
        ps = [pa[i // 2][:, (i % 2) * 512:(i % 2 + 1) * 512] for i in range(8)]
        ps4, ps5, ps6, ps7 = ps[4], ps[5], ps[6], ps[7]
        sLD = en(nc.semaphore("sLD"))
        sSP = en(nc.semaphore("sSP"))
        sMM = en(nc.semaphore("sMM"))
        sVE = en(nc.semaphore("sVE"))
        sAC = en(nc.semaphore("sAC"))
        sCC = en(nc.semaphore("sCC"))
        sL2 = en(nc.semaphore("sL2"))
        blk = en(nc.Block())
        stA = lambda b: util[:, (0 + b) * 1024:(1 + b) * 1024]
        stB = lambda b: util[:, (2 + b) * 1024:(3 + b) * 1024]
        stC = lambda b: util[:, (4 + b) * 1024:(5 + b) * 1024]
        stD = lambda b: util[:, (6 + b) * 1024:(7 + b) * 1024]
        x1nT = util[:, 0:4096]
        tmp56 = util[:, 0:2048]
        big1b = big1.bitcast(BF16)
        big2b = big2.bitcast(BF16)
        big4b = big4.bitcast(BF16)
        utilb = util.bitcast(BF16)
        w2stb = w2st.bitcast(BF16)
        x1nTb = utilb[:, 0:8192]
        stBb = lambda b: utilb[:, (2 + b) * 2048:(3 + b) * 2048]
        stDb = lambda b: utilb[:, (6 + b) * 2048:(7 + b) * 2048]
        ar = lambda i: arena[0:1, i * 1024:(i + 1) * 1024]
        arh = lambda i, h: arena[0:1, i * 1024 + h * 512: i * 1024 + (h + 1) * 512]
        arsl = lambda i, m: arena[0:1, i * 1024 + m * 128: i * 1024 + (m + 1) * 128]
        cs_col = cols[:, 0:8]; kinv_col = cols[:, 8:16]
        bv_col = cols[:, 16:24]; bvk_col = cols[:, 56:64]
        ones_col = ones[:, 0:1]; ones_row = ones[0:1, 0:128]

        E = {k: [] for k in ("sync", "tensor", "vector", "scalar", "gpsimd")}
        cnt = {"LD": [sLD, 16, 0], "SP": [sSP, 16, 0], "MM": [sMM, 1, 0],
               "VE": [sVE, 1, 0], "AC": [sAC, 1, 0], "CC": [sCC, 1, 0],
               "L2": [sL2, 16, 0]}
        waits_seen = []

        def emit(eng, f, waits=(), inc_name=None):
            waits = list(waits)
            auto = {"vector": "VE", "scalar": "AC"}.get(eng)
            if auto is not None:
                if inc_name is None:
                    inc_name = auto
                waits.append((auto, cnt[auto][2]))
            if inc_name in ("LD", "SP", "L2"):
                waits.append((inc_name, cnt[inc_name][2]))
            wl = [(cnt[w][0], v, w) for (w, v) in waits if v]
            for (_, v, w) in wl:
                waits_seen.append((w, v))
            step = 0
            sem = None
            if inc_name:
                c = cnt[inc_name]
                c[2] += c[1]
                step = c[1]
                sem = c[0]
            tgt = cnt[inc_name][2] if inc_name else None

            def g(eng_obj, f=f, wl=wl, sem=sem, step=step):
                for (s, v, _) in wl:
                    eng_obj.wait_ge(s, v)
                ins = f(eng_obj)
                if sem is not None:
                    ins.then_inc(sem, step)
            E[eng].append(g)
            return tgt

        emit("vector", lambda e: e.memset(ones[:, :], 1.0))
        ve_init = emit("vector", lambda e: e.memset(gsc[:, 2:3], EPS_LN), inc_name="VE")
        eps_col = gsc[:, 2:3]
        ld_ident = emit("sync", lambda e: e.dma_start(out=ident[:, :], in_=id_d), inc_name="LD")
        ld_bq = emit("sync", lambda e: e.dma_start(out=ar(0), in_=bq_d), inc_name="LD")
        ld_bk = emit("sync", lambda e: e.dma_start(out=ar(1), in_=bk_d), inc_name="LD")
        ld_gam = emit("sync", lambda e: e.dma_start(out=gsc[0:1, 0:1], in_=gam_d), inc_name="LD")
        ld_bv8 = emit("sync", lambda e: e.dma_start(out=big2[0:8, 0:128], in_=bv8_d), inc_name="LD")
        ld_b132 = emit("sync", lambda e: e.dma_start(out=big2[0:32, 128:256], in_=b1_d), inc_name="LD")
        t_bv = emit("tensor", lambda e: e.transpose(ps6[:, 0:8], big2[0:8, 0:128], ident[0:8, 0:8]),
                    waits=[("LD", ld_bv8), ("LD", ld_ident)], inc_name="MM")
        t_b1 = emit("tensor", lambda e: e.transpose(ps7[:, 0:32], big2[0:32, 128:256], ident[0:32, 0:32]),
                    waits=[("LD", ld_b132)], inc_name="MM")
        v_bv = emit("vector", lambda e: e.tensor_copy(bv_col, ps6[:, 0:8]), waits=[("MM", t_bv)],
                    inc_name="VE")
        v_b1 = emit("vector", lambda e: e.tensor_copy(cols[:, 24:56], ps7[:, 0:32]),
                    waits=[("MM", t_b1)], inc_name="VE")

        x_ld = [0] * NCH
        xn_ac = [0] * NCH
        sp_done = [0] * NCH
        ev_done = [0] * NCH
        cs_mm_c = [0] * NCH
        tr_ev = {0: v_bv, 1: v_b1}
        w2_in = {0: 0, 1: 0}; w2_lsub = {0: 0, 1: 0}; w2_out = {0: 0, 1: 0}
        for c in range(NCH):
            w = [("AC", xn_ac[c - 2])] if c >= 2 else []
            x_ld[c] = emit("sync", lambda e, c=c: e.dma_start(
                out=stC(c % 2), in_=x_d[c * 128:(c + 1) * 128, :]), waits=w, inc_name="LD")
            b = (c % 2) * 16
            sum_c = mcol[:, b + 0:b + 1]; sq0 = mcol[:, b + 1:b + 2]; sq1 = mcol[:, b + 2:b + 3]
            mean_c = mcol[:, b + 3:b + 4]; msq = mcol[:, b + 4:b + 5]; var_c = mcol[:, b + 5:b + 6]
            std_c = mcol[:, b + 6:b + 7]
            rstd_c = mcol[:, 32 + c:33 + c]
            negmr_c = mcol[:, 48 + c:49 + c]
            emit("vector", lambda e, s=sum_c, c=c: e.reduce_sum(s, stC(c % 2), axis=AX.X),
                 waits=[("LD", x_ld[c])])
            emit("scalar", lambda e, s=sq0, c=c: e.activation(
                junk[:, :], stC(c % 2)[:, 0:512], AF.Square, accum_out=s),
                waits=[("LD", x_ld[c])], inc_name="AC")
            ac2 = emit("scalar", lambda e, s=sq1, c=c: e.activation(
                junk[:, :], stC(c % 2)[:, 512:1024], AF.Square, accum_out=s), inc_name="AC")
            emit("vector", lambda e, m=mean_c, s=sum_c: e.tensor_scalar(
                m, s, 1.0 / D, None, ALU.mult, ALU.bypass), waits=[("AC", ac2)])
            emit("vector", lambda e, q=sq0, q1=sq1, o=msq: e.tensor_tensor(o, q, q1, ALU.add))
            emit("vector", lambda e, m=mean_c, o=var_c: e.scalar_tensor_tensor(
                o, m, -1.0, m, ALU.mult, ALU.mult))
            v_var = emit("vector", lambda e, q=msq, o=var_c: e.scalar_tensor_tensor(
                o, q, 1.0 / D, o, ALU.mult, ALU.add))
            ac3 = emit("scalar", lambda e, v=var_c, o=std_c: e.activation(
                o, v, AF.Sqrt, bias=eps_col),
                waits=([("VE", ve_init), ("VE", v_var)] if c == 0 else [("VE", v_var)]),
                inc_name="AC")
            emit("vector", lambda e, o=rstd_c, i=std_c: e.reciprocal(o, i), waits=[("AC", ac3)])
            st_ve = emit("vector", lambda e, o=negmr_c, m=mean_c, r=rstd_c: e.scalar_tensor_tensor(
                o, m, -1.0, r, ALU.mult, ALU.mult), inc_name="VE")
            w = [("VE", st_ve)]
            if c >= 2:
                w.append(("MM", cs_mm_c[c - 2]))
            xn_ac[c] = emit("scalar", lambda e, c=c, r=rstd_c, n=negmr_c: e.activation(
                stA(c % 2), stC(c % 2), AF.Identity, bias=n, scale=r), waits=w, inc_name="AC")
            for r in range(FC):
                pt = ps[6 + (r % 2)]
                w = [("AC", xn_ac[c]), ("VE", tr_ev[r % 2])]
                m = emit("tensor", lambda e, p=pt, c=c, r=r: e.transpose(
                    p[:, 0:128], stA(c % 2)[:, r * 128:(r + 1) * 128], ident[:, :]),
                    waits=w, inc_name="MM")
                wv = [("MM", m)]
                if r == 0 and c >= 2:
                    wv.append(("SP", sp_done[c - 2]))
                emit("vector", lambda e, p=pt, c=c, r=r: e.tensor_copy(
                    stBb(c % 2)[:, r * 256:r * 256 + 128], p[:, 0:128]), waits=wv, inc_name="VE")
                ev = emit("vector", lambda e, p=pt, c=c, r=r: e.tensor_tensor(
                    stBb(c % 2)[:, r * 256 + 128:r * 256 + 256], p[:, 0:128],
                    stBb(c % 2)[:, r * 256:r * 256 + 128], ALU.subtract), inc_name="VE")
                tr_ev[r % 2] = ev
            ev_done[c] = ev
            emit("tensor", lambda e, c=c: e.matmul(
                ps4[0:1, :], ones_col, stA(c % 2)[:, 0:512],
                start=(c == 0), stop=(c == NCH - 1)), inc_name="MM")
            cs_mm_c[c] = emit("tensor", lambda e, c=c: e.matmul(
                ps5[0:1, :], ones_col, stA(c % 2)[:, 512:1024],
                start=(c == 0), stop=(c == NCH - 1)), inc_name="MM")
            sp_done[c] = emit("gpsimd", lambda e, c=c: e.dma_start(
                out=xnt_d[c], in_=stB(c % 2)), waits=[("VE", ev_done[c])], inc_name="SP")
            for j in (2 * c, 2 * c + 1):
                buf = j % 2
                srcw = big2[:, buf * 1024:(buf + 1) * 1024]
                dstw = big2b[:, 8192 + buf * 2048:8192 + (buf + 1) * 2048]
                wlw = [("VE", w2_lsub[buf])] if w2_lsub[buf] else [("MM", t_b1)]
                w2_in[buf] = emit("sync", lambda e, j=j, s=srcw: e.dma_start(
                    out=s, in_=w2_d[j * 128:(j + 1) * 128, :]), waits=wlw, inc_name="LD")
                wlh = [("LD", w2_in[buf])]
                if w2_out[buf]:
                    wlh.append(("SP", w2_out[buf]))
                hcp = emit("scalar", lambda e, s=srcw, d=dstw: e.activation(
                    d[:, 0:1024], s, AF.Identity), waits=wlh, inc_name="AC")
                w2_lsub[buf] = emit("vector", lambda e, s=srcw, d=dstw: e.tensor_tensor(
                    d[:, 1024:2048], s, d[:, 0:1024], ALU.subtract), waits=[("AC", hcp)],
                    inc_name="VE")
                w2_out[buf] = emit("gpsimd", lambda e, j=j, d=dstw: e.dma_start(
                    out=w2s_d[j * 128:(j + 1) * 128, :], in_=d),
                    waits=[("VE", w2_lsub[buf])], inc_name="SP")

        w2s_done = w2_out[1]
        emit("vector", lambda e: e.tensor_copy(arh(6, 0), ps4[0:1, :]),
             waits=[("MM", cs_mm_c[NCH - 1])])
        cs_ve = emit("vector", lambda e: e.tensor_copy(arh(6, 1), ps5[0:1, :]), inc_name="VE")

        s_evacs = []
        last_apply = {0: xn_ac[NCH - 2], 1: xn_ac[NCH - 1]}
        last_s_mm = {0: 0, 1: 0}
        split_free = {0: cs_mm_c[NCH - 2], 1: cs_mm_c[NCH - 1]}
        split_kind = {0: "MM", 1: "MM"}
        stB_free = {0: sp_done[NCH - 2], 1: sp_done[NCH - 1]}
        w1_in = 0; w1_lsub = 0; w1_out = 0
        for p_i in range(2):
            for c in range(NCH):
                lx = emit("sync", lambda e, c=c: e.dma_start(
                    out=stC(c % 2), in_=x_d[c * 128:(c + 1) * 128, :]),
                    waits=[("AC", last_apply[c % 2])], inc_name="LD")
                aply = emit("scalar", lambda e, c=c: e.activation(
                    stA(c % 2), stC(c % 2), AF.Identity,
                    bias=mcol[:, 48 + c:49 + c], scale=mcol[:, 32 + c:33 + c]),
                    waits=[("LD", lx), (split_kind[c % 2], split_free[c % 2])], inc_name="AC")
                last_apply[c % 2] = aply
                wsp = [("AC", aply), ("SP", stB_free[c % 2])]
                if last_s_mm[c % 2]:
                    wsp.append(("MM", last_s_mm[c % 2]))
                hcp = emit("scalar", lambda e, c=c: e.activation(
                    stBb(c % 2)[:, 0:1024], stA(c % 2), AF.Identity), waits=wsp, inc_name="AC")
                lsub = emit("vector", lambda e, c=c: e.tensor_tensor(
                    stBb(c % 2)[:, 1024:2048], stA(c % 2), stBb(c % 2)[:, 0:1024],
                    ALU.subtract), waits=[("AC", hcp)], inc_name="VE")
                split_free[c % 2] = lsub
                split_kind[c % 2] = "VE"
                stB_free[c % 2] = 0
                wmm = [("VE", lsub)]
                if c == 0 and p_i == 0:
                    wmm += [("VE", cs_ve), ("VE", ev_done[NCH - 1])]
                if c == 0 and p_i == 1:
                    wmm.append(("VE", s_evacs[-1]))
                for m in range(4):
                    mg = m + 4 * p_i
                    for pp in range(3):
                        lt = (stBb(c % 2)[:, mg * 128:(mg + 1) * 128] if pp < 2
                              else stBb(c % 2)[:, 1024 + mg * 128:1024 + (mg + 1) * 128])
                        rbase = 0 if pp != 1 else 1024
                        for hh in range(2):
                            rt = stBb(c % 2)[:, rbase + hh * 512:rbase + (hh + 1) * 512]
                            mm = emit("tensor", lambda e, m=m, lt=lt, rt=rt, hh=hh, c=c, pp=pp: e.matmul(
                                pa[m][:, hh * 512:(hh + 1) * 512], lt, rt,
                                start=(c == 0 and pp == 0), stop=(c == NCH - 1 and pp == 2)),
                                waits=(wmm if (m == 0 and pp == 0 and hh == 0) else []),
                                inc_name="MM")
                last_s_mm[c % 2] = mm
                if c % 4 == 0:
                    k = p_i * 4 + c // 4
                    wlw = []
                    if k == 0:
                        wlw = [("VE", w2_lsub[1]), ("SP", w2_out[1]), ("SP", w2_out[0])]
                    elif w1_lsub:
                        wlw = [("VE", w1_lsub), ("SP", w1_out)]
                    w1_in = emit("sync", lambda e, k=k: e.dma_start(
                        out=big2[:, 0:4096], in_=w1_d[k * 128:(k + 1) * 128, :]),
                        waits=wlw, inc_name="LD")
                    for q in range(4):
                        wlh = [("LD", w1_in)] if q == 0 else []
                        if q == 0 and w1_out:
                            wlh.append(("SP", w1_out))
                        hq = emit("scalar", lambda e, q=q: e.activation(
                            big2b[:, 8192 + q * 2048:8192 + q * 2048 + 1024],
                            big2[:, q * 1024:(q + 1) * 1024], AF.Identity),
                            waits=wlh, inc_name="AC")
                        w1_lsub = emit("vector", lambda e, q=q: e.tensor_tensor(
                            big2b[:, 8192 + q * 2048 + 1024:8192 + (q + 1) * 2048],
                            big2[:, q * 1024:(q + 1) * 1024],
                            big2b[:, 8192 + q * 2048:8192 + q * 2048 + 1024],
                            ALU.subtract), waits=[("AC", hq)], inc_name="VE")
                    w1_out = emit("gpsimd", lambda e, k=k: e.dma_start(
                        out=w1s_d[k * 128:(k + 1) * 128, :], in_=big2b[:, 8192:16384]),
                        waits=[("VE", w1_lsub)], inc_name="SP")
            for m in range(4):
                mg = m + 4 * p_i
                sev = emit("vector", lambda e, m=m, mg=mg: e.tensor_copy(
                    big1[:, mg * 1024:(mg + 1) * 1024], pa[m][:, :]),
                    waits=([("MM", mm)] if m == 0 else []), inc_name="VE")
                s_evacs.append(sev)
        w1s_done = w1_out

        spx = 0
        for r in range(FC):
            spx = emit("gpsimd", lambda e, r=r: e.dma_start(
                out=ccin[r * 128:(r + 1) * 128, :], in_=big1[:, r * 1024:(r + 1) * 1024]),
                waits=([("VE", s_evacs[-1]), ("VE", cs_ve)] if r == 0 else []), inc_name="SP")
        spx = emit("gpsimd", lambda e: e.dma_start(out=ccin[1024:1025, :], in_=ar(6)),
                   inc_name="SP")
        cc_t = emit("gpsimd", lambda e: e.collective_compute(
            "AllReduce", ALU.add, replica_groups=[list(range(8))],
            ins=[ccin_h.ap().opt()], outs=[ccout_h.ap().opt()]),
            waits=[("SP", spx)], inc_name="CC")

        def transpose_1024(src, dst, wait_first):
            evs = []
            for rr in range(FC):
                for cc2 in range(FC):
                    i = rr * FC + cc2
                    pt = ps[i % 4]
                    w = list(wait_first) if i == 0 else []
                    if i >= 4:
                        w.append(("VE", evs[i - 4]))
                    m = emit("tensor", lambda e, p=pt, rr=rr, cc2=cc2, src=src: e.transpose(
                        p[:, 0:128], src[:, rr * 1024 + cc2 * 128: rr * 1024 + (cc2 + 1) * 128],
                        ident[:, :]), waits=w, inc_name="MM")
                    ev = emit("vector", lambda e, p=pt, rr=rr, cc2=cc2, dst=dst: e.tensor_copy(
                        dst[:, cc2 * 1024 + rr * 128: cc2 * 1024 + (rr + 1) * 128], p[:, 0:128]),
                        waits=[("MM", m)], inc_name="VE")
                    evs.append(ev)
            return m, ev

        def mm1024(lhsT_fn, rhs_fn, evac_fn, wait_first, extra_rank1=()):
            evs = {}
            lastmm = 0
            for m in range(FC):
                for h in range(2):
                    i = m * 2 + h
                    pt = ps[i % 4]
                    w = list(wait_first) if i == 0 else []
                    if i >= 4:
                        w.append(("VE", evs[i - 4]))
                    nr = len(extra_rank1)
                    for k in range(FC):
                        lastmm = emit("tensor", lambda e, p=pt, k=k, m=m, h=h: e.matmul(
                            p[:, :], lhsT_fn(k, m), rhs_fn(k, h),
                            start=(k == 0), stop=(k == FC - 1 and nr == 0)),
                            waits=(w if k == 0 else []), inc_name="MM")
                    for j, (lf, rf) in enumerate(extra_rank1):
                        lastmm = emit("tensor", lambda e, p=pt, lf=lf, rf=rf, m=m, h=h,
                                      st=(j == nr - 1): e.matmul(
                            p[:, :], lf(m), rf(h), start=False, stop=st), inc_name="MM")
                    evs[i] = evac_fn(m, h, pt, lastmm)
            return lastmm, evs[15]

        def ev_to(dst):
            def f(m, h, pt, lastmm):
                return emit("vector", lambda e, m=m, h=h, p=pt: e.tensor_copy(
                    dst[:, m * 1024 + h * 512: m * 1024 + (h + 1) * 512], p[:, :]),
                    waits=[("MM", lastmm)], inc_name="VE")
            return f

        def matvec_row(rhs_src, dst_slot, wait_first, lhs_fn):
            last_ev = 0
            for h in range(2):
                for k in range(FC):
                    m = emit("tensor", lambda e, k=k, h=h: e.matmul(
                        ps[4 + h][0:1, :], lhs_fn(k),
                        rhs_src[:, k * 1024 + h * 512: k * 1024 + (h + 1) * 512],
                        start=(k == 0), stop=(k == FC - 1)),
                        waits=(list(wait_first) if (h == 0 and k == 0) else []), inc_name="MM")
                last_ev = emit("vector", lambda e, h=h, m=m, d=dst_slot: e.tensor_copy(
                    arh(d, h), ps[4 + h][0:1, :]), waits=[("MM", m)], inc_name="VE")
            return last_ev

        def row_transpose8(src_row, dst_col8, wait_first):
            last = 0
            for k in range(FC):
                w = list(wait_first) if k == 0 else [("VE", last)]
                m = emit("tensor", lambda e, k=k, s=src_row: e.transpose(
                    ps[4][:, 0:1], s[0:1, k * 128:(k + 1) * 128], ident[0:1, 0:1]),
                    waits=w, inc_name="MM")
                last = emit("vector", lambda e, k=k, m=m, d=dst_col8: e.tensor_copy(
                    d[:, k:k + 1], ps[4][:, 0:1]), waits=[("MM", m)], inc_name="VE")
            return last


        wq_ld = 0
        for r in range(FC):
            wq_ld = emit("sync", lambda e, r=r: e.dma_start(
                out=big3[:, r * 1024:(r + 1) * 1024], in_=wq_d[r * 128:(r + 1) * 128, :]),
                inc_name="LD")
        wqt_mm, wqt_ev = transpose_1024(big3, big2, [
            ("LD", wq_ld), ("VE", s_evacs[-1]), ("SP", w1s_done), ("VE", w1_lsub)])
        wk_ld = 0
        for r in range(FC):
            wk_ld = emit("sync", lambda e, r=r: e.dma_start(
                out=big3[:, r * 1024:(r + 1) * 1024], in_=wk_d[r * 128:(r + 1) * 128, :]),
                waits=([("MM", wqt_mm)] if r == 0 else []), inc_name="LD")
        _, wkt_ev = transpose_1024(big3, big4, [("LD", wk_ld), ("VE", wqt_ev)])
        for r in range(FC):
            emit("sync", lambda e, r=r: e.dma_start(
                out=big1[:, r * 1024:(r + 1) * 1024], in_=ccout[r * 128:(r + 1) * 128, :]),
                waits=([("CC", cc_t)] if r == 0 else []), inc_name="LD")
        gl_ld = emit("sync", lambda e: e.dma_start(out=ar(6), in_=ccout[1024:1025, :]),
                     inc_name="LD")
        csc_ev = row_transpose8(ar(6), cs_col, [("LD", gl_ld), ("VE", wkt_ev)])
        swqt_mm, swqt_ev = mm1024(
            lambda k, m: big1[:, k * 1024 + m * 128: k * 1024 + (m + 1) * 128],
            lambda k, h: big2[:, k * 1024 + h * 512: k * 1024 + (h + 1) * 512],
            ev_to(big3), [("LD", gl_ld), ("VE", wkt_ev)])
        wqcs_ev = matvec_row(big2, 4, [("VE", csc_ev), ("VE", swqt_ev)], lambda k: cs_col[:, k:k + 1])
        pq_ev = 0
        for r in range(FC):
            pq_ev = emit("vector", lambda e, r=r: e.tensor_tensor(
                big3[:, r * 1024:(r + 1) * 1024], big3[:, r * 1024:(r + 1) * 1024],
                big2[:, r * 1024:(r + 1) * 1024], ALU.mult),
                waits=([("VE", swqt_ev)] if r == 0 else []), inc_name="VE")
        qsq_ev = matvec_row(big3, 2, [("VE", pq_ev), ("VE", wqcs_ev)], lambda k: ones_col)
        swkt_mm, swkt_ev = mm1024(
            lambda k, m: big1[:, k * 1024 + m * 128: k * 1024 + (m + 1) * 128],
            lambda k, h: big4[:, k * 1024 + h * 512: k * 1024 + (h + 1) * 512],
            ev_to(big3), [("VE", qsq_ev)])
        wkcs_ev = matvec_row(big4, 5, [("VE", swkt_ev)], lambda k: cs_col[:, k:k + 1])
        pk_ev = 0
        for r in range(FC):
            pk_ev = emit("vector", lambda e, r=r: e.tensor_tensor(
                big4[:, r * 1024:(r + 1) * 1024], big3[:, r * 1024:(r + 1) * 1024],
                big4[:, r * 1024:(r + 1) * 1024], ALU.mult),
                waits=([("VE", wkcs_ev)] if r == 0 else []), inc_name="VE")
        ksq_ev = matvec_row(big4, 3, [("VE", pk_ev)], lambda k: ones_col)

        tmprow = bcast[0:1, :]
        qcs_ve = emit("vector", lambda e: e.scalar_tensor_tensor(
            ar(4), ar(0), CTOT, ar(4), ALU.mult, ALU.add),
            waits=[("VE", ksq_ev), ("LD", ld_bq), ("LD", ld_bk), ("LD", ld_gam)], inc_name="VE")
        emit("vector", lambda e: e.scalar_tensor_tensor(tmprow, ar(0), -CTOT, ar(4), ALU.mult, ALU.add))
        emit("vector", lambda e: e.tensor_tensor(tmprow, tmprow, ar(4), ALU.add))
        emit("vector", lambda e: e.tensor_tensor(tmprow, tmprow, ar(0), ALU.mult))
        emit("vector", lambda e: e.tensor_tensor(ar(2), ar(2), tmprow, ALU.add))
        emit("vector", lambda e: e.scalar_tensor_tensor(tmprow, ar(1), CTOT, ar(5), ALU.mult, ALU.add))
        emit("vector", lambda e: e.tensor_tensor(tmprow, tmprow, ar(5), ALU.add))
        emit("vector", lambda e: e.tensor_tensor(tmprow, tmprow, ar(1), ALU.mult))
        kfull_ve = emit("vector", lambda e: e.tensor_tensor(ar(3), ar(3), tmprow, ALU.add),
                        inc_name="VE")
        acq = emit("scalar", lambda e: e.activation(ar(2), ar(2), AF.Sqrt),
                   waits=[("VE", kfull_ve)], inc_name="AC")
        emit("vector", lambda e: e.tensor_copy(junk[0:1, 0:512], ar(2)[0:1, 0:512]),
             waits=[("AC", acq)])
        qsq_cp = emit("vector", lambda e: e.tensor_copy(w2st[0:1, 0:512], ar(2)[0:1, 512:1024]),
                      inc_name="VE")
        qinv_ve = emit("vector", lambda e: e.reciprocal(ar(2), ar(2)), inc_name="VE")
        ack = emit("scalar", lambda e: e.activation(ar(3), ar(3), AF.Sqrt),
                   waits=[("VE", kfull_ve)], inc_name="AC")
        kinv_ve = emit("vector", lambda e: e.reciprocal(ar(3), ar(3)), waits=[("AC", ack)],
                       inc_name="VE")
        kc_ev = row_transpose8(ar(3), kinv_col, [("VE", kinv_ve), ("VE", qsq_ev)])
        bvk_ve = emit("vector", lambda e: e.tensor_tensor(bvk_col, bv_col, kinv_col, ALU.mult),
                      waits=[("VE", kc_ev), ("VE", v_bv)], inc_name="VE")

        def ev_kg(m, h, pt, lastmm):
            return emit("vector", lambda e, m=m, h=h, p=pt: e.tensor_scalar(
                big4[:, m * 1024 + h * 512: m * 1024 + (h + 1) * 512], p[:, :],
                kinv_col[:, m:m + 1], None, ALU.mult, ALU.bypass),
                waits=[("MM", lastmm)], inc_name="VE")
        gt_mm, kg_ev = mm1024(
            lambda k, m: big3[:, k * 1024 + m * 128: k * 1024 + (m + 1) * 128],
            lambda k, h: big2[:, k * 1024 + h * 512: k * 1024 + (h + 1) * 512],
            ev_kg, [("VE", bvk_ve), ("VE", qcs_ve)],
            extra_rank1=[
                (lambda m: arsl(5, m), lambda h: arh(0, h)),
                (lambda m: arsl(1, m), lambda h: arh(4, h)),
            ])

        kivg_ev = matvec_row(big4, 7, [("VE", kg_ev)], lambda k: ones_col)
        emit("vector", lambda e: e.scalar_tensor_tensor(ar(7), ar(4), EPS_AT, ar(7), ALU.mult, ALU.add),
             waits=[("VE", kivg_ev), ("VE", qinv_ve)])
        emit("vector", lambda e: e.tensor_tensor(ar(7), ar(7), ar(2), ALU.mult))
        emit("vector", lambda e: e.tensor_scalar(ar(7), ar(7), float(D), None, ALU.add, ALU.bypass))
        tail_ve = emit("vector", lambda e: e.reciprocal(ar(5), ar(7)), inc_name="VE")
        emit("vector", lambda e: e.tensor_tensor(ar(6), ar(2), ar(5), ALU.mult),
             waits=[("VE", tail_ve)])
        csc_ve2 = emit("vector", lambda e: e.tensor_scalar(
            ar(6), ar(6), gsc[0:1, 0:1], None, ALU.mult, ALU.bypass), inc_name="VE")
        emit("vector", lambda e: e.tensor_copy(arh(1, 0), junk[0:1, 0:512]),
             waits=[("VE", qsq_cp), ("MM", gt_mm)])
        qsqr_ve = emit("vector", lambda e: e.tensor_copy(arh(1, 1), w2st[0:1, 0:512]),
                       inc_name="VE")
        bv_ld = emit("sync", lambda e: e.dma_start(out=ar(0), in_=bv_d), waits=[("MM", gt_mm)],
                     inc_name="LD")
        bvs_ve = emit("vector", lambda e: e.reduce_sum(gsc[0:1, 1:2], ar(0), axis=AX.X),
                      waits=[("LD", bv_ld)], inc_name="VE")
        bvkg_ev = matvec_row(big4, 4, [("VE", kivg_ev), ("VE", tail_ve), ("VE", v_bv)], lambda k: bv_col[:, k:k + 1])
        emit("vector", lambda e: e.tensor_tensor(ar(4), ar(4), ar(6), ALU.mult),
             waits=[("VE", bvkg_ev), ("VE", csc_ve2), ("VE", bvs_ve)])
        emit("vector", lambda e: e.tensor_scalar(
            ar(0), ar(5), gsc[0:1, 1:2], gsc[0:1, 0:1], ALU.mult, ALU.mult))
        gcon_ve = emit("vector", lambda e: e.tensor_tensor(ar(0), ar(0), ar(4), ALU.add),
                       inc_name="VE")
        bc_mm = emit("tensor", lambda e: e.matmul(ps[4][:, :], ones_row, arh(6, 0),
                                                  start=True, stop=True),
                     waits=[("VE", gcon_ve), ("VE", qsqr_ve)], inc_name="MM")
        bc_mm = emit("tensor", lambda e: e.matmul(ps[5][:, :], ones_row, arh(6, 1),
                                                  start=True, stop=True), inc_name="MM")
        emit("vector", lambda e: e.tensor_copy(bcast[:, 0:512], ps[4][:, :]),
             waits=[("MM", bc_mm)])
        bc_ev = emit("vector", lambda e: e.tensor_copy(bcast[:, 512:1024], ps[5][:, :]),
                     inc_name="VE")
        wv_ld = 0
        for r in range(FC):
            wv_ld = emit("sync", lambda e, r=r: e.dma_start(
                out=big3[:, r * 1024:(r + 1) * 1024], in_=wv_d[r * 128:(r + 1) * 128, :]),
                waits=([("MM", gt_mm)] if r == 0 else []), inc_name="LD")
        wvs_ev = matvec_row(big3, 5, [("LD", wv_ld), ("VE", gcon_ve), ("VE", bc_ev)], lambda k: ones_col)
        def ev_d2(m, h, pt, lastmm):
            return emit("vector", lambda e, m=m, h=h, p=pt: e.tensor_tensor(
                big1[:, m * 1024 + h * 512: m * 1024 + (h + 1) * 512], p[:, :],
                bcast[:, h * 512:(h + 1) * 512], ALU.mult),
                waits=[("MM", lastmm)], inc_name="VE")
        d2_mm, d2_ev = mm1024(
            lambda k, m: big3[:, k * 1024 + m * 128: k * 1024 + (m + 1) * 128],
            lambda k, h: big4[:, k * 1024 + h * 512: k * 1024 + (h + 1) * 512],
            ev_d2, [("VE", wvs_ev)],
            extra_rank1=[(lambda m: arsl(5, m), lambda h: arh(1, h))])
        dg_ve = 0
        for r in range(FC):
            dg_ve = emit("vector", lambda e, r=r: e.tensor_tensor(
                big1[:, r * 1024 + r * 128: r * 1024 + (r + 1) * 128],
                big1[:, r * 1024 + r * 128: r * 1024 + (r + 1) * 128],
                ident[:, :], ALU.add),
                waits=([("VE", d2_ev)] if r == 0 else []), inc_name="VE")
        dsp = 0
        for r in range(FC):
            wls = [("VE", dg_ve), ("MM", d2_mm)] if r == 0 else []
            hq = emit("scalar", lambda e, r=r: e.activation(
                big4b[:, r * 2048:r * 2048 + 1024], big1[:, r * 1024:(r + 1) * 1024],
                AF.Identity), waits=wls, inc_name="AC")
            dsp = emit("vector", lambda e, r=r: e.tensor_tensor(
                big4b[:, r * 2048 + 1024:(r + 1) * 2048], big1[:, r * 1024:(r + 1) * 1024],
                big4b[:, r * 2048:r * 2048 + 1024], ALU.subtract),
                waits=[("AC", hq)], inc_name="VE")
        d2sp_ve = dsp
        b2_ld = emit("sync", lambda e: e.dma_start(out=ar(7), in_=b2_d),
                     waits=[("VE", tail_ve)], inc_name="LD")

        def x1sl(c, h=None):
            t = big2 if c < 8 else big3
            cc = c % 8
            if h is None:
                return t[:, cc * 1024:(cc + 1) * 1024]
            return t[:, cc * 1024 + h * 512: cc * 1024 + (h + 1) * 512]

        att_ve = [0] * NCH
        ln3_nm = [0] * NCH
        att_ev_ph = {}
        last_att_mm = {}
        mm_h = {}
        for c in range(NCH):
            w = []
            if c == 0:
                w = [("MM", d2_mm), ("SP", sp_done[NCH - 1])]
            if c >= 2:
                w.append(("MM", last_att_mm[c - 2]))
            xnt_ld = emit("sync", lambda e, c=c: e.dma_start(out=stB(c % 2), in_=xnt_d[c]),
                          waits=w, inc_name="LD")
            w2_ = []
            if c < 2:
                w2_.append(("AC", last_apply[c % 2]))
            else:
                w2_.append(("VE", att_ve[c - 2]))
            xr_ld = emit("sync", lambda e, c=c: e.dma_start(
                out=stC(c % 2), in_=x_d[c * 128:(c + 1) * 128, :]), waits=w2_, inc_name="LD")
            for h in range(2):
                pt = ps[(2 * c + h) % 4]
                w = [("LD", xnt_ld)]
                if c == 0 and h == 0:
                    w += [("VE", d2sp_ve)]
                if 2 * c + h >= 4:
                    w.append(("VE", att_ev_ph[(2 * c + h) - 4]))
                first = True
                for r in range(FC):
                    for pp in range(3):
                        lt = (stBb(c % 2)[:, r * 256:r * 256 + 128] if pp < 2
                              else stBb(c % 2)[:, r * 256 + 128:r * 256 + 256])
                        rt = (big4b[:, r * 2048 + h * 512: r * 2048 + (h + 1) * 512] if pp != 1
                              else big4b[:, r * 2048 + 1024 + h * 512: r * 2048 + 1024 + (h + 1) * 512])
                        mm = emit("tensor", lambda e, p=pt, lt=lt, rt=rt, st=first: e.matmul(
                            p[:, :], lt, rt, start=st, stop=False),
                            waits=(w if first else []), inc_name="MM")
                        first = False
                mm = emit("tensor", lambda e, p=pt, h=h: e.matmul(
                    p[:, :], ones_row, arh(0, h), start=False, stop=True), inc_name="MM")
                mm_h[h] = mm
            last_att_mm[c] = mm
            b = (c % 2) * 16
            sum0 = mcol[:, b + 0:b + 1]; sum1 = mcol[:, b + 1:b + 2]
            sq0 = mcol[:, b + 2:b + 3]; sq1 = mcol[:, b + 3:b + 4]
            mean_c = mcol[:, b + 4:b + 5]; var_c = mcol[:, b + 5:b + 6]
            std_c = mcol[:, b + 6:b + 7]; rstd_c = mcol[:, b + 7:b + 8]
            negmr_c = mcol[:, b + 8:b + 9]; msc = mcol[:, b + 9:b + 10]
            p0_, p1_ = ps[(2 * c) % 4], ps[(2 * c + 1) % 4]
            t5a = tmp56[:, (c % 2) * 1024:(c % 2) * 1024 + 512]
            t5b = tmp56[:, (c % 2) * 1024 + 512:(c % 2) * 1024 + 1024]
            att_ev_ph[2 * c] = emit("vector", lambda e, t=t5a, p=p0_: e.tensor_copy(t, p[:, :]),
                                    waits=[("MM", mm_h[0])], inc_name="VE")
            ev1 = emit("vector", lambda e, t=t5b, p=p1_: e.tensor_copy(t, p[:, :]),
                       waits=[("MM", mm_h[1])], inc_name="VE")
            att_ev_ph[2 * c + 1] = ev1
            emit("vector", lambda e, s=sum0, t=t5a: e.reduce_sum(s, t, axis=AX.X))
            emit("vector", lambda e, s=sum1, t=t5b: e.reduce_sum(s, t, axis=AX.X))
            emit("scalar", lambda e, s=sq0, t=t5a: e.activation(
                junk[:, :], t, AF.Square, accum_out=s), waits=[("VE", ev1)], inc_name="AC")
            a2 = emit("scalar", lambda e, s=sq1, t=t5b: e.activation(
                junk[:, :], t, AF.Square, accum_out=s), inc_name="AC")
            emit("vector", lambda e, a=sum0, b_=sum1, o=mean_c: e.tensor_tensor(o, a, b_, ALU.add),
                 waits=[("AC", a2)])
            emit("vector", lambda e, o=mean_c: e.tensor_scalar(o, o, 1.0 / D, None, ALU.mult,
                                                               ALU.bypass))
            emit("vector", lambda e, a=sq0, b_=sq1, o=var_c: e.tensor_tensor(o, a, b_, ALU.add))
            emit("vector", lambda e, o=var_c: e.tensor_scalar(o, o, 1.0 / D, None, ALU.mult,
                                                              ALU.bypass))
            emit("vector", lambda e, m=mean_c, o=msc: e.scalar_tensor_tensor(
                o, m, -1.0, m, ALU.mult, ALU.mult))
            v_var = emit("vector", lambda e, o=var_c, s=msc: e.tensor_tensor(o, o, s, ALU.add))
            a3 = emit("scalar", lambda e, v=var_c, o=std_c: e.activation(
                o, v, AF.Sqrt, bias=eps_col), waits=[("VE", v_var)], inc_name="AC")
            emit("vector", lambda e, o=rstd_c, i=std_c: e.reciprocal(o, i), waits=[("AC", a3)])
            nm_ve = emit("vector", lambda e, o=negmr_c, m=mean_c, r=rstd_c: e.scalar_tensor_tensor(
                o, m, -1.0, r, ALU.mult, ALU.mult), inc_name="VE")
            for h in range(2):
                th = tmp56[:, (c % 2) * 1024 + h * 512:(c % 2) * 1024 + (h + 1) * 512]
                emit("vector", lambda e, c=c, h=h, n=negmr_c: e.tensor_scalar(
                    x1sl(c, h), stC(c % 2)[:, h * 512:(h + 1) * 512], n, None,
                    ALU.add, ALU.bypass), waits=[("LD", xr_ld)])
                att_ve[c] = emit("vector", lambda e, c=c, h=h, t=th, r=rstd_c: e.scalar_tensor_tensor(
                    x1sl(c, h), t, r, x1sl(c, h), ALU.mult, ALU.add), inc_name="VE")

        for c in range(NCH):
            b = 64 + (c % 2) * 16
            sum_c = mcol[:, b + 0:b + 1]; sq0 = mcol[:, b + 1:b + 2]; sq1 = mcol[:, b + 2:b + 3]
            mean_c = mcol[:, b + 3:b + 4]; var_c = mcol[:, b + 5:b + 6]
            std_c = mcol[:, b + 6:b + 7]; msc = mcol[:, b + 9:b + 10]
            rstd_c = mcol[:, 96 + c:97 + c]; negmr_c = mcol[:, 112 + c:113 + c]
            w = [("VE", att_ve[c])]
            emit("vector", lambda e, s=sum_c, c=c: e.reduce_sum(s, x1sl(c), axis=AX.X), waits=w)
            emit("scalar", lambda e, s=sq0, c=c: e.activation(
                junk[:, :], x1sl(c, 0), AF.Square, accum_out=s), waits=w, inc_name="AC")
            a2 = emit("scalar", lambda e, s=sq1, c=c: e.activation(
                junk[:, :], x1sl(c, 1), AF.Square, accum_out=s), inc_name="AC")
            emit("vector", lambda e, m=mean_c, s=sum_c: e.tensor_scalar(
                m, s, 1.0 / D, None, ALU.mult, ALU.bypass), waits=[("AC", a2)])
            emit("vector", lambda e, q0=sq0, q1=sq1, o=var_c: e.tensor_tensor(o, q0, q1, ALU.add))
            emit("vector", lambda e, o=var_c: e.tensor_scalar(o, o, 1.0 / D, None, ALU.mult,
                                                              ALU.bypass))
            emit("vector", lambda e, m=mean_c, o=msc: e.scalar_tensor_tensor(
                o, m, -1.0, m, ALU.mult, ALU.mult))
            v_var = emit("vector", lambda e, o=var_c, s=msc: e.tensor_tensor(o, o, s, ALU.add))
            a3 = emit("scalar", lambda e, v=var_c, o=std_c: e.activation(
                o, v, AF.Sqrt, bias=eps_col), waits=[("VE", v_var)], inc_name="AC")
            emit("vector", lambda e, o=rstd_c, i=std_c: e.reciprocal(o, i), waits=[("AC", a3)])
            ln3_nm[c] = emit("vector", lambda e, o=negmr_c, m=mean_c, r=rstd_c: e.scalar_tensor_tensor(
                o, m, -1.0, r, ALU.mult, ALU.mult), inc_name="VE")

        def h1sl(mi, sl=None):
            t = big4 if mi < 16 else big1
            m = mi % 16
            if sl is None:
                return t[:, m * 512:(m + 1) * 512]
            return t[:, m * 512 + sl * 128: m * 512 + (sl + 1) * 128]

        st_done = [0] * NCH
        trv_prev = {0: 0, 1: 0}
        w1_mm_prev = {0: 0, 1: 0}
        w2_mm_prev = {0: 0, 1: 0}
        last_tr3 = {}
        st6 = {"last_tr_ve": 0}
        out_ve_prev = 0

        def do_ln3(g, bank67=()):
            for cc in range(CPG):
                c = g * CPG + cc
                rstd_c = mcol[:, 96 + c:97 + c]; negmr_c = mcol[:, 112 + c:113 + c]
                wa = [("VE", ln3_nm[c])]
                if g == 0 and cc < 2:
                    wa.append(("VE", att_ve[NCH - 2 + cc]))
                key = (g, cc - 2) if cc >= 2 else (g - 1, cc + 2)
                if key in last_tr3:
                    wa.append(("MM", last_tr3[key]))
                x1n_ac = emit("scalar", lambda e, c=c, cc=cc, r=rstd_c, n=negmr_c: e.activation(
                    stC(cc % 2), x1sl(c), AF.Identity, bias=n, scale=r), waits=wa, inc_name="AC")
                for r in range(FC):
                    pt = ps[6 + (r % 2)]
                    w = [("AC", x1n_ac), ("VE", trv_prev[r % 2])]
                    if cc == 0 and r < 2:
                        if g == 0:
                            w.append(("VE", att_ve[NCH - 1]))
                        elif bank67:
                            w.append(("VE", bank67[r % 2]))
                    m = emit("tensor", lambda e, p=pt, cc=cc, r=r: e.transpose(
                        p[:, 0:128], stC(cc % 2)[:, r * 128:(r + 1) * 128], ident[:, :]),
                        waits=w, inc_name="MM")
                    emit("vector", lambda e, p=pt, cc=cc, r=r: e.tensor_copy(
                        x1nTb[:, r * 1024 + cc * 128: r * 1024 + (cc + 1) * 128], p[:, 0:128]),
                        waits=[("MM", m)], inc_name="VE")
                    tv = emit("vector", lambda e, p=pt, cc=cc, r=r: e.tensor_tensor(
                        x1nTb[:, r * 1024 + 512 + cc * 128: r * 1024 + 512 + (cc + 1) * 128],
                        p[:, 0:128],
                        x1nTb[:, r * 1024 + cc * 128: r * 1024 + (cc + 1) * 128],
                        ALU.subtract), inc_name="VE")
                    trv_prev[r % 2] = tv
                last_tr3[(g, cc)] = m
                st6["last_tr_ve"] = tv

        def do_ln4(g, cc, ffn2_done):
            c = g * CPG + cc
            b = (cc % 2) * 16
            sum0 = mcol[:, b + 0:b + 1]; sum1 = mcol[:, b + 1:b + 2]
            sq0 = mcol[:, b + 2:b + 3]; sq1 = mcol[:, b + 3:b + 4]
            mean_c = mcol[:, b + 4:b + 5]; var_c = mcol[:, b + 5:b + 6]
            std_c = mcol[:, b + 6:b + 7]; rstd_c = mcol[:, b + 7:b + 8]
            negmr_c = mcol[:, b + 8:b + 9]; msc = mcol[:, b + 9:b + 10]
            p0_, p1_ = ps[cc * 2], ps[cc * 2 + 1]
            t6a = bcast[:, 0:512]
            t6b = bcast[:, 512:1024]
            emit("vector", lambda e, t=t6a, p=p0_: e.tensor_copy(t, p[:, :]),
                 waits=[("MM", ffn2_done), ("AC", st6.get("ln4_a2", 0))])
            ev1 = emit("vector", lambda e, t=t6b, p=p1_: e.tensor_copy(t, p[:, :]),
                       inc_name="VE")
            emit("vector", lambda e, s=sum0, t=t6a: e.reduce_sum(s, t, axis=AX.X))
            emit("vector", lambda e, s=sum1, t=t6b: e.reduce_sum(s, t, axis=AX.X))
            emit("scalar", lambda e, s=sq0, t=t6a: e.activation(
                junk[:, :], t, AF.Square, accum_out=s), waits=[("VE", ev1)], inc_name="AC")
            a2 = emit("scalar", lambda e, s=sq1, t=t6b: e.activation(
                junk[:, :], t, AF.Square, accum_out=s), inc_name="AC")
            st6["ln4_a2"] = a2
            emit("vector", lambda e, a=sum0, b_=sum1, o=mean_c: e.tensor_tensor(
                o, a, b_, ALU.add), waits=[("AC", a2)])
            emit("vector", lambda e, o=mean_c: e.tensor_scalar(o, o, 1.0 / D, None, ALU.mult,
                                                               ALU.bypass))
            emit("vector", lambda e, a=sq0, b_=sq1, o=var_c: e.tensor_tensor(o, a, b_, ALU.add))
            emit("vector", lambda e, o=var_c: e.tensor_scalar(o, o, 1.0 / D, None, ALU.mult,
                                                              ALU.bypass))
            emit("vector", lambda e, m=mean_c, o=msc: e.scalar_tensor_tensor(
                o, m, -1.0, m, ALU.mult, ALU.mult))
            v_var = emit("vector", lambda e, o=var_c, s=msc: e.tensor_tensor(o, o, s, ALU.add))
            a3 = emit("scalar", lambda e, v=var_c, o=std_c: e.activation(
                o, v, AF.Sqrt, bias=eps_col), waits=[("VE", v_var)], inc_name="AC")
            emit("vector", lambda e, o=rstd_c, i=std_c: e.reciprocal(o, i), waits=[("AC", a3)])
            nm_ve = emit("vector", lambda e, o=negmr_c, m=mean_c, r=rstd_c: e.scalar_tensor_tensor(
                o, m, -1.0, r, ALU.mult, ALU.mult), inc_name="VE")
            for h in range(2):
                t6h = bcast[:, h * 512:(h + 1) * 512]
                emit("vector", lambda e, c=c, h=h, n=negmr_c: e.tensor_scalar(
                    x1sl(c, h), x1sl(c, h), n, None, ALU.add, ALU.bypass),
                    waits=[("VE", nm_ve)])
                ov = emit("vector", lambda e, c=c, h=h, t=t6h, r=rstd_c: e.scalar_tensor_tensor(
                    x1sl(c, h), t, r, x1sl(c, h), ALU.mult, ALU.add), inc_name="VE")
            st_done[c] = emit("gpsimd", lambda e, c=c: e.dma_start(
                out=y_d[c * 128:(c + 1) * 128, :], in_=x1sl(c)),
                waits=[("VE", ov)], inc_name="SP")
            return ev1, ov

        do_ln3(0)
        bank_free_next = {i: att_ev_ph[2 * NCH - 1] for i in range(8)}
        for g in range(NG):
            relu_ac = {}
            bank_free = dict(bank_free_next)
            last_tr_ve = st6["last_tr_ve"]
            for ig in range(4):
                for ka in range(FC):
                    w = [("MM", w1_mm_prev[ka % 2])]
                    if g == 0 and ig == 0 and ka == 0:
                        w.append(("SP", w1s_done))
                    qeng, qsem = ("sync", "LD") if ka % 2 == 0 else ("scalar", "L2")
                    ld = emit(qeng, lambda e, ig=ig, ka=ka: e.dma_start(
                        out=stDb(ka % 2)[:, 0:2048],
                        in_=w1s_d[ka * 128:(ka + 1) * 128, ig * 2048:(ig + 1) * 2048]),
                        waits=w, inc_name=qsem)
                    wmm0 = [(qsem, ld)]
                    if ka == 0:
                        wmm0.append(("VE", last_tr_ve))
                    for mi_l in range(8):
                        wmm = list(wmm0) if mi_l == 0 else []
                        if ka == 0 and bank_free[mi_l]:
                            wmm.append(("VE", bank_free[mi_l]))
                        for pp in range(3):
                            lt = (stDb(ka % 2)[:, mi_l * 128:(mi_l + 1) * 128] if pp < 2
                                  else stDb(ka % 2)[:, 1024 + mi_l * 128:1024 + (mi_l + 1) * 128])
                            rt = (x1nTb[:, ka * 1024:ka * 1024 + 512] if pp != 1
                                  else x1nTb[:, ka * 1024 + 512:ka * 1024 + 1024])
                            mm = emit("tensor", lambda e, p=ps[mi_l], lt=lt, rt=rt, ka=ka, pp=pp: e.matmul(
                                p[:, :], lt, rt,
                                start=(ka == 0 and pp == 0), stop=(ka == FC - 1 and pp == 2)),
                                waits=(wmm if pp == 0 else []), inc_name="MM")
                    w1_mm_prev[ka % 2] = mm
                for mi_l in range(8):
                    mi = ig * 8 + mi_l
                    hb = (big4b if mi < 16 else big1b)
                    hoff = (mi % 16) * 1024
                    rv = emit("vector", lambda e, p=ps[mi_l], mi=mi: e.tensor_scalar(
                        junk[:, 0:512], p[:, :], cols[:, 24 + mi:25 + mi], 0.0, ALU.add, ALU.max),
                        waits=([("MM", mm)] if mi_l == 0 else []), inc_name="VE")
                    hc = emit("scalar", lambda e, hb=hb, hoff=hoff: e.activation(
                        hb[:, hoff:hoff + 512], junk[:, 0:512], AF.Identity),
                        waits=[("VE", rv)], inc_name="AC")
                    ra = emit("vector", lambda e, hb=hb, hoff=hoff: e.tensor_tensor(
                        hb[:, hoff + 512:hoff + 1024], junk[:, 0:512], hb[:, hoff:hoff + 512],
                        ALU.subtract), waits=[("AC", hc)], inc_name="VE")
                    relu_ac[mi] = ra
                    bank_free[mi_l] = rv
                last_psum_free = ra
            if g + 1 < NG:
                do_ln3(g + 1, bank67=(bank_free[6], bank_free[7]))
            for mi in range(32):
                w = [("MM", w2_mm_prev[mi % 2])]
                if g == 0 and mi < 2:
                    w.append(("VE", qsqr_ve))
                qeng, qsem = ("gpsimd", "SP") if mi % 2 == 0 else ("sync", "LD")
                ld = emit(qeng, lambda e, mi=mi: e.dma_start(
                    out=w2stb[:, (mi % 2) * 2048:(mi % 2 + 1) * 2048],
                    in_=w2s_d[mi * 128:(mi + 1) * 128, :]), waits=w, inc_name=qsem)
                wmm0 = [(qsem, ld), ("VE", relu_ac[mi])]
                if mi == 0:
                    wmm0.append(("VE", last_psum_free))
                    wmm0.append(("VE", out_ve_prev))
                    wmm0.append(("VE", st6["last_tr_ve"]))
                hb = (big4b if mi < 16 else big1b)
                hoff = (mi % 16) * 1024
                for cc in range(CPG):
                    for pp in range(3):
                        lt_off = hoff + cc * 128 if pp < 2 else hoff + 512 + cc * 128
                        rbase = (mi % 2) * 2048 + (0 if pp != 1 else 1024)
                        for hh in range(2):
                            rt = w2stb[:, rbase + hh * 512:rbase + (hh + 1) * 512]
                            mm = emit("tensor", lambda e, p=pa[cc], hb=hb, lo=lt_off, rt=rt, hh=hh,
                                      mi=mi, pp=pp: e.matmul(
                                p[:, hh * 512:(hh + 1) * 512], hb[:, lo:lo + 128], rt,
                                start=(mi == 0 and pp == 0), stop=False),
                                waits=(wmm0 if (cc == 0 and pp == 0 and hh == 0) else []),
                                inc_name="MM")
                w2_mm_prev[mi % 2] = mm
            for cc in range(CPG):
                for h in range(2):
                    mm = emit("tensor", lambda e, p=ps[cc * 2 + h], h=h: e.matmul(
                        p[:, :], ones_row, arh(7, h), start=False, stop=True),
                        waits=([("LD", b2_ld)] if (g == 0 and cc == 0 and h == 0) else []),
                        inc_name="MM")
            ffn2_done = mm
            evs = {}
            for cc in range(CPG):
                evs[cc], ov = do_ln4(g, cc, ffn2_done)
            out_ve_prev = ov
            bank_free_next = {}
            for cc in range(4):
                bank_free_next[2 * cc] = evs[cc]
                bank_free_next[2 * cc + 1] = evs[cc]

        emit("gpsimd", lambda e: e.nop(), waits=[("SP", st_done[NCH - 1])])

        totals = {k: v[2] for k, v in cnt.items()}
        for (w, v) in waits_seen:
            assert v <= totals[w], f"wait {w} {v} > total {totals[w]}"

        @blk.sync
        def _(eng):
            for f in E["sync"]:
                f(eng)

        @blk.tensor
        def _(eng):
            for f in E["tensor"]:
                f(eng)

        @blk.vector
        def _(eng):
            for f in E["vector"]:
                f(eng)

        @blk.scalar
        def _(eng):
            for f in E["scalar"]:
                f(eng)

        @blk.gpsimd
        def _(eng):
            for f in E["gpsimd"]:
                f(eng)

    return nc


_NC_CACHE = None


def kernel(L0_attn_x, L0_attn_Wq, L0_attn_bq, L0_attn_Wk, L0_attn_bk,
           L0_attn_Wv, L0_attn_bv, L0_attn_gamma,
           L0_ffn_W1, L0_ffn_b1, L0_ffn_W2, L0_ffn_b2):
    global _NC_CACHE
    x = np.ascontiguousarray(np.asarray(L0_attn_x, dtype=np.float32).reshape(16384, 1024))
    com = dict(
        wq=np.ascontiguousarray(np.asarray(L0_attn_Wq, dtype=np.float32)),
        wk=np.ascontiguousarray(np.asarray(L0_attn_Wk, dtype=np.float32)),
        wv=np.ascontiguousarray(np.asarray(L0_attn_Wv, dtype=np.float32)),
        bq=np.asarray(L0_attn_bq, dtype=np.float32).reshape(1, 1024),
        bk=np.asarray(L0_attn_bk, dtype=np.float32).reshape(1, 1024),
        bv_row=np.asarray(L0_attn_bv, dtype=np.float32).reshape(1, 1024),
        bv8=np.ascontiguousarray(np.asarray(L0_attn_bv, dtype=np.float32).reshape(8, 128)),
        gamma=np.asarray(L0_attn_gamma, dtype=np.float32).reshape(1, 1),
        w1=np.ascontiguousarray(np.asarray(L0_ffn_W1, dtype=np.float32)),
        b1_32=np.ascontiguousarray(np.asarray(L0_ffn_b1, dtype=np.float32).reshape(32, 128)),
        w2=np.ascontiguousarray(np.asarray(L0_ffn_W2, dtype=np.float32)),
        b2=np.asarray(L0_ffn_b2, dtype=np.float32).reshape(1, 1024),
        ident=np.eye(128, dtype=np.float32),
    )
    if _NC_CACHE is None:
        _NC_CACHE = build_core()
    nc = _NC_CACHE
    in_maps = []
    for i in range(8):
        m = dict(com)
        m["x"] = np.ascontiguousarray(x[i * TOK:(i + 1) * TOK, :])
        in_maps.append(m)
    res = run_bass_kernel_spmd(nc, in_maps, core_ids=list(range(8)))
    out = np.concatenate([res.results[i]["y"] for i in range(8)], axis=0)
    return out.reshape(1, 16384, 1024).astype(np.float32)


# version 09
# speedup vs baseline: 1.1053x; 1.0065x over previous
import numpy as np
import concourse.bass as bass
import concourse.mybir as mybir
from concourse.bass_utils import run_bass_kernel_spmd

F32 = mybir.dt.float32
BF16 = mybir.dt.bfloat16
AF = mybir.ActivationFunctionType
ALU = mybir.AluOpType
AX = mybir.AxisListType

TOK = 2048
NCH = 16
D = 1024
FC = 8
INTER = 4096
NG = 4
CPG = 4
CTOT = 16384.0
EPS_LN = 1e-5
EPS_AT = 1e-6


def build_core():
    nc = bass.Bass("TRN2", target_bir_lowering=False)
    dt = lambda n, s: nc.dram_tensor(n, s, F32, kind="ExternalInput").ap()
    x_d = dt("x", [TOK, D])
    wq_d = dt("wq", [D, D]); wk_d = dt("wk", [D, D]); wv_d = dt("wv", [D, D])
    bq_d = dt("bq", [1, D]); bk_d = dt("bk", [1, D])
    bv_d = dt("bv_row", [1, D]); bv8_d = dt("bv8", [8, 128])
    gam_d = dt("gamma", [1, 1])
    w1_d = dt("w1", [D, INTER]); b1_d = dt("b1_32", [32, 128])
    w2_d = dt("w2", [INTER, D]); b2_d = dt("b2", [1, D])
    id_d = dt("ident", [128, 128])
    y_d = nc.dram_tensor("y", [TOK, D], F32, kind="ExternalOutput").ap()
    ccin_h = nc.dram_tensor("ccin", [1025, 1024], F32)
    ccout_h = nc.dram_tensor("ccout", [1025, 1024], F32)
    ccin, ccout = ccin_h.ap(), ccout_h.ap()
    xnt_d = nc.dram_tensor("xnt", [NCH, 128, D], F32).ap()
    w1s_d = nc.dram_tensor("w1s", [D, 2 * INTER], BF16).ap()
    w2s_d = nc.dram_tensor("w2s", [INTER, 2 * D], BF16).ap()

    import contextlib
    with contextlib.ExitStack() as stack:
        en = stack.enter_context
        big1 = en(nc.sbuf_tensor("big1", [128, 8192], F32))
        big2 = en(nc.sbuf_tensor("big2", [128, 8192], F32))
        big3 = en(nc.sbuf_tensor("big3", [128, 8192], F32))
        big4 = en(nc.sbuf_tensor("big4", [128, 8192], F32))
        util = en(nc.sbuf_tensor("util", [128, 8192], F32))
        w2st = en(nc.sbuf_tensor("w2st", [128, 2048], F32))
        arena = en(nc.sbuf_tensor("arena", [128, 8192], F32))
        cols = en(nc.sbuf_tensor("cols", [128, 64], F32))
        mcol = en(nc.sbuf_tensor("mcol", [128, 128], F32))
        bcast = en(nc.sbuf_tensor("bcast", [128, 1024], F32))
        ident = en(nc.sbuf_tensor("ident_sb", [128, 128], F32))
        ones = en(nc.sbuf_tensor("ones", [128, 128], F32))
        gsc = en(nc.sbuf_tensor("gsc", [128, 8], F32))
        junk = en(nc.sbuf_tensor("junk", [128, 512], F32))
        pa = [en(nc.psum_tensor(f"pa{i}", [128, 1024], F32)) for i in range(4)]
        ps = [pa[i // 2][:, (i % 2) * 512:(i % 2 + 1) * 512] for i in range(8)]
        ps4, ps5, ps6, ps7 = ps[4], ps[5], ps[6], ps[7]
        sLD = en(nc.semaphore("sLD"))
        sSP = en(nc.semaphore("sSP"))
        sMM = en(nc.semaphore("sMM"))
        sVE = en(nc.semaphore("sVE"))
        sAC = en(nc.semaphore("sAC"))
        sCC = en(nc.semaphore("sCC"))
        sL2 = en(nc.semaphore("sL2"))
        blk = en(nc.Block())
        stA = lambda b: util[:, (0 + b) * 1024:(1 + b) * 1024]
        stB = lambda b: util[:, (2 + b) * 1024:(3 + b) * 1024]
        stC = lambda b: util[:, (4 + b) * 1024:(5 + b) * 1024]
        stD = lambda b: util[:, (6 + b) * 1024:(7 + b) * 1024]
        x1nT = util[:, 0:4096]
        tmp56 = util[:, 0:2048]
        big1b = big1.bitcast(BF16)
        big2b = big2.bitcast(BF16)
        big4b = big4.bitcast(BF16)
        utilb = util.bitcast(BF16)
        w2stb = w2st.bitcast(BF16)
        x1nTb = utilb[:, 0:8192]
        stBb = lambda b: utilb[:, (2 + b) * 2048:(3 + b) * 2048]
        stDb = lambda b: utilb[:, (6 + b) * 2048:(7 + b) * 2048]
        ar = lambda i: arena[0:1, i * 1024:(i + 1) * 1024]
        arh = lambda i, h: arena[0:1, i * 1024 + h * 512: i * 1024 + (h + 1) * 512]
        arsl = lambda i, m: arena[0:1, i * 1024 + m * 128: i * 1024 + (m + 1) * 128]
        cs_col = cols[:, 0:8]; kinv_col = cols[:, 8:16]
        bv_col = cols[:, 16:24]; bvk_col = cols[:, 56:64]
        ones_col = ones[:, 0:1]; ones_row = ones[0:1, 0:128]

        E = {k: [] for k in ("sync", "tensor", "vector", "scalar", "gpsimd")}
        cnt = {"LD": [sLD, 16, 0], "SP": [sSP, 16, 0], "MM": [sMM, 1, 0],
               "VE": [sVE, 1, 0], "AC": [sAC, 1, 0], "CC": [sCC, 1, 0],
               "L2": [sL2, 16, 0]}
        waits_seen = []

        def emit(eng, f, waits=(), inc_name=None):
            waits = list(waits)
            auto = {"vector": "VE", "scalar": "AC"}.get(eng)
            if auto is not None:
                if inc_name is None:
                    inc_name = auto
                waits.append((auto, cnt[auto][2]))
            if inc_name in ("LD", "SP", "L2"):
                waits.append((inc_name, cnt[inc_name][2]))
            wl = [(cnt[w][0], v, w) for (w, v) in waits if v]
            for (_, v, w) in wl:
                waits_seen.append((w, v))
            step = 0
            sem = None
            if inc_name:
                c = cnt[inc_name]
                c[2] += c[1]
                step = c[1]
                sem = c[0]
            tgt = cnt[inc_name][2] if inc_name else None

            def g(eng_obj, f=f, wl=wl, sem=sem, step=step):
                for (s, v, _) in wl:
                    eng_obj.wait_ge(s, v)
                ins = f(eng_obj)
                if sem is not None:
                    ins.then_inc(sem, step)
            E[eng].append(g)
            return tgt

        emit("vector", lambda e: e.memset(ones[:, :], 1.0))
        ve_init = emit("vector", lambda e: e.memset(gsc[:, 2:3], EPS_LN), inc_name="VE")
        eps_col = gsc[:, 2:3]
        ld_ident = emit("sync", lambda e: e.dma_start(out=ident[:, :], in_=id_d), inc_name="LD")
        ld_bq = emit("sync", lambda e: e.dma_start(out=ar(0), in_=bq_d), inc_name="LD")
        ld_bk = emit("sync", lambda e: e.dma_start(out=ar(1), in_=bk_d), inc_name="LD")
        ld_gam = emit("sync", lambda e: e.dma_start(out=gsc[0:1, 0:1], in_=gam_d), inc_name="LD")
        ld_bv8 = emit("sync", lambda e: e.dma_start(out=big2[0:8, 0:128], in_=bv8_d), inc_name="LD")
        ld_b132 = emit("sync", lambda e: e.dma_start(out=big2[0:32, 128:256], in_=b1_d), inc_name="LD")
        t_bv = emit("tensor", lambda e: e.transpose(ps6[:, 0:8], big2[0:8, 0:128], ident[0:8, 0:8]),
                    waits=[("LD", ld_bv8), ("LD", ld_ident)], inc_name="MM")
        t_b1 = emit("tensor", lambda e: e.transpose(ps7[:, 0:32], big2[0:32, 128:256], ident[0:32, 0:32]),
                    waits=[("LD", ld_b132)], inc_name="MM")
        v_bv = emit("vector", lambda e: e.tensor_copy(bv_col, ps6[:, 0:8]), waits=[("MM", t_bv)],
                    inc_name="VE")
        v_b1 = emit("vector", lambda e: e.tensor_copy(cols[:, 24:56], ps7[:, 0:32]),
                    waits=[("MM", t_b1)], inc_name="VE")

        x_ld = [0] * NCH
        xn_ac = [0] * NCH
        sp_done = [0] * NCH
        ev_done = [0] * NCH
        cs_mm_c = [0] * NCH
        tr_ev = {0: v_bv, 1: v_b1}
        w2_in = {0: 0, 1: 0}; w2_lsub = {0: 0, 1: 0}; w2_out = {0: 0, 1: 0}
        for c in range(NCH):
            w = [("AC", xn_ac[c - 2])] if c >= 2 else []
            x_ld[c] = emit("sync", lambda e, c=c: e.dma_start(
                out=stC(c % 2), in_=x_d[c * 128:(c + 1) * 128, :]), waits=w, inc_name="LD")
            b = (c % 2) * 16
            sum_c = mcol[:, b + 0:b + 1]; sq0 = mcol[:, b + 1:b + 2]; sq1 = mcol[:, b + 2:b + 3]
            mean_c = mcol[:, b + 3:b + 4]; msq = mcol[:, b + 4:b + 5]; var_c = mcol[:, b + 5:b + 6]
            std_c = mcol[:, b + 6:b + 7]
            rstd_c = mcol[:, 32 + c:33 + c]
            negmr_c = mcol[:, 48 + c:49 + c]
            emit("vector", lambda e, s=sum_c, c=c: e.reduce_sum(s, stC(c % 2), axis=AX.X),
                 waits=[("LD", x_ld[c])])
            emit("scalar", lambda e, s=sq0, c=c: e.activation(
                junk[:, :], stC(c % 2)[:, 0:512], AF.Square, accum_out=s),
                waits=[("LD", x_ld[c])], inc_name="AC")
            ac2 = emit("scalar", lambda e, s=sq1, c=c: e.activation(
                junk[:, :], stC(c % 2)[:, 512:1024], AF.Square, accum_out=s), inc_name="AC")
            emit("vector", lambda e, m=mean_c, s=sum_c: e.tensor_scalar(
                m, s, 1.0 / D, None, ALU.mult, ALU.bypass), waits=[("AC", ac2)])
            emit("vector", lambda e, q=sq0, q1=sq1, o=msq: e.tensor_tensor(o, q, q1, ALU.add))
            emit("vector", lambda e, m=mean_c, o=var_c: e.scalar_tensor_tensor(
                o, m, -1.0, m, ALU.mult, ALU.mult))
            v_var = emit("vector", lambda e, q=msq, o=var_c: e.scalar_tensor_tensor(
                o, q, 1.0 / D, o, ALU.mult, ALU.add))
            ac3 = emit("scalar", lambda e, v=var_c, o=std_c: e.activation(
                o, v, AF.Sqrt, bias=eps_col),
                waits=([("VE", ve_init), ("VE", v_var)] if c == 0 else [("VE", v_var)]),
                inc_name="AC")
            emit("vector", lambda e, o=rstd_c, i=std_c: e.reciprocal(o, i), waits=[("AC", ac3)])
            st_ve = emit("vector", lambda e, o=negmr_c, m=mean_c, r=rstd_c: e.scalar_tensor_tensor(
                o, m, -1.0, r, ALU.mult, ALU.mult), inc_name="VE")
            w = [("VE", st_ve)]
            if c >= 2:
                w.append(("MM", cs_mm_c[c - 2]))
            xn_ac[c] = emit("scalar", lambda e, c=c, r=rstd_c, n=negmr_c: e.activation(
                stA(c % 2), stC(c % 2), AF.Identity, bias=n, scale=r), waits=w, inc_name="AC")
            for r in range(FC):
                pt = ps[6 + (r % 2)]
                w = [("AC", xn_ac[c]), ("VE", tr_ev[r % 2])]
                m = emit("tensor", lambda e, p=pt, c=c, r=r: e.transpose(
                    p[:, 0:128], stA(c % 2)[:, r * 128:(r + 1) * 128], ident[:, :]),
                    waits=w, inc_name="MM")
                wv = [("MM", m)]
                if r == 0 and c >= 2:
                    wv.append(("SP", sp_done[c - 2]))
                emit("vector", lambda e, p=pt, c=c, r=r: e.tensor_copy(
                    stBb(c % 2)[:, r * 256:r * 256 + 128], p[:, 0:128]), waits=wv, inc_name="VE")
                ev = emit("vector", lambda e, p=pt, c=c, r=r: e.tensor_tensor(
                    stBb(c % 2)[:, r * 256 + 128:r * 256 + 256], p[:, 0:128],
                    stBb(c % 2)[:, r * 256:r * 256 + 128], ALU.subtract), inc_name="VE")
                tr_ev[r % 2] = ev
            ev_done[c] = ev
            emit("tensor", lambda e, c=c: e.matmul(
                ps4[0:1, :], ones_col, stA(c % 2)[:, 0:512],
                start=(c == 0), stop=(c == NCH - 1)), inc_name="MM")
            cs_mm_c[c] = emit("tensor", lambda e, c=c: e.matmul(
                ps5[0:1, :], ones_col, stA(c % 2)[:, 512:1024],
                start=(c == 0), stop=(c == NCH - 1)), inc_name="MM")
            sp_done[c] = emit("gpsimd", lambda e, c=c: e.dma_start(
                out=xnt_d[c], in_=stB(c % 2)), waits=[("VE", ev_done[c])], inc_name="SP")
            for j in (2 * c, 2 * c + 1):
                buf = j % 2
                srcw = big2[:, buf * 1024:(buf + 1) * 1024]
                dstw = big2b[:, 8192 + buf * 2048:8192 + (buf + 1) * 2048]
                wlw = [("VE", w2_lsub[buf])] if w2_lsub[buf] else [("MM", t_b1)]
                w2_in[buf] = emit("sync", lambda e, j=j, s=srcw: e.dma_start(
                    out=s, in_=w2_d[j * 128:(j + 1) * 128, :]), waits=wlw, inc_name="LD")
                wlh = [("LD", w2_in[buf])]
                if w2_out[buf]:
                    wlh.append(("SP", w2_out[buf]))
                hcp = emit("scalar", lambda e, s=srcw, d=dstw: e.activation(
                    d[:, 0:1024], s, AF.Identity), waits=wlh, inc_name="AC")
                w2_lsub[buf] = emit("vector", lambda e, s=srcw, d=dstw: e.tensor_tensor(
                    d[:, 1024:2048], s, d[:, 0:1024], ALU.subtract), waits=[("AC", hcp)],
                    inc_name="VE")
                w2_out[buf] = emit("gpsimd", lambda e, j=j, d=dstw: e.dma_start(
                    out=w2s_d[j * 128:(j + 1) * 128, :], in_=d),
                    waits=[("VE", w2_lsub[buf])], inc_name="SP")

        w2s_done = w2_out[1]
        emit("vector", lambda e: e.tensor_copy(arh(6, 0), ps4[0:1, :]),
             waits=[("MM", cs_mm_c[NCH - 1])])
        cs_ve = emit("vector", lambda e: e.tensor_copy(arh(6, 1), ps5[0:1, :]), inc_name="VE")

        s_evacs = []
        last_apply = {0: xn_ac[NCH - 2], 1: xn_ac[NCH - 1]}
        last_s_mm = {0: 0, 1: 0}
        split_free = {0: cs_mm_c[NCH - 2], 1: cs_mm_c[NCH - 1]}
        split_kind = {0: "MM", 1: "MM"}
        stB_free = {0: sp_done[NCH - 2], 1: sp_done[NCH - 1]}
        w1_in = 0; w1_lsub = 0; w1_out = 0
        for p_i in range(2):
            for c in range(NCH):
                lx = emit("sync", lambda e, c=c: e.dma_start(
                    out=stC(c % 2), in_=x_d[c * 128:(c + 1) * 128, :]),
                    waits=[("AC", last_apply[c % 2])], inc_name="LD")
                aply = emit("scalar", lambda e, c=c: e.activation(
                    stA(c % 2), stC(c % 2), AF.Identity,
                    bias=mcol[:, 48 + c:49 + c], scale=mcol[:, 32 + c:33 + c]),
                    waits=[("LD", lx), (split_kind[c % 2], split_free[c % 2])], inc_name="AC")
                last_apply[c % 2] = aply
                wsp = [("AC", aply), ("SP", stB_free[c % 2])]
                if last_s_mm[c % 2]:
                    wsp.append(("MM", last_s_mm[c % 2]))
                hcp = emit("scalar", lambda e, c=c: e.activation(
                    stBb(c % 2)[:, 0:1024], stA(c % 2), AF.Identity), waits=wsp, inc_name="AC")
                lsub = emit("vector", lambda e, c=c: e.tensor_tensor(
                    stBb(c % 2)[:, 1024:2048], stA(c % 2), stBb(c % 2)[:, 0:1024],
                    ALU.subtract), waits=[("AC", hcp)], inc_name="VE")
                split_free[c % 2] = lsub
                split_kind[c % 2] = "VE"
                stB_free[c % 2] = 0
                wmm = [("VE", lsub)]
                if c == 0 and p_i == 0:
                    wmm += [("VE", cs_ve), ("VE", ev_done[NCH - 1])]
                if c == 0 and p_i == 1:
                    wmm.append(("VE", s_evacs[-1]))
                for m in range(4):
                    mg = m + 4 * p_i
                    for pp in range(3):
                        lt = (stBb(c % 2)[:, mg * 128:(mg + 1) * 128] if pp < 2
                              else stBb(c % 2)[:, 1024 + mg * 128:1024 + (mg + 1) * 128])
                        rbase = 0 if pp != 1 else 1024
                        for hh in range(2):
                            rt = stBb(c % 2)[:, rbase + hh * 512:rbase + (hh + 1) * 512]
                            mm = emit("tensor", lambda e, m=m, lt=lt, rt=rt, hh=hh, c=c, pp=pp: e.matmul(
                                pa[m][:, hh * 512:(hh + 1) * 512], lt, rt,
                                start=(c == 0 and pp == 0), stop=(c == NCH - 1 and pp == 2)),
                                waits=(wmm if (m == 0 and pp == 0 and hh == 0) else []),
                                inc_name="MM")
                last_s_mm[c % 2] = mm
                if c % 4 == 0:
                    k = p_i * 4 + c // 4
                    wlw = []
                    if k == 0:
                        wlw = [("VE", w2_lsub[1]), ("SP", w2_out[1]), ("SP", w2_out[0])]
                    elif w1_lsub:
                        wlw = [("VE", w1_lsub), ("SP", w1_out)]
                    w1_in = emit("sync", lambda e, k=k: e.dma_start(
                        out=big2[:, 0:4096], in_=w1_d[k * 128:(k + 1) * 128, :]),
                        waits=wlw, inc_name="LD")
                    for q in range(4):
                        wlh = [("LD", w1_in)] if q == 0 else []
                        if q == 0 and w1_out:
                            wlh.append(("SP", w1_out))
                        hq = emit("scalar", lambda e, q=q: e.activation(
                            big2b[:, 8192 + q * 2048:8192 + q * 2048 + 1024],
                            big2[:, q * 1024:(q + 1) * 1024], AF.Identity),
                            waits=wlh, inc_name="AC")
                        w1_lsub = emit("vector", lambda e, q=q: e.tensor_tensor(
                            big2b[:, 8192 + q * 2048 + 1024:8192 + (q + 1) * 2048],
                            big2[:, q * 1024:(q + 1) * 1024],
                            big2b[:, 8192 + q * 2048:8192 + q * 2048 + 1024],
                            ALU.subtract), waits=[("AC", hq)], inc_name="VE")
                    w1_out = emit("gpsimd", lambda e, k=k: e.dma_start(
                        out=w1s_d[k * 128:(k + 1) * 128, :], in_=big2b[:, 8192:16384]),
                        waits=[("VE", w1_lsub)], inc_name="SP")
            for m in range(4):
                mg = m + 4 * p_i
                sev = emit("vector", lambda e, m=m, mg=mg: e.tensor_copy(
                    big1[:, mg * 1024:(mg + 1) * 1024], pa[m][:, :]),
                    waits=([("MM", mm)] if m == 0 else []), inc_name="VE")
                s_evacs.append(sev)
        w1s_done = w1_out

        spx = 0
        for r in range(FC):
            spx = emit("gpsimd", lambda e, r=r: e.dma_start(
                out=ccin[r * 128:(r + 1) * 128, :], in_=big1[:, r * 1024:(r + 1) * 1024]),
                waits=([("VE", s_evacs[-1]), ("VE", cs_ve)] if r == 0 else []), inc_name="SP")
        spx = emit("gpsimd", lambda e: e.dma_start(out=ccin[1024:1025, :], in_=ar(6)),
                   inc_name="SP")
        cc_t = emit("gpsimd", lambda e: e.collective_compute(
            "AllReduce", ALU.add, replica_groups=[list(range(8))],
            ins=[ccin_h.ap().opt()], outs=[ccout_h.ap().opt()]),
            waits=[("SP", spx)], inc_name="CC")

        def transpose_1024(src, dst, wait_first):
            evs = []
            for rr in range(FC):
                for cc2 in range(FC):
                    i = rr * FC + cc2
                    pt = ps[i % 4]
                    w = list(wait_first) if i == 0 else []
                    if i >= 4:
                        w.append(("VE", evs[i - 4]))
                    m = emit("tensor", lambda e, p=pt, rr=rr, cc2=cc2, src=src: e.transpose(
                        p[:, 0:128], src[:, rr * 1024 + cc2 * 128: rr * 1024 + (cc2 + 1) * 128],
                        ident[:, :]), waits=w, inc_name="MM")
                    ev = emit("vector", lambda e, p=pt, rr=rr, cc2=cc2, dst=dst: e.tensor_copy(
                        dst[:, cc2 * 1024 + rr * 128: cc2 * 1024 + (rr + 1) * 128], p[:, 0:128]),
                        waits=[("MM", m)], inc_name="VE")
                    evs.append(ev)
            return m, ev

        def mm1024(lhsT_fn, rhs_fn, evac_fn, wait_first, extra_rank1=()):
            evs = {}
            lastmm = 0
            for m in range(FC):
                for h in range(2):
                    i = m * 2 + h
                    pt = ps[i % 4]
                    w = list(wait_first) if i == 0 else []
                    if i >= 4:
                        w.append(("VE", evs[i - 4]))
                    nr = len(extra_rank1)
                    for k in range(FC):
                        lastmm = emit("tensor", lambda e, p=pt, k=k, m=m, h=h: e.matmul(
                            p[:, :], lhsT_fn(k, m), rhs_fn(k, h),
                            start=(k == 0), stop=(k == FC - 1 and nr == 0)),
                            waits=(w if k == 0 else []), inc_name="MM")
                    for j, (lf, rf) in enumerate(extra_rank1):
                        lastmm = emit("tensor", lambda e, p=pt, lf=lf, rf=rf, m=m, h=h,
                                      st=(j == nr - 1): e.matmul(
                            p[:, :], lf(m), rf(h), start=False, stop=st), inc_name="MM")
                    evs[i] = evac_fn(m, h, pt, lastmm)
            return lastmm, evs[15]

        def ev_to(dst):
            def f(m, h, pt, lastmm):
                return emit("vector", lambda e, m=m, h=h, p=pt: e.tensor_copy(
                    dst[:, m * 1024 + h * 512: m * 1024 + (h + 1) * 512], p[:, :]),
                    waits=[("MM", lastmm)], inc_name="VE")
            return f

        def matvec_row(rhs_src, dst_slot, wait_first, lhs_fn):
            last_ev = 0
            for h in range(2):
                for k in range(FC):
                    m = emit("tensor", lambda e, k=k, h=h: e.matmul(
                        ps[4 + h][0:1, :], lhs_fn(k),
                        rhs_src[:, k * 1024 + h * 512: k * 1024 + (h + 1) * 512],
                        start=(k == 0), stop=(k == FC - 1)),
                        waits=(list(wait_first) if (h == 0 and k == 0) else []), inc_name="MM")
                last_ev = emit("vector", lambda e, h=h, m=m, d=dst_slot: e.tensor_copy(
                    arh(d, h), ps[4 + h][0:1, :]), waits=[("MM", m)], inc_name="VE")
            return last_ev

        def row_transpose8(src_row, dst_col8, wait_first):
            last = 0
            for k in range(FC):
                w = list(wait_first) if k == 0 else [("VE", last)]
                m = emit("tensor", lambda e, k=k, s=src_row: e.transpose(
                    ps[4][:, 0:1], s[0:1, k * 128:(k + 1) * 128], ident[0:1, 0:1]),
                    waits=w, inc_name="MM")
                last = emit("vector", lambda e, k=k, m=m, d=dst_col8: e.tensor_copy(
                    d[:, k:k + 1], ps[4][:, 0:1]), waits=[("MM", m)], inc_name="VE")
            return last


        wq_ld = 0
        for r in range(FC):
            wq_ld = emit("sync", lambda e, r=r: e.dma_start(
                out=big3[:, r * 1024:(r + 1) * 1024], in_=wq_d[r * 128:(r + 1) * 128, :]),
                inc_name="LD")
        wqt_mm, wqt_ev = transpose_1024(big3, big2, [
            ("LD", wq_ld), ("VE", s_evacs[-1]), ("SP", w1s_done), ("VE", w1_lsub)])
        wk_ld = 0
        for r in range(FC):
            wk_ld = emit("sync", lambda e, r=r: e.dma_start(
                out=big3[:, r * 1024:(r + 1) * 1024], in_=wk_d[r * 128:(r + 1) * 128, :]),
                waits=([("MM", wqt_mm)] if r == 0 else []), inc_name="LD")
        _, wkt_ev = transpose_1024(big3, big4, [("LD", wk_ld), ("VE", wqt_ev)])
        for r in range(FC):
            emit("sync", lambda e, r=r: e.dma_start(
                out=big1[:, r * 1024:(r + 1) * 1024], in_=ccout[r * 128:(r + 1) * 128, :]),
                waits=([("CC", cc_t)] if r == 0 else []), inc_name="LD")
        gl_ld = emit("sync", lambda e: e.dma_start(out=ar(6), in_=ccout[1024:1025, :]),
                     inc_name="LD")
        csc_ev = row_transpose8(ar(6), cs_col, [("LD", gl_ld), ("VE", wkt_ev)])
        swqt_mm, swqt_ev = mm1024(
            lambda k, m: big1[:, k * 1024 + m * 128: k * 1024 + (m + 1) * 128],
            lambda k, h: big2[:, k * 1024 + h * 512: k * 1024 + (h + 1) * 512],
            ev_to(big3), [("LD", gl_ld), ("VE", wkt_ev)])
        wqcs_ev = matvec_row(big2, 4, [("VE", csc_ev), ("VE", swqt_ev)], lambda k: cs_col[:, k:k + 1])
        pq_ev = 0
        for r in range(FC):
            pq_ev = emit("vector", lambda e, r=r: e.tensor_tensor(
                big3[:, r * 1024:(r + 1) * 1024], big3[:, r * 1024:(r + 1) * 1024],
                big2[:, r * 1024:(r + 1) * 1024], ALU.mult),
                waits=([("VE", swqt_ev)] if r == 0 else []), inc_name="VE")
        qsq_ev = matvec_row(big3, 2, [("VE", pq_ev), ("VE", wqcs_ev)], lambda k: ones_col)
        swkt_mm, swkt_ev = mm1024(
            lambda k, m: big1[:, k * 1024 + m * 128: k * 1024 + (m + 1) * 128],
            lambda k, h: big4[:, k * 1024 + h * 512: k * 1024 + (h + 1) * 512],
            ev_to(big3), [("VE", qsq_ev)])
        wkcs_ev = matvec_row(big4, 5, [("VE", swkt_ev)], lambda k: cs_col[:, k:k + 1])
        pk_ev = 0
        for r in range(FC):
            pk_ev = emit("vector", lambda e, r=r: e.tensor_tensor(
                big4[:, r * 1024:(r + 1) * 1024], big3[:, r * 1024:(r + 1) * 1024],
                big4[:, r * 1024:(r + 1) * 1024], ALU.mult),
                waits=([("VE", wkcs_ev)] if r == 0 else []), inc_name="VE")
        ksq_ev = matvec_row(big4, 3, [("VE", pk_ev)], lambda k: ones_col)

        tmprow = bcast[0:1, :]
        qcs_ve = emit("vector", lambda e: e.scalar_tensor_tensor(
            ar(4), ar(0), CTOT, ar(4), ALU.mult, ALU.add),
            waits=[("VE", ksq_ev), ("LD", ld_bq), ("LD", ld_bk), ("LD", ld_gam)], inc_name="VE")
        emit("vector", lambda e: e.scalar_tensor_tensor(tmprow, ar(0), -CTOT, ar(4), ALU.mult, ALU.add))
        emit("vector", lambda e: e.tensor_tensor(tmprow, tmprow, ar(4), ALU.add))
        emit("vector", lambda e: e.tensor_tensor(tmprow, tmprow, ar(0), ALU.mult))
        emit("vector", lambda e: e.tensor_tensor(ar(2), ar(2), tmprow, ALU.add))
        emit("vector", lambda e: e.scalar_tensor_tensor(tmprow, ar(1), CTOT, ar(5), ALU.mult, ALU.add))
        emit("vector", lambda e: e.tensor_tensor(tmprow, tmprow, ar(5), ALU.add))
        emit("vector", lambda e: e.tensor_tensor(tmprow, tmprow, ar(1), ALU.mult))
        kfull_ve = emit("vector", lambda e: e.tensor_tensor(ar(3), ar(3), tmprow, ALU.add),
                        inc_name="VE")
        acq = emit("scalar", lambda e: e.activation(ar(2), ar(2), AF.Sqrt),
                   waits=[("VE", kfull_ve)], inc_name="AC")
        emit("vector", lambda e: e.tensor_copy(junk[0:1, 0:512], ar(2)[0:1, 0:512]),
             waits=[("AC", acq)])
        qsq_cp = emit("vector", lambda e: e.tensor_copy(w2st[0:1, 0:512], ar(2)[0:1, 512:1024]),
                      inc_name="VE")
        qinv_ve = emit("vector", lambda e: e.reciprocal(ar(2), ar(2)), inc_name="VE")
        ack = emit("scalar", lambda e: e.activation(ar(3), ar(3), AF.Sqrt),
                   waits=[("VE", kfull_ve)], inc_name="AC")
        kinv_ve = emit("vector", lambda e: e.reciprocal(ar(3), ar(3)), waits=[("AC", ack)],
                       inc_name="VE")
        kc_ev = row_transpose8(ar(3), kinv_col, [("VE", kinv_ve), ("VE", qsq_ev)])
        bvk_ve = emit("vector", lambda e: e.tensor_tensor(bvk_col, bv_col, kinv_col, ALU.mult),
                      waits=[("VE", kc_ev), ("VE", v_bv)], inc_name="VE")

        def ev_kg(m, h, pt, lastmm):
            return emit("vector", lambda e, m=m, h=h, p=pt: e.tensor_scalar(
                big4[:, m * 1024 + h * 512: m * 1024 + (h + 1) * 512], p[:, :],
                kinv_col[:, m:m + 1], None, ALU.mult, ALU.bypass),
                waits=[("MM", lastmm)], inc_name="VE")
        gt_mm, kg_ev = mm1024(
            lambda k, m: big3[:, k * 1024 + m * 128: k * 1024 + (m + 1) * 128],
            lambda k, h: big2[:, k * 1024 + h * 512: k * 1024 + (h + 1) * 512],
            ev_kg, [("VE", bvk_ve), ("VE", qcs_ve)],
            extra_rank1=[
                (lambda m: arsl(5, m), lambda h: arh(0, h)),
                (lambda m: arsl(1, m), lambda h: arh(4, h)),
            ])

        kivg_ev = matvec_row(big4, 7, [("VE", kg_ev)], lambda k: ones_col)
        emit("vector", lambda e: e.scalar_tensor_tensor(ar(7), ar(4), EPS_AT, ar(7), ALU.mult, ALU.add),
             waits=[("VE", kivg_ev), ("VE", qinv_ve)])
        emit("vector", lambda e: e.tensor_tensor(ar(7), ar(7), ar(2), ALU.mult))
        emit("vector", lambda e: e.tensor_scalar(ar(7), ar(7), float(D), None, ALU.add, ALU.bypass))
        tail_ve = emit("vector", lambda e: e.reciprocal(ar(5), ar(7)), inc_name="VE")
        emit("vector", lambda e: e.tensor_tensor(ar(6), ar(2), ar(5), ALU.mult),
             waits=[("VE", tail_ve)])
        csc_ve2 = emit("vector", lambda e: e.tensor_scalar(
            ar(6), ar(6), gsc[0:1, 0:1], None, ALU.mult, ALU.bypass), inc_name="VE")
        emit("vector", lambda e: e.tensor_copy(arh(1, 0), junk[0:1, 0:512]),
             waits=[("VE", qsq_cp), ("MM", gt_mm)])
        qsqr_ve = emit("vector", lambda e: e.tensor_copy(arh(1, 1), w2st[0:1, 0:512]),
                       inc_name="VE")
        bv_ld = emit("sync", lambda e: e.dma_start(out=ar(0), in_=bv_d), waits=[("MM", gt_mm)],
                     inc_name="LD")
        bvs_ve = emit("vector", lambda e: e.reduce_sum(gsc[0:1, 1:2], ar(0), axis=AX.X),
                      waits=[("LD", bv_ld)], inc_name="VE")
        bvkg_ev = matvec_row(big4, 4, [("VE", kivg_ev), ("VE", tail_ve), ("VE", v_bv)], lambda k: bv_col[:, k:k + 1])
        emit("vector", lambda e: e.tensor_tensor(ar(4), ar(4), ar(6), ALU.mult),
             waits=[("VE", bvkg_ev), ("VE", csc_ve2), ("VE", bvs_ve)])
        emit("vector", lambda e: e.tensor_scalar(
            ar(0), ar(5), gsc[0:1, 1:2], gsc[0:1, 0:1], ALU.mult, ALU.mult))
        gcon_ve = emit("vector", lambda e: e.tensor_tensor(ar(0), ar(0), ar(4), ALU.add),
                       inc_name="VE")
        bc_mm = emit("tensor", lambda e: e.matmul(ps[4][:, :], ones_row, arh(6, 0),
                                                  start=True, stop=True),
                     waits=[("VE", gcon_ve), ("VE", qsqr_ve)], inc_name="MM")
        bc_mm = emit("tensor", lambda e: e.matmul(ps[5][:, :], ones_row, arh(6, 1),
                                                  start=True, stop=True), inc_name="MM")
        emit("vector", lambda e: e.tensor_copy(bcast[:, 0:512], ps[4][:, :]),
             waits=[("MM", bc_mm)])
        bc_ev = emit("vector", lambda e: e.tensor_copy(bcast[:, 512:1024], ps[5][:, :]),
                     inc_name="VE")
        wv_ld = 0
        for r in range(FC):
            wv_ld = emit("sync", lambda e, r=r: e.dma_start(
                out=big3[:, r * 1024:(r + 1) * 1024], in_=wv_d[r * 128:(r + 1) * 128, :]),
                waits=([("MM", gt_mm)] if r == 0 else []), inc_name="LD")
        wvs_ev = matvec_row(big3, 5, [("LD", wv_ld), ("VE", gcon_ve), ("VE", bc_ev)], lambda k: ones_col)
        def ev_d2(m, h, pt, lastmm):
            return emit("vector", lambda e, m=m, h=h, p=pt: e.tensor_tensor(
                big1[:, m * 1024 + h * 512: m * 1024 + (h + 1) * 512], p[:, :],
                bcast[:, h * 512:(h + 1) * 512], ALU.mult),
                waits=[("MM", lastmm)], inc_name="VE")
        d2_mm, d2_ev = mm1024(
            lambda k, m: big3[:, k * 1024 + m * 128: k * 1024 + (m + 1) * 128],
            lambda k, h: big4[:, k * 1024 + h * 512: k * 1024 + (h + 1) * 512],
            ev_d2, [("VE", wvs_ev)],
            extra_rank1=[(lambda m: arsl(5, m), lambda h: arh(1, h))])
        dg_ve = 0
        for r in range(FC):
            dg_ve = emit("vector", lambda e, r=r: e.tensor_tensor(
                big1[:, r * 1024 + r * 128: r * 1024 + (r + 1) * 128],
                big1[:, r * 1024 + r * 128: r * 1024 + (r + 1) * 128],
                ident[:, :], ALU.add),
                waits=([("VE", d2_ev)] if r == 0 else []), inc_name="VE")
        dsp = 0
        for r in range(FC):
            wls = [("VE", dg_ve), ("MM", d2_mm)] if r == 0 else []
            hq = emit("scalar", lambda e, r=r: e.activation(
                big4b[:, r * 2048:r * 2048 + 1024], big1[:, r * 1024:(r + 1) * 1024],
                AF.Identity), waits=wls, inc_name="AC")
            dsp = emit("vector", lambda e, r=r: e.tensor_tensor(
                big4b[:, r * 2048 + 1024:(r + 1) * 2048], big1[:, r * 1024:(r + 1) * 1024],
                big4b[:, r * 2048:r * 2048 + 1024], ALU.subtract),
                waits=[("AC", hq)], inc_name="VE")
        d2sp_ve = dsp
        b2_ld = emit("sync", lambda e: e.dma_start(out=ar(7), in_=b2_d),
                     waits=[("VE", tail_ve)], inc_name="LD")

        def x1sl(c, h=None):
            t = big2 if c < 8 else big3
            cc = c % 8
            if h is None:
                return t[:, cc * 1024:(cc + 1) * 1024]
            return t[:, cc * 1024 + h * 512: cc * 1024 + (h + 1) * 512]

        att_ve = [0] * NCH
        ln3_nm = [0] * NCH
        att_ev_ph = {}
        last_att_mm = {}
        mm_h = {}
        for c in range(NCH):
            w = []
            if c == 0:
                w = [("MM", d2_mm), ("SP", sp_done[NCH - 1])]
            if c >= 2:
                w.append(("MM", last_att_mm[c - 2]))
            xnt_ld = emit("sync", lambda e, c=c: e.dma_start(out=stB(c % 2), in_=xnt_d[c]),
                          waits=w, inc_name="LD")
            w2_ = []
            if c < 2:
                w2_.append(("AC", last_apply[c % 2]))
            else:
                w2_.append(("VE", att_ve[c - 2]))
            xr_ld = emit("sync", lambda e, c=c: e.dma_start(
                out=stC(c % 2), in_=x_d[c * 128:(c + 1) * 128, :]), waits=w2_, inc_name="LD")
            for h in range(2):
                pt = ps[(2 * c + h) % 4]
                w = [("LD", xnt_ld)]
                if c == 0 and h == 0:
                    w += [("VE", d2sp_ve)]
                if 2 * c + h >= 4:
                    w.append(("VE", att_ev_ph[(2 * c + h) - 4]))
                first = True
                for r in range(FC):
                    for pp in range(3):
                        lt = (stBb(c % 2)[:, r * 256:r * 256 + 128] if pp < 2
                              else stBb(c % 2)[:, r * 256 + 128:r * 256 + 256])
                        rt = (big4b[:, r * 2048 + h * 512: r * 2048 + (h + 1) * 512] if pp != 1
                              else big4b[:, r * 2048 + 1024 + h * 512: r * 2048 + 1024 + (h + 1) * 512])
                        mm = emit("tensor", lambda e, p=pt, lt=lt, rt=rt, st=first: e.matmul(
                            p[:, :], lt, rt, start=st, stop=False),
                            waits=(w if first else []), inc_name="MM")
                        first = False
                mm = emit("tensor", lambda e, p=pt, h=h: e.matmul(
                    p[:, :], ones_row, arh(0, h), start=False, stop=True), inc_name="MM")
                mm_h[h] = mm
            last_att_mm[c] = mm
            b = (c % 2) * 16
            sum0 = mcol[:, b + 0:b + 1]; sum1 = mcol[:, b + 1:b + 2]
            sq0 = mcol[:, b + 2:b + 3]; sq1 = mcol[:, b + 3:b + 4]
            mean_c = mcol[:, b + 4:b + 5]; var_c = mcol[:, b + 5:b + 6]
            std_c = mcol[:, b + 6:b + 7]; rstd_c = mcol[:, b + 7:b + 8]
            negmr_c = mcol[:, b + 8:b + 9]; msc = mcol[:, b + 9:b + 10]
            p0_, p1_ = ps[(2 * c) % 4], ps[(2 * c + 1) % 4]
            t5a = tmp56[:, (c % 2) * 1024:(c % 2) * 1024 + 512]
            t5b = tmp56[:, (c % 2) * 1024 + 512:(c % 2) * 1024 + 1024]
            att_ev_ph[2 * c] = emit("vector", lambda e, t=t5a, p=p0_: e.tensor_copy(t, p[:, :]),
                                    waits=[("MM", mm_h[0])], inc_name="VE")
            ev1 = emit("vector", lambda e, t=t5b, p=p1_: e.tensor_copy(t, p[:, :]),
                       waits=[("MM", mm_h[1])], inc_name="VE")
            att_ev_ph[2 * c + 1] = ev1
            emit("vector", lambda e, s=sum0, t=t5a: e.reduce_sum(s, t, axis=AX.X))
            emit("vector", lambda e, s=sum1, t=t5b: e.reduce_sum(s, t, axis=AX.X))
            emit("scalar", lambda e, s=sq0, t=t5a: e.activation(
                junk[:, :], t, AF.Square, accum_out=s), waits=[("VE", ev1)], inc_name="AC")
            a2 = emit("scalar", lambda e, s=sq1, t=t5b: e.activation(
                junk[:, :], t, AF.Square, accum_out=s), inc_name="AC")
            emit("vector", lambda e, a=sum0, b_=sum1, o=mean_c: e.tensor_tensor(o, a, b_, ALU.add),
                 waits=[("AC", a2)])
            emit("vector", lambda e, o=mean_c: e.tensor_scalar(o, o, 1.0 / D, None, ALU.mult,
                                                               ALU.bypass))
            emit("vector", lambda e, a=sq0, b_=sq1, o=var_c: e.tensor_tensor(o, a, b_, ALU.add))
            emit("vector", lambda e, o=var_c: e.tensor_scalar(o, o, 1.0 / D, None, ALU.mult,
                                                              ALU.bypass))
            emit("vector", lambda e, m=mean_c, o=msc: e.scalar_tensor_tensor(
                o, m, -1.0, m, ALU.mult, ALU.mult))
            v_var = emit("vector", lambda e, o=var_c, s=msc: e.tensor_tensor(o, o, s, ALU.add))
            a3 = emit("scalar", lambda e, v=var_c, o=std_c: e.activation(
                o, v, AF.Sqrt, bias=eps_col), waits=[("VE", v_var)], inc_name="AC")
            emit("vector", lambda e, o=rstd_c, i=std_c: e.reciprocal(o, i), waits=[("AC", a3)])
            nm_ve = emit("vector", lambda e, o=negmr_c, m=mean_c, r=rstd_c: e.scalar_tensor_tensor(
                o, m, -1.0, r, ALU.mult, ALU.mult), inc_name="VE")
            for h in range(2):
                th = tmp56[:, (c % 2) * 1024 + h * 512:(c % 2) * 1024 + (h + 1) * 512]
                emit("vector", lambda e, c=c, h=h, n=negmr_c: e.tensor_scalar(
                    x1sl(c, h), stC(c % 2)[:, h * 512:(h + 1) * 512], n, None,
                    ALU.add, ALU.bypass), waits=[("LD", xr_ld)])
                att_ve[c] = emit("vector", lambda e, c=c, h=h, t=th, r=rstd_c: e.scalar_tensor_tensor(
                    x1sl(c, h), t, r, x1sl(c, h), ALU.mult, ALU.add), inc_name="VE")

        for c in range(NCH):
            b = 64 + (c % 2) * 16
            sum_c = mcol[:, b + 0:b + 1]; sq0 = mcol[:, b + 1:b + 2]; sq1 = mcol[:, b + 2:b + 3]
            mean_c = mcol[:, b + 3:b + 4]; var_c = mcol[:, b + 5:b + 6]
            std_c = mcol[:, b + 6:b + 7]; msc = mcol[:, b + 9:b + 10]
            rstd_c = mcol[:, 96 + c:97 + c]; negmr_c = mcol[:, 112 + c:113 + c]
            w = [("VE", att_ve[c])]
            emit("vector", lambda e, s=sum_c, c=c: e.reduce_sum(s, x1sl(c), axis=AX.X), waits=w)
            emit("scalar", lambda e, s=sq0, c=c: e.activation(
                junk[:, :], x1sl(c, 0), AF.Square, accum_out=s), waits=w, inc_name="AC")
            a2 = emit("scalar", lambda e, s=sq1, c=c: e.activation(
                junk[:, :], x1sl(c, 1), AF.Square, accum_out=s), inc_name="AC")
            emit("vector", lambda e, m=mean_c, s=sum_c: e.tensor_scalar(
                m, s, 1.0 / D, None, ALU.mult, ALU.bypass), waits=[("AC", a2)])
            emit("vector", lambda e, q0=sq0, q1=sq1, o=var_c: e.tensor_tensor(o, q0, q1, ALU.add))
            emit("vector", lambda e, o=var_c: e.tensor_scalar(o, o, 1.0 / D, None, ALU.mult,
                                                              ALU.bypass))
            emit("vector", lambda e, m=mean_c, o=msc: e.scalar_tensor_tensor(
                o, m, -1.0, m, ALU.mult, ALU.mult))
            v_var = emit("vector", lambda e, o=var_c, s=msc: e.tensor_tensor(o, o, s, ALU.add))
            a3 = emit("scalar", lambda e, v=var_c, o=std_c: e.activation(
                o, v, AF.Sqrt, bias=eps_col), waits=[("VE", v_var)], inc_name="AC")
            emit("vector", lambda e, o=rstd_c, i=std_c: e.reciprocal(o, i), waits=[("AC", a3)])
            ln3_nm[c] = emit("vector", lambda e, o=negmr_c, m=mean_c, r=rstd_c: e.scalar_tensor_tensor(
                o, m, -1.0, r, ALU.mult, ALU.mult), inc_name="VE")

        def h1sl(mi, sl=None):
            t = big4 if mi < 16 else big1
            m = mi % 16
            if sl is None:
                return t[:, m * 512:(m + 1) * 512]
            return t[:, m * 512 + sl * 128: m * 512 + (sl + 1) * 128]

        st_done = [0] * NCH
        trv_prev = {0: 0, 1: 0}
        w1_mm_prev = {0: 0, 1: 0}
        w2_mm_prev = {0: 0, 1: 0}
        last_tr3 = {}
        st6 = {"last_tr_ve": 0}
        out_ve_prev = 0

        def do_ln3(g):
            for cc in range(CPG):
                c = g * CPG + cc
                rstd_c = mcol[:, 96 + c:97 + c]; negmr_c = mcol[:, 112 + c:113 + c]
                wa = [("VE", ln3_nm[c])]
                if g == 0 and cc < 2:
                    wa.append(("VE", att_ve[NCH - 2 + cc]))
                key = (g, cc - 2) if cc >= 2 else (g - 1, cc + 2)
                if key in last_tr3:
                    wa.append(("MM", last_tr3[key]))
                x1n_ac = emit("scalar", lambda e, c=c, cc=cc, r=rstd_c, n=negmr_c: e.activation(
                    stC(cc % 2), x1sl(c), AF.Identity, bias=n, scale=r), waits=wa, inc_name="AC")
                for r in range(FC):
                    pt = ps[6 + (r % 2)]
                    w = [("AC", x1n_ac), ("VE", trv_prev[r % 2])]
                    if g == 0 and cc == 0 and r < 2:
                        w.append(("VE", att_ve[NCH - 1]))
                    m = emit("tensor", lambda e, p=pt, cc=cc, r=r: e.transpose(
                        p[:, 0:128], stC(cc % 2)[:, r * 128:(r + 1) * 128], ident[:, :]),
                        waits=w, inc_name="MM")
                    emit("vector", lambda e, p=pt, cc=cc, r=r: e.tensor_copy(
                        x1nTb[:, r * 1024 + cc * 128: r * 1024 + (cc + 1) * 128], p[:, 0:128]),
                        waits=[("MM", m)], inc_name="VE")
                    tv = emit("vector", lambda e, p=pt, cc=cc, r=r: e.tensor_tensor(
                        x1nTb[:, r * 1024 + 512 + cc * 128: r * 1024 + 512 + (cc + 1) * 128],
                        p[:, 0:128],
                        x1nTb[:, r * 1024 + cc * 128: r * 1024 + (cc + 1) * 128],
                        ALU.subtract), inc_name="VE")
                    trv_prev[r % 2] = tv
                last_tr3[(g, cc)] = m
                st6["last_tr_ve"] = tv

        def do_ln4(g, cc, ffn2_done):
            c = g * CPG + cc
            b = (cc % 2) * 16
            sum0 = mcol[:, b + 0:b + 1]; sum1 = mcol[:, b + 1:b + 2]
            sq0 = mcol[:, b + 2:b + 3]; sq1 = mcol[:, b + 3:b + 4]
            mean_c = mcol[:, b + 4:b + 5]; var_c = mcol[:, b + 5:b + 6]
            std_c = mcol[:, b + 6:b + 7]; rstd_c = mcol[:, b + 7:b + 8]
            negmr_c = mcol[:, b + 8:b + 9]; msc = mcol[:, b + 9:b + 10]
            p0_, p1_ = ps[cc * 2], ps[cc * 2 + 1]
            t6a = bcast[:, 0:512]
            t6b = bcast[:, 512:1024]
            emit("vector", lambda e, t=t6a, p=p0_: e.tensor_copy(t, p[:, :]),
                 waits=[("MM", ffn2_done), ("AC", st6.get("ln4_a2", 0))])
            ev1 = emit("vector", lambda e, t=t6b, p=p1_: e.tensor_copy(t, p[:, :]),
                       inc_name="VE")
            emit("vector", lambda e, s=sum0, t=t6a: e.reduce_sum(s, t, axis=AX.X))
            emit("vector", lambda e, s=sum1, t=t6b: e.reduce_sum(s, t, axis=AX.X))
            emit("scalar", lambda e, s=sq0, t=t6a: e.activation(
                junk[:, :], t, AF.Square, accum_out=s), waits=[("VE", ev1)], inc_name="AC")
            a2 = emit("scalar", lambda e, s=sq1, t=t6b: e.activation(
                junk[:, :], t, AF.Square, accum_out=s), inc_name="AC")
            st6["ln4_a2"] = a2
            emit("vector", lambda e, a=sum0, b_=sum1, o=mean_c: e.tensor_tensor(
                o, a, b_, ALU.add), waits=[("AC", a2)])
            emit("vector", lambda e, o=mean_c: e.tensor_scalar(o, o, 1.0 / D, None, ALU.mult,
                                                               ALU.bypass))
            emit("vector", lambda e, a=sq0, b_=sq1, o=var_c: e.tensor_tensor(o, a, b_, ALU.add))
            emit("vector", lambda e, o=var_c: e.tensor_scalar(o, o, 1.0 / D, None, ALU.mult,
                                                              ALU.bypass))
            emit("vector", lambda e, m=mean_c, o=msc: e.scalar_tensor_tensor(
                o, m, -1.0, m, ALU.mult, ALU.mult))
            v_var = emit("vector", lambda e, o=var_c, s=msc: e.tensor_tensor(o, o, s, ALU.add))
            a3 = emit("scalar", lambda e, v=var_c, o=std_c: e.activation(
                o, v, AF.Sqrt, bias=eps_col), waits=[("VE", v_var)], inc_name="AC")
            emit("vector", lambda e, o=rstd_c, i=std_c: e.reciprocal(o, i), waits=[("AC", a3)])
            nm_ve = emit("vector", lambda e, o=negmr_c, m=mean_c, r=rstd_c: e.scalar_tensor_tensor(
                o, m, -1.0, r, ALU.mult, ALU.mult), inc_name="VE")
            for h in range(2):
                t6h = bcast[:, h * 512:(h + 1) * 512]
                emit("vector", lambda e, c=c, h=h, n=negmr_c: e.tensor_scalar(
                    x1sl(c, h), x1sl(c, h), n, None, ALU.add, ALU.bypass),
                    waits=[("VE", nm_ve)])
                ov = emit("vector", lambda e, c=c, h=h, t=t6h, r=rstd_c: e.scalar_tensor_tensor(
                    x1sl(c, h), t, r, x1sl(c, h), ALU.mult, ALU.add), inc_name="VE")
            st_done[c] = emit("gpsimd", lambda e, c=c: e.dma_start(
                out=y_d[c * 128:(c + 1) * 128, :], in_=x1sl(c)),
                waits=[("VE", ov)], inc_name="SP")
            return ev1, ov

        do_ln3(0)
        bank_free_next = {i: att_ev_ph[2 * NCH - 1] for i in range(8)}
        for g in range(NG):
            relu_ac = {}
            bank_free = dict(bank_free_next)
            last_tr_ve = st6["last_tr_ve"]
            for ig in range(4):
                for ka in range(FC):
                    w = [("MM", w1_mm_prev[ka % 2])]
                    if g == 0 and ig == 0 and ka == 0:
                        w.append(("SP", w1s_done))
                    qeng, qsem = ("sync", "LD") if ka % 2 == 0 else ("scalar", "L2")
                    ld = emit(qeng, lambda e, ig=ig, ka=ka: e.dma_start(
                        out=stDb(ka % 2)[:, 0:2048],
                        in_=w1s_d[ka * 128:(ka + 1) * 128, ig * 2048:(ig + 1) * 2048]),
                        waits=w, inc_name=qsem)
                    wmm0 = [(qsem, ld)]
                    if ka == 0:
                        wmm0.append(("VE", last_tr_ve))
                    for mi_l in range(8):
                        wmm = list(wmm0) if mi_l == 0 else []
                        if ka == 0 and bank_free[mi_l]:
                            wmm.append(("VE", bank_free[mi_l]))
                        for pp in range(3):
                            lt = (stDb(ka % 2)[:, mi_l * 128:(mi_l + 1) * 128] if pp < 2
                                  else stDb(ka % 2)[:, 1024 + mi_l * 128:1024 + (mi_l + 1) * 128])
                            rt = (x1nTb[:, ka * 1024:ka * 1024 + 512] if pp != 1
                                  else x1nTb[:, ka * 1024 + 512:ka * 1024 + 1024])
                            mm = emit("tensor", lambda e, p=ps[mi_l], lt=lt, rt=rt, ka=ka, pp=pp: e.matmul(
                                p[:, :], lt, rt,
                                start=(ka == 0 and pp == 0), stop=(ka == FC - 1 and pp == 2)),
                                waits=(wmm if pp == 0 else []), inc_name="MM")
                    w1_mm_prev[ka % 2] = mm
                for mi_l in range(8):
                    mi = ig * 8 + mi_l
                    hb = (big4b if mi < 16 else big1b)
                    hoff = (mi % 16) * 1024
                    rv = emit("vector", lambda e, p=ps[mi_l], mi=mi: e.tensor_scalar(
                        junk[:, 0:512], p[:, :], cols[:, 24 + mi:25 + mi], 0.0, ALU.add, ALU.max),
                        waits=([("MM", mm)] if mi_l == 0 else []), inc_name="VE")
                    hc = emit("scalar", lambda e, hb=hb, hoff=hoff: e.activation(
                        hb[:, hoff:hoff + 512], junk[:, 0:512], AF.Identity),
                        waits=[("VE", rv)], inc_name="AC")
                    ra = emit("vector", lambda e, hb=hb, hoff=hoff: e.tensor_tensor(
                        hb[:, hoff + 512:hoff + 1024], junk[:, 0:512], hb[:, hoff:hoff + 512],
                        ALU.subtract), waits=[("AC", hc)], inc_name="VE")
                    relu_ac[mi] = ra
                    bank_free[mi_l] = rv
                last_psum_free = ra
            for mi in range(32):
                w = [("MM", w2_mm_prev[mi % 2])]
                if g == 0 and mi < 2:
                    w.append(("VE", qsqr_ve))
                qeng, qsem = ("gpsimd", "SP") if mi % 2 == 0 else ("sync", "LD")
                ld = emit(qeng, lambda e, mi=mi: e.dma_start(
                    out=w2stb[:, (mi % 2) * 2048:(mi % 2 + 1) * 2048],
                    in_=w2s_d[mi * 128:(mi + 1) * 128, :]), waits=w, inc_name=qsem)
                wmm0 = [(qsem, ld), ("VE", relu_ac[mi])]
                if mi == 0:
                    wmm0.append(("VE", last_psum_free))
                    wmm0.append(("VE", out_ve_prev))
                hb = (big4b if mi < 16 else big1b)
                hoff = (mi % 16) * 1024
                for cc in range(CPG):
                    for pp in range(3):
                        lt_off = hoff + cc * 128 if pp < 2 else hoff + 512 + cc * 128
                        rbase = (mi % 2) * 2048 + (0 if pp != 1 else 1024)
                        for hh in range(2):
                            rt = w2stb[:, rbase + hh * 512:rbase + (hh + 1) * 512]
                            mm = emit("tensor", lambda e, p=pa[cc], hb=hb, lo=lt_off, rt=rt, hh=hh,
                                      mi=mi, pp=pp: e.matmul(
                                p[:, hh * 512:(hh + 1) * 512], hb[:, lo:lo + 128], rt,
                                start=(mi == 0 and pp == 0), stop=False),
                                waits=(wmm0 if (cc == 0 and pp == 0 and hh == 0) else []),
                                inc_name="MM")
                w2_mm_prev[mi % 2] = mm
            for cc in range(CPG):
                for h in range(2):
                    mm = emit("tensor", lambda e, p=ps[cc * 2 + h], h=h: e.matmul(
                        p[:, :], ones_row, arh(7, h), start=False, stop=True),
                        waits=([("LD", b2_ld)] if (g == 0 and cc == 0 and h == 0) else []),
                        inc_name="MM")
            ffn2_done = mm
            ev3, ov = do_ln4(g, 3, ffn2_done)
            evs = {3: ev3}
            if g + 1 < NG:
                do_ln3(g + 1)
            for cc in (0, 1, 2):
                evs[cc], ov = do_ln4(g, cc, ffn2_done)
            out_ve_prev = ov
            bank_free_next = {}
            for cc in range(4):
                bank_free_next[2 * cc] = evs[cc]
                bank_free_next[2 * cc + 1] = evs[cc]

        emit("gpsimd", lambda e: e.nop(), waits=[("SP", st_done[NCH - 1])])

        totals = {k: v[2] for k, v in cnt.items()}
        for (w, v) in waits_seen:
            assert v <= totals[w], f"wait {w} {v} > total {totals[w]}"

        @blk.sync
        def _(eng):
            for f in E["sync"]:
                f(eng)

        @blk.tensor
        def _(eng):
            for f in E["tensor"]:
                f(eng)

        @blk.vector
        def _(eng):
            for f in E["vector"]:
                f(eng)

        @blk.scalar
        def _(eng):
            for f in E["scalar"]:
                f(eng)

        @blk.gpsimd
        def _(eng):
            for f in E["gpsimd"]:
                f(eng)

    return nc


_NC_CACHE = None


def kernel(L0_attn_x, L0_attn_Wq, L0_attn_bq, L0_attn_Wk, L0_attn_bk,
           L0_attn_Wv, L0_attn_bv, L0_attn_gamma,
           L0_ffn_W1, L0_ffn_b1, L0_ffn_W2, L0_ffn_b2):
    global _NC_CACHE
    x = np.ascontiguousarray(np.asarray(L0_attn_x, dtype=np.float32).reshape(16384, 1024))
    com = dict(
        wq=np.ascontiguousarray(np.asarray(L0_attn_Wq, dtype=np.float32)),
        wk=np.ascontiguousarray(np.asarray(L0_attn_Wk, dtype=np.float32)),
        wv=np.ascontiguousarray(np.asarray(L0_attn_Wv, dtype=np.float32)),
        bq=np.asarray(L0_attn_bq, dtype=np.float32).reshape(1, 1024),
        bk=np.asarray(L0_attn_bk, dtype=np.float32).reshape(1, 1024),
        bv_row=np.asarray(L0_attn_bv, dtype=np.float32).reshape(1, 1024),
        bv8=np.ascontiguousarray(np.asarray(L0_attn_bv, dtype=np.float32).reshape(8, 128)),
        gamma=np.asarray(L0_attn_gamma, dtype=np.float32).reshape(1, 1),
        w1=np.ascontiguousarray(np.asarray(L0_ffn_W1, dtype=np.float32)),
        b1_32=np.ascontiguousarray(np.asarray(L0_ffn_b1, dtype=np.float32).reshape(32, 128)),
        w2=np.ascontiguousarray(np.asarray(L0_ffn_W2, dtype=np.float32)),
        b2=np.asarray(L0_ffn_b2, dtype=np.float32).reshape(1, 1024),
        ident=np.eye(128, dtype=np.float32),
    )
    if _NC_CACHE is None:
        _NC_CACHE = build_core()
    nc = _NC_CACHE
    in_maps = []
    for i in range(8):
        m = dict(com)
        m["x"] = np.ascontiguousarray(x[i * TOK:(i + 1) * TOK, :])
        in_maps.append(m)
    res = run_bass_kernel_spmd(nc, in_maps, core_ids=list(range(8)))
    out = np.concatenate([res.results[i]["y"] for i in range(8)], axis=0)
    return out.reshape(1, 16384, 1024).astype(np.float32)


# version 10
# speedup vs baseline: 1.1204x; 1.0136x over previous
import numpy as np
import concourse.bass as bass
import concourse.mybir as mybir
from concourse.bass_utils import run_bass_kernel_spmd

F32 = mybir.dt.float32
BF16 = mybir.dt.bfloat16
AF = mybir.ActivationFunctionType
ALU = mybir.AluOpType
AX = mybir.AxisListType

TOK = 2048
NCH = 16
D = 1024
FC = 8
INTER = 4096
NG = 4
CPG = 4
CTOT = 16384.0
EPS_LN = 1e-5
EPS_AT = 1e-6


def build_core():
    nc = bass.Bass("TRN2", target_bir_lowering=False)
    dt = lambda n, s: nc.dram_tensor(n, s, F32, kind="ExternalInput").ap()
    x_d = dt("x", [TOK, D])
    wq_d = dt("wq", [D, D]); wk_d = dt("wk", [D, D]); wv_d = dt("wv", [D, D])
    bq_d = dt("bq", [1, D]); bk_d = dt("bk", [1, D])
    bv_d = dt("bv_row", [1, D]); bv8_d = dt("bv8", [8, 128])
    gam_d = dt("gamma", [1, 1])
    w1_d = dt("w1", [D, INTER]); b1_d = dt("b1_32", [32, 128])
    w2_d = dt("w2", [INTER, D]); b2_d = dt("b2", [1, D])
    id_d = dt("ident", [128, 128])
    y_d = nc.dram_tensor("y", [TOK, D], F32, kind="ExternalOutput").ap()
    ccin_h = nc.dram_tensor("ccin", [1025, 1024], F32)
    ccout_h = nc.dram_tensor("ccout", [1025, 1024], F32, addr_space="Shared")
    ccin, ccout = ccin_h.ap(), ccout_h.ap()
    xnt_d = nc.dram_tensor("xnt", [NCH, 128, D], F32).ap()
    w1s_d = nc.dram_tensor("w1s", [D, 2 * INTER], BF16).ap()
    w2s_d = nc.dram_tensor("w2s", [INTER, 2 * D], BF16).ap()

    import contextlib
    with contextlib.ExitStack() as stack:
        en = stack.enter_context
        big1 = en(nc.sbuf_tensor("big1", [128, 8192], F32))
        big2 = en(nc.sbuf_tensor("big2", [128, 8192], F32))
        big3 = en(nc.sbuf_tensor("big3", [128, 8192], F32))
        big4 = en(nc.sbuf_tensor("big4", [128, 8192], F32))
        util = en(nc.sbuf_tensor("util", [128, 8192], F32))
        w2st = en(nc.sbuf_tensor("w2st", [128, 2048], F32))
        arena = en(nc.sbuf_tensor("arena", [128, 8192], F32))
        cols = en(nc.sbuf_tensor("cols", [128, 64], F32))
        mcol = en(nc.sbuf_tensor("mcol", [128, 128], F32))
        bcast = en(nc.sbuf_tensor("bcast", [128, 1024], F32))
        ident = en(nc.sbuf_tensor("ident_sb", [128, 128], F32))
        ones = en(nc.sbuf_tensor("ones", [128, 128], F32))
        gsc = en(nc.sbuf_tensor("gsc", [128, 8], F32))
        junk = en(nc.sbuf_tensor("junk", [128, 512], F32))
        pa = [en(nc.psum_tensor(f"pa{i}", [128, 1024], F32)) for i in range(4)]
        ps = [pa[i // 2][:, (i % 2) * 512:(i % 2 + 1) * 512] for i in range(8)]
        ps4, ps5, ps6, ps7 = ps[4], ps[5], ps[6], ps[7]
        sLD = en(nc.semaphore("sLD"))
        sSP = en(nc.semaphore("sSP"))
        sMM = en(nc.semaphore("sMM"))
        sVE = en(nc.semaphore("sVE"))
        sAC = en(nc.semaphore("sAC"))
        sCC = en(nc.semaphore("sCC"))
        sL2 = en(nc.semaphore("sL2"))
        blk = en(nc.Block())
        stA = lambda b: util[:, (0 + b) * 1024:(1 + b) * 1024]
        stB = lambda b: util[:, (2 + b) * 1024:(3 + b) * 1024]
        stC = lambda b: util[:, (4 + b) * 1024:(5 + b) * 1024]
        stD = lambda b: util[:, (6 + b) * 1024:(7 + b) * 1024]
        x1nT = util[:, 0:4096]
        tmp56 = util[:, 0:2048]
        big1b = big1.bitcast(BF16)
        big2b = big2.bitcast(BF16)
        big4b = big4.bitcast(BF16)
        utilb = util.bitcast(BF16)
        w2stb = w2st.bitcast(BF16)
        x1nTb = utilb[:, 0:8192]
        stBb = lambda b: utilb[:, (2 + b) * 2048:(3 + b) * 2048]
        stDb = lambda b: utilb[:, (6 + b) * 2048:(7 + b) * 2048]
        ar = lambda i: arena[0:1, i * 1024:(i + 1) * 1024]
        arh = lambda i, h: arena[0:1, i * 1024 + h * 512: i * 1024 + (h + 1) * 512]
        arsl = lambda i, m: arena[0:1, i * 1024 + m * 128: i * 1024 + (m + 1) * 128]
        cs_col = cols[:, 0:8]; kinv_col = cols[:, 8:16]
        bv_col = cols[:, 16:24]; bvk_col = cols[:, 56:64]
        ones_col = ones[:, 0:1]; ones_row = ones[0:1, 0:128]

        E = {k: [] for k in ("sync", "tensor", "vector", "scalar", "gpsimd")}
        cnt = {"LD": [sLD, 16, 0], "SP": [sSP, 16, 0], "MM": [sMM, 1, 0],
               "VE": [sVE, 1, 0], "AC": [sAC, 1, 0], "CC": [sCC, 1, 0],
               "L2": [sL2, 16, 0]}
        waits_seen = []

        def emit(eng, f, waits=(), inc_name=None):
            waits = list(waits)
            auto = {"vector": "VE", "scalar": "AC"}.get(eng)
            if auto is not None:
                if inc_name is None:
                    inc_name = auto
                waits.append((auto, cnt[auto][2]))
            if inc_name in ("LD", "SP", "L2"):
                waits.append((inc_name, cnt[inc_name][2]))
            wl = [(cnt[w][0], v, w) for (w, v) in waits if v]
            for (_, v, w) in wl:
                waits_seen.append((w, v))
            step = 0
            sem = None
            if inc_name:
                c = cnt[inc_name]
                c[2] += c[1]
                step = c[1]
                sem = c[0]
            tgt = cnt[inc_name][2] if inc_name else None

            def g(eng_obj, f=f, wl=wl, sem=sem, step=step):
                for (s, v, _) in wl:
                    eng_obj.wait_ge(s, v)
                ins = f(eng_obj)
                if sem is not None:
                    ins.then_inc(sem, step)
            E[eng].append(g)
            return tgt

        emit("vector", lambda e: e.memset(ones[:, :], 1.0))
        ve_init = emit("vector", lambda e: e.memset(gsc[:, 2:3], EPS_LN), inc_name="VE")
        eps_col = gsc[:, 2:3]
        ld_ident = emit("sync", lambda e: e.dma_start(out=ident[:, :], in_=id_d), inc_name="LD")
        ld_bq = emit("sync", lambda e: e.dma_start(out=ar(0), in_=bq_d), inc_name="LD")
        ld_bk = emit("sync", lambda e: e.dma_start(out=ar(1), in_=bk_d), inc_name="LD")
        ld_gam = emit("sync", lambda e: e.dma_start(out=gsc[0:1, 0:1], in_=gam_d), inc_name="LD")
        ld_bv8 = emit("sync", lambda e: e.dma_start(out=big2[0:8, 0:128], in_=bv8_d), inc_name="LD")
        ld_b132 = emit("sync", lambda e: e.dma_start(out=big2[0:32, 128:256], in_=b1_d), inc_name="LD")
        t_bv = emit("tensor", lambda e: e.transpose(ps6[:, 0:8], big2[0:8, 0:128], ident[0:8, 0:8]),
                    waits=[("LD", ld_bv8), ("LD", ld_ident)], inc_name="MM")
        t_b1 = emit("tensor", lambda e: e.transpose(ps7[:, 0:32], big2[0:32, 128:256], ident[0:32, 0:32]),
                    waits=[("LD", ld_b132)], inc_name="MM")
        v_bv = emit("vector", lambda e: e.tensor_copy(bv_col, ps6[:, 0:8]), waits=[("MM", t_bv)],
                    inc_name="VE")
        v_b1 = emit("vector", lambda e: e.tensor_copy(cols[:, 24:56], ps7[:, 0:32]),
                    waits=[("MM", t_b1)], inc_name="VE")

        x_ld = [0] * NCH
        xn_ac = [0] * NCH
        sp_done = [0] * NCH
        ev_done = [0] * NCH
        cs_mm_c = [0] * NCH
        tr_ev = {0: v_bv, 1: v_b1}
        w2_in = {0: 0, 1: 0}; w2_lsub = {0: 0, 1: 0}; w2_out = {0: 0, 1: 0}
        for c in range(NCH):
            w = [("AC", xn_ac[c - 2])] if c >= 2 else []
            x_ld[c] = emit("sync", lambda e, c=c: e.dma_start(
                out=stC(c % 2), in_=x_d[c * 128:(c + 1) * 128, :]), waits=w, inc_name="LD")
            b = (c % 2) * 16
            sum_c = mcol[:, b + 0:b + 1]; sq0 = mcol[:, b + 1:b + 2]; sq1 = mcol[:, b + 2:b + 3]
            mean_c = mcol[:, b + 3:b + 4]; msq = mcol[:, b + 4:b + 5]; var_c = mcol[:, b + 5:b + 6]
            std_c = mcol[:, b + 6:b + 7]
            rstd_c = mcol[:, 32 + c:33 + c]
            negmr_c = mcol[:, 48 + c:49 + c]
            emit("vector", lambda e, s=sum_c, c=c: e.reduce_sum(s, stC(c % 2), axis=AX.X),
                 waits=[("LD", x_ld[c])])
            emit("scalar", lambda e, s=sq0, c=c: e.activation(
                junk[:, :], stC(c % 2)[:, 0:512], AF.Square, accum_out=s),
                waits=[("LD", x_ld[c])], inc_name="AC")
            ac2 = emit("scalar", lambda e, s=sq1, c=c: e.activation(
                junk[:, :], stC(c % 2)[:, 512:1024], AF.Square, accum_out=s), inc_name="AC")
            emit("vector", lambda e, m=mean_c, s=sum_c: e.tensor_scalar(
                m, s, 1.0 / D, None, ALU.mult, ALU.bypass), waits=[("AC", ac2)])
            emit("vector", lambda e, q=sq0, q1=sq1, o=msq: e.tensor_tensor(o, q, q1, ALU.add))
            emit("vector", lambda e, m=mean_c, o=var_c: e.scalar_tensor_tensor(
                o, m, -1.0, m, ALU.mult, ALU.mult))
            v_var = emit("vector", lambda e, q=msq, o=var_c: e.scalar_tensor_tensor(
                o, q, 1.0 / D, o, ALU.mult, ALU.add))
            ac3 = emit("scalar", lambda e, v=var_c, o=std_c: e.activation(
                o, v, AF.Sqrt, bias=eps_col),
                waits=([("VE", ve_init), ("VE", v_var)] if c == 0 else [("VE", v_var)]),
                inc_name="AC")
            emit("vector", lambda e, o=rstd_c, i=std_c: e.reciprocal(o, i), waits=[("AC", ac3)])
            st_ve = emit("vector", lambda e, o=negmr_c, m=mean_c, r=rstd_c: e.scalar_tensor_tensor(
                o, m, -1.0, r, ALU.mult, ALU.mult), inc_name="VE")
            w = [("VE", st_ve)]
            if c >= 2:
                w.append(("MM", cs_mm_c[c - 2]))
            xn_ac[c] = emit("scalar", lambda e, c=c, r=rstd_c, n=negmr_c: e.activation(
                stA(c % 2), stC(c % 2), AF.Identity, bias=n, scale=r), waits=w, inc_name="AC")
            for r in range(FC):
                pt = ps[6 + (r % 2)]
                w = [("AC", xn_ac[c]), ("VE", tr_ev[r % 2])]
                m = emit("tensor", lambda e, p=pt, c=c, r=r: e.transpose(
                    p[:, 0:128], stA(c % 2)[:, r * 128:(r + 1) * 128], ident[:, :]),
                    waits=w, inc_name="MM")
                wv = [("MM", m)]
                if r == 0 and c >= 2:
                    wv.append(("SP", sp_done[c - 2]))
                emit("vector", lambda e, p=pt, c=c, r=r: e.tensor_copy(
                    stBb(c % 2)[:, r * 256:r * 256 + 128], p[:, 0:128]), waits=wv, inc_name="VE")
                ev = emit("vector", lambda e, p=pt, c=c, r=r: e.tensor_tensor(
                    stBb(c % 2)[:, r * 256 + 128:r * 256 + 256], p[:, 0:128],
                    stBb(c % 2)[:, r * 256:r * 256 + 128], ALU.subtract), inc_name="VE")
                tr_ev[r % 2] = ev
            ev_done[c] = ev
            emit("tensor", lambda e, c=c: e.matmul(
                ps4[0:1, :], ones_col, stA(c % 2)[:, 0:512],
                start=(c == 0), stop=(c == NCH - 1)), inc_name="MM")
            cs_mm_c[c] = emit("tensor", lambda e, c=c: e.matmul(
                ps5[0:1, :], ones_col, stA(c % 2)[:, 512:1024],
                start=(c == 0), stop=(c == NCH - 1)), inc_name="MM")
            sp_done[c] = emit("gpsimd", lambda e, c=c: e.dma_start(
                out=xnt_d[c], in_=stB(c % 2)), waits=[("VE", ev_done[c])], inc_name="SP")
            for j in (2 * c, 2 * c + 1):
                buf = j % 2
                srcw = big2[:, buf * 1024:(buf + 1) * 1024]
                dstw = big2b[:, 8192 + buf * 2048:8192 + (buf + 1) * 2048]
                wlw = [("VE", w2_lsub[buf])] if w2_lsub[buf] else [("MM", t_b1)]
                w2_in[buf] = emit("sync", lambda e, j=j, s=srcw: e.dma_start(
                    out=s, in_=w2_d[j * 128:(j + 1) * 128, :]), waits=wlw, inc_name="LD")
                wlh = [("LD", w2_in[buf])]
                if w2_out[buf]:
                    wlh.append(("SP", w2_out[buf]))
                hcp = emit("scalar", lambda e, s=srcw, d=dstw: e.activation(
                    d[:, 0:1024], s, AF.Identity), waits=wlh, inc_name="AC")
                w2_lsub[buf] = emit("vector", lambda e, s=srcw, d=dstw: e.tensor_tensor(
                    d[:, 1024:2048], s, d[:, 0:1024], ALU.subtract), waits=[("AC", hcp)],
                    inc_name="VE")
                w2_out[buf] = emit("gpsimd", lambda e, j=j, d=dstw: e.dma_start(
                    out=w2s_d[j * 128:(j + 1) * 128, :], in_=d),
                    waits=[("VE", w2_lsub[buf])], inc_name="SP")

        w2s_done = w2_out[1]
        emit("vector", lambda e: e.tensor_copy(arh(6, 0), ps4[0:1, :]),
             waits=[("MM", cs_mm_c[NCH - 1])])
        cs_ve = emit("vector", lambda e: e.tensor_copy(arh(6, 1), ps5[0:1, :]), inc_name="VE")

        s_evacs = []
        last_apply = {0: xn_ac[NCH - 2], 1: xn_ac[NCH - 1]}
        last_s_mm = {0: 0, 1: 0}
        split_free = {0: cs_mm_c[NCH - 2], 1: cs_mm_c[NCH - 1]}
        split_kind = {0: "MM", 1: "MM"}
        stB_free = {0: sp_done[NCH - 2], 1: sp_done[NCH - 1]}
        w1_in = 0; w1_lsub = 0; w1_out = 0
        for p_i in range(2):
            for c in range(NCH):
                lx = emit("sync", lambda e, c=c: e.dma_start(
                    out=stC(c % 2), in_=x_d[c * 128:(c + 1) * 128, :]),
                    waits=[("AC", last_apply[c % 2])], inc_name="LD")
                aply = emit("scalar", lambda e, c=c: e.activation(
                    stA(c % 2), stC(c % 2), AF.Identity,
                    bias=mcol[:, 48 + c:49 + c], scale=mcol[:, 32 + c:33 + c]),
                    waits=[("LD", lx), (split_kind[c % 2], split_free[c % 2])], inc_name="AC")
                last_apply[c % 2] = aply
                wsp = [("AC", aply), ("SP", stB_free[c % 2])]
                if last_s_mm[c % 2]:
                    wsp.append(("MM", last_s_mm[c % 2]))
                hcp = emit("scalar", lambda e, c=c: e.activation(
                    stBb(c % 2)[:, 0:1024], stA(c % 2), AF.Identity), waits=wsp, inc_name="AC")
                lsub = emit("vector", lambda e, c=c: e.tensor_tensor(
                    stBb(c % 2)[:, 1024:2048], stA(c % 2), stBb(c % 2)[:, 0:1024],
                    ALU.subtract), waits=[("AC", hcp)], inc_name="VE")
                split_free[c % 2] = lsub
                split_kind[c % 2] = "VE"
                stB_free[c % 2] = 0
                wmm = [("VE", lsub)]
                if c == 0 and p_i == 0:
                    wmm += [("VE", cs_ve), ("VE", ev_done[NCH - 1])]
                if c == 0 and p_i == 1:
                    wmm.append(("VE", s_evacs[-1]))
                for m in range(4):
                    mg = m + 4 * p_i
                    for pp in range(3):
                        lt = (stBb(c % 2)[:, mg * 128:(mg + 1) * 128] if pp < 2
                              else stBb(c % 2)[:, 1024 + mg * 128:1024 + (mg + 1) * 128])
                        rbase = 0 if pp != 1 else 1024
                        for hh in range(2):
                            rt = stBb(c % 2)[:, rbase + hh * 512:rbase + (hh + 1) * 512]
                            mm = emit("tensor", lambda e, m=m, lt=lt, rt=rt, hh=hh, c=c, pp=pp: e.matmul(
                                pa[m][:, hh * 512:(hh + 1) * 512], lt, rt,
                                start=(c == 0 and pp == 0), stop=(c == NCH - 1 and pp == 2)),
                                waits=(wmm if (m == 0 and pp == 0 and hh == 0) else []),
                                inc_name="MM")
                last_s_mm[c % 2] = mm
                if c % 4 == 0:
                    k = p_i * 4 + c // 4
                    wlw = []
                    if k == 0:
                        wlw = [("VE", w2_lsub[1]), ("SP", w2_out[1]), ("SP", w2_out[0])]
                    elif w1_lsub:
                        wlw = [("VE", w1_lsub), ("SP", w1_out)]
                    w1_in = emit("sync", lambda e, k=k: e.dma_start(
                        out=big2[:, 0:4096], in_=w1_d[k * 128:(k + 1) * 128, :]),
                        waits=wlw, inc_name="LD")
                    for q in range(4):
                        wlh = [("LD", w1_in)] if q == 0 else []
                        if q == 0 and w1_out:
                            wlh.append(("SP", w1_out))
                        hq = emit("scalar", lambda e, q=q: e.activation(
                            big2b[:, 8192 + q * 2048:8192 + q * 2048 + 1024],
                            big2[:, q * 1024:(q + 1) * 1024], AF.Identity),
                            waits=wlh, inc_name="AC")
                        w1_lsub = emit("vector", lambda e, q=q: e.tensor_tensor(
                            big2b[:, 8192 + q * 2048 + 1024:8192 + (q + 1) * 2048],
                            big2[:, q * 1024:(q + 1) * 1024],
                            big2b[:, 8192 + q * 2048:8192 + q * 2048 + 1024],
                            ALU.subtract), waits=[("AC", hq)], inc_name="VE")
                    w1_out = emit("gpsimd", lambda e, k=k: e.dma_start(
                        out=w1s_d[k * 128:(k + 1) * 128, :], in_=big2b[:, 8192:16384]),
                        waits=[("VE", w1_lsub)], inc_name="SP")
            for m in range(4):
                mg = m + 4 * p_i
                sev = emit("vector", lambda e, m=m, mg=mg: e.tensor_copy(
                    big1[:, mg * 1024:(mg + 1) * 1024], pa[m][:, :]),
                    waits=([("MM", mm)] if m == 0 else []), inc_name="VE")
                s_evacs.append(sev)
        w1s_done = w1_out

        spx = 0
        for r in range(FC):
            spx = emit("gpsimd", lambda e, r=r: e.dma_start(
                out=ccin[r * 128:(r + 1) * 128, :], in_=big1[:, r * 1024:(r + 1) * 1024]),
                waits=([("VE", s_evacs[-1]), ("VE", cs_ve)] if r == 0 else []), inc_name="SP")
        spx = emit("gpsimd", lambda e: e.dma_start(out=ccin[1024:1025, :], in_=ar(6)),
                   inc_name="SP")
        cc_t = emit("gpsimd", lambda e: e.collective_compute(
            "AllReduce", ALU.add, replica_groups=[list(range(8))],
            ins=[ccin_h.ap().opt()], outs=[ccout_h.ap().opt()]),
            waits=[("SP", spx)], inc_name="CC")

        def transpose_1024(src, dst, wait_first):
            evs = []
            for rr in range(FC):
                for cc2 in range(FC):
                    i = rr * FC + cc2
                    pt = ps[i % 4]
                    w = list(wait_first) if i == 0 else []
                    if i >= 4:
                        w.append(("VE", evs[i - 4]))
                    m = emit("tensor", lambda e, p=pt, rr=rr, cc2=cc2, src=src: e.transpose(
                        p[:, 0:128], src[:, rr * 1024 + cc2 * 128: rr * 1024 + (cc2 + 1) * 128],
                        ident[:, :]), waits=w, inc_name="MM")
                    ev = emit("vector", lambda e, p=pt, rr=rr, cc2=cc2, dst=dst: e.tensor_copy(
                        dst[:, cc2 * 1024 + rr * 128: cc2 * 1024 + (rr + 1) * 128], p[:, 0:128]),
                        waits=[("MM", m)], inc_name="VE")
                    evs.append(ev)
            return m, ev

        def mm1024(lhsT_fn, rhs_fn, evac_fn, wait_first, extra_rank1=()):
            evs = {}
            lastmm = 0
            for m in range(FC):
                for h in range(2):
                    i = m * 2 + h
                    pt = ps[i % 4]
                    w = list(wait_first) if i == 0 else []
                    if i >= 4:
                        w.append(("VE", evs[i - 4]))
                    nr = len(extra_rank1)
                    for k in range(FC):
                        lastmm = emit("tensor", lambda e, p=pt, k=k, m=m, h=h: e.matmul(
                            p[:, :], lhsT_fn(k, m), rhs_fn(k, h),
                            start=(k == 0), stop=(k == FC - 1 and nr == 0)),
                            waits=(w if k == 0 else []), inc_name="MM")
                    for j, (lf, rf) in enumerate(extra_rank1):
                        lastmm = emit("tensor", lambda e, p=pt, lf=lf, rf=rf, m=m, h=h,
                                      st=(j == nr - 1): e.matmul(
                            p[:, :], lf(m), rf(h), start=False, stop=st), inc_name="MM")
                    evs[i] = evac_fn(m, h, pt, lastmm)
            return lastmm, evs[15]

        def ev_to(dst):
            def f(m, h, pt, lastmm):
                return emit("vector", lambda e, m=m, h=h, p=pt: e.tensor_copy(
                    dst[:, m * 1024 + h * 512: m * 1024 + (h + 1) * 512], p[:, :]),
                    waits=[("MM", lastmm)], inc_name="VE")
            return f

        def matvec_row(rhs_src, dst_slot, wait_first, lhs_fn):
            last_ev = 0
            for h in range(2):
                for k in range(FC):
                    m = emit("tensor", lambda e, k=k, h=h: e.matmul(
                        ps[4 + h][0:1, :], lhs_fn(k),
                        rhs_src[:, k * 1024 + h * 512: k * 1024 + (h + 1) * 512],
                        start=(k == 0), stop=(k == FC - 1)),
                        waits=(list(wait_first) if (h == 0 and k == 0) else []), inc_name="MM")
                last_ev = emit("vector", lambda e, h=h, m=m, d=dst_slot: e.tensor_copy(
                    arh(d, h), ps[4 + h][0:1, :]), waits=[("MM", m)], inc_name="VE")
            return last_ev

        def row_transpose8(src_row, dst_col8, wait_first):
            last = 0
            for k in range(FC):
                w = list(wait_first) if k == 0 else [("VE", last)]
                m = emit("tensor", lambda e, k=k, s=src_row: e.transpose(
                    ps[4][:, 0:1], s[0:1, k * 128:(k + 1) * 128], ident[0:1, 0:1]),
                    waits=w, inc_name="MM")
                last = emit("vector", lambda e, k=k, m=m, d=dst_col8: e.tensor_copy(
                    d[:, k:k + 1], ps[4][:, 0:1]), waits=[("MM", m)], inc_name="VE")
            return last


        wq_ld = 0
        for r in range(FC):
            wq_ld = emit("sync", lambda e, r=r: e.dma_start(
                out=big3[:, r * 1024:(r + 1) * 1024], in_=wq_d[r * 128:(r + 1) * 128, :]),
                inc_name="LD")
        wqt_mm, wqt_ev = transpose_1024(big3, big2, [
            ("LD", wq_ld), ("VE", s_evacs[-1]), ("SP", w1s_done), ("VE", w1_lsub)])
        wk_ld = 0
        for r in range(FC):
            wk_ld = emit("sync", lambda e, r=r: e.dma_start(
                out=big3[:, r * 1024:(r + 1) * 1024], in_=wk_d[r * 128:(r + 1) * 128, :]),
                waits=([("MM", wqt_mm)] if r == 0 else []), inc_name="LD")
        _, wkt_ev = transpose_1024(big3, big4, [("LD", wk_ld), ("VE", wqt_ev)])
        for r in range(FC):
            emit("sync", lambda e, r=r: e.dma_start(
                out=big1[:, r * 1024:(r + 1) * 1024], in_=ccout[r * 128:(r + 1) * 128, :]),
                waits=([("CC", cc_t)] if r == 0 else []), inc_name="LD")
        gl_ld = emit("sync", lambda e: e.dma_start(out=ar(6), in_=ccout[1024:1025, :]),
                     inc_name="LD")
        csc_ev = row_transpose8(ar(6), cs_col, [("LD", gl_ld), ("VE", wkt_ev)])
        swqt_mm, swqt_ev = mm1024(
            lambda k, m: big1[:, k * 1024 + m * 128: k * 1024 + (m + 1) * 128],
            lambda k, h: big2[:, k * 1024 + h * 512: k * 1024 + (h + 1) * 512],
            ev_to(big3), [("LD", gl_ld), ("VE", wkt_ev)])
        wqcs_ev = matvec_row(big2, 4, [("VE", csc_ev), ("VE", swqt_ev)], lambda k: cs_col[:, k:k + 1])
        pq_ev = 0
        for r in range(FC):
            pq_ev = emit("vector", lambda e, r=r: e.tensor_tensor(
                big3[:, r * 1024:(r + 1) * 1024], big3[:, r * 1024:(r + 1) * 1024],
                big2[:, r * 1024:(r + 1) * 1024], ALU.mult),
                waits=([("VE", swqt_ev)] if r == 0 else []), inc_name="VE")
        qsq_ev = matvec_row(big3, 2, [("VE", pq_ev), ("VE", wqcs_ev)], lambda k: ones_col)
        swkt_mm, swkt_ev = mm1024(
            lambda k, m: big1[:, k * 1024 + m * 128: k * 1024 + (m + 1) * 128],
            lambda k, h: big4[:, k * 1024 + h * 512: k * 1024 + (h + 1) * 512],
            ev_to(big3), [("VE", qsq_ev)])
        wkcs_ev = matvec_row(big4, 5, [("VE", swkt_ev)], lambda k: cs_col[:, k:k + 1])
        pk_ev = 0
        for r in range(FC):
            pk_ev = emit("vector", lambda e, r=r: e.tensor_tensor(
                big4[:, r * 1024:(r + 1) * 1024], big3[:, r * 1024:(r + 1) * 1024],
                big4[:, r * 1024:(r + 1) * 1024], ALU.mult),
                waits=([("VE", wkcs_ev)] if r == 0 else []), inc_name="VE")
        ksq_ev = matvec_row(big4, 3, [("VE", pk_ev)], lambda k: ones_col)

        tmprow = bcast[0:1, :]
        qcs_ve = emit("vector", lambda e: e.scalar_tensor_tensor(
            ar(4), ar(0), CTOT, ar(4), ALU.mult, ALU.add),
            waits=[("VE", ksq_ev), ("LD", ld_bq), ("LD", ld_bk), ("LD", ld_gam)], inc_name="VE")
        emit("vector", lambda e: e.scalar_tensor_tensor(tmprow, ar(0), -CTOT, ar(4), ALU.mult, ALU.add))
        emit("vector", lambda e: e.tensor_tensor(tmprow, tmprow, ar(4), ALU.add))
        emit("vector", lambda e: e.tensor_tensor(tmprow, tmprow, ar(0), ALU.mult))
        emit("vector", lambda e: e.tensor_tensor(ar(2), ar(2), tmprow, ALU.add))
        emit("vector", lambda e: e.scalar_tensor_tensor(tmprow, ar(1), CTOT, ar(5), ALU.mult, ALU.add))
        emit("vector", lambda e: e.tensor_tensor(tmprow, tmprow, ar(5), ALU.add))
        emit("vector", lambda e: e.tensor_tensor(tmprow, tmprow, ar(1), ALU.mult))
        kfull_ve = emit("vector", lambda e: e.tensor_tensor(ar(3), ar(3), tmprow, ALU.add),
                        inc_name="VE")
        acq = emit("scalar", lambda e: e.activation(ar(2), ar(2), AF.Sqrt),
                   waits=[("VE", kfull_ve)], inc_name="AC")
        emit("vector", lambda e: e.tensor_copy(junk[0:1, 0:512], ar(2)[0:1, 0:512]),
             waits=[("AC", acq)])
        qsq_cp = emit("vector", lambda e: e.tensor_copy(w2st[0:1, 0:512], ar(2)[0:1, 512:1024]),
                      inc_name="VE")
        qinv_ve = emit("vector", lambda e: e.reciprocal(ar(2), ar(2)), inc_name="VE")
        ack = emit("scalar", lambda e: e.activation(ar(3), ar(3), AF.Sqrt),
                   waits=[("VE", kfull_ve)], inc_name="AC")
        kinv_ve = emit("vector", lambda e: e.reciprocal(ar(3), ar(3)), waits=[("AC", ack)],
                       inc_name="VE")
        kc_ev = row_transpose8(ar(3), kinv_col, [("VE", kinv_ve), ("VE", qsq_ev)])
        bvk_ve = emit("vector", lambda e: e.tensor_tensor(bvk_col, bv_col, kinv_col, ALU.mult),
                      waits=[("VE", kc_ev), ("VE", v_bv)], inc_name="VE")

        def ev_kg(m, h, pt, lastmm):
            return emit("vector", lambda e, m=m, h=h, p=pt: e.tensor_scalar(
                big4[:, m * 1024 + h * 512: m * 1024 + (h + 1) * 512], p[:, :],
                kinv_col[:, m:m + 1], None, ALU.mult, ALU.bypass),
                waits=[("MM", lastmm)], inc_name="VE")
        gt_mm, kg_ev = mm1024(
            lambda k, m: big3[:, k * 1024 + m * 128: k * 1024 + (m + 1) * 128],
            lambda k, h: big2[:, k * 1024 + h * 512: k * 1024 + (h + 1) * 512],
            ev_kg, [("VE", bvk_ve), ("VE", qcs_ve)],
            extra_rank1=[
                (lambda m: arsl(5, m), lambda h: arh(0, h)),
                (lambda m: arsl(1, m), lambda h: arh(4, h)),
            ])

        kivg_ev = matvec_row(big4, 7, [("VE", kg_ev)], lambda k: ones_col)
        emit("vector", lambda e: e.scalar_tensor_tensor(ar(7), ar(4), EPS_AT, ar(7), ALU.mult, ALU.add),
             waits=[("VE", kivg_ev), ("VE", qinv_ve)])
        emit("vector", lambda e: e.tensor_tensor(ar(7), ar(7), ar(2), ALU.mult))
        emit("vector", lambda e: e.tensor_scalar(ar(7), ar(7), float(D), None, ALU.add, ALU.bypass))
        tail_ve = emit("vector", lambda e: e.reciprocal(ar(5), ar(7)), inc_name="VE")
        emit("vector", lambda e: e.tensor_tensor(ar(6), ar(2), ar(5), ALU.mult),
             waits=[("VE", tail_ve)])
        csc_ve2 = emit("vector", lambda e: e.tensor_scalar(
            ar(6), ar(6), gsc[0:1, 0:1], None, ALU.mult, ALU.bypass), inc_name="VE")
        emit("vector", lambda e: e.tensor_copy(arh(1, 0), junk[0:1, 0:512]),
             waits=[("VE", qsq_cp), ("MM", gt_mm)])
        qsqr_ve = emit("vector", lambda e: e.tensor_copy(arh(1, 1), w2st[0:1, 0:512]),
                       inc_name="VE")
        bv_ld = emit("sync", lambda e: e.dma_start(out=ar(0), in_=bv_d), waits=[("MM", gt_mm)],
                     inc_name="LD")
        bvs_ve = emit("vector", lambda e: e.reduce_sum(gsc[0:1, 1:2], ar(0), axis=AX.X),
                      waits=[("LD", bv_ld)], inc_name="VE")
        bvkg_ev = matvec_row(big4, 4, [("VE", kivg_ev), ("VE", tail_ve), ("VE", v_bv)], lambda k: bv_col[:, k:k + 1])
        emit("vector", lambda e: e.tensor_tensor(ar(4), ar(4), ar(6), ALU.mult),
             waits=[("VE", bvkg_ev), ("VE", csc_ve2), ("VE", bvs_ve)])
        emit("vector", lambda e: e.tensor_scalar(
            ar(0), ar(5), gsc[0:1, 1:2], gsc[0:1, 0:1], ALU.mult, ALU.mult))
        gcon_ve = emit("vector", lambda e: e.tensor_tensor(ar(0), ar(0), ar(4), ALU.add),
                       inc_name="VE")
        bc_mm = emit("tensor", lambda e: e.matmul(ps[4][:, :], ones_row, arh(6, 0),
                                                  start=True, stop=True),
                     waits=[("VE", gcon_ve), ("VE", qsqr_ve)], inc_name="MM")
        bc_mm = emit("tensor", lambda e: e.matmul(ps[5][:, :], ones_row, arh(6, 1),
                                                  start=True, stop=True), inc_name="MM")
        emit("vector", lambda e: e.tensor_copy(bcast[:, 0:512], ps[4][:, :]),
             waits=[("MM", bc_mm)])
        bc_ev = emit("vector", lambda e: e.tensor_copy(bcast[:, 512:1024], ps[5][:, :]),
                     inc_name="VE")
        wv_ld = 0
        for r in range(FC):
            wv_ld = emit("sync", lambda e, r=r: e.dma_start(
                out=big3[:, r * 1024:(r + 1) * 1024], in_=wv_d[r * 128:(r + 1) * 128, :]),
                waits=([("MM", gt_mm)] if r == 0 else []), inc_name="LD")
        wvs_ev = matvec_row(big3, 5, [("LD", wv_ld), ("VE", gcon_ve), ("VE", bc_ev)], lambda k: ones_col)
        def ev_d2(m, h, pt, lastmm):
            return emit("vector", lambda e, m=m, h=h, p=pt: e.tensor_tensor(
                big1[:, m * 1024 + h * 512: m * 1024 + (h + 1) * 512], p[:, :],
                bcast[:, h * 512:(h + 1) * 512], ALU.mult),
                waits=[("MM", lastmm)], inc_name="VE")
        d2_mm, d2_ev = mm1024(
            lambda k, m: big3[:, k * 1024 + m * 128: k * 1024 + (m + 1) * 128],
            lambda k, h: big4[:, k * 1024 + h * 512: k * 1024 + (h + 1) * 512],
            ev_d2, [("VE", wvs_ev)],
            extra_rank1=[(lambda m: arsl(5, m), lambda h: arh(1, h))])
        dg_ve = 0
        for r in range(FC):
            dg_ve = emit("vector", lambda e, r=r: e.tensor_tensor(
                big1[:, r * 1024 + r * 128: r * 1024 + (r + 1) * 128],
                big1[:, r * 1024 + r * 128: r * 1024 + (r + 1) * 128],
                ident[:, :], ALU.add),
                waits=([("VE", d2_ev)] if r == 0 else []), inc_name="VE")
        dsp = 0
        for r in range(FC):
            wls = [("VE", dg_ve), ("MM", d2_mm)] if r == 0 else []
            hq = emit("scalar", lambda e, r=r: e.activation(
                big4b[:, r * 2048:r * 2048 + 1024], big1[:, r * 1024:(r + 1) * 1024],
                AF.Identity), waits=wls, inc_name="AC")
            dsp = emit("vector", lambda e, r=r: e.tensor_tensor(
                big4b[:, r * 2048 + 1024:(r + 1) * 2048], big1[:, r * 1024:(r + 1) * 1024],
                big4b[:, r * 2048:r * 2048 + 1024], ALU.subtract),
                waits=[("AC", hq)], inc_name="VE")
        d2sp_ve = dsp
        b2_ld = emit("sync", lambda e: e.dma_start(out=ar(7), in_=b2_d),
                     waits=[("VE", tail_ve)], inc_name="LD")

        def x1sl(c, h=None):
            t = big2 if c < 8 else big3
            cc = c % 8
            if h is None:
                return t[:, cc * 1024:(cc + 1) * 1024]
            return t[:, cc * 1024 + h * 512: cc * 1024 + (h + 1) * 512]

        att_ve = [0] * NCH
        ln3_nm = [0] * NCH
        att_ev_ph = {}
        last_att_mm = {}
        mm_h = {}
        for c in range(NCH):
            w = []
            if c == 0:
                w = [("MM", d2_mm), ("SP", sp_done[NCH - 1])]
            if c >= 2:
                w.append(("MM", last_att_mm[c - 2]))
            xnt_ld = emit("sync", lambda e, c=c: e.dma_start(out=stB(c % 2), in_=xnt_d[c]),
                          waits=w, inc_name="LD")
            w2_ = []
            if c < 2:
                w2_.append(("AC", last_apply[c % 2]))
            else:
                w2_.append(("VE", att_ve[c - 2]))
            xr_ld = emit("sync", lambda e, c=c: e.dma_start(
                out=stC(c % 2), in_=x_d[c * 128:(c + 1) * 128, :]), waits=w2_, inc_name="LD")
            for h in range(2):
                pt = ps[(2 * c + h) % 4]
                w = [("LD", xnt_ld)]
                if c == 0 and h == 0:
                    w += [("VE", d2sp_ve)]
                if 2 * c + h >= 4:
                    w.append(("VE", att_ev_ph[(2 * c + h) - 4]))
                first = True
                for r in range(FC):
                    for pp in range(3):
                        lt = (stBb(c % 2)[:, r * 256:r * 256 + 128] if pp < 2
                              else stBb(c % 2)[:, r * 256 + 128:r * 256 + 256])
                        rt = (big4b[:, r * 2048 + h * 512: r * 2048 + (h + 1) * 512] if pp != 1
                              else big4b[:, r * 2048 + 1024 + h * 512: r * 2048 + 1024 + (h + 1) * 512])
                        mm = emit("tensor", lambda e, p=pt, lt=lt, rt=rt, st=first: e.matmul(
                            p[:, :], lt, rt, start=st, stop=False),
                            waits=(w if first else []), inc_name="MM")
                        first = False
                mm = emit("tensor", lambda e, p=pt, h=h: e.matmul(
                    p[:, :], ones_row, arh(0, h), start=False, stop=True), inc_name="MM")
                mm_h[h] = mm
            last_att_mm[c] = mm
            b = (c % 2) * 16
            sum0 = mcol[:, b + 0:b + 1]; sum1 = mcol[:, b + 1:b + 2]
            sq0 = mcol[:, b + 2:b + 3]; sq1 = mcol[:, b + 3:b + 4]
            mean_c = mcol[:, b + 4:b + 5]; var_c = mcol[:, b + 5:b + 6]
            std_c = mcol[:, b + 6:b + 7]; rstd_c = mcol[:, b + 7:b + 8]
            negmr_c = mcol[:, b + 8:b + 9]; msc = mcol[:, b + 9:b + 10]
            p0_, p1_ = ps[(2 * c) % 4], ps[(2 * c + 1) % 4]
            t5a = tmp56[:, (c % 2) * 1024:(c % 2) * 1024 + 512]
            t5b = tmp56[:, (c % 2) * 1024 + 512:(c % 2) * 1024 + 1024]
            att_ev_ph[2 * c] = emit("vector", lambda e, t=t5a, p=p0_: e.tensor_copy(t, p[:, :]),
                                    waits=[("MM", mm_h[0])], inc_name="VE")
            ev1 = emit("vector", lambda e, t=t5b, p=p1_: e.tensor_copy(t, p[:, :]),
                       waits=[("MM", mm_h[1])], inc_name="VE")
            att_ev_ph[2 * c + 1] = ev1
            emit("vector", lambda e, s=sum0, t=t5a: e.reduce_sum(s, t, axis=AX.X))
            emit("vector", lambda e, s=sum1, t=t5b: e.reduce_sum(s, t, axis=AX.X))
            emit("scalar", lambda e, s=sq0, t=t5a: e.activation(
                junk[:, :], t, AF.Square, accum_out=s), waits=[("VE", ev1)], inc_name="AC")
            a2 = emit("scalar", lambda e, s=sq1, t=t5b: e.activation(
                junk[:, :], t, AF.Square, accum_out=s), inc_name="AC")
            emit("vector", lambda e, a=sum0, b_=sum1, o=mean_c: e.tensor_tensor(o, a, b_, ALU.add),
                 waits=[("AC", a2)])
            emit("vector", lambda e, o=mean_c: e.tensor_scalar(o, o, 1.0 / D, None, ALU.mult,
                                                               ALU.bypass))
            emit("vector", lambda e, a=sq0, b_=sq1, o=var_c: e.tensor_tensor(o, a, b_, ALU.add))
            emit("vector", lambda e, o=var_c: e.tensor_scalar(o, o, 1.0 / D, None, ALU.mult,
                                                              ALU.bypass))
            emit("vector", lambda e, m=mean_c, o=msc: e.scalar_tensor_tensor(
                o, m, -1.0, m, ALU.mult, ALU.mult))
            v_var = emit("vector", lambda e, o=var_c, s=msc: e.tensor_tensor(o, o, s, ALU.add))
            a3 = emit("scalar", lambda e, v=var_c, o=std_c: e.activation(
                o, v, AF.Sqrt, bias=eps_col), waits=[("VE", v_var)], inc_name="AC")
            emit("vector", lambda e, o=rstd_c, i=std_c: e.reciprocal(o, i), waits=[("AC", a3)])
            nm_ve = emit("vector", lambda e, o=negmr_c, m=mean_c, r=rstd_c: e.scalar_tensor_tensor(
                o, m, -1.0, r, ALU.mult, ALU.mult), inc_name="VE")
            for h in range(2):
                th = tmp56[:, (c % 2) * 1024 + h * 512:(c % 2) * 1024 + (h + 1) * 512]
                emit("vector", lambda e, c=c, h=h, n=negmr_c: e.tensor_scalar(
                    x1sl(c, h), stC(c % 2)[:, h * 512:(h + 1) * 512], n, None,
                    ALU.add, ALU.bypass), waits=[("LD", xr_ld)])
                att_ve[c] = emit("vector", lambda e, c=c, h=h, t=th, r=rstd_c: e.scalar_tensor_tensor(
                    x1sl(c, h), t, r, x1sl(c, h), ALU.mult, ALU.add), inc_name="VE")

        for c in range(NCH):
            b = 64 + (c % 2) * 16
            sum_c = mcol[:, b + 0:b + 1]; sq0 = mcol[:, b + 1:b + 2]; sq1 = mcol[:, b + 2:b + 3]
            mean_c = mcol[:, b + 3:b + 4]; var_c = mcol[:, b + 5:b + 6]
            std_c = mcol[:, b + 6:b + 7]; msc = mcol[:, b + 9:b + 10]
            rstd_c = mcol[:, 96 + c:97 + c]; negmr_c = mcol[:, 112 + c:113 + c]
            w = [("VE", att_ve[c])]
            emit("vector", lambda e, s=sum_c, c=c: e.reduce_sum(s, x1sl(c), axis=AX.X), waits=w)
            emit("scalar", lambda e, s=sq0, c=c: e.activation(
                junk[:, :], x1sl(c, 0), AF.Square, accum_out=s), waits=w, inc_name="AC")
            a2 = emit("scalar", lambda e, s=sq1, c=c: e.activation(
                junk[:, :], x1sl(c, 1), AF.Square, accum_out=s), inc_name="AC")
            emit("vector", lambda e, m=mean_c, s=sum_c: e.tensor_scalar(
                m, s, 1.0 / D, None, ALU.mult, ALU.bypass), waits=[("AC", a2)])
            emit("vector", lambda e, q0=sq0, q1=sq1, o=var_c: e.tensor_tensor(o, q0, q1, ALU.add))
            emit("vector", lambda e, o=var_c: e.tensor_scalar(o, o, 1.0 / D, None, ALU.mult,
                                                              ALU.bypass))
            emit("vector", lambda e, m=mean_c, o=msc: e.scalar_tensor_tensor(
                o, m, -1.0, m, ALU.mult, ALU.mult))
            v_var = emit("vector", lambda e, o=var_c, s=msc: e.tensor_tensor(o, o, s, ALU.add))
            a3 = emit("scalar", lambda e, v=var_c, o=std_c: e.activation(
                o, v, AF.Sqrt, bias=eps_col), waits=[("VE", v_var)], inc_name="AC")
            emit("vector", lambda e, o=rstd_c, i=std_c: e.reciprocal(o, i), waits=[("AC", a3)])
            ln3_nm[c] = emit("vector", lambda e, o=negmr_c, m=mean_c, r=rstd_c: e.scalar_tensor_tensor(
                o, m, -1.0, r, ALU.mult, ALU.mult), inc_name="VE")

        def h1sl(mi, sl=None):
            t = big4 if mi < 16 else big1
            m = mi % 16
            if sl is None:
                return t[:, m * 512:(m + 1) * 512]
            return t[:, m * 512 + sl * 128: m * 512 + (sl + 1) * 128]

        st_done = [0] * NCH
        trv_prev = {0: 0, 1: 0}
        w1_mm_prev = {0: 0, 1: 0}
        w2_mm_prev = {0: 0, 1: 0}
        last_tr3 = {}
        st6 = {"last_tr_ve": 0}
        out_ve_prev = 0

        def do_ln3(g):
            for cc in range(CPG):
                c = g * CPG + cc
                rstd_c = mcol[:, 96 + c:97 + c]; negmr_c = mcol[:, 112 + c:113 + c]
                wa = [("VE", ln3_nm[c])]
                if g == 0 and cc < 2:
                    wa.append(("VE", att_ve[NCH - 2 + cc]))
                key = (g, cc - 2) if cc >= 2 else (g - 1, cc + 2)
                if key in last_tr3:
                    wa.append(("MM", last_tr3[key]))
                x1n_ac = emit("scalar", lambda e, c=c, cc=cc, r=rstd_c, n=negmr_c: e.activation(
                    stC(cc % 2), x1sl(c), AF.Identity, bias=n, scale=r), waits=wa, inc_name="AC")
                for r in range(FC):
                    pt = ps[6 + (r % 2)]
                    w = [("AC", x1n_ac), ("VE", trv_prev[r % 2])]
                    if g == 0 and cc == 0 and r < 2:
                        w.append(("VE", att_ve[NCH - 1]))
                    m = emit("tensor", lambda e, p=pt, cc=cc, r=r: e.transpose(
                        p[:, 0:128], stC(cc % 2)[:, r * 128:(r + 1) * 128], ident[:, :]),
                        waits=w, inc_name="MM")
                    emit("vector", lambda e, p=pt, cc=cc, r=r: e.tensor_copy(
                        x1nTb[:, r * 1024 + cc * 128: r * 1024 + (cc + 1) * 128], p[:, 0:128]),
                        waits=[("MM", m)], inc_name="VE")
                    tv = emit("vector", lambda e, p=pt, cc=cc, r=r: e.tensor_tensor(
                        x1nTb[:, r * 1024 + 512 + cc * 128: r * 1024 + 512 + (cc + 1) * 128],
                        p[:, 0:128],
                        x1nTb[:, r * 1024 + cc * 128: r * 1024 + (cc + 1) * 128],
                        ALU.subtract), inc_name="VE")
                    trv_prev[r % 2] = tv
                last_tr3[(g, cc)] = m
                st6["last_tr_ve"] = tv

        def do_ln4(g, cc, ffn2_done):
            c = g * CPG + cc
            b = (cc % 2) * 16
            sum0 = mcol[:, b + 0:b + 1]; sum1 = mcol[:, b + 1:b + 2]
            sq0 = mcol[:, b + 2:b + 3]; sq1 = mcol[:, b + 3:b + 4]
            mean_c = mcol[:, b + 4:b + 5]; var_c = mcol[:, b + 5:b + 6]
            std_c = mcol[:, b + 6:b + 7]; rstd_c = mcol[:, b + 7:b + 8]
            negmr_c = mcol[:, b + 8:b + 9]; msc = mcol[:, b + 9:b + 10]
            p0_, p1_ = ps[cc * 2], ps[cc * 2 + 1]
            t6a = bcast[:, 0:512]
            t6b = bcast[:, 512:1024]
            emit("vector", lambda e, t=t6a, p=p0_: e.tensor_copy(t, p[:, :]),
                 waits=[("MM", ffn2_done), ("AC", st6.get("ln4_a2", 0))])
            ev1 = emit("vector", lambda e, t=t6b, p=p1_: e.tensor_copy(t, p[:, :]),
                       inc_name="VE")
            emit("vector", lambda e, s=sum0, t=t6a: e.reduce_sum(s, t, axis=AX.X))
            emit("vector", lambda e, s=sum1, t=t6b: e.reduce_sum(s, t, axis=AX.X))
            emit("scalar", lambda e, s=sq0, t=t6a: e.activation(
                junk[:, :], t, AF.Square, accum_out=s), waits=[("VE", ev1)], inc_name="AC")
            a2 = emit("scalar", lambda e, s=sq1, t=t6b: e.activation(
                junk[:, :], t, AF.Square, accum_out=s), inc_name="AC")
            st6["ln4_a2"] = a2
            emit("vector", lambda e, a=sum0, b_=sum1, o=mean_c: e.tensor_tensor(
                o, a, b_, ALU.add), waits=[("AC", a2)])
            emit("vector", lambda e, o=mean_c: e.tensor_scalar(o, o, 1.0 / D, None, ALU.mult,
                                                               ALU.bypass))
            emit("vector", lambda e, a=sq0, b_=sq1, o=var_c: e.tensor_tensor(o, a, b_, ALU.add))
            emit("vector", lambda e, o=var_c: e.tensor_scalar(o, o, 1.0 / D, None, ALU.mult,
                                                              ALU.bypass))
            emit("vector", lambda e, m=mean_c, o=msc: e.scalar_tensor_tensor(
                o, m, -1.0, m, ALU.mult, ALU.mult))
            v_var = emit("vector", lambda e, o=var_c, s=msc: e.tensor_tensor(o, o, s, ALU.add))
            a3 = emit("scalar", lambda e, v=var_c, o=std_c: e.activation(
                o, v, AF.Sqrt, bias=eps_col), waits=[("VE", v_var)], inc_name="AC")
            emit("vector", lambda e, o=rstd_c, i=std_c: e.reciprocal(o, i), waits=[("AC", a3)])
            nm_ve = emit("vector", lambda e, o=negmr_c, m=mean_c, r=rstd_c: e.scalar_tensor_tensor(
                o, m, -1.0, r, ALU.mult, ALU.mult), inc_name="VE")
            for h in range(2):
                t6h = bcast[:, h * 512:(h + 1) * 512]
                emit("vector", lambda e, c=c, h=h, n=negmr_c: e.tensor_scalar(
                    x1sl(c, h), x1sl(c, h), n, None, ALU.add, ALU.bypass),
                    waits=[("VE", nm_ve)])
                ov = emit("vector", lambda e, c=c, h=h, t=t6h, r=rstd_c: e.scalar_tensor_tensor(
                    x1sl(c, h), t, r, x1sl(c, h), ALU.mult, ALU.add), inc_name="VE")
            st_done[c] = emit("gpsimd", lambda e, c=c: e.dma_start(
                out=y_d[c * 128:(c + 1) * 128, :], in_=x1sl(c)),
                waits=[("VE", ov)], inc_name="SP")
            return ev1, ov

        do_ln3(0)
        bank_free_next = {i: att_ev_ph[2 * NCH - 1] for i in range(8)}
        for g in range(NG):
            relu_ac = {}
            bank_free = dict(bank_free_next)
            last_tr_ve = st6["last_tr_ve"]
            for ig in range(4):
                for ka in range(FC):
                    w = [("MM", w1_mm_prev[ka % 2])]
                    if g == 0 and ig == 0 and ka == 0:
                        w.append(("SP", w1s_done))
                    qeng, qsem = ("sync", "LD") if ka % 2 == 0 else ("scalar", "L2")
                    ld = emit(qeng, lambda e, ig=ig, ka=ka: e.dma_start(
                        out=stDb(ka % 2)[:, 0:2048],
                        in_=w1s_d[ka * 128:(ka + 1) * 128, ig * 2048:(ig + 1) * 2048]),
                        waits=w, inc_name=qsem)
                    wmm0 = [(qsem, ld)]
                    if ka == 0:
                        wmm0.append(("VE", last_tr_ve))
                    for mi_l in range(8):
                        wmm = list(wmm0) if mi_l == 0 else []
                        if ka == 0 and bank_free[mi_l]:
                            wmm.append(("VE", bank_free[mi_l]))
                        for pp in range(3):
                            lt = (stDb(ka % 2)[:, mi_l * 128:(mi_l + 1) * 128] if pp < 2
                                  else stDb(ka % 2)[:, 1024 + mi_l * 128:1024 + (mi_l + 1) * 128])
                            rt = (x1nTb[:, ka * 1024:ka * 1024 + 512] if pp != 1
                                  else x1nTb[:, ka * 1024 + 512:ka * 1024 + 1024])
                            mm = emit("tensor", lambda e, p=ps[mi_l], lt=lt, rt=rt, ka=ka, pp=pp: e.matmul(
                                p[:, :], lt, rt,
                                start=(ka == 0 and pp == 0), stop=(ka == FC - 1 and pp == 2)),
                                waits=(wmm if pp == 0 else []), inc_name="MM")
                    w1_mm_prev[ka % 2] = mm
                for mi_l in range(8):
                    mi = ig * 8 + mi_l
                    hb = (big4b if mi < 16 else big1b)
                    hoff = (mi % 16) * 1024
                    rv = emit("vector", lambda e, p=ps[mi_l], mi=mi: e.tensor_scalar(
                        junk[:, 0:512], p[:, :], cols[:, 24 + mi:25 + mi], 0.0, ALU.add, ALU.max),
                        waits=([("MM", mm)] if mi_l == 0 else []), inc_name="VE")
                    hc = emit("scalar", lambda e, hb=hb, hoff=hoff: e.activation(
                        hb[:, hoff:hoff + 512], junk[:, 0:512], AF.Identity),
                        waits=[("VE", rv)], inc_name="AC")
                    ra = emit("vector", lambda e, hb=hb, hoff=hoff: e.tensor_tensor(
                        hb[:, hoff + 512:hoff + 1024], junk[:, 0:512], hb[:, hoff:hoff + 512],
                        ALU.subtract), waits=[("AC", hc)], inc_name="VE")
                    relu_ac[mi] = ra
                    bank_free[mi_l] = rv
                last_psum_free = ra
            for mi in range(32):
                w = [("MM", w2_mm_prev[mi % 2])]
                if g == 0 and mi < 2:
                    w.append(("VE", qsqr_ve))
                qeng, qsem = ("gpsimd", "SP") if mi % 2 == 0 else ("sync", "LD")
                ld = emit(qeng, lambda e, mi=mi: e.dma_start(
                    out=w2stb[:, (mi % 2) * 2048:(mi % 2 + 1) * 2048],
                    in_=w2s_d[mi * 128:(mi + 1) * 128, :]), waits=w, inc_name=qsem)
                wmm0 = [(qsem, ld), ("VE", relu_ac[mi])]
                if mi == 0:
                    wmm0.append(("VE", last_psum_free))
                    wmm0.append(("VE", out_ve_prev))
                hb = (big4b if mi < 16 else big1b)
                hoff = (mi % 16) * 1024
                for cc in range(CPG):
                    for pp in range(3):
                        lt_off = hoff + cc * 128 if pp < 2 else hoff + 512 + cc * 128
                        rbase = (mi % 2) * 2048 + (0 if pp != 1 else 1024)
                        for hh in range(2):
                            rt = w2stb[:, rbase + hh * 512:rbase + (hh + 1) * 512]
                            mm = emit("tensor", lambda e, p=pa[cc], hb=hb, lo=lt_off, rt=rt, hh=hh,
                                      mi=mi, pp=pp: e.matmul(
                                p[:, hh * 512:(hh + 1) * 512], hb[:, lo:lo + 128], rt,
                                start=(mi == 0 and pp == 0), stop=False),
                                waits=(wmm0 if (cc == 0 and pp == 0 and hh == 0) else []),
                                inc_name="MM")
                w2_mm_prev[mi % 2] = mm
            for cc in range(CPG):
                for h in range(2):
                    mm = emit("tensor", lambda e, p=ps[cc * 2 + h], h=h: e.matmul(
                        p[:, :], ones_row, arh(7, h), start=False, stop=True),
                        waits=([("LD", b2_ld)] if (g == 0 and cc == 0 and h == 0) else []),
                        inc_name="MM")
            ffn2_done = mm
            ev3, ov = do_ln4(g, 3, ffn2_done)
            evs = {3: ev3}
            if g + 1 < NG:
                do_ln3(g + 1)
            for cc in (0, 1, 2):
                evs[cc], ov = do_ln4(g, cc, ffn2_done)
            out_ve_prev = ov
            bank_free_next = {}
            for cc in range(4):
                bank_free_next[2 * cc] = evs[cc]
                bank_free_next[2 * cc + 1] = evs[cc]

        emit("gpsimd", lambda e: e.nop(), waits=[("SP", st_done[NCH - 1])])

        totals = {k: v[2] for k, v in cnt.items()}
        for (w, v) in waits_seen:
            assert v <= totals[w], f"wait {w} {v} > total {totals[w]}"

        @blk.sync
        def _(eng):
            for f in E["sync"]:
                f(eng)

        @blk.tensor
        def _(eng):
            for f in E["tensor"]:
                f(eng)

        @blk.vector
        def _(eng):
            for f in E["vector"]:
                f(eng)

        @blk.scalar
        def _(eng):
            for f in E["scalar"]:
                f(eng)

        @blk.gpsimd
        def _(eng):
            for f in E["gpsimd"]:
                f(eng)

    return nc


_NC_CACHE = None


def kernel(L0_attn_x, L0_attn_Wq, L0_attn_bq, L0_attn_Wk, L0_attn_bk,
           L0_attn_Wv, L0_attn_bv, L0_attn_gamma,
           L0_ffn_W1, L0_ffn_b1, L0_ffn_W2, L0_ffn_b2):
    global _NC_CACHE
    x = np.ascontiguousarray(np.asarray(L0_attn_x, dtype=np.float32).reshape(16384, 1024))
    com = dict(
        wq=np.ascontiguousarray(np.asarray(L0_attn_Wq, dtype=np.float32)),
        wk=np.ascontiguousarray(np.asarray(L0_attn_Wk, dtype=np.float32)),
        wv=np.ascontiguousarray(np.asarray(L0_attn_Wv, dtype=np.float32)),
        bq=np.asarray(L0_attn_bq, dtype=np.float32).reshape(1, 1024),
        bk=np.asarray(L0_attn_bk, dtype=np.float32).reshape(1, 1024),
        bv_row=np.asarray(L0_attn_bv, dtype=np.float32).reshape(1, 1024),
        bv8=np.ascontiguousarray(np.asarray(L0_attn_bv, dtype=np.float32).reshape(8, 128)),
        gamma=np.asarray(L0_attn_gamma, dtype=np.float32).reshape(1, 1),
        w1=np.ascontiguousarray(np.asarray(L0_ffn_W1, dtype=np.float32)),
        b1_32=np.ascontiguousarray(np.asarray(L0_ffn_b1, dtype=np.float32).reshape(32, 128)),
        w2=np.ascontiguousarray(np.asarray(L0_ffn_W2, dtype=np.float32)),
        b2=np.asarray(L0_ffn_b2, dtype=np.float32).reshape(1, 1024),
        ident=np.eye(128, dtype=np.float32),
    )
    if _NC_CACHE is None:
        _NC_CACHE = build_core()
    nc = _NC_CACHE
    in_maps = []
    for i in range(8):
        m = dict(com)
        m["x"] = np.ascontiguousarray(x[i * TOK:(i + 1) * TOK, :])
        in_maps.append(m)
    res = run_bass_kernel_spmd(nc, in_maps, core_ids=list(range(8)))
    out = np.concatenate([res.results[i]["y"] for i in range(8)], axis=0)
    return out.reshape(1, 16384, 1024).astype(np.float32)


# version 11 (final)
# speedup vs baseline: 1.1247x; 1.0039x over previous
import numpy as np
import concourse.bass as bass
import concourse.mybir as mybir
from concourse.bass_utils import run_bass_kernel_spmd

F32 = mybir.dt.float32
BF16 = mybir.dt.bfloat16
AF = mybir.ActivationFunctionType
ALU = mybir.AluOpType
AX = mybir.AxisListType

TOK = 2048
NCH = 16
D = 1024
FC = 8
INTER = 4096
NG = 4
CPG = 4
CTOT = 16384.0
EPS_LN = 1e-5
EPS_AT = 1e-6


def build_core():
    nc = bass.Bass("TRN2", target_bir_lowering=False)
    dt = lambda n, s: nc.dram_tensor(n, s, F32, kind="ExternalInput").ap()
    x_d = dt("x", [TOK, D])
    wq_d = dt("wq", [D, D]); wk_d = dt("wk", [D, D]); wv_d = dt("wv", [D, D])
    bq_d = dt("bq", [1, D]); bk_d = dt("bk", [1, D])
    bv_d = dt("bv_row", [1, D]); bv8_d = dt("bv8", [8, 128])
    gam_d = dt("gamma", [1, 1])
    w1_d = dt("w1", [D, INTER]); b1_d = dt("b1_32", [32, 128])
    w2_d = dt("w2", [INTER, D]); b2_d = dt("b2", [1, D])
    id_d = dt("ident", [128, 128])
    y_d = nc.dram_tensor("y", [TOK, D], F32, kind="ExternalOutput").ap()
    ccin_h = nc.dram_tensor("ccin", [1025, 1024], F32)
    ccout_h = nc.dram_tensor("ccout", [1025, 1024], F32, addr_space="Shared")
    ccin, ccout = ccin_h.ap(), ccout_h.ap()
    xnt_d = nc.dram_tensor("xnt", [NCH, 128, D], F32).ap()
    w1s_d = nc.dram_tensor("w1s", [D, 2 * INTER], BF16).ap()
    w2s_d = nc.dram_tensor("w2s", [INTER, 2 * D], BF16).ap()

    import contextlib
    with contextlib.ExitStack() as stack:
        en = stack.enter_context
        big1 = en(nc.sbuf_tensor("big1", [128, 8192], F32))
        big2 = en(nc.sbuf_tensor("big2", [128, 8192], F32))
        big3 = en(nc.sbuf_tensor("big3", [128, 8192], F32))
        big4 = en(nc.sbuf_tensor("big4", [128, 8192], F32))
        util = en(nc.sbuf_tensor("util", [128, 8192], F32))
        w2st = en(nc.sbuf_tensor("w2st", [128, 2048], F32))
        arena = en(nc.sbuf_tensor("arena", [128, 8192], F32))
        cols = en(nc.sbuf_tensor("cols", [128, 64], F32))
        mcol = en(nc.sbuf_tensor("mcol", [128, 128], F32))
        bcast = en(nc.sbuf_tensor("bcast", [128, 1024], F32))
        ident = en(nc.sbuf_tensor("ident_sb", [128, 128], F32))
        ones = en(nc.sbuf_tensor("ones", [128, 128], F32))
        gsc = en(nc.sbuf_tensor("gsc", [128, 8], F32))
        junk = en(nc.sbuf_tensor("junk", [128, 512], F32))
        pa = [en(nc.psum_tensor(f"pa{i}", [128, 1024], F32)) for i in range(4)]
        ps = [pa[i // 2][:, (i % 2) * 512:(i % 2 + 1) * 512] for i in range(8)]
        ps4, ps5, ps6, ps7 = ps[4], ps[5], ps[6], ps[7]
        sLD = en(nc.semaphore("sLD"))
        sSP = en(nc.semaphore("sSP"))
        sMM = en(nc.semaphore("sMM"))
        sVE = en(nc.semaphore("sVE"))
        sAC = en(nc.semaphore("sAC"))
        sCC = en(nc.semaphore("sCC"))
        sL2 = en(nc.semaphore("sL2"))
        blk = en(nc.Block())
        stA = lambda b: util[:, (0 + b) * 1024:(1 + b) * 1024]
        stB = lambda b: util[:, (2 + b) * 1024:(3 + b) * 1024]
        stC = lambda b: util[:, (4 + b) * 1024:(5 + b) * 1024]
        stD = lambda b: util[:, (6 + b) * 1024:(7 + b) * 1024]
        x1nT = util[:, 0:4096]
        tmp56 = util[:, 0:2048]
        big1b = big1.bitcast(BF16)
        big2b = big2.bitcast(BF16)
        big3b = big3.bitcast(BF16)
        big4b = big4.bitcast(BF16)
        utilb = util.bitcast(BF16)
        w2stb = w2st.bitcast(BF16)
        x1nTb = utilb[:, 0:8192]
        stBb = lambda b: utilb[:, (2 + b) * 2048:(3 + b) * 2048]
        stDb = lambda b: utilb[:, (6 + b) * 2048:(7 + b) * 2048]
        ar = lambda i: arena[0:1, i * 1024:(i + 1) * 1024]
        arh = lambda i, h: arena[0:1, i * 1024 + h * 512: i * 1024 + (h + 1) * 512]
        arsl = lambda i, m: arena[0:1, i * 1024 + m * 128: i * 1024 + (m + 1) * 128]
        cs_col = cols[:, 0:8]; kinv_col = cols[:, 8:16]
        bv_col = cols[:, 16:24]; bvk_col = cols[:, 56:64]
        ones_col = ones[:, 0:1]; ones_row = ones[0:1, 0:128]

        E = {k: [] for k in ("sync", "tensor", "vector", "scalar", "gpsimd")}
        cnt = {"LD": [sLD, 16, 0], "SP": [sSP, 16, 0], "MM": [sMM, 1, 0],
               "VE": [sVE, 1, 0], "AC": [sAC, 1, 0], "CC": [sCC, 1, 0],
               "L2": [sL2, 16, 0]}
        waits_seen = []

        def emit(eng, f, waits=(), inc_name=None):
            waits = list(waits)
            auto = {"vector": "VE", "scalar": "AC"}.get(eng)
            if auto is not None:
                if inc_name is None:
                    inc_name = auto
                waits.append((auto, cnt[auto][2]))
            if inc_name in ("LD", "SP", "L2"):
                waits.append((inc_name, cnt[inc_name][2]))
            wl = [(cnt[w][0], v, w) for (w, v) in waits if v]
            for (_, v, w) in wl:
                waits_seen.append((w, v))
            step = 0
            sem = None
            if inc_name:
                c = cnt[inc_name]
                c[2] += c[1]
                step = c[1]
                sem = c[0]
            tgt = cnt[inc_name][2] if inc_name else None

            def g(eng_obj, f=f, wl=wl, sem=sem, step=step):
                for (s, v, _) in wl:
                    eng_obj.wait_ge(s, v)
                ins = f(eng_obj)
                if sem is not None:
                    ins.then_inc(sem, step)
            E[eng].append(g)
            return tgt

        emit("vector", lambda e: e.memset(ones[:, :], 1.0))
        ve_init = emit("vector", lambda e: e.memset(gsc[:, 2:3], EPS_LN), inc_name="VE")
        eps_col = gsc[:, 2:3]
        ld_ident = emit("sync", lambda e: e.dma_start(out=ident[:, :], in_=id_d), inc_name="LD")
        ld_bq = emit("sync", lambda e: e.dma_start(out=ar(0), in_=bq_d), inc_name="LD")
        ld_bk = emit("sync", lambda e: e.dma_start(out=ar(1), in_=bk_d), inc_name="LD")
        ld_gam = emit("sync", lambda e: e.dma_start(out=gsc[0:1, 0:1], in_=gam_d), inc_name="LD")
        ld_bv8 = emit("sync", lambda e: e.dma_start(out=big2[0:8, 0:128], in_=bv8_d), inc_name="LD")
        ld_b132 = emit("sync", lambda e: e.dma_start(out=big2[0:32, 128:256], in_=b1_d), inc_name="LD")
        t_bv = emit("tensor", lambda e: e.transpose(ps6[:, 0:8], big2[0:8, 0:128], ident[0:8, 0:8]),
                    waits=[("LD", ld_bv8), ("LD", ld_ident)], inc_name="MM")
        t_b1 = emit("tensor", lambda e: e.transpose(ps7[:, 0:32], big2[0:32, 128:256], ident[0:32, 0:32]),
                    waits=[("LD", ld_b132)], inc_name="MM")
        v_bv = emit("vector", lambda e: e.tensor_copy(bv_col, ps6[:, 0:8]), waits=[("MM", t_bv)],
                    inc_name="VE")
        v_b1 = emit("vector", lambda e: e.tensor_copy(cols[:, 24:56], ps7[:, 0:32]),
                    waits=[("MM", t_b1)], inc_name="VE")

        x_ld = [0] * NCH
        xn_ac = [0] * NCH
        sp_done = [0] * NCH
        ev_done = [0] * NCH
        cs_mm_c = [0] * NCH
        xn_sp = [0] * NCH
        tr_ev = {0: v_bv, 1: v_b1}
        w2_in = {0: 0, 1: 0}; w2_lsub = {0: 0, 1: 0}; w2_out = {0: 0, 1: 0}
        for c in range(NCH):
            w = [("AC", xn_ac[c - 2])] if c >= 2 else []
            x_ld[c] = emit("sync", lambda e, c=c: e.dma_start(
                out=stC(c % 2), in_=x_d[c * 128:(c + 1) * 128, :]), waits=w, inc_name="LD")
            b = (c % 2) * 16
            sum_c = mcol[:, b + 0:b + 1]; sq0 = mcol[:, b + 1:b + 2]; sq1 = mcol[:, b + 2:b + 3]
            mean_c = mcol[:, b + 3:b + 4]; msq = mcol[:, b + 4:b + 5]; var_c = mcol[:, b + 5:b + 6]
            std_c = mcol[:, b + 6:b + 7]
            rstd_c = mcol[:, 32 + c:33 + c]
            negmr_c = mcol[:, 48 + c:49 + c]
            emit("vector", lambda e, s=sum_c, c=c: e.reduce_sum(s, stC(c % 2), axis=AX.X),
                 waits=[("LD", x_ld[c])])
            emit("scalar", lambda e, s=sq0, c=c: e.activation(
                junk[:, :], stC(c % 2)[:, 0:512], AF.Square, accum_out=s),
                waits=[("LD", x_ld[c])], inc_name="AC")
            ac2 = emit("scalar", lambda e, s=sq1, c=c: e.activation(
                junk[:, :], stC(c % 2)[:, 512:1024], AF.Square, accum_out=s), inc_name="AC")
            emit("vector", lambda e, m=mean_c, s=sum_c: e.tensor_scalar(
                m, s, 1.0 / D, None, ALU.mult, ALU.bypass), waits=[("AC", ac2)])
            emit("vector", lambda e, q=sq0, q1=sq1, o=msq: e.tensor_tensor(o, q, q1, ALU.add))
            emit("vector", lambda e, m=mean_c, o=var_c: e.scalar_tensor_tensor(
                o, m, -1.0, m, ALU.mult, ALU.mult))
            v_var = emit("vector", lambda e, q=msq, o=var_c: e.scalar_tensor_tensor(
                o, q, 1.0 / D, o, ALU.mult, ALU.add))
            ac3 = emit("scalar", lambda e, v=var_c, o=std_c: e.activation(
                o, v, AF.Sqrt, bias=eps_col),
                waits=([("VE", ve_init), ("VE", v_var)] if c == 0 else [("VE", v_var)]),
                inc_name="AC")
            emit("vector", lambda e, o=rstd_c, i=std_c: e.reciprocal(o, i), waits=[("AC", ac3)])
            st_ve = emit("vector", lambda e, o=negmr_c, m=mean_c, r=rstd_c: e.scalar_tensor_tensor(
                o, m, -1.0, r, ALU.mult, ALU.mult), inc_name="VE")
            w = [("VE", st_ve)]
            if c >= 2:
                w.append(("MM", cs_mm_c[c - 2]))
                w.append(("VE", xn_sp[c - 2]))
            xn_ac[c] = emit("scalar", lambda e, c=c, r=rstd_c, n=negmr_c: e.activation(
                stA(c % 2), stC(c % 2), AF.Identity, bias=n, scale=r), waits=w, inc_name="AC")
            xb = big3b if c < 8 else big4b
            coff = (c % 8) * 2048
            hsp = emit("scalar", lambda e, xb=xb, co=coff, c=c: e.activation(
                xb[:, co:co + 1024], stA(c % 2), AF.Identity), inc_name="AC")
            xn_sp[c] = emit("vector", lambda e, xb=xb, co=coff, c=c: e.tensor_tensor(
                xb[:, co + 1024:co + 2048], stA(c % 2), xb[:, co:co + 1024],
                ALU.subtract), waits=[("AC", hsp)], inc_name="VE")
            for r in range(FC):
                pt = ps[6 + (r % 2)]
                w = [("AC", xn_ac[c]), ("VE", tr_ev[r % 2])]
                m = emit("tensor", lambda e, p=pt, c=c, r=r: e.transpose(
                    p[:, 0:128], stA(c % 2)[:, r * 128:(r + 1) * 128], ident[:, :]),
                    waits=w, inc_name="MM")
                wv = [("MM", m)]
                if r == 0 and c >= 2:
                    wv.append(("SP", sp_done[c - 2]))
                emit("vector", lambda e, p=pt, c=c, r=r: e.tensor_copy(
                    stBb(c % 2)[:, r * 256:r * 256 + 128], p[:, 0:128]), waits=wv, inc_name="VE")
                ev = emit("vector", lambda e, p=pt, c=c, r=r: e.tensor_tensor(
                    stBb(c % 2)[:, r * 256 + 128:r * 256 + 256], p[:, 0:128],
                    stBb(c % 2)[:, r * 256:r * 256 + 128], ALU.subtract), inc_name="VE")
                tr_ev[r % 2] = ev
            ev_done[c] = ev
            emit("tensor", lambda e, c=c: e.matmul(
                ps4[0:1, :], ones_col, stA(c % 2)[:, 0:512],
                start=(c == 0), stop=(c == NCH - 1)), inc_name="MM")
            cs_mm_c[c] = emit("tensor", lambda e, c=c: e.matmul(
                ps5[0:1, :], ones_col, stA(c % 2)[:, 512:1024],
                start=(c == 0), stop=(c == NCH - 1)), inc_name="MM")
            sp_done[c] = emit("gpsimd", lambda e, c=c: e.dma_start(
                out=xnt_d[c], in_=stB(c % 2)), waits=[("VE", ev_done[c])], inc_name="SP")
            for j in (2 * c, 2 * c + 1):
                buf = j % 2
                srcw = big2[:, buf * 1024:(buf + 1) * 1024]
                dstw = big2b[:, 8192 + buf * 2048:8192 + (buf + 1) * 2048]
                wlw = [("VE", w2_lsub[buf])] if w2_lsub[buf] else [("MM", t_b1)]
                w2_in[buf] = emit("sync", lambda e, j=j, s=srcw: e.dma_start(
                    out=s, in_=w2_d[j * 128:(j + 1) * 128, :]), waits=wlw, inc_name="LD")
                wlh = [("LD", w2_in[buf])]
                if w2_out[buf]:
                    wlh.append(("SP", w2_out[buf]))
                hcp = emit("scalar", lambda e, s=srcw, d=dstw: e.activation(
                    d[:, 0:1024], s, AF.Identity), waits=wlh, inc_name="AC")
                w2_lsub[buf] = emit("vector", lambda e, s=srcw, d=dstw: e.tensor_tensor(
                    d[:, 1024:2048], s, d[:, 0:1024], ALU.subtract), waits=[("AC", hcp)],
                    inc_name="VE")
                w2_out[buf] = emit("gpsimd", lambda e, j=j, d=dstw: e.dma_start(
                    out=w2s_d[j * 128:(j + 1) * 128, :], in_=d),
                    waits=[("VE", w2_lsub[buf])], inc_name="SP")

        w2s_done = w2_out[1]
        emit("vector", lambda e: e.tensor_copy(arh(6, 0), ps4[0:1, :]),
             waits=[("MM", cs_mm_c[NCH - 1])])
        cs_ve = emit("vector", lambda e: e.tensor_copy(arh(6, 1), ps5[0:1, :]), inc_name="VE")

        s_evacs = []
        last_apply = {0: xn_ac[NCH - 2], 1: xn_ac[NCH - 1]}
        last_s_mm = {0: 0, 1: 0}
        split_free = {0: cs_mm_c[NCH - 2], 1: cs_mm_c[NCH - 1]}
        split_kind = {0: "MM", 1: "MM"}
        stB_free = {0: sp_done[NCH - 2], 1: sp_done[NCH - 1]}
        w1_in = 0; w1_lsub = 0; w1_out = 0
        for p_i in range(2):
            for c in range(NCH):
                xb = big3b if c < 8 else big4b
                coff = (c % 8) * 2048
                wmm = []
                if c == 0 and p_i == 0:
                    wmm = [("VE", xn_sp[NCH - 1]), ("VE", cs_ve), ("VE", ev_done[NCH - 1])]
                if c == 0 and p_i == 1:
                    wmm = [("VE", s_evacs[-1])]
                for m in range(4):
                    mg = m + 4 * p_i
                    for pp in range(3):
                        lt = (xb[:, coff + mg * 128:coff + (mg + 1) * 128] if pp < 2
                              else xb[:, coff + 1024 + mg * 128:coff + 1024 + (mg + 1) * 128])
                        rbase = coff if pp != 1 else coff + 1024
                        for hh in range(2):
                            rt = xb[:, rbase + hh * 512:rbase + (hh + 1) * 512]
                            mm = emit("tensor", lambda e, m=m, lt=lt, rt=rt, hh=hh, c=c, pp=pp: e.matmul(
                                pa[m][:, hh * 512:(hh + 1) * 512], lt, rt,
                                start=(c == 0 and pp == 0), stop=(c == NCH - 1 and pp == 2)),
                                waits=(wmm if (m == 0 and pp == 0 and hh == 0) else []),
                                inc_name="MM")
                last_s_mm[c % 2] = mm
                if c % 4 == 0:
                    k = p_i * 4 + c // 4
                    wlw = []
                    if k == 0:
                        wlw = [("VE", w2_lsub[1]), ("SP", w2_out[1]), ("SP", w2_out[0])]
                    elif w1_lsub:
                        wlw = [("VE", w1_lsub), ("SP", w1_out)]
                    w1_in = emit("sync", lambda e, k=k: e.dma_start(
                        out=big2[:, 0:4096], in_=w1_d[k * 128:(k + 1) * 128, :]),
                        waits=wlw, inc_name="LD")
                    for q in range(4):
                        wlh = [("LD", w1_in)] if q == 0 else []
                        if q == 0 and w1_out:
                            wlh.append(("SP", w1_out))
                        hq = emit("scalar", lambda e, q=q: e.activation(
                            big2b[:, 8192 + q * 2048:8192 + q * 2048 + 1024],
                            big2[:, q * 1024:(q + 1) * 1024], AF.Identity),
                            waits=wlh, inc_name="AC")
                        w1_lsub = emit("vector", lambda e, q=q: e.tensor_tensor(
                            big2b[:, 8192 + q * 2048 + 1024:8192 + (q + 1) * 2048],
                            big2[:, q * 1024:(q + 1) * 1024],
                            big2b[:, 8192 + q * 2048:8192 + q * 2048 + 1024],
                            ALU.subtract), waits=[("AC", hq)], inc_name="VE")
                    w1_out = emit("gpsimd", lambda e, k=k: e.dma_start(
                        out=w1s_d[k * 128:(k + 1) * 128, :], in_=big2b[:, 8192:16384]),
                        waits=[("VE", w1_lsub)], inc_name="SP")
            for m in range(4):
                mg = m + 4 * p_i
                sev = emit("vector", lambda e, m=m, mg=mg: e.tensor_copy(
                    big1[:, mg * 1024:(mg + 1) * 1024], pa[m][:, :]),
                    waits=([("MM", mm)] if m == 0 else []), inc_name="VE")
                s_evacs.append(sev)
        w1s_done = w1_out

        spx = 0
        for r in range(FC):
            spx = emit("gpsimd", lambda e, r=r: e.dma_start(
                out=ccin[r * 128:(r + 1) * 128, :], in_=big1[:, r * 1024:(r + 1) * 1024]),
                waits=([("VE", s_evacs[-1]), ("VE", cs_ve)] if r == 0 else []), inc_name="SP")
        spx = emit("gpsimd", lambda e: e.dma_start(out=ccin[1024:1025, :], in_=ar(6)),
                   inc_name="SP")
        cc_t = emit("gpsimd", lambda e: e.collective_compute(
            "AllReduce", ALU.add, replica_groups=[list(range(8))],
            ins=[ccin_h.ap().opt()], outs=[ccout_h.ap().opt()]),
            waits=[("SP", spx)], inc_name="CC")

        def transpose_1024(src, dst, wait_first):
            evs = []
            for rr in range(FC):
                for cc2 in range(FC):
                    i = rr * FC + cc2
                    pt = ps[i % 4]
                    w = list(wait_first) if i == 0 else []
                    if i >= 4:
                        w.append(("VE", evs[i - 4]))
                    m = emit("tensor", lambda e, p=pt, rr=rr, cc2=cc2, src=src: e.transpose(
                        p[:, 0:128], src[:, rr * 1024 + cc2 * 128: rr * 1024 + (cc2 + 1) * 128],
                        ident[:, :]), waits=w, inc_name="MM")
                    ev = emit("vector", lambda e, p=pt, rr=rr, cc2=cc2, dst=dst: e.tensor_copy(
                        dst[:, cc2 * 1024 + rr * 128: cc2 * 1024 + (rr + 1) * 128], p[:, 0:128]),
                        waits=[("MM", m)], inc_name="VE")
                    evs.append(ev)
            return m, ev

        def mm1024(lhsT_fn, rhs_fn, evac_fn, wait_first, extra_rank1=()):
            evs = {}
            lastmm = 0
            for m in range(FC):
                for h in range(2):
                    i = m * 2 + h
                    pt = ps[i % 4]
                    w = list(wait_first) if i == 0 else []
                    if i >= 4:
                        w.append(("VE", evs[i - 4]))
                    nr = len(extra_rank1)
                    for k in range(FC):
                        lastmm = emit("tensor", lambda e, p=pt, k=k, m=m, h=h: e.matmul(
                            p[:, :], lhsT_fn(k, m), rhs_fn(k, h),
                            start=(k == 0), stop=(k == FC - 1 and nr == 0)),
                            waits=(w if k == 0 else []), inc_name="MM")
                    for j, (lf, rf) in enumerate(extra_rank1):
                        lastmm = emit("tensor", lambda e, p=pt, lf=lf, rf=rf, m=m, h=h,
                                      st=(j == nr - 1): e.matmul(
                            p[:, :], lf(m), rf(h), start=False, stop=st), inc_name="MM")
                    evs[i] = evac_fn(m, h, pt, lastmm)
            return lastmm, evs[15]

        def ev_to(dst):
            def f(m, h, pt, lastmm):
                return emit("vector", lambda e, m=m, h=h, p=pt: e.tensor_copy(
                    dst[:, m * 1024 + h * 512: m * 1024 + (h + 1) * 512], p[:, :]),
                    waits=[("MM", lastmm)], inc_name="VE")
            return f

        def matvec_row(rhs_src, dst_slot, wait_first, lhs_fn):
            last_ev = 0
            for h in range(2):
                for k in range(FC):
                    m = emit("tensor", lambda e, k=k, h=h: e.matmul(
                        ps[4 + h][0:1, :], lhs_fn(k),
                        rhs_src[:, k * 1024 + h * 512: k * 1024 + (h + 1) * 512],
                        start=(k == 0), stop=(k == FC - 1)),
                        waits=(list(wait_first) if (h == 0 and k == 0) else []), inc_name="MM")
                last_ev = emit("vector", lambda e, h=h, m=m, d=dst_slot: e.tensor_copy(
                    arh(d, h), ps[4 + h][0:1, :]), waits=[("MM", m)], inc_name="VE")
            return last_ev

        def row_transpose8(src_row, dst_col8, wait_first):
            last = 0
            for k in range(FC):
                w = list(wait_first) if k == 0 else [("VE", last)]
                m = emit("tensor", lambda e, k=k, s=src_row: e.transpose(
                    ps[4][:, 0:1], s[0:1, k * 128:(k + 1) * 128], ident[0:1, 0:1]),
                    waits=w, inc_name="MM")
                last = emit("vector", lambda e, k=k, m=m, d=dst_col8: e.tensor_copy(
                    d[:, k:k + 1], ps[4][:, 0:1]), waits=[("MM", m)], inc_name="VE")
            return last


        wq_ld = 0
        for r in range(FC):
            wq_ld = emit("sync", lambda e, r=r: e.dma_start(
                out=big3[:, r * 1024:(r + 1) * 1024], in_=wq_d[r * 128:(r + 1) * 128, :]),
                waits=([("MM", last_s_mm[1])] if r == 0 else []), inc_name="LD")
        wqt_mm, wqt_ev = transpose_1024(big3, big2, [
            ("LD", wq_ld), ("VE", s_evacs[-1]), ("SP", w1s_done), ("VE", w1_lsub)])
        wk_ld = 0
        for r in range(FC):
            wk_ld = emit("sync", lambda e, r=r: e.dma_start(
                out=big3[:, r * 1024:(r + 1) * 1024], in_=wk_d[r * 128:(r + 1) * 128, :]),
                waits=([("MM", wqt_mm)] if r == 0 else []), inc_name="LD")
        _, wkt_ev = transpose_1024(big3, big4, [("LD", wk_ld), ("VE", wqt_ev)])
        for r in range(FC):
            emit("sync", lambda e, r=r: e.dma_start(
                out=big1[:, r * 1024:(r + 1) * 1024], in_=ccout[r * 128:(r + 1) * 128, :]),
                waits=([("CC", cc_t)] if r == 0 else []), inc_name="LD")
        gl_ld = emit("sync", lambda e: e.dma_start(out=ar(6), in_=ccout[1024:1025, :]),
                     inc_name="LD")
        csc_ev = row_transpose8(ar(6), cs_col, [("LD", gl_ld), ("VE", wkt_ev)])
        swqt_mm, swqt_ev = mm1024(
            lambda k, m: big1[:, k * 1024 + m * 128: k * 1024 + (m + 1) * 128],
            lambda k, h: big2[:, k * 1024 + h * 512: k * 1024 + (h + 1) * 512],
            ev_to(big3), [("LD", gl_ld), ("VE", wkt_ev)])
        wqcs_ev = matvec_row(big2, 4, [("VE", csc_ev), ("VE", swqt_ev)], lambda k: cs_col[:, k:k + 1])
        pq_ev = 0
        for r in range(FC):
            pq_ev = emit("vector", lambda e, r=r: e.tensor_tensor(
                big3[:, r * 1024:(r + 1) * 1024], big3[:, r * 1024:(r + 1) * 1024],
                big2[:, r * 1024:(r + 1) * 1024], ALU.mult),
                waits=([("VE", swqt_ev)] if r == 0 else []), inc_name="VE")
        qsq_ev = matvec_row(big3, 2, [("VE", pq_ev), ("VE", wqcs_ev)], lambda k: ones_col)
        swkt_mm, swkt_ev = mm1024(
            lambda k, m: big1[:, k * 1024 + m * 128: k * 1024 + (m + 1) * 128],
            lambda k, h: big4[:, k * 1024 + h * 512: k * 1024 + (h + 1) * 512],
            ev_to(big3), [("VE", qsq_ev)])
        wkcs_ev = matvec_row(big4, 5, [("VE", swkt_ev)], lambda k: cs_col[:, k:k + 1])
        pk_ev = 0
        for r in range(FC):
            pk_ev = emit("vector", lambda e, r=r: e.tensor_tensor(
                big4[:, r * 1024:(r + 1) * 1024], big3[:, r * 1024:(r + 1) * 1024],
                big4[:, r * 1024:(r + 1) * 1024], ALU.mult),
                waits=([("VE", wkcs_ev)] if r == 0 else []), inc_name="VE")
        ksq_ev = matvec_row(big4, 3, [("VE", pk_ev)], lambda k: ones_col)

        tmprow = bcast[0:1, :]
        qcs_ve = emit("vector", lambda e: e.scalar_tensor_tensor(
            ar(4), ar(0), CTOT, ar(4), ALU.mult, ALU.add),
            waits=[("VE", ksq_ev), ("LD", ld_bq), ("LD", ld_bk), ("LD", ld_gam)], inc_name="VE")
        emit("vector", lambda e: e.scalar_tensor_tensor(tmprow, ar(0), -CTOT, ar(4), ALU.mult, ALU.add))
        emit("vector", lambda e: e.tensor_tensor(tmprow, tmprow, ar(4), ALU.add))
        emit("vector", lambda e: e.tensor_tensor(tmprow, tmprow, ar(0), ALU.mult))
        emit("vector", lambda e: e.tensor_tensor(ar(2), ar(2), tmprow, ALU.add))
        emit("vector", lambda e: e.scalar_tensor_tensor(tmprow, ar(1), CTOT, ar(5), ALU.mult, ALU.add))
        emit("vector", lambda e: e.tensor_tensor(tmprow, tmprow, ar(5), ALU.add))
        emit("vector", lambda e: e.tensor_tensor(tmprow, tmprow, ar(1), ALU.mult))
        kfull_ve = emit("vector", lambda e: e.tensor_tensor(ar(3), ar(3), tmprow, ALU.add),
                        inc_name="VE")
        acq = emit("scalar", lambda e: e.activation(ar(2), ar(2), AF.Sqrt),
                   waits=[("VE", kfull_ve)], inc_name="AC")
        emit("vector", lambda e: e.tensor_copy(junk[0:1, 0:512], ar(2)[0:1, 0:512]),
             waits=[("AC", acq)])
        qsq_cp = emit("vector", lambda e: e.tensor_copy(w2st[0:1, 0:512], ar(2)[0:1, 512:1024]),
                      inc_name="VE")
        qinv_ve = emit("vector", lambda e: e.reciprocal(ar(2), ar(2)), inc_name="VE")
        ack = emit("scalar", lambda e: e.activation(ar(3), ar(3), AF.Sqrt),
                   waits=[("VE", kfull_ve)], inc_name="AC")
        kinv_ve = emit("vector", lambda e: e.reciprocal(ar(3), ar(3)), waits=[("AC", ack)],
                       inc_name="VE")
        kc_ev = row_transpose8(ar(3), kinv_col, [("VE", kinv_ve), ("VE", qsq_ev)])
        bvk_ve = emit("vector", lambda e: e.tensor_tensor(bvk_col, bv_col, kinv_col, ALU.mult),
                      waits=[("VE", kc_ev), ("VE", v_bv)], inc_name="VE")

        def ev_kg(m, h, pt, lastmm):
            return emit("vector", lambda e, m=m, h=h, p=pt: e.tensor_scalar(
                big4[:, m * 1024 + h * 512: m * 1024 + (h + 1) * 512], p[:, :],
                kinv_col[:, m:m + 1], None, ALU.mult, ALU.bypass),
                waits=[("MM", lastmm)], inc_name="VE")
        gt_mm, kg_ev = mm1024(
            lambda k, m: big3[:, k * 1024 + m * 128: k * 1024 + (m + 1) * 128],
            lambda k, h: big2[:, k * 1024 + h * 512: k * 1024 + (h + 1) * 512],
            ev_kg, [("VE", bvk_ve), ("VE", qcs_ve)],
            extra_rank1=[
                (lambda m: arsl(5, m), lambda h: arh(0, h)),
                (lambda m: arsl(1, m), lambda h: arh(4, h)),
            ])

        kivg_ev = matvec_row(big4, 7, [("VE", kg_ev)], lambda k: ones_col)
        emit("vector", lambda e: e.scalar_tensor_tensor(ar(7), ar(4), EPS_AT, ar(7), ALU.mult, ALU.add),
             waits=[("VE", kivg_ev), ("VE", qinv_ve)])
        emit("vector", lambda e: e.tensor_tensor(ar(7), ar(7), ar(2), ALU.mult))
        emit("vector", lambda e: e.tensor_scalar(ar(7), ar(7), float(D), None, ALU.add, ALU.bypass))
        tail_ve = emit("vector", lambda e: e.reciprocal(ar(5), ar(7)), inc_name="VE")
        emit("vector", lambda e: e.tensor_tensor(ar(6), ar(2), ar(5), ALU.mult),
             waits=[("VE", tail_ve)])
        csc_ve2 = emit("vector", lambda e: e.tensor_scalar(
            ar(6), ar(6), gsc[0:1, 0:1], None, ALU.mult, ALU.bypass), inc_name="VE")
        emit("vector", lambda e: e.tensor_copy(arh(1, 0), junk[0:1, 0:512]),
             waits=[("VE", qsq_cp), ("MM", gt_mm)])
        qsqr_ve = emit("vector", lambda e: e.tensor_copy(arh(1, 1), w2st[0:1, 0:512]),
                       inc_name="VE")
        bv_ld = emit("sync", lambda e: e.dma_start(out=ar(0), in_=bv_d), waits=[("MM", gt_mm)],
                     inc_name="LD")
        bvs_ve = emit("vector", lambda e: e.reduce_sum(gsc[0:1, 1:2], ar(0), axis=AX.X),
                      waits=[("LD", bv_ld)], inc_name="VE")
        bvkg_ev = matvec_row(big4, 4, [("VE", kivg_ev), ("VE", tail_ve), ("VE", v_bv)], lambda k: bv_col[:, k:k + 1])
        emit("vector", lambda e: e.tensor_tensor(ar(4), ar(4), ar(6), ALU.mult),
             waits=[("VE", bvkg_ev), ("VE", csc_ve2), ("VE", bvs_ve)])
        emit("vector", lambda e: e.tensor_scalar(
            ar(0), ar(5), gsc[0:1, 1:2], gsc[0:1, 0:1], ALU.mult, ALU.mult))
        gcon_ve = emit("vector", lambda e: e.tensor_tensor(ar(0), ar(0), ar(4), ALU.add),
                       inc_name="VE")
        bc_mm = emit("tensor", lambda e: e.matmul(ps[4][:, :], ones_row, arh(6, 0),
                                                  start=True, stop=True),
                     waits=[("VE", gcon_ve), ("VE", qsqr_ve)], inc_name="MM")
        bc_mm = emit("tensor", lambda e: e.matmul(ps[5][:, :], ones_row, arh(6, 1),
                                                  start=True, stop=True), inc_name="MM")
        emit("vector", lambda e: e.tensor_copy(bcast[:, 0:512], ps[4][:, :]),
             waits=[("MM", bc_mm)])
        bc_ev = emit("vector", lambda e: e.tensor_copy(bcast[:, 512:1024], ps[5][:, :]),
                     inc_name="VE")
        wv_ld = 0
        for r in range(FC):
            wv_ld = emit("sync", lambda e, r=r: e.dma_start(
                out=big3[:, r * 1024:(r + 1) * 1024], in_=wv_d[r * 128:(r + 1) * 128, :]),
                waits=([("MM", gt_mm)] if r == 0 else []), inc_name="LD")
        wvs_ev = matvec_row(big3, 5, [("LD", wv_ld), ("VE", gcon_ve), ("VE", bc_ev)], lambda k: ones_col)
        def ev_d2(m, h, pt, lastmm):
            return emit("vector", lambda e, m=m, h=h, p=pt: e.tensor_tensor(
                big1[:, m * 1024 + h * 512: m * 1024 + (h + 1) * 512], p[:, :],
                bcast[:, h * 512:(h + 1) * 512], ALU.mult),
                waits=[("MM", lastmm)], inc_name="VE")
        d2_mm, d2_ev = mm1024(
            lambda k, m: big3[:, k * 1024 + m * 128: k * 1024 + (m + 1) * 128],
            lambda k, h: big4[:, k * 1024 + h * 512: k * 1024 + (h + 1) * 512],
            ev_d2, [("VE", wvs_ev)],
            extra_rank1=[(lambda m: arsl(5, m), lambda h: arh(1, h))])
        dg_ve = 0
        for r in range(FC):
            dg_ve = emit("vector", lambda e, r=r: e.tensor_tensor(
                big1[:, r * 1024 + r * 128: r * 1024 + (r + 1) * 128],
                big1[:, r * 1024 + r * 128: r * 1024 + (r + 1) * 128],
                ident[:, :], ALU.add),
                waits=([("VE", d2_ev)] if r == 0 else []), inc_name="VE")
        dsp = 0
        for r in range(FC):
            wls = [("VE", dg_ve), ("MM", d2_mm)] if r == 0 else []
            hq = emit("scalar", lambda e, r=r: e.activation(
                big4b[:, r * 2048:r * 2048 + 1024], big1[:, r * 1024:(r + 1) * 1024],
                AF.Identity), waits=wls, inc_name="AC")
            dsp = emit("vector", lambda e, r=r: e.tensor_tensor(
                big4b[:, r * 2048 + 1024:(r + 1) * 2048], big1[:, r * 1024:(r + 1) * 1024],
                big4b[:, r * 2048:r * 2048 + 1024], ALU.subtract),
                waits=[("AC", hq)], inc_name="VE")
        d2sp_ve = dsp
        b2_ld = emit("sync", lambda e: e.dma_start(out=ar(7), in_=b2_d),
                     waits=[("VE", tail_ve)], inc_name="LD")

        def x1sl(c, h=None):
            t = big2 if c < 8 else big3
            cc = c % 8
            if h is None:
                return t[:, cc * 1024:(cc + 1) * 1024]
            return t[:, cc * 1024 + h * 512: cc * 1024 + (h + 1) * 512]

        att_ve = [0] * NCH
        ln3_nm = [0] * NCH
        att_ev_ph = {}
        last_att_mm = {}
        mm_h = {}
        for c in range(NCH):
            w = []
            if c == 0:
                w = [("MM", d2_mm), ("SP", sp_done[NCH - 1])]
            if c >= 2:
                w.append(("MM", last_att_mm[c - 2]))
            xnt_ld = emit("sync", lambda e, c=c: e.dma_start(out=stB(c % 2), in_=xnt_d[c]),
                          waits=w, inc_name="LD")
            w2_ = []
            if c < 2:
                w2_.append(("AC", last_apply[c % 2]))
            else:
                w2_.append(("VE", att_ve[c - 2]))
            xr_ld = emit("sync", lambda e, c=c: e.dma_start(
                out=stC(c % 2), in_=x_d[c * 128:(c + 1) * 128, :]), waits=w2_, inc_name="LD")
            for h in range(2):
                pt = ps[(2 * c + h) % 4]
                w = [("LD", xnt_ld)]
                if c == 0 and h == 0:
                    w += [("VE", d2sp_ve)]
                if 2 * c + h >= 4:
                    w.append(("VE", att_ev_ph[(2 * c + h) - 4]))
                first = True
                for r in range(FC):
                    for pp in range(3):
                        lt = (stBb(c % 2)[:, r * 256:r * 256 + 128] if pp < 2
                              else stBb(c % 2)[:, r * 256 + 128:r * 256 + 256])
                        rt = (big4b[:, r * 2048 + h * 512: r * 2048 + (h + 1) * 512] if pp != 1
                              else big4b[:, r * 2048 + 1024 + h * 512: r * 2048 + 1024 + (h + 1) * 512])
                        mm = emit("tensor", lambda e, p=pt, lt=lt, rt=rt, st=first: e.matmul(
                            p[:, :], lt, rt, start=st, stop=False),
                            waits=(w if first else []), inc_name="MM")
                        first = False
                mm = emit("tensor", lambda e, p=pt, h=h: e.matmul(
                    p[:, :], ones_row, arh(0, h), start=False, stop=True), inc_name="MM")
                mm_h[h] = mm
            last_att_mm[c] = mm
            b = (c % 2) * 16
            sum0 = mcol[:, b + 0:b + 1]; sum1 = mcol[:, b + 1:b + 2]
            sq0 = mcol[:, b + 2:b + 3]; sq1 = mcol[:, b + 3:b + 4]
            mean_c = mcol[:, b + 4:b + 5]; var_c = mcol[:, b + 5:b + 6]
            std_c = mcol[:, b + 6:b + 7]; rstd_c = mcol[:, b + 7:b + 8]
            negmr_c = mcol[:, b + 8:b + 9]; msc = mcol[:, b + 9:b + 10]
            p0_, p1_ = ps[(2 * c) % 4], ps[(2 * c + 1) % 4]
            t5a = tmp56[:, (c % 2) * 1024:(c % 2) * 1024 + 512]
            t5b = tmp56[:, (c % 2) * 1024 + 512:(c % 2) * 1024 + 1024]
            att_ev_ph[2 * c] = emit("vector", lambda e, t=t5a, p=p0_: e.tensor_copy(t, p[:, :]),
                                    waits=[("MM", mm_h[0])], inc_name="VE")
            ev1 = emit("vector", lambda e, t=t5b, p=p1_: e.tensor_copy(t, p[:, :]),
                       waits=[("MM", mm_h[1])], inc_name="VE")
            att_ev_ph[2 * c + 1] = ev1
            emit("vector", lambda e, s=sum0, t=t5a: e.reduce_sum(s, t, axis=AX.X))
            emit("vector", lambda e, s=sum1, t=t5b: e.reduce_sum(s, t, axis=AX.X))
            emit("scalar", lambda e, s=sq0, t=t5a: e.activation(
                junk[:, :], t, AF.Square, accum_out=s), waits=[("VE", ev1)], inc_name="AC")
            a2 = emit("scalar", lambda e, s=sq1, t=t5b: e.activation(
                junk[:, :], t, AF.Square, accum_out=s), inc_name="AC")
            emit("vector", lambda e, a=sum0, b_=sum1, o=mean_c: e.tensor_tensor(o, a, b_, ALU.add),
                 waits=[("AC", a2)])
            emit("vector", lambda e, o=mean_c: e.tensor_scalar(o, o, 1.0 / D, None, ALU.mult,
                                                               ALU.bypass))
            emit("vector", lambda e, a=sq0, b_=sq1, o=var_c: e.tensor_tensor(o, a, b_, ALU.add))
            emit("vector", lambda e, o=var_c: e.tensor_scalar(o, o, 1.0 / D, None, ALU.mult,
                                                              ALU.bypass))
            emit("vector", lambda e, m=mean_c, o=msc: e.scalar_tensor_tensor(
                o, m, -1.0, m, ALU.mult, ALU.mult))
            v_var = emit("vector", lambda e, o=var_c, s=msc: e.tensor_tensor(o, o, s, ALU.add))
            a3 = emit("scalar", lambda e, v=var_c, o=std_c: e.activation(
                o, v, AF.Sqrt, bias=eps_col), waits=[("VE", v_var)], inc_name="AC")
            emit("vector", lambda e, o=rstd_c, i=std_c: e.reciprocal(o, i), waits=[("AC", a3)])
            nm_ve = emit("vector", lambda e, o=negmr_c, m=mean_c, r=rstd_c: e.scalar_tensor_tensor(
                o, m, -1.0, r, ALU.mult, ALU.mult), inc_name="VE")
            for h in range(2):
                th = tmp56[:, (c % 2) * 1024 + h * 512:(c % 2) * 1024 + (h + 1) * 512]
                emit("vector", lambda e, c=c, h=h, n=negmr_c: e.tensor_scalar(
                    x1sl(c, h), stC(c % 2)[:, h * 512:(h + 1) * 512], n, None,
                    ALU.add, ALU.bypass), waits=[("LD", xr_ld)])
                att_ve[c] = emit("vector", lambda e, c=c, h=h, t=th, r=rstd_c: e.scalar_tensor_tensor(
                    x1sl(c, h), t, r, x1sl(c, h), ALU.mult, ALU.add), inc_name="VE")

        for c in range(NCH):
            b = 64 + (c % 2) * 16
            sum_c = mcol[:, b + 0:b + 1]; sq0 = mcol[:, b + 1:b + 2]; sq1 = mcol[:, b + 2:b + 3]
            mean_c = mcol[:, b + 3:b + 4]; var_c = mcol[:, b + 5:b + 6]
            std_c = mcol[:, b + 6:b + 7]; msc = mcol[:, b + 9:b + 10]
            rstd_c = mcol[:, 96 + c:97 + c]; negmr_c = mcol[:, 112 + c:113 + c]
            w = [("VE", att_ve[c])]
            emit("vector", lambda e, s=sum_c, c=c: e.reduce_sum(s, x1sl(c), axis=AX.X), waits=w)
            emit("scalar", lambda e, s=sq0, c=c: e.activation(
                junk[:, :], x1sl(c, 0), AF.Square, accum_out=s), waits=w, inc_name="AC")
            a2 = emit("scalar", lambda e, s=sq1, c=c: e.activation(
                junk[:, :], x1sl(c, 1), AF.Square, accum_out=s), inc_name="AC")
            emit("vector", lambda e, m=mean_c, s=sum_c: e.tensor_scalar(
                m, s, 1.0 / D, None, ALU.mult, ALU.bypass), waits=[("AC", a2)])
            emit("vector", lambda e, q0=sq0, q1=sq1, o=var_c: e.tensor_tensor(o, q0, q1, ALU.add))
            emit("vector", lambda e, o=var_c: e.tensor_scalar(o, o, 1.0 / D, None, ALU.mult,
                                                              ALU.bypass))
            emit("vector", lambda e, m=mean_c, o=msc: e.scalar_tensor_tensor(
                o, m, -1.0, m, ALU.mult, ALU.mult))
            v_var = emit("vector", lambda e, o=var_c, s=msc: e.tensor_tensor(o, o, s, ALU.add))
            a3 = emit("scalar", lambda e, v=var_c, o=std_c: e.activation(
                o, v, AF.Sqrt, bias=eps_col), waits=[("VE", v_var)], inc_name="AC")
            emit("vector", lambda e, o=rstd_c, i=std_c: e.reciprocal(o, i), waits=[("AC", a3)])
            ln3_nm[c] = emit("vector", lambda e, o=negmr_c, m=mean_c, r=rstd_c: e.scalar_tensor_tensor(
                o, m, -1.0, r, ALU.mult, ALU.mult), inc_name="VE")

        def h1sl(mi, sl=None):
            t = big4 if mi < 16 else big1
            m = mi % 16
            if sl is None:
                return t[:, m * 512:(m + 1) * 512]
            return t[:, m * 512 + sl * 128: m * 512 + (sl + 1) * 128]

        st_done = [0] * NCH
        trv_prev = {0: 0, 1: 0}
        w1_mm_prev = {0: 0, 1: 0}
        w2_mm_prev = {0: 0, 1: 0}
        last_tr3 = {}
        st6 = {"last_tr_ve": 0}
        out_ve_prev = 0

        def do_ln3(g):
            for cc in range(CPG):
                c = g * CPG + cc
                rstd_c = mcol[:, 96 + c:97 + c]; negmr_c = mcol[:, 112 + c:113 + c]
                wa = [("VE", ln3_nm[c])]
                if g == 0 and cc < 2:
                    wa.append(("VE", att_ve[NCH - 2 + cc]))
                key = (g, cc - 2) if cc >= 2 else (g - 1, cc + 2)
                if key in last_tr3:
                    wa.append(("MM", last_tr3[key]))
                x1n_ac = emit("scalar", lambda e, c=c, cc=cc, r=rstd_c, n=negmr_c: e.activation(
                    stC(cc % 2), x1sl(c), AF.Identity, bias=n, scale=r), waits=wa, inc_name="AC")
                for r in range(FC):
                    pt = ps[6 + (r % 2)]
                    w = [("AC", x1n_ac), ("VE", trv_prev[r % 2])]
                    if g == 0 and cc == 0 and r < 2:
                        w.append(("VE", att_ve[NCH - 1]))
                    m = emit("tensor", lambda e, p=pt, cc=cc, r=r: e.transpose(
                        p[:, 0:128], stC(cc % 2)[:, r * 128:(r + 1) * 128], ident[:, :]),
                        waits=w, inc_name="MM")
                    emit("vector", lambda e, p=pt, cc=cc, r=r: e.tensor_copy(
                        x1nTb[:, r * 1024 + cc * 128: r * 1024 + (cc + 1) * 128], p[:, 0:128]),
                        waits=[("MM", m)], inc_name="VE")
                    tv = emit("vector", lambda e, p=pt, cc=cc, r=r: e.tensor_tensor(
                        x1nTb[:, r * 1024 + 512 + cc * 128: r * 1024 + 512 + (cc + 1) * 128],
                        p[:, 0:128],
                        x1nTb[:, r * 1024 + cc * 128: r * 1024 + (cc + 1) * 128],
                        ALU.subtract), inc_name="VE")
                    trv_prev[r % 2] = tv
                last_tr3[(g, cc)] = m
                st6["last_tr_ve"] = tv

        def do_ln4(g, cc, ffn2_done):
            c = g * CPG + cc
            b = (cc % 2) * 16
            sum0 = mcol[:, b + 0:b + 1]; sum1 = mcol[:, b + 1:b + 2]
            sq0 = mcol[:, b + 2:b + 3]; sq1 = mcol[:, b + 3:b + 4]
            mean_c = mcol[:, b + 4:b + 5]; var_c = mcol[:, b + 5:b + 6]
            std_c = mcol[:, b + 6:b + 7]; rstd_c = mcol[:, b + 7:b + 8]
            negmr_c = mcol[:, b + 8:b + 9]; msc = mcol[:, b + 9:b + 10]
            p0_, p1_ = ps[cc * 2], ps[cc * 2 + 1]
            t6a = bcast[:, 0:512]
            t6b = bcast[:, 512:1024]
            emit("vector", lambda e, t=t6a, p=p0_: e.tensor_copy(t, p[:, :]),
                 waits=[("MM", ffn2_done), ("AC", st6.get("ln4_a2", 0))])
            ev1 = emit("vector", lambda e, t=t6b, p=p1_: e.tensor_copy(t, p[:, :]),
                       inc_name="VE")
            emit("vector", lambda e, s=sum0, t=t6a: e.reduce_sum(s, t, axis=AX.X))
            emit("vector", lambda e, s=sum1, t=t6b: e.reduce_sum(s, t, axis=AX.X))
            emit("scalar", lambda e, s=sq0, t=t6a: e.activation(
                junk[:, :], t, AF.Square, accum_out=s), waits=[("VE", ev1)], inc_name="AC")
            a2 = emit("scalar", lambda e, s=sq1, t=t6b: e.activation(
                junk[:, :], t, AF.Square, accum_out=s), inc_name="AC")
            st6["ln4_a2"] = a2
            emit("vector", lambda e, a=sum0, b_=sum1, o=mean_c: e.tensor_tensor(
                o, a, b_, ALU.add), waits=[("AC", a2)])
            emit("vector", lambda e, o=mean_c: e.tensor_scalar(o, o, 1.0 / D, None, ALU.mult,
                                                               ALU.bypass))
            emit("vector", lambda e, a=sq0, b_=sq1, o=var_c: e.tensor_tensor(o, a, b_, ALU.add))
            emit("vector", lambda e, o=var_c: e.tensor_scalar(o, o, 1.0 / D, None, ALU.mult,
                                                              ALU.bypass))
            emit("vector", lambda e, m=mean_c, o=msc: e.scalar_tensor_tensor(
                o, m, -1.0, m, ALU.mult, ALU.mult))
            v_var = emit("vector", lambda e, o=var_c, s=msc: e.tensor_tensor(o, o, s, ALU.add))
            a3 = emit("scalar", lambda e, v=var_c, o=std_c: e.activation(
                o, v, AF.Sqrt, bias=eps_col), waits=[("VE", v_var)], inc_name="AC")
            emit("vector", lambda e, o=rstd_c, i=std_c: e.reciprocal(o, i), waits=[("AC", a3)])
            nm_ve = emit("vector", lambda e, o=negmr_c, m=mean_c, r=rstd_c: e.scalar_tensor_tensor(
                o, m, -1.0, r, ALU.mult, ALU.mult), inc_name="VE")
            for h in range(2):
                t6h = bcast[:, h * 512:(h + 1) * 512]
                emit("vector", lambda e, c=c, h=h, n=negmr_c: e.tensor_scalar(
                    x1sl(c, h), x1sl(c, h), n, None, ALU.add, ALU.bypass),
                    waits=[("VE", nm_ve)])
                ov = emit("vector", lambda e, c=c, h=h, t=t6h, r=rstd_c: e.scalar_tensor_tensor(
                    x1sl(c, h), t, r, x1sl(c, h), ALU.mult, ALU.add), inc_name="VE")
            st_done[c] = emit("gpsimd", lambda e, c=c: e.dma_start(
                out=y_d[c * 128:(c + 1) * 128, :], in_=x1sl(c)),
                waits=[("VE", ov)], inc_name="SP")
            return ev1, ov

        do_ln3(0)
        bank_free_next = {i: att_ev_ph[2 * NCH - 1] for i in range(8)}
        for g in range(NG):
            relu_ac = {}
            bank_free = dict(bank_free_next)
            last_tr_ve = st6["last_tr_ve"]
            for ig in range(4):
                for ka in range(FC):
                    w = [("MM", w1_mm_prev[ka % 2])]
                    if g == 0 and ig == 0 and ka == 0:
                        w.append(("SP", w1s_done))
                    qeng, qsem = ("sync", "LD") if ka % 2 == 0 else ("scalar", "L2")
                    ld = emit(qeng, lambda e, ig=ig, ka=ka: e.dma_start(
                        out=stDb(ka % 2)[:, 0:2048],
                        in_=w1s_d[ka * 128:(ka + 1) * 128, ig * 2048:(ig + 1) * 2048]),
                        waits=w, inc_name=qsem)
                    wmm0 = [(qsem, ld)]
                    if ka == 0:
                        wmm0.append(("VE", last_tr_ve))
                    for mi_l in range(8):
                        wmm = list(wmm0) if mi_l == 0 else []
                        if ka == 0 and bank_free[mi_l]:
                            wmm.append(("VE", bank_free[mi_l]))
                        for pp in range(3):
                            lt = (stDb(ka % 2)[:, mi_l * 128:(mi_l + 1) * 128] if pp < 2
                                  else stDb(ka % 2)[:, 1024 + mi_l * 128:1024 + (mi_l + 1) * 128])
                            rt = (x1nTb[:, ka * 1024:ka * 1024 + 512] if pp != 1
                                  else x1nTb[:, ka * 1024 + 512:ka * 1024 + 1024])
                            mm = emit("tensor", lambda e, p=ps[mi_l], lt=lt, rt=rt, ka=ka, pp=pp: e.matmul(
                                p[:, :], lt, rt,
                                start=(ka == 0 and pp == 0), stop=(ka == FC - 1 and pp == 2)),
                                waits=(wmm if pp == 0 else []), inc_name="MM")
                    w1_mm_prev[ka % 2] = mm
                for mi_l in range(8):
                    mi = ig * 8 + mi_l
                    hb = (big4b if mi < 16 else big1b)
                    hoff = (mi % 16) * 1024
                    rv = emit("vector", lambda e, p=ps[mi_l], mi=mi: e.tensor_scalar(
                        junk[:, 0:512], p[:, :], cols[:, 24 + mi:25 + mi], 0.0, ALU.add, ALU.max),
                        waits=([("MM", mm)] if mi_l == 0 else []), inc_name="VE")
                    hc = emit("scalar", lambda e, hb=hb, hoff=hoff: e.activation(
                        hb[:, hoff:hoff + 512], junk[:, 0:512], AF.Identity),
                        waits=[("VE", rv)], inc_name="AC")
                    ra = emit("vector", lambda e, hb=hb, hoff=hoff: e.tensor_tensor(
                        hb[:, hoff + 512:hoff + 1024], junk[:, 0:512], hb[:, hoff:hoff + 512],
                        ALU.subtract), waits=[("AC", hc)], inc_name="VE")
                    relu_ac[mi] = ra
                    bank_free[mi_l] = rv
                last_psum_free = ra
            for mi in range(32):
                w = [("MM", w2_mm_prev[mi % 2])]
                if g == 0 and mi < 2:
                    w.append(("VE", qsqr_ve))
                qeng, qsem = ("gpsimd", "SP") if mi % 2 == 0 else ("sync", "LD")
                ld = emit(qeng, lambda e, mi=mi: e.dma_start(
                    out=w2stb[:, (mi % 2) * 2048:(mi % 2 + 1) * 2048],
                    in_=w2s_d[mi * 128:(mi + 1) * 128, :]), waits=w, inc_name=qsem)
                wmm0 = [(qsem, ld), ("VE", relu_ac[mi])]
                if mi == 0:
                    wmm0.append(("VE", last_psum_free))
                    wmm0.append(("VE", out_ve_prev))
                hb = (big4b if mi < 16 else big1b)
                hoff = (mi % 16) * 1024
                for cc in range(CPG):
                    for pp in range(3):
                        lt_off = hoff + cc * 128 if pp < 2 else hoff + 512 + cc * 128
                        rbase = (mi % 2) * 2048 + (0 if pp != 1 else 1024)
                        for hh in range(2):
                            rt = w2stb[:, rbase + hh * 512:rbase + (hh + 1) * 512]
                            mm = emit("tensor", lambda e, p=pa[cc], hb=hb, lo=lt_off, rt=rt, hh=hh,
                                      mi=mi, pp=pp: e.matmul(
                                p[:, hh * 512:(hh + 1) * 512], hb[:, lo:lo + 128], rt,
                                start=(mi == 0 and pp == 0), stop=False),
                                waits=(wmm0 if (cc == 0 and pp == 0 and hh == 0) else []),
                                inc_name="MM")
                w2_mm_prev[mi % 2] = mm
            for cc in range(CPG):
                for h in range(2):
                    mm = emit("tensor", lambda e, p=ps[cc * 2 + h], h=h: e.matmul(
                        p[:, :], ones_row, arh(7, h), start=False, stop=True),
                        waits=([("LD", b2_ld)] if (g == 0 and cc == 0 and h == 0) else []),
                        inc_name="MM")
            ffn2_done = mm
            ev3, ov = do_ln4(g, 3, ffn2_done)
            evs = {3: ev3}
            if g + 1 < NG:
                do_ln3(g + 1)
            for cc in (0, 1, 2):
                evs[cc], ov = do_ln4(g, cc, ffn2_done)
            out_ve_prev = ov
            bank_free_next = {}
            for cc in range(4):
                bank_free_next[2 * cc] = evs[cc]
                bank_free_next[2 * cc + 1] = evs[cc]

        emit("gpsimd", lambda e: e.nop(), waits=[("SP", st_done[NCH - 1])])

        totals = {k: v[2] for k, v in cnt.items()}
        for (w, v) in waits_seen:
            assert v <= totals[w], f"wait {w} {v} > total {totals[w]}"

        @blk.sync
        def _(eng):
            for f in E["sync"]:
                f(eng)

        @blk.tensor
        def _(eng):
            for f in E["tensor"]:
                f(eng)

        @blk.vector
        def _(eng):
            for f in E["vector"]:
                f(eng)

        @blk.scalar
        def _(eng):
            for f in E["scalar"]:
                f(eng)

        @blk.gpsimd
        def _(eng):
            for f in E["gpsimd"]:
                f(eng)

    return nc


_NC_CACHE = None


def kernel(L0_attn_x, L0_attn_Wq, L0_attn_bq, L0_attn_Wk, L0_attn_bk,
           L0_attn_Wv, L0_attn_bv, L0_attn_gamma,
           L0_ffn_W1, L0_ffn_b1, L0_ffn_W2, L0_ffn_b2):
    global _NC_CACHE
    x = np.ascontiguousarray(np.asarray(L0_attn_x, dtype=np.float32).reshape(16384, 1024))
    com = dict(
        wq=np.ascontiguousarray(np.asarray(L0_attn_Wq, dtype=np.float32)),
        wk=np.ascontiguousarray(np.asarray(L0_attn_Wk, dtype=np.float32)),
        wv=np.ascontiguousarray(np.asarray(L0_attn_Wv, dtype=np.float32)),
        bq=np.asarray(L0_attn_bq, dtype=np.float32).reshape(1, 1024),
        bk=np.asarray(L0_attn_bk, dtype=np.float32).reshape(1, 1024),
        bv_row=np.asarray(L0_attn_bv, dtype=np.float32).reshape(1, 1024),
        bv8=np.ascontiguousarray(np.asarray(L0_attn_bv, dtype=np.float32).reshape(8, 128)),
        gamma=np.asarray(L0_attn_gamma, dtype=np.float32).reshape(1, 1),
        w1=np.ascontiguousarray(np.asarray(L0_ffn_W1, dtype=np.float32)),
        b1_32=np.ascontiguousarray(np.asarray(L0_ffn_b1, dtype=np.float32).reshape(32, 128)),
        w2=np.ascontiguousarray(np.asarray(L0_ffn_W2, dtype=np.float32)),
        b2=np.asarray(L0_ffn_b2, dtype=np.float32).reshape(1, 1024),
        ident=np.eye(128, dtype=np.float32),
    )
    if _NC_CACHE is None:
        _NC_CACHE = build_core()
    nc = _NC_CACHE
    in_maps = []
    for i in range(8):
        m = dict(com)
        m["x"] = np.ascontiguousarray(x[i * TOK:(i + 1) * TOK, :])
        in_maps.append(m)
    res = run_bass_kernel_spmd(nc, in_maps, core_ids=list(range(8)))
    out = np.concatenate([res.results[i]["y"] for i in range(8)], axis=0)
    return out.reshape(1, 16384, 1024).astype(np.float32)
